# version 02
# baseline (speedup 1.0000x reference)
import numpy as np
import ml_dtypes
import concourse.bass as bass
import concourse.mybir as mybir
from concourse.bass_utils import run_bass_kernel_spmd
from concourse.bass import IndirectOffsetOnAxis

F32 = mybir.dt.float32
BF16 = mybir.dt.bfloat16
I32 = mybir.dt.int32
U16 = mybir.dt.uint16
AF = mybir.ActivationFunctionType
ALU = mybir.AluOpType
AX = mybir.AxisListType
BF = ml_dtypes.bfloat16

NCORE = 8
BATCH, SEQ, D = 16, 2048, 1024
BPC = BATCH // NCORE
TPC = BPC * SEQ
NT = TPC // 128
NDC = D // 128
LNEPS = 1e-5


def f32(x):
    return np.ascontiguousarray(x, dtype=np.float32)


def hT(w):
    return np.ascontiguousarray(np.asarray(w).T)


class P:

    def __init__(self, nc):
        self.nc = nc
        self.sems = {}
        self._ctx = []

    def sem(self, *names):
        for name in names:
            h = self.nc.semaphore(name)
            s = h.__enter__()
            self._ctx.append(h)
            self.sems[name] = s

    def __getitem__(self, k):
        return self.sems[k]

    def close(self):
        for h in reversed(self._ctx):
            h.__exit__(None, None, None)


def emit_ln_v(v, xt, st, sqv, lnf_out):
    su, ssq, mean, var = st[:, 0:1], st[:, 1:2], st[:, 2:3], st[:, 3:4]
    v.tensor_reduce(su, xt, axis=AX.X, op=ALU.add)
    v.tensor_tensor(sqv, xt, xt, ALU.mult)
    v.drain()
    v.tensor_reduce(ssq, sqv, axis=AX.X, op=ALU.add)
    v.tensor_scalar(mean, su, 1.0 / D, None, ALU.mult)
    v.drain()
    v.tensor_scalar(var, ssq, 1.0 / D, LNEPS, ALU.mult, ALU.add)
    v.tensor_mul(su, mean, mean)
    v.drain()
    return v.tensor_sub(var, var, su)


def emit_rstd_s(s, st):
    s.activation(st[:, 4:5], st[:, 3:4], AF.Ln, bias=0.0, scale=1.0)
    s.drain()
    return s.activation(st[:, 4:5], st[:, 4:5], AF.Exp, bias=0.0, scale=-0.5)




def build_ph_l0f():
    nc = bass.Bass("TRN2", target_bir_lowering=False)
    x_in = nc.dram_tensor("x", [TPC, D], F32, kind="ExternalInput").ap()
    gwT = nc.dram_tensor("gwT", [D, 8], F32, kind="ExternalInput").ap()
    weT = nc.dram_tensor("weT", [8, D, D], F32, kind="ExternalInput").ap()
    idm = nc.dram_tensor("idm", [128, 128], BF16, kind="ExternalInput").ap()
    x_out = nc.dram_tensor("xo", [TPC, D], F32, kind="ExternalOutput").ap()

    p = P(nc)
    p.sem("ldx0", "ldx1", "ldw", "ldc", "sq", "stat", "lnr", "tr", "trc", "gsc", "msk",
          "yev", "yco", "fin", "st0", "st1")

    xs = nc.alloc_sbuf_tensor("xs", [128, 2 * D], F32)
    lnf = nc.alloc_sbuf_tensor("lnf", [128, 2 * D], F32)
    lnb = nc.alloc_sbuf_tensor("lnb", [128, 2 * D], BF16)
    lnT = nc.alloc_sbuf_tensor("lnT", [128, 2 * D], BF16)
    sqt = nc.alloc_sbuf_tensor("sqs", [128, 2 * D], F32)
    ot = nc.alloc_sbuf_tensor("ot", [128, 2 * D], F32)
    stt = nc.alloc_sbuf_tensor("stt", [128, 16], F32)
    wb = nc.alloc_sbuf_tensor("wb", [128, 65536], BF16)
    gwb = nc.alloc_sbuf_tensor("gwb", [128, 64], BF16)
    ident = nc.alloc_sbuf_tensor("ident", [128, 128], BF16)
    sct = nc.alloc_sbuf_tensor("sc", [128, 32], F32)
    acct = nc.alloc_sbuf_tensor("acc", [128, 2 * D], F32)
    ptr = nc.alloc_psum_tensor("ptr", [128, D], BF16)
    psc = nc.alloc_psum_tensor("psc", [128, 512], F32)
    pyA = nc.alloc_psum_tensor("pyA", [128, 1024], F32)
    pyB = nc.alloc_psum_tensor("pyB", [128, 1024], F32)
    with nc.Block() as blk:
        @blk.sync
        def _(sync):
            sync.dma_start(out=ident[:], in_=idm[:]).then_inc(p["ldc"], 16)
            for i in range(NT):
                if i >= 2:
                    sync.wait_ge(p["st%d" % (i % 2)], 16 * (i // 2))
                sync.dma_start(out=xs[:, (i % 2) * D:(i % 2 + 1) * D],
                               in_=x_in[i * 128:(i + 1) * 128, :]).then_inc(p["ldx%d" % (i % 2)], 16)

        @blk.gpsimd
        def _(gp):
            for e in range(8):
                for k in range(8):
                    gp.dma_start(out=wb[:, (e * 16 + k * 2) * 512:(e * 16 + k * 2 + 2) * 512],
                                 in_=weT[e, k * 128:(k + 1) * 128, :]).then_inc(p["ldw"], 16)
            for k in range(8):
                gp.dma_start(out=gwb[:, k * 8:(k + 1) * 8],
                             in_=gwT[k * 128:(k + 1) * 128, :]).then_inc(p["ldw"], 16)
            for i in range(NT):
                gp.wait_ge(p["fin"], i + 1)
                gp.dma_start(out=x_out[i * 128:(i + 1) * 128, :],
                             in_=ot[:, (i % 2) * D:(i % 2 + 1) * D]).then_inc(p["st%d" % (i % 2)], 16)
            gp.wait_ge(p["st0"], 16 * (NT // 2))
            gp.wait_ge(p["st1"], 16 * (NT // 2))

        @blk.vector
        def _(v):
            for i in range(NT):
                o = (i % 2) * D
                xt = xs[:, o:o + D]
                st = stt[:, (i % 2) * 8:(i % 2) * 8 + 8]
                v.wait_ge(p["ldx%d" % (i % 2)], 16 * (i // 2 + 1))
                if i >= 2:
                    v.wait_ge(p["st%d" % (i % 2)], 16 * (i // 2))
                emit_ln_v(v, xt, st, sqt[:, o:o + D], None).then_inc(p["stat"], 1)
                v.wait_ge(p["lnr"], i + 1)
                v.tensor_scalar(lnf[:, o:o + D], xt, st[:, 2:3], st[:, 4:5],
                                ALU.subtract, ALU.mult)
                v.drain()
                v.tensor_copy(lnb[:, o:o + D], lnf[:, o:o + D]).then_inc(p["sq"], 1)
                v.wait_ge(p["tr"], i + 1)
                v.tensor_copy(lnT[:, o:o + D], ptr[:, :]).then_inc(p["trc"], 1)
                v.wait_ge(p["gsc"], i + 1)
                s0 = sct[:, (i % 2) * 16:(i % 2) * 16 + 8]
                s1 = sct[:, (i % 2) * 16 + 8:(i % 2) * 16 + 16]
                v.tensor_copy(s0, psc[:, 0:8]).then_inc(p["msk"], 1)
                v.drain()
                v.reduce_max(st[:, 5:6], s0, axis=AX.X)
                v.drain()
                v.tensor_scalar(s0, s0, st[:, 5:6], None, ALU.is_ge)
                v.memset(st[:, 6:7], 0.0)
                v.drain()
                v.tensor_tensor_scan(s1, s0, st[:, 6:7].broadcast_to((128, 8)),
                                     0.0, ALU.add, ALU.add)
                v.drain()
                v.tensor_scalar(s1, s1, 1.0, None, ALU.is_le)
                v.drain()
                v.tensor_mul(s0, s0, s1)
                av = acct[:, o:o + D]
                v.memset(av, 0.0)
                v.drain()
                for e in range(8):
                    v.wait_ge(p["yev"], i * 8 + e + 1)
                    py = pyA if e % 2 == 0 else pyB
                    v.scalar_tensor_tensor(av[:, 0:512], py[:, 0:512], s0[:, e:e + 1],
                                           av[:, 0:512], ALU.mult, ALU.add)
                    v.scalar_tensor_tensor(av[:, 512:1024], py[:, 512:1024], s0[:, e:e + 1],
                                           av[:, 512:1024], ALU.mult, ALU.add).then_inc(p["yco"], 1)
                    v.drain()
                v.tensor_add(av, av, lnf[:, o:o + D])
                v.drain()
                v.scalar_tensor_tensor(ot[:, o:o + D], av, 0.5, xt,
                                       ALU.mult, ALU.add).then_inc(p["fin"], 1)

        @blk.scalar
        def _(s):
            for i in range(NT):
                st = stt[:, (i % 2) * 8:(i % 2) * 8 + 8]
                s.wait_ge(p["stat"], i + 1)
                emit_rstd_s(s, st).then_inc(p["lnr"], 1)

        @blk.tensor
        def _(t):
            t.wait_ge(p["ldw"], 16 * 72)
            t.wait_ge(p["ldc"], 16)
            for i in range(NT):
                o = (i % 2) * D
                t.wait_ge(p["sq"], i + 1)
                if i >= 1:
                    t.wait_ge(p["trc"], i)
                for k in range(NDC):
                    ins = t.transpose(ptr[:, k * 128:(k + 1) * 128],
                                      lnb[:, o + k * 128:o + (k + 1) * 128], ident[:])
                ins.then_inc(p["tr"], 1)
                t.wait_ge(p["trc"], i + 1)
                if i >= 1:
                    t.wait_ge(p["msk"], i)
                for k in range(8):
                    ins = t.matmul(psc[:, 0:8], lnT[:, o + k * 128:o + (k + 1) * 128],
                                   gwb[:, k * 8:(k + 1) * 8], start=(k == 0), stop=(k == 7))
                ins.then_inc(p["gsc"], 1)
                for e in range(8):
                    if i * 8 + e >= 2:
                        t.wait_ge(p["yco"], i * 8 + e - 1)
                    py = pyA if e % 2 == 0 else pyB
                    for half in range(2):
                        for k in range(8):
                            ins = t.matmul(py[:, half * 512:(half + 1) * 512],
                                           lnT[:, o + k * 128:o + (k + 1) * 128],
                                           wb[:, (e * 16 + k * 2 + half) * 512:
                                              (e * 16 + k * 2 + half + 1) * 512],
                                           start=(k == 0), stop=(k == 7))
                    ins.then_inc(p["yev"], 1)

    p.close()
    return nc






def build_ph_l2f():
    nc = bass.Bass("TRN2", target_bir_lowering=False)
    x_in = nc.dram_tensor("x", [TPC, D], F32, kind="ExternalInput").ap()
    protoT = nc.dram_tensor("protoT", [D, 32], F32, kind="ExternalInput").ap()
    BaT = nc.dram_tensor("BaT", [D, D], F32, kind="ExternalInput").ap()
    Aa = nc.dram_tensor("Aa", [D, D], F32, kind="ExternalInput").ap()
    biasA = nc.dram_tensor("biasA", [32, D], F32, kind="ExternalInput").ap()
    tempv = nc.dram_tensor("tempv", [1, 1], F32, kind="ExternalInput").ap()
    idm = nc.dram_tensor("idm", [128, 128], BF16, kind="ExternalInput").ap()
    x_out = nc.dram_tensor("xo", [TPC, D], F32, kind="ExternalOutput").ap()

    p = P(nc)
    p.sem("ldx0", "ldx1", "ldw", "ldc", "stat", "lnr", "sq", "tr", "trc", "pd", "dst",
          "sqd", "rt", "rtb", "wm", "wme", "wmt", "wmtc", "wmtd", "hmm", "hme",
          "hmc", "htr", "htrc", "yac", "fin", "st0", "st1", "ldx20", "ldx21",
          "p2a", "p2b", "p2c", "p2d", "rtcs")

    xs = nc.alloc_sbuf_tensor("xs", [128, 2 * D], F32)
    lnf = nc.alloc_sbuf_tensor("lnf", [128, 2 * D], F32)
    lnb = nc.alloc_sbuf_tensor("lnb", [128, D], BF16)
    lnTr = nc.alloc_sbuf_tensor("lnTr", [128, NT * D], BF16)
    dstr = nc.alloc_sbuf_tensor("dstr", [128, NT * 32], F32)
    sqt = nc.alloc_sbuf_tensor("sqs", [128, 2 * D], F32)
    ot = nc.alloc_sbuf_tensor("ot", [128, 2 * D], F32)
    stt = nc.alloc_sbuf_tensor("stt", [128, 32], F32)
    BaTb = nc.alloc_sbuf_tensor("BaTb", [128, NDC * D], BF16)
    Aab = nc.alloc_sbuf_tensor("Aab", [128, NDC * D], BF16)
    bsb = nc.alloc_sbuf_tensor("bsb", [32, D], BF16)
    ptf = nc.alloc_sbuf_tensor("ptf", [128, 256], F32)
    ptb = nc.alloc_sbuf_tensor("ptb", [128, 256], BF16)
    onesf = nc.alloc_sbuf_tensor("onesf", [128, 128], F32)
    p2r = nc.alloc_sbuf_tensor("p2r", [128, 32], F32)
    rtc = nc.alloc_sbuf_tensor("rtc", [128, 4], F32)
    wmk = nc.alloc_sbuf_tensor("wmask", [128, 64], F32)
    wmb = nc.alloc_sbuf_tensor("wmb", [128, 32], BF16)
    wmtb = nc.alloc_sbuf_tensor("wmtb", [32, 128], BF16)
    hm = nc.alloc_sbuf_tensor("hm", [128, D], BF16)
    hmT = nc.alloc_sbuf_tensor("hmT", [128, D], BF16)
    ident = nc.alloc_sbuf_tensor("ident", [128, 128], BF16)
    tmps = nc.alloc_sbuf_tensor("tmps", [1, 4], F32)
    ptr = nc.alloc_psum_tensor("ptr", [128, D], BF16)
    pdd = nc.alloc_psum_tensor("pdd", [128, 512], F32)
    ph = nc.alloc_psum_tensor("ph", [128, 1024], F32)
    py = nc.alloc_psum_tensor("py", [128, 1024], F32)
    pmisc = nc.alloc_psum_tensor("pmisc", [128, 512], F32)
    pwt = nc.alloc_psum_tensor("pwt", [32, 1024], BF16)
    with nc.Block() as blk:
        @blk.sync
        def _(sync):
            sync.dma_start(out=ident[:], in_=idm[:]).then_inc(p["ldc"], 16)
            sync.dma_start(out=tmps[0:1, 0:1], in_=tempv[:, :]).then_inc(p["ldc"], 16)
            for k in range(NDC):
                sync.dma_start(out=ptf[:, k * 32:(k + 1) * 32],
                               in_=protoT[k * 128:(k + 1) * 128, :]).then_inc(p["ldc"], 16)
            for i in range(NT):
                if i >= 2:
                    sync.wait_ge(p["sq"], i - 1)
                sync.dma_start(out=xs[:, (i % 2) * D:(i % 2 + 1) * D],
                               in_=x_in[i * 128:(i + 1) * 128, :]).then_inc(p["ldx%d" % (i % 2)], 16)
            for i in range(NT):
                if i < 2:
                    sync.wait_ge(p["sq"], NT)
                else:
                    sync.wait_ge(p["st%d" % (i % 2)], 16 * (i // 2))
                sync.dma_start(out=xs[:, (i % 2) * D:(i % 2 + 1) * D],
                               in_=x_in[i * 128:(i + 1) * 128, :]).then_inc(p["ldx2%d" % (i % 2)], 16)

        @blk.gpsimd
        def _(gp):
            for k in range(NDC):
                gp.dma_start(out=BaTb[:, k * D:(k + 1) * D],
                             in_=BaT[k * 128:(k + 1) * 128, :]).then_inc(p["ldw"], 16)
                gp.dma_start(out=Aab[:, k * D:(k + 1) * D],
                             in_=Aa[k * 128:(k + 1) * 128, :]).then_inc(p["ldw"], 16)
                gp.dma_start(out=ptb[:, k * 32:(k + 1) * 32],
                             in_=protoT[k * 128:(k + 1) * 128, :]).then_inc(p["ldw"], 16)
            gp.dma_start(out=bsb[:, :], in_=biasA[:, :]).then_inc(p["ldw"], 16)
            for i in range(NT):
                gp.wait_ge(p["fin"], i + 1)
                gp.dma_start(out=x_out[i * 128:(i + 1) * 128, :],
                             in_=ot[:, (i % 2) * D:(i % 2 + 1) * D]).then_inc(p["st%d" % (i % 2)], 16)
            gp.wait_ge(p["st0"], 16 * (NT // 2))
            gp.wait_ge(p["st1"], 16 * (NT // 2))

        @blk.vector
        def _(v):
            v.wait_ge(p["ldc"], 16 * (2 + NDC))
            v.memset(onesf[:, :], 1.0)
            for k in range(NDC):
                ins = v.tensor_mul(ptf[:, k * 32:(k + 1) * 32], ptf[:, k * 32:(k + 1) * 32],
                                   ptf[:, k * 32:(k + 1) * 32])
            ins.then_inc(p["p2a"], 1)
            v.drain()
            v.tensor_scalar(tmps[0:1, 1:2], tmps[0:1, 0:1], -1.0, None, ALU.mult)
            v.drain()
            v.tensor_max(tmps[0:1, 2:3], tmps[0:1, 0:1], tmps[0:1, 1:2])
            v.drain()
            v.tensor_scalar(tmps[0:1, 2:3], tmps[0:1, 2:3], 0.1, None, ALU.max)
            v.drain()
            v.reciprocal(tmps[0:1, 3:4], tmps[0:1, 2:3]).then_inc(p["rt"], 1)
            v.wait_ge(p["rtb"], 1)
            v.tensor_copy(rtc[:, 0:1], pmisc[:, 0:1]).then_inc(p["rtcs"], 1)
            v.wait_ge(p["p2b"], 1)
            v.tensor_copy(p2r[0:1, :], pmisc[0:1, 32:64]).then_inc(p["p2c"], 1)
            v.wait_ge(p["p2d"], 1)
            v.tensor_copy(p2r[:, :], pmisc[:, 64:96])
            v.drain()
            v.tensor_scalar(rtc[:, 1:2], rtc[:, 0:1], -1.0, None, ALU.mult)
            v.drain()
            for i in range(NT):
                o = (i % 2) * D
                xt = xs[:, o:o + D]
                st = stt[:, (i % 2) * 8:(i % 2) * 8 + 8]
                v.wait_ge(p["ldx%d" % (i % 2)], 16 * (i // 2 + 1))
                emit_ln_v(v, xt, st, sqt[:, o:o + D], None).then_inc(p["stat"], 1)
                v.wait_ge(p["lnr"], i + 1)
                v.reciprocal(st[:, 4:5], st[:, 4:5])
                v.drain()
                v.tensor_scalar(lnf[:, o:o + D], xt, st[:, 2:3], st[:, 4:5],
                                ALU.subtract, ALU.mult)
                v.drain()
                v.tensor_copy(lnb[:, :], lnf[:, o:o + D]).then_inc(p["sq"], 1)
                v.tensor_mul(sqt[:, o:o + D], lnf[:, o:o + D], lnf[:, o:o + D])
                v.drain()
                v.tensor_reduce(st[:, 5:6], sqt[:, o:o + D], axis=AX.X, op=ALU.add)
                v.wait_ge(p["tr"], i + 1)
                v.tensor_copy(lnTr[:, i * D:(i + 1) * D], ptr[:, :]).then_inc(p["trc"], 1)
                v.wait_ge(p["pd"], i + 1)
                dsl = dstr[:, i * 32:(i + 1) * 32]
                v.drain()
                v.tensor_scalar(dsl, pdd[:, 0:32], -2.0, st[:, 5:6], ALU.mult, ALU.add)
                v.drain()
                v.tensor_add(dsl, dsl, p2r[:, :]).then_inc(p["hmm"], 1)
                v.drain()
                v.tensor_scalar(dsl, dsl, 0.0, None, ALU.max).then_inc(p["dst"], 1)
            for i in range(NT):
                o = (i % 2) * D
                st = stt[:, (i % 2) * 8:(i % 2) * 8 + 8]
                dsl = dstr[:, i * 32:(i + 1) * 32]
                wcur = wmk[:, 0:32]
                wsec = wmk[:, 32:64]
                v.wait_ge(p["sqd"], i + 1)
                v.tensor_reduce(st[:, 0:1], dsl, axis=AX.X, op=ALU.min)
                v.drain()
                v.tensor_mul(st[:, 1:2], st[:, 0:1], rtc[:, 0:1]).then_inc(p["wm"], 1)
                v.wait_ge(p["wme"], i + 1)
                v.reduce_max(st[:, 2:3], wcur, axis=AX.X)
                v.drain()
                v.tensor_scalar(wsec, wcur, st[:, 2:3], None, ALU.is_equal)
                v.drain()
                v.scalar_tensor_tensor(wsec, wsec, -1e30, wcur, ALU.mult, ALU.add)
                v.drain()
                v.reduce_max(st[:, 3:4], wsec, axis=AX.X)
                v.drain()
                v.tensor_add(st[:, 4:5], st[:, 2:3], st[:, 3:4])
                v.tensor_scalar(wsec, wcur, st[:, 3:4], None, ALU.is_ge)
                v.drain()
                v.reciprocal(st[:, 5:6], st[:, 4:5])
                v.tensor_mul(wcur, wcur, wsec)
                v.drain()
                v.tensor_scalar(wcur, wcur, st[:, 5:6], None, ALU.mult)
                v.drain()
                v.tensor_copy(wmb[:, :], wcur).then_inc(p["wmt"], 1)
                v.wait_ge(p["hme"], i + 1)
                for pr in range(32):
                    ins = v.tensor_scalar(hm[:, pr * 32:(pr + 1) * 32], ph[:, pr * 32:(pr + 1) * 32],
                                          wcur[:, pr:pr + 1], None, ALU.mult)
                ins.then_inc(p["hmc"], 1)
                v.drain()
                v.wait_ge(p["htr"], i + 1)
                v.tensor_copy(hmT[:, :], ptr[:, :]).then_inc(p["htrc"], 1)
                v.wait_ge(p["wmtc"], i + 1)
                v.tensor_copy(wmtb[:, :], pwt[:, 0:128]).then_inc(p["wmtd"], 1)
                v.wait_ge(p["yac"], i + 1)
                v.wait_ge(p["ldx2%d" % (i % 2)], 16 * (i // 2 + 1))
                if i >= 2:
                    v.wait_ge(p["st%d" % (i % 2)], 16 * (i // 2))
                v.scalar_tensor_tensor(ot[:, o:o + 512], py[:, 0:512], 0.5,
                                       xs[:, o:o + 512], ALU.mult, ALU.add)
                v.scalar_tensor_tensor(ot[:, o + 512:o + D], py[:, 512:1024], 0.5,
                                       xs[:, o + 512:o + D], ALU.mult, ALU.add).then_inc(p["fin"], 1)

        @blk.scalar
        def _(s):
            for i in range(NT):
                st = stt[:, (i % 2) * 8:(i % 2) * 8 + 8]
                s.wait_ge(p["stat"], i + 1)
                s.activation(st[:, 4:5], st[:, 3:4], AF.Sqrt, bias=0.0, scale=1.0).then_inc(p["lnr"], 1)
                s.wait_ge(p["dst"], i + 1)
                dsl = dstr[:, i * 32:(i + 1) * 32]
                s.activation(dsl, dsl, AF.Sqrt, bias=0.0, scale=1.0).then_inc(p["sqd"], 1)
            for i in range(NT):
                st = stt[:, (i % 2) * 8:(i % 2) * 8 + 8]
                dsl = dstr[:, i * 32:(i + 1) * 32]
                s.wait_ge(p["wm"], i + 1)
                if i >= 1:
                    s.wait_ge(p["hmc"], i)
                s.activation(wmk[:, 0:32], dsl, AF.Exp, bias=st[:, 1:2],
                             scale=rtc[:, 1:2]).then_inc(p["wme"], 1)

        @blk.tensor
        def _(t):
            t.wait_ge(p["ldw"], 16 * 25)
            t.wait_ge(p["ldc"], 16 * (2 + NDC))
            t.wait_ge(p["rt"], 1)
            t.matmul(pmisc[:, 0:1], onesf[0:1, :], tmps[0:1, 3:4],
                     start=True, stop=True).then_inc(p["rtb"], 1)
            t.wait_ge(p["p2a"], 1)
            t.wait_ge(p["rtcs"], 1)
            for k in range(NDC):
                ins = t.matmul(pmisc[0:1, 32:64], onesf[:, 0:1], ptf[:, k * 32:(k + 1) * 32],
                               start=(k == 0), stop=(k == NDC - 1))
            ins.then_inc(p["p2b"], 1)
            t.wait_ge(p["p2c"], 1)
            t.matmul(pmisc[:, 64:96], onesf[0:1, :], p2r[0:1, :],
                     start=True, stop=True).then_inc(p["p2d"], 1)
            for i in range(NT):
                o = (i % 2) * D
                t.wait_ge(p["sq"], i + 1)
                if i >= 1:
                    t.wait_ge(p["trc"], i)
                for k in range(NDC):
                    ins = t.transpose(ptr[:, k * 128:(k + 1) * 128],
                                      lnb[:, k * 128:(k + 1) * 128], ident[:])
                ins.then_inc(p["tr"], 1)
                t.wait_ge(p["trc"], i + 1)
                if i >= 1:
                    t.wait_ge(p["hmm"], i)
                for k in range(NDC):
                    ins = t.matmul(pdd[:, 0:32], lnTr[:, i * D + k * 128:i * D + (k + 1) * 128],
                                   ptb[:, k * 32:(k + 1) * 32], start=(k == 0), stop=(k == NDC - 1))
                ins.then_inc(p["pd"], 1)
            for i in range(NT):
                if i >= 1:
                    t.wait_ge(p["hmc"], i)
                for half in range(2):
                    for k in range(NDC):
                        ins = t.matmul(ph[:, half * 512:(half + 1) * 512],
                                       lnTr[:, i * D + k * 128:i * D + (k + 1) * 128],
                                       BaTb[:, k * D + half * 512:k * D + (half + 1) * 512],
                                       start=(k == 0), stop=(k == NDC - 1))
                ins.then_inc(p["hme"], 1)
                t.wait_ge(p["wmt"], i + 1)
                if i >= 1:
                    t.wait_ge(p["wmtd"], i)
                t.transpose(pwt[:, 0:128], wmb[:, :], ident[:]).then_inc(p["wmtc"], 1)
                t.wait_ge(p["hmc"], i + 1)
                if i >= 1:
                    t.wait_ge(p["htrc"], i)
                for k in range(NDC):
                    ins = t.transpose(ptr[:, k * 128:(k + 1) * 128],
                                      hm[:, k * 128:(k + 1) * 128], ident[:])
                ins.then_inc(p["htr"], 1)
                t.wait_ge(p["htrc"], i + 1)
                t.wait_ge(p["wmtd"], i + 1)
                if i >= 1:
                    t.wait_ge(p["fin"], i)
                for half in range(2):
                    for k in range(NDC):
                        t.matmul(py[:, half * 512:(half + 1) * 512],
                                 hmT[:, k * 128:(k + 1) * 128],
                                 Aab[:, k * D + half * 512:k * D + (half + 1) * 512],
                                 start=(k == 0), stop=False)
                    ins = t.matmul(py[:, half * 512:(half + 1) * 512], wmtb[:, :],
                                   bsb[:, half * 512:(half + 1) * 512],
                                   start=False, stop=True)
                ins.then_inc(p["yac"], 1)

    p.close()
    return nc


def maps_l2f(inputs, xsh):
    proto = f32(inputs["L2_ffn_prototypes"])
    Bm = f32(inputs["L2_ffn_B"])
    Am = f32(inputs["L2_ffn_A"])
    bias = f32(inputs["L2_ffn_bias"])
    Baug = np.zeros((32, 32, D), np.float32)
    Baug[:, :16, :] = Bm
    BaT = hT(Baug.reshape(1024, D))
    Aaug = np.zeros((32, 32, D), np.float32)
    Aaug[:, :16, :] = np.transpose(Am, (0, 2, 1))
    Aa = f32(Aaug.reshape(1024, D))
    tempv = f32(inputs["L2_ffn_temp"]).reshape(1, 1)
    pT = hT(proto)
    return [{"x": xsh[c], "protoT": pT, "BaT": BaT, "Aa": Aa, "biasA": bias,
             "tempv": tempv, "idm": _IDM} for c in range(NCORE)]





L0A_NTB = SEQ // 128


def build_ph_l0a():
    nc = bass.Bass("TRN2", target_bir_lowering=False)
    x_in = nc.dram_tensor("x", [TPC, D], F32, kind="ExternalInput").ap()
    wqT = nc.dram_tensor("wqT", [D + 128, D], F32, kind="ExternalInput").ap()
    wkT = nc.dram_tensor("wkT", [D + 128, D], F32, kind="ExternalInput").ap()
    wvT = nc.dram_tensor("wvT", [D + 128, D], F32, kind="ExternalInput").ap()
    woT = nc.dram_tensor("woT", [D + 128, D], F32, kind="ExternalInput").ap()
    bmask = nc.dram_tensor("bmask", [4, 128, 512], BF16, kind="ExternalInput").ap()
    idm = nc.dram_tensor("idm", [128, 128], BF16, kind="ExternalInput").ap()
    x_out = nc.dram_tensor("xo", [TPC, D], F32, kind="ExternalOutput").ap()
    o_dram = nc.dram_tensor("odr", [TPC, D], BF16, kind="Internal").ap()

    NTB = L0A_NTB
    HGN = 8
    SCALE = 1.0 / np.sqrt(128.0)
    p = P(nc)
    p.sem("ldc", "ldw", "ldx0", "ldx1", "xl0", "xl1", "stat", "lnr", "qnr",
          "sq", "tr", "trc", "qkv", "qke", "nsq", "qnm", "knm", "qtr", "qtrc",
          "ktr", "ktrc", "sTr", "aTr", "avu", "avd", "oev", "otr", "otrc",
          "opj", "ope", "fin", "st0", "st1", "on1", "ogd", "ogs", "obl", "obc")

    xs = nc.alloc_sbuf_tensor("xs", [128, 2 * D], F32)
    xr = nc.alloc_sbuf_tensor("xr", [128, 2 * D], F32)
    lnb = nc.alloc_sbuf_tensor("lnb", [128, 2 * D], BF16)
    sqt = nc.alloc_sbuf_tensor("sqs", [128, D], F32)
    qf = nc.alloc_sbuf_tensor("qf", [128, 512], F32)
    ot = nc.alloc_sbuf_tensor("ot", [128, 2 * D], F32)
    stt = nc.alloc_sbuf_tensor("stt", [128, 16], F32)
    nst = nc.alloc_sbuf_tensor("nst", [128, 64], F32)
    lnT = nc.alloc_sbuf_tensor("lnT", [128, NTB * D], BF16)
    wq = nc.alloc_sbuf_tensor("wq", [128, 8 * D], BF16)
    wk = nc.alloc_sbuf_tensor("wk", [128, 8 * D], BF16)
    wv = nc.alloc_sbuf_tensor("wv", [128, 8 * D], BF16)
    wo = nc.alloc_sbuf_tensor("wo", [128, 8 * D], BF16)
    biasb = nc.alloc_sbuf_tensor("biasb", [1, 4 * D], BF16)
    qn = nc.alloc_sbuf_tensor("qn", [128, 512], BF16)
    kn = nc.alloc_sbuf_tensor("kn", [128, 512], BF16)
    qT = nc.alloc_sbuf_tensor("qT", [128, 4 * SEQ], BF16)
    kT = nc.alloc_sbuf_tensor("kT", [128, 4 * SEQ], BF16)
    vab = nc.alloc_sbuf_tensor("vab", [128, NTB * 520], BF16)
    aT = nc.alloc_sbuf_tensor("aT", [128, 2 * 512], BF16)
    ident = nc.alloc_sbuf_tensor("ident", [128, 128], BF16)
    bmk = nc.alloc_sbuf_tensor("bmk", [128, 4 * 512], BF16)
    ones1 = nc.alloc_sbuf_tensor("ones1", [1, 128], BF16)
    og = nc.alloc_sbuf_tensor("og", [128, 4 * 512], BF16)
    obt = nc.alloc_sbuf_tensor("obt", [128, D], BF16)
    ptr = nc.alloc_psum_tensor("ptr", [128, 1024], BF16)
    ppA = nc.alloc_psum_tensor("ppA", [128, 512], F32)
    ppB = nc.alloc_psum_tensor("ppB", [128, 512], F32)
    psT0 = nc.alloc_psum_tensor("psT0", [128, 512], F32)
    psT1 = nc.alloc_psum_tensor("psT1", [128, 512], F32)
    pso0 = nc.alloc_psum_tensor("pso0", [128, 512], F32)
    pso1 = nc.alloc_psum_tensor("pso1", [128, 512], F32)

    sched = [(h, qc) for qc in range(4) for h in range(HGN)]

    def kmax(qc):
        return 4 * qc + 4

    n_kt = sum(kmax(qc) for _, qc in sched)
    band_cum = []
    nb = 0
    for (_h, qc) in sched:
        for kt in range(kmax(qc)):
            nb += 1 if kt >= 4 * qc else 0
            band_cum.append(nb)
    n_band = nb
    NU = len(sched)
    unit_end = []
    _c = 0
    for (_h, qc) in sched:
        _c += kmax(qc)
        unit_end.append(_c)

    with nc.Block() as blk:
        @blk.sync
        def _(sync):
            sync.dma_start(out=ident[:], in_=idm[:]).then_inc(p["ldc"], 16)
            for o in range(4):
                sync.dma_start(out=bmk[:, o * 512:(o + 1) * 512],
                               in_=bmask[o]).then_inc(p["ldc"], 16)
            for b in range(BPC):
                for i in range(NTB):
                    g = b * NTB + i
                    if g >= 2:
                        sync.wait_ge(p["sq"], g - 1)
                    sync.dma_start(out=xs[:, (g % 2) * D:(g % 2 + 1) * D],
                                   in_=x_in[g * 128:(g + 1) * 128, :]).then_inc(
                        p["ldx%d" % (g % 2)], 16)
                for i in range(NTB):
                    g = b * NTB + i
                    if g >= 2:
                        sync.wait_ge(p["fin"], g - 1)
                    sync.dma_start(out=xr[:, (g % 2) * D:(g % 2 + 1) * D],
                                   in_=x_in[g * 128:(g + 1) * 128, :]).then_inc(
                        p["xl%d" % (g % 2)], 16)

        @blk.gpsimd
        def _(gp):
            for wi, (wsb, wdr) in enumerate(((wq, wqT), (wk, wkT), (wv, wvT), (wo, woT))):
                for k in range(8):
                    gp.dma_start(out=wsb[:, k * D:(k + 1) * D],
                                 in_=wdr[k * 128:(k + 1) * 128, :]).then_inc(p["ldw"], 16)
                gp.dma_start(out=biasb[0:1, wi * D:(wi + 1) * D],
                             in_=wdr[1024:1025, :]).then_inc(p["ldw"], 16)
            for b in range(BPC):
                for hg in range(2):
                    hb = b * 2 + hg
                    for qc in range(4):
                        gq = hb * 4 + qc
                        gp.wait_ge(p["ogd"], gq + 1)
                        for qt in range(4):
                            gt = b * NTB + qc * 4 + qt
                            gp.dma_start(
                                out=o_dram[gt * 128:(gt + 1) * 128, hg * 512:(hg + 1) * 512],
                                in_=og[:, qt * 512:(qt + 1) * 512]).then_inc(p["ogs"], 16)
                for i in range(NTB):
                    g = b * NTB + i
                    if i == 0:
                        gp.wait_ge(p["ogs"], 512 * (b + 1))
                    if g >= 1:
                        gp.wait_ge(p["otr"], g)
                    gp.dma_start(out=obt[:, :],
                                 in_=o_dram[g * 128:(g + 1) * 128, :]).then_inc(p["obl"], 16)
                for i in range(NTB):
                    g = b * NTB + i
                    gp.wait_ge(p["fin"], g + 1)
                    gp.dma_start(out=x_out[g * 128:(g + 1) * 128, :],
                                 in_=ot[:, (g % 2) * D:(g % 2 + 1) * D]).then_inc(
                        p["st%d" % (g % 2)], 16)
            gp.wait_ge(p["st0"], 16 * NT // 2)
            gp.wait_ge(p["st1"], 16 * NT // 2)

        @blk.vector
        def _(v):
            v.memset(ones1[:, :], 1.0).then_inc(p["on1"], 1)
            for b in range(BPC):
                for i in range(NTB):
                    g = b * NTB + i
                    o = (g % 2) * D
                    xt = xs[:, o:o + D]
                    st = stt[:, (g % 2) * 8:(g % 2) * 8 + 8]
                    v.wait_ge(p["ldx%d" % (g % 2)], 16 * (g // 2 + 1))
                    emit_ln_v(v, xt, st, sqt[:, :], None).then_inc(p["stat"], 1)
                    v.wait_ge(p["lnr"], g + 1)
                    v.reciprocal(st[:, 4:5], st[:, 4:5])
                    v.drain()
                    v.tensor_scalar(lnb[:, o:o + D], xt, st[:, 2:3], st[:, 4:5],
                                    ALU.subtract, ALU.mult).then_inc(p["sq"], 1)
                    v.wait_ge(p["tr"], g + 1)
                    v.tensor_copy(lnT[:, i * D:(i + 1) * D], ptr[:, :]).then_inc(p["trc"], 1)
                for hg in range(2):
                    hb = (b * 2 + hg)
                    v.memset(vab[:, :], 1.0)
                    v.drain()
                    for i in range(NTB):
                        nidx = hb * 48 + i * 3
                        st8 = nst[:, (i % 2) * 32:(i % 2) * 32 + 32]
                        for which in range(3):
                            n = nidx + which
                            v.wait_ge(p["qkv"], n + 1)
                            pp = ppA if n % 2 == 0 else ppB
                            if which == 2:
                                for h in range(8):
                                    ins = v.tensor_copy(
                                        vab[:, i * 520 + h * 65:i * 520 + h * 65 + 64],
                                        pp[:, h * 64:(h + 1) * 64])
                                ins.then_inc(p["qke"], 1)
                                v.drain()
                                continue
                            v.tensor_copy(qf[:, :], pp[:, :]).then_inc(p["qke"], 1)
                            v.drain()
                            v.tensor_mul(sqt[:, 0:512], qf[:, :], qf[:, :])
                            v.drain()
                            for h in range(8):
                                ins = v.tensor_reduce(st8[:, h:h + 1],
                                                      sqt[:, h * 64:(h + 1) * 64],
                                                      axis=AX.X, op=ALU.add)
                            ins.then_inc(p["nsq"], 1)
                            v.wait_ge(p["qnr"], hb * 32 + i * 2 + which + 1)
                            v.tensor_scalar(st8[:, 16:24], st8[:, 16:24], 1e-8, None, ALU.max)
                            v.drain()
                            v.reciprocal(st8[:, 0:8], st8[:, 16:24])
                            v.drain()
                            dst = qn if which == 0 else kn
                            for h in range(8):
                                ins = v.tensor_scalar(dst[:, h * 64:(h + 1) * 64],
                                                      qf[:, h * 64:(h + 1) * 64],
                                                      st8[:, h:h + 1], None, ALU.mult)
                            ins.then_inc(p["qnm" if which == 0 else "knm"], 1)
                            v.drain()
                        g2 = hb * NTB + i
                        v.wait_ge(p["qtr"], g2 + 1)
                        for hp in range(4):
                            ins = v.tensor_copy(
                                qT[:, hp * SEQ + i * 128:hp * SEQ + (i + 1) * 128],
                                ptr[:, hp * 128:(hp + 1) * 128])
                        ins.then_inc(p["qtrc"], 1)
                        v.wait_ge(p["ktr"], g2 + 1)
                        for hp in range(4):
                            ins = v.tensor_copy(
                                kT[:, hp * SEQ + i * 128:hp * SEQ + (i + 1) * 128],
                                ptr[:, hp * 128:(hp + 1) * 128])
                        ins.then_inc(p["ktrc"], 1)
                    nunit = 0
                    nbk = 0
                    for (h, qc) in sched:
                        for kt in range(kmax(qc)):
                            nbk += 1
                            if kt >= 4 * qc:
                                o_ = kt - 4 * qc
                                sl = aT[:, ((nbk - 1) % 2) * 512:((nbk - 1) % 2 + 1) * 512]
                                v.wait_ge(p["aTr"], hb * n_kt + nbk)
                                v.tensor_mul(sl, sl, bmk[:, o_ * 512:(o_ + 1) * 512]).then_inc(
                                    p["avd"], 1)
                                v.drain()
                        nunit += 1
                        if h == 0:
                            gq = hb * 4 + qc
                            if gq >= 1:
                                v.wait_ge(p["ogs"], 64 * gq)
                        v.wait_ge(p["avu"], hb * n_kt + unit_end[nunit - 1])
                        pso = pso0 if nunit % 2 else pso1
                        st = stt[:, 0:8]
                        for qt in range(4):
                            v.tensor_copy(st[:, qt:qt + 1], pso[:, qt * 65 + 64:qt * 65 + 65])
                        v.drain()
                        v.reciprocal(st[:, 4:8], st[:, 0:4])
                        v.drain()
                        for qt in range(4):
                            ins = v.tensor_scalar(
                                og[:, qt * 512 + h * 64:qt * 512 + (h + 1) * 64],
                                pso[:, qt * 65:qt * 65 + 64],
                                st[:, 4 + qt:5 + qt], None, ALU.mult)
                        ins.then_inc(p["ope"], 1)
                        v.drain()
                        if h == HGN - 1:
                            v.sem_inc(p["ogd"], 1)
                for i in range(NTB):
                    g = b * NTB + i
                    v.wait_ge(p["otr"], g + 1)
                    v.tensor_copy(lnT[:, i * D:(i + 1) * D], ptr[:, :]).then_inc(p["otrc"], 1)
                for i in range(NTB):
                    g = b * NTB + i
                    o = (g % 2) * D
                    if g >= 2:
                        v.wait_ge(p["st%d" % (g % 2)], 16 * (g // 2))
                    v.wait_ge(p["xl%d" % (g % 2)], 16 * (g // 2 + 1))
                    for half in range(2):
                        n = g * 2 + half
                        v.wait_ge(p["opj"], n + 1)
                        pp = ppA if n % 2 == 0 else ppB
                        ins = v.scalar_tensor_tensor(
                            ot[:, o + half * 512:o + (half + 1) * 512], pp[:, :], 0.5,
                            xr[:, o + half * 512:o + (half + 1) * 512], ALU.mult, ALU.add)
                        if half == 1:
                            ins.then_inc(p["fin"], 1)
                    v.drain()

        @blk.scalar
        def _(s):
            for b in range(BPC):
                for i in range(NTB):
                    g = b * NTB + i
                    st = stt[:, (g % 2) * 8:(g % 2) * 8 + 8]
                    s.wait_ge(p["stat"], g + 1)
                    s.activation(st[:, 4:5], st[:, 3:4], AF.Sqrt, bias=0.0,
                                 scale=1.0).then_inc(p["lnr"], 1)
                for hg in range(2):
                    hb = b * 2 + hg
                    for i in range(NTB):
                        for which in range(2):
                            st8 = nst[:, (i % 2) * 32:(i % 2) * 32 + 32]
                            s.wait_ge(p["nsq"], hb * 32 + i * 2 + which + 1)
                            s.activation(st8[:, 16:24], st8[:, 0:8], AF.Sqrt, bias=0.0,
                                         scale=1.0).then_inc(p["qnr"], 1)
                    nbk = 0
                    for (h, qc) in sched:
                        for kt in range(kmax(qc)):
                            nbk += 1
                            gkt = hb * n_kt + nbk
                            s.wait_ge(p["sTr"], gkt)
                            if gkt > 2:
                                s.wait_ge(p["avu"], gkt - 2)
                            psT = psT0 if nbk % 2 else psT1
                            s.activation(aT[:, ((nbk - 1) % 2) * 512:((nbk - 1) % 2 + 1) * 512],
                                         psT[:, :], AF.Exp, bias=0.0,
                                         scale=SCALE / 1.5).then_inc(p["aTr"], 1)

        @blk.tensor
        def _(t):
            t.wait_ge(p["ldw"], 16 * 36)
            t.wait_ge(p["ldc"], 16 * 5)
            t.wait_ge(p["on1"], 1)
            for b in range(BPC):
                for i in range(NTB):
                    g = b * NTB + i
                    o = (g % 2) * D
                    t.wait_ge(p["sq"], g + 1)
                    if g >= 1:
                        t.wait_ge(p["trc"], g)
                    if b >= 1 and i == 0:
                        t.wait_ge(p["otrc"], b * NTB)
                        t.wait_ge(p["fin"], b * NTB)
                    for k in range(NDC):
                        ins = t.transpose(ptr[:, k * 128:(k + 1) * 128],
                                          lnb[:, o + k * 128:o + (k + 1) * 128], ident[:])
                    ins.then_inc(p["tr"], 1)
                for hg in range(2):
                    hb = b * 2 + hg
                    for i in range(NTB):
                        nidx = hb * 48 + i * 3
                        g2 = hb * NTB + i
                        for which, wsb in enumerate((wq, wk, wv)):
                            n = nidx + which
                            pp = ppA if n % 2 == 0 else ppB
                            if n >= 2:
                                t.wait_ge(p["qke"], n - 1)
                            for k in range(NDC):
                                t.matmul(pp[:, :], lnT[:, i * D + k * 128:i * D + (k + 1) * 128],
                                         wsb[:, k * D + hg * 512:k * D + hg * 512 + 512],
                                         start=(k == 0), stop=False)
                            t.matmul(pp[:, :], ones1[0:1, :],
                                     biasb[0:1, which * D + hg * 512:which * D + hg * 512 + 512],
                                     start=False, stop=True).then_inc(p["qkv"], 1)
                        t.wait_ge(p["qnm"], g2 + 1)
                        t.wait_ge(p["trc"], (b + 1) * NTB)
                        if g2 >= 1:
                            t.wait_ge(p["qtrc"], g2)
                        if hg == 0 and i == 0:
                            t.wait_ge(p["otrc"], b * NTB)
                        for h in range(8):
                            ins = t.transpose(ptr[64 * (h % 2):64 * (h % 2) + 64,
                                                  (h // 2) * 128:(h // 2) * 128 + 128],
                                              qn[:, h * 64:(h + 1) * 64], ident[:])
                        ins.then_inc(p["qtr"], 1)
                        t.wait_ge(p["knm"], g2 + 1)
                        t.wait_ge(p["qtrc"], g2 + 1)
                        for h in range(8):
                            ins = t.transpose(ptr[64 * (h % 2):64 * (h % 2) + 64,
                                                  (h // 2) * 128:(h // 2) * 128 + 128],
                                              kn[:, h * 64:(h + 1) * 64], ident[:])
                        ins.then_inc(p["ktr"], 1)
                        t.wait_ge(p["ktrc"], g2 + 1)
                    nunit = 0
                    nbk = 0
                    for (h, qc) in sched:
                        hp, hl = h // 2, h % 2
                        nunit += 1
                        gu = hb * NU + nunit
                        pso = pso0 if nunit % 2 else pso1
                        if gu > 2:
                            t.wait_ge(p["ope"], gu - 2)
                        for kt in range(kmax(qc)):
                            nbk += 1
                            gkt = hb * n_kt + nbk
                            psT = psT0 if nbk % 2 else psT1
                            if gkt > 2:
                                t.wait_ge(p["aTr"], gkt - 2)
                            t.matmul(psT[:, :],
                                     kT[64 * hl:64 * hl + 64,
                                        hp * SEQ + kt * 128:hp * SEQ + (kt + 1) * 128],
                                     qT[64 * hl:64 * hl + 64,
                                        hp * SEQ + qc * 512:hp * SEQ + (qc + 1) * 512],
                                     start=True, stop=True).then_inc(p["sTr"], 1)
                            if kt >= 4 * qc:
                                t.wait_ge(p["avd"], hb * n_band + band_cum[nbk - 1])
                            else:
                                t.wait_ge(p["aTr"], gkt)
                            sl = aT[:, ((nbk - 1) % 2) * 512:((nbk - 1) % 2 + 1) * 512]
                            for qt in range(4):
                                ins = t.matmul(pso[:, qt * 65:qt * 65 + 65],
                                               sl[:, qt * 128:(qt + 1) * 128],
                                               vab[:, kt * 520 + h * 65:kt * 520 + h * 65 + 65],
                                               start=(kt == 0 and qt == 0),
                                               stop=(kt == kmax(qc) - 1 and qt == 3))
                            ins.then_inc(p["avu"], 1)
                for i in range(NTB):
                    g = b * NTB + i
                    if i == 0:
                        t.wait_ge(p["ktrc"], (b * 2 + 2) * NTB)
                    if g >= 1:
                        t.wait_ge(p["otrc"], g)
                    t.wait_ge(p["obl"], 16 * (g + 1))
                    for k in range(NDC):
                        ins = t.transpose(ptr[:, k * 128:(k + 1) * 128],
                                          obt[:, k * 128:(k + 1) * 128], ident[:])
                    ins.then_inc(p["otr"], 1)
                for i in range(NTB):
                    g = b * NTB + i
                    if i == 0:
                        t.wait_ge(p["otrc"], (b + 1) * NTB)
                        t.wait_ge(p["qke"], (b + 1) * 96)
                    for half in range(2):
                        n = g * 2 + half
                        pp = ppA if n % 2 == 0 else ppB
                        if n >= 2:
                            t.wait_ge(p["fin"], n // 2)
                        for k in range(NDC):
                            t.matmul(pp[:, :], lnT[:, i * D + k * 128:i * D + (k + 1) * 128],
                                     wo[:, k * D + half * 512:k * D + (half + 1) * 512],
                                     start=(k == 0), stop=False)
                        t.matmul(pp[:, :], ones1[0:1, :],
                                 biasb[0:1, 3 * D + half * 512:3 * D + (half + 1) * 512],
                                 start=False, stop=True).then_inc(p["opj"], 1)

    p.close()
    return nc


def maps_l0a(inputs, xsh):
    def wstack(w, bias):
        wt = hT(f32(w))
        pad = np.zeros((128, D), np.float32)
        pad[0] = f32(bias)
        return np.ascontiguousarray(np.concatenate([wt, pad], 0))
    bm = np.zeros((4, 128, 512), np.float32)
    for o in range(4):
        j = np.arange(128)[:, None]
        ii = np.arange(512)[None, :]
        bm[o] = (ii >= j + 128 * o).astype(np.float32)
    bm = bm.astype(BF)
    return [{"x": xsh[c], "wqT": wstack(inputs["L0_attn_Wq"], inputs["L0_attn_bq"]),
             "wkT": wstack(inputs["L0_attn_Wk"], inputs["L0_attn_bk"]),
             "wvT": wstack(inputs["L0_attn_Wv"], inputs["L0_attn_bv"]),
             "woT": wstack(inputs["L0_attn_Wo"], inputs["L0_attn_bo"]),
             "bmask": bm, "idm": _IDM} for c in range(NCORE)]




def build_ph_l1a():
    nc = bass.Bass("TRN2", target_bir_lowering=False)
    x_in = nc.dram_tensor("x", [TPC, D], F32, kind="ExternalInput").ap()
    wstk = nc.dram_tensor("wstk", [6 * D, D], F32, kind="ExternalInput").ap()
    bstk = nc.dram_tensor("bstk", [1, 6 * D], F32, kind="ExternalInput").ap()
    blkm = nc.dram_tensor("blkm", [128, 128], BF16, kind="ExternalInput").ap()
    idm = nc.dram_tensor("idm", [128, 128], BF16, kind="ExternalInput").ap()
    x_out = nc.dram_tensor("xo", [TPC, D], F32, kind="ExternalOutput").ap()

    SC = float(1.0 / np.sqrt(128.0))
    p = P(nc)
    p.sem("ldc", "ldw", "ldx0", "ldx1", "stat", "lnr", "sq", "tr", "trc",
          "pj", "pje", "tr2", "trc2", "tr3", "trc3", "qkv", "qkve",
          "tr4", "trc4", "sT", "aTe", "msk", "oev", "avm", "tr5", "trc5",
          "opj", "fin", "st0", "st1", "on1")

    xs = nc.alloc_sbuf_tensor("xs", [128, 2 * D], F32)
    lnb = nc.alloc_sbuf_tensor("lnb", [128, 2 * D], BF16)
    sqt = nc.alloc_sbuf_tensor("sqs", [128, D], F32)
    ot = nc.alloc_sbuf_tensor("ot", [128, 2 * D], F32)
    stt = nc.alloc_sbuf_tensor("stt", [128, 16], F32)
    lnT = nc.alloc_sbuf_tensor("lnT", [128, D], BF16)
    qkf = nc.alloc_sbuf_tensor("qkf", [128, D], BF16)
    vf = nc.alloc_sbuf_tensor("vf", [128, D], BF16)
    qkT = nc.alloc_sbuf_tensor("qkT", [128, D], BF16)
    vT = nc.alloc_sbuf_tensor("vT", [128, D], BF16)
    qb = nc.alloc_sbuf_tensor("qb", [128, D], BF16)
    kb = nc.alloc_sbuf_tensor("kb", [128, D], BF16)
    qbT = nc.alloc_sbuf_tensor("qbT", [128, D], BF16)
    kbT = nc.alloc_sbuf_tensor("kbT", [128, D], BF16)
    vaug = nc.alloc_sbuf_tensor("vaug", [128, 8 * 129], BF16)
    aTb = nc.alloc_sbuf_tensor("aTb", [128, 2 * 128], BF16)
    ob = nc.alloc_sbuf_tensor("ob", [128, D], BF16)
    w6 = [nc.alloc_sbuf_tensor("w6_%d" % i, [128, 8 * D], BF16) for i in range(6)]
    b6 = nc.alloc_sbuf_tensor("b6", [1, 6 * D], BF16)
    ident = nc.alloc_sbuf_tensor("ident", [128, 128], BF16)
    bmsk = nc.alloc_sbuf_tensor("bmsk", [128, 128], BF16)
    ones1 = nc.alloc_sbuf_tensor("ones1", [1, 128], BF16)
    ptr = nc.alloc_psum_tensor("ptr", [128, 1024], BF16)
    ppA = nc.alloc_psum_tensor("ppA", [128, 512], F32)
    ppB = nc.alloc_psum_tensor("ppB", [128, 512], F32)
    psT0 = nc.alloc_psum_tensor("psT0", [128, 512], F32)
    psT1 = nc.alloc_psum_tensor("psT1", [128, 512], F32)
    pso0 = nc.alloc_psum_tensor("pso0", [128, 512], F32)
    pso1 = nc.alloc_psum_tensor("pso1", [128, 512], F32)

    def trp(t, sem, dst_rows, src, n=8):
        for k in range(n):
            ins = t.transpose(ptr[:, k * 128:(k + 1) * 128],
                              src[:, k * 128:(k + 1) * 128], ident[:])
        ins.then_inc(p[sem], 1)

    with nc.Block() as blk:
        @blk.sync
        def _(sync):
            sync.dma_start(out=ident[:], in_=idm[:]).then_inc(p["ldc"], 16)
            sync.dma_start(out=bmsk[:], in_=blkm[:]).then_inc(p["ldc"], 16)
            for g in range(NT):
                if g >= 2:
                    sync.wait_ge(p["st%d" % (g % 2)], 16 * (g // 2))
                sync.dma_start(out=xs[:, (g % 2) * D:(g % 2 + 1) * D],
                               in_=x_in[g * 128:(g + 1) * 128, :]).then_inc(
                    p["ldx%d" % (g % 2)], 16)

        @blk.gpsimd
        def _(gp):
            for wi in range(6):
                for k in range(8):
                    gp.dma_start(out=w6[wi][:, k * D:(k + 1) * D],
                                 in_=wstk[wi * D + k * 128:wi * D + (k + 1) * 128, :]
                                 ).then_inc(p["ldw"], 16)
            gp.dma_start(out=b6[0:1, :], in_=bstk[:, :]).then_inc(p["ldw"], 16)
            for g in range(NT):
                gp.wait_ge(p["fin"], g + 1)
                gp.dma_start(out=x_out[g * 128:(g + 1) * 128, :],
                             in_=ot[:, (g % 2) * D:(g % 2 + 1) * D]).then_inc(
                    p["st%d" % (g % 2)], 16)
            gp.wait_ge(p["st0"], 16 * NT // 2)
            gp.wait_ge(p["st1"], 16 * NT // 2)

        @blk.vector
        def _(v):
            v.memset(ones1[:, :], 1.0)
            v.memset(vaug[:, :], 1.0).then_inc(p["on1"], 1)
            for g in range(NT):
                o = (g % 2) * D
                xt = xs[:, o:o + D]
                st = stt[:, (g % 2) * 8:(g % 2) * 8 + 8]
                v.wait_ge(p["ldx%d" % (g % 2)], 16 * (g // 2 + 1))
                emit_ln_v(v, xt, st, sqt[:, :], None).then_inc(p["stat"], 1)
                v.wait_ge(p["lnr"], g + 1)
                v.tensor_scalar(lnb[:, o:o + D], xt, st[:, 2:3], st[:, 4:5],
                                ALU.subtract, ALU.mult).then_inc(p["sq"], 1)
                v.wait_ge(p["tr"], g + 1)
                v.tensor_copy(lnT[:, :], ptr[:, :]).then_inc(p["trc"], 1)
                for n in range(4):
                    v.wait_ge(p["pj"], g * 4 + n + 1)
                    pp = ppA if n % 2 == 0 else ppB
                    dst = qkf if n < 2 else vf
                    v.tensor_copy(dst[:, (n % 2) * 512:(n % 2) * 512 + 512],
                                  pp[:, :]).then_inc(p["pje"], 1)
                v.wait_ge(p["tr2"], g + 1)
                v.tensor_copy(qkT[:, :], ptr[:, :]).then_inc(p["trc2"], 1)
                v.wait_ge(p["tr3"], g + 1)
                v.tensor_copy(vT[:, :], ptr[:, :]).then_inc(p["trc3"], 1)
                for n in range(6):
                    v.wait_ge(p["qkv"], g * 6 + n + 1)
                    pp = ppA if n % 2 == 0 else ppB
                    if n >= 4 and n == 4:
                        v.wait_ge(p["oev"], g * 8)
                    if n < 2:
                        v.tensor_scalar(qb[:, (n % 2) * 512:(n % 2) * 512 + 512],
                                        pp[:, :], SC, None, ALU.mult).then_inc(p["qkve"], 1)
                    elif n < 4:
                        v.tensor_copy(kb[:, (n % 2) * 512:(n % 2) * 512 + 512],
                                      pp[:, :]).then_inc(p["qkve"], 1)
                    else:
                        for h in range(4):
                            hh = (n - 4) * 4 + h
                            ins = v.tensor_copy(vaug[:, hh * 129:hh * 129 + 128],
                                                pp[:, h * 128:(h + 1) * 128])
                        ins.then_inc(p["qkve"], 1)
                v.wait_ge(p["tr4"], g * 2 + 1)
                v.tensor_copy(qbT[:, :], ptr[:, :]).then_inc(p["trc4"], 1)
                v.wait_ge(p["tr4"], g * 2 + 2)
                v.tensor_copy(kbT[:, :], ptr[:, :]).then_inc(p["trc4"], 1)
                for h in range(8):
                    u = g * 8 + h
                    v.wait_ge(p["aTe"], u + 1)
                    sl = aTb[:, (u % 2) * 128:(u % 2) * 128 + 128]
                    v.tensor_mul(sl, sl, bmsk[:, :]).then_inc(p["msk"], 1)
                    v.drain()
                    v.wait_ge(p["oev"], u + 1)
                    pso = pso0 if u % 2 == 0 else pso1
                    st2 = stt[:, 0:2]
                    v.tensor_copy(st2[:, 0:1], pso[:, 128:129])
                    v.drain()
                    v.reciprocal(st2[:, 1:2], st2[:, 0:1])
                    v.drain()
                    v.tensor_scalar(ob[:, h * 128:(h + 1) * 128], pso[:, 0:128],
                                    st2[:, 1:2], None, ALU.mult).then_inc(p["avm"], 1)
                    v.drain()
                v.wait_ge(p["tr5"], g + 1)
                v.tensor_copy(lnT[:, :], ptr[:, :]).then_inc(p["trc5"], 1)
                for half in range(2):
                    n = g * 2 + half
                    v.wait_ge(p["opj"], n + 1)
                    pp = ppA if n % 2 == 0 else ppB
                    if g >= 2 and half == 0:
                        v.wait_ge(p["st%d" % (g % 2)], 16 * (g // 2))
                    ins = v.tensor_add(ot[:, o + half * 512:o + (half + 1) * 512],
                                       pp[:, :], xt[:, half * 512:(half + 1) * 512])
                    if half == 1:
                        ins.then_inc(p["fin"], 1)
                v.drain()

        @blk.scalar
        def _(s):
            for g in range(NT):
                st = stt[:, (g % 2) * 8:(g % 2) * 8 + 8]
                s.wait_ge(p["stat"], g + 1)
                emit_rstd_s(s, st).then_inc(p["lnr"], 1)
                for h in range(8):
                    u = g * 8 + h
                    s.wait_ge(p["sT"], u + 1)
                    if u >= 2:
                        s.wait_ge(p["oev"], u - 1)
                    psT = psT0 if u % 2 == 0 else psT1
                    s.activation(aTb[:, (u % 2) * 128:(u % 2) * 128 + 128],
                                 psT[:, 0:128], AF.Exp, bias=0.0,
                                 scale=1.0).then_inc(p["aTe"], 1)

        @blk.tensor
        def _(t):
            t.wait_ge(p["ldw"], 16 * 49)
            t.wait_ge(p["ldc"], 16 * 2)
            t.wait_ge(p["on1"], 1)
            for g in range(NT):
                o = (g % 2) * D
                t.wait_ge(p["sq"], g + 1)
                if g >= 1:
                    t.wait_ge(p["trc5"], g)
                trp(t, "tr", None, lnb[:, o:o + D])
                t.wait_ge(p["trc"], g + 1)
                for n in range(4):
                    gn = g * 4 + n
                    pp = ppA if n % 2 == 0 else ppB
                    if gn >= 2:
                        t.wait_ge(p["pje"], gn - 1)
                    wi = 0 if n < 2 else 1
                    half = n % 2
                    for k in range(NDC):
                        t.matmul(pp[:, :], lnT[:, k * 128:(k + 1) * 128],
                                 w6[wi][:, k * D + half * 512:k * D + (half + 1) * 512],
                                 start=(k == 0), stop=False)
                    t.matmul(pp[:, :], ones1[0:1, :],
                             b6[0:1, wi * D + half * 512:wi * D + (half + 1) * 512],
                             start=False, stop=True).then_inc(p["pj"], 1)
                t.wait_ge(p["pje"], g * 4 + 2)
                trp(t, "tr2", None, qkf)
                t.wait_ge(p["trc2"], g + 1)
                t.wait_ge(p["pje"], g * 4 + 4)
                trp(t, "tr3", None, vf)
                t.wait_ge(p["trc3"], g + 1)
                for n in range(6):
                    gn = g * 6 + n
                    pp = ppA if n % 2 == 0 else ppB
                    if gn >= 2:
                        t.wait_ge(p["qkve"], gn - 1)
                    wi = 2 + n // 2
                    half = n % 2
                    src = qkT if n < 4 else vT
                    for k in range(NDC):
                        t.matmul(pp[:, :], src[:, k * 128:(k + 1) * 128],
                                 w6[wi][:, k * D + half * 512:k * D + (half + 1) * 512],
                                 start=(k == 0), stop=False)
                    t.matmul(pp[:, :], ones1[0:1, :],
                             b6[0:1, wi * D + half * 512:wi * D + (half + 1) * 512],
                             start=False, stop=True).then_inc(p["qkv"], 1)
                t.wait_ge(p["qkve"], g * 6 + 2)
                trp(t, "tr4", None, qb)
                t.wait_ge(p["trc4"], g * 2 + 1)
                t.wait_ge(p["qkve"], g * 6 + 4)
                trp(t, "tr4", None, kb)
                t.wait_ge(p["trc4"], g * 2 + 2)
                t.wait_ge(p["qkve"], g * 6 + 6)
                for h in range(8):
                    u = g * 8 + h
                    psT = psT0 if u % 2 == 0 else psT1
                    if u >= 2:
                        t.wait_ge(p["aTe"], u - 1)
                    t.matmul(psT[:, 0:128], kbT[:, h * 128:(h + 1) * 128],
                             qbT[:, h * 128:(h + 1) * 128],
                             start=True, stop=True).then_inc(p["sT"], 1)
                    t.wait_ge(p["msk"], u + 1)
                    pso = pso0 if u % 2 == 0 else pso1
                    if u >= 2:
                        t.wait_ge(p["avm"], u - 1)
                    t.matmul(pso[:, 0:129], aTb[:, (u % 2) * 128:(u % 2) * 128 + 128],
                             vaug[:, h * 129:(h + 1) * 129],
                             start=True, stop=True).then_inc(p["oev"], 1)
                t.wait_ge(p["avm"], g * 8 + 8)
                trp(t, "tr5", None, ob)
                t.wait_ge(p["trc5"], g + 1)
                for half in range(2):
                    n = g * 2 + half
                    pp = ppA if n % 2 == 0 else ppB
                    if n >= 2:
                        t.wait_ge(p["fin"], n // 2)
                    for k in range(NDC):
                        t.matmul(pp[:, :], lnT[:, k * 128:(k + 1) * 128],
                                 w6[5][:, k * D + half * 512:k * D + (half + 1) * 512],
                                 start=(k == 0), stop=False)
                    t.matmul(pp[:, :], ones1[0:1, :],
                             b6[0:1, 5 * D + half * 512:5 * D + (half + 1) * 512],
                             start=False, stop=True).then_inc(p["opj"], 1)

    p.close()
    return nc


def maps_l1a(inputs, xsh):
    wq, wk, wv = np.split(f32(inputs["L1_attn_in_proj_w"]), 3, axis=0)
    bq, bk, bv = np.split(f32(inputs["L1_attn_in_proj_b"]), 3)
    ws = [hT(f32(inputs["L1_attn_proj_qk_w"])), hT(f32(inputs["L1_attn_proj_v_w"])),
          hT(wq), hT(wk), hT(wv), hT(f32(inputs["L1_attn_out_proj_w"]))]
    bs = [f32(inputs["L1_attn_proj_qk_b"]), f32(inputs["L1_attn_proj_v_b"]),
          bq, bk, bv, f32(inputs["L1_attn_out_proj_b"])]
    wstk = np.ascontiguousarray(np.concatenate(ws, 0))
    bstk = np.ascontiguousarray(np.concatenate(bs)).reshape(1, 6 * D)
    j = np.arange(128)[:, None]
    ii = np.arange(128)[None, :]
    bm = (((ii // 4) == (j // 4)) & (ii != j)).astype(np.float32).astype(BF)
    return [{"x": xsh[c], "wstk": wstk, "bstk": bstk, "blkm": bm, "idm": _IDM}
            for c in range(NCORE)]



L1E, L1CAP, L1I, MYE = 64, 256, 768, 8
L1TT = NCORE * TPC
L1NTT = L1TT // 128


def emit_dequant(v, src_pk, nshift, i32a, i32b, u16a, u16b, dqz, dqo):
    n = src_pk.shape[-1]
    t1 = i32a.bitcast(U16)[:, 0:n]
    t2 = i32b.bitcast(U16)[:, 0:n]
    v.tensor_copy(u16a, src_pk)
    v.drain()
    if nshift:
        v.tensor_scalar(u16a, u16a, 4, None, ALU.logical_shift_right)
        v.drain()
    v.tensor_scalar(u16b, u16a, 7, None, ALU.bitwise_and)
    v.drain()
    v.tensor_scalar(u16a, u16a, 8, 12, ALU.bitwise_and,
                    ALU.logical_shift_left)
    v.tensor_scalar(dqz, u16b, 1, None, ALU.min)
    v.tensor_scalar(t1, u16b, 1, None, ALU.is_le)
    v.drain()
    v.tensor_scalar(t1, t1, 6, None, ALU.logical_shift_left)
    v.tensor_scalar(t2, u16b, 6, None, ALU.logical_shift_left)
    v.drain()
    v.tensor_sub(t2, t2, t1)
    v.drain()
    v.tensor_scalar(t2, t2, 16128, None, ALU.add)
    v.drain()
    v.tensor_add(u16a, u16a, t2)
    v.drain()
    v.tensor_mul(dqo, u16a.bitcast(BF16), dqz)
    v.drain()


def build_ph_l1f_pre():
    nc = bass.Bass("TRN2", target_bir_lowering=False)
    x_in = nc.dram_tensor("x", [TPC, D], F32, kind="ExternalInput").ap()
    idm = nc.dram_tensor("idm", [128, 128], BF16, kind="ExternalInput").ap()
    idmF = nc.dram_tensor("idmF", [128, 128], F32, kind="ExternalInput").ap()
    lnp_o = nc.dram_tensor("lnp", [TPC, D], BF16, kind="ExternalOutput").ap()
    lnpT_o = nc.dram_tensor("lnpT", [TPC, D], BF16, kind="ExternalOutput").ap()
    lnfT_o = nc.dram_tensor("lnfT", [TPC, D], F32, kind="ExternalOutput").ap()

    p = P(nc)
    p.sem("ldc", "ldx0", "ldx1", "stat", "lnr", "sq", "tr", "trc",
          "st0", "st1", "su0", "su1", "sqf", "trf", "trfc", "suf0", "suf1")
    xs = nc.alloc_sbuf_tensor("xs", [128, 2 * D], F32)
    lnf = nc.alloc_sbuf_tensor("lnf", [128, D], F32)
    lnp = nc.alloc_sbuf_tensor("lnpb", [128, 2 * D], BF16)
    lnT = nc.alloc_sbuf_tensor("lnT", [128, 2 * D], BF16)
    lnpf = nc.alloc_sbuf_tensor("lnpf", [128, 2 * D], F32)
    lnTf = nc.alloc_sbuf_tensor("lnTf", [128, 2 * D], F32)
    sqt = nc.alloc_sbuf_tensor("sqs", [128, D], F32)
    stt = nc.alloc_sbuf_tensor("stt", [128, 16], F32)
    ident = nc.alloc_sbuf_tensor("ident", [128, 128], BF16)
    identF = nc.alloc_sbuf_tensor("identF", [128, 128], F32)
    ptr = nc.alloc_psum_tensor("ptr", [128, 1024], BF16)
    ptrF = nc.alloc_psum_tensor("ptrF", [128, 1024], F32)

    with nc.Block() as blk:
        @blk.sync
        def _(sync):
            sync.dma_start(out=ident[:], in_=idm[:]).then_inc(p["ldc"], 16)
            sync.dma_start(out=identF[:], in_=idmF[:]).then_inc(p["ldc"], 16)
            for g in range(NT):
                if g >= 2:
                    sync.wait_ge(p["st%d" % (g % 2)], 16 * (g // 2))
                sync.dma_start(out=xs[:, (g % 2) * D:(g % 2 + 1) * D],
                               in_=x_in[g * 128:(g + 1) * 128, :]).then_inc(
                    p["ldx%d" % (g % 2)], 16)

        @blk.gpsimd
        def _(gp):
            for g in range(NT):
                gp.wait_ge(p["sq"], g + 1)
                gp.dma_start(out=lnp_o[g * 128:(g + 1) * 128, :],
                             in_=lnp[:, (g % 2) * D:(g % 2 + 1) * D]).then_inc(
                    p["st%d" % (g % 2)], 16)
                gp.wait_ge(p["trc"], g + 1)
                gp.dma_start(out=lnpT_o[g * 128:(g + 1) * 128, :],
                             in_=lnT[:, (g % 2) * D:(g % 2 + 1) * D]).then_inc(
                    p["su%d" % (g % 2)], 16)
                gp.wait_ge(p["trfc"], g + 1)
                gp.dma_start(out=lnfT_o[g * 128:(g + 1) * 128, :],
                             in_=lnTf[:, (g % 2) * D:(g % 2 + 1) * D]).then_inc(
                    p["suf%d" % (g % 2)], 16)
            gp.wait_ge(p["st0"], 16 * NT // 2)
            gp.wait_ge(p["st1"], 16 * NT // 2)
            gp.wait_ge(p["su0"], 16 * NT // 2)
            gp.wait_ge(p["su1"], 16 * NT // 2)
            gp.wait_ge(p["suf0"], 16 * NT // 2)
            gp.wait_ge(p["suf1"], 16 * NT // 2)

        @blk.vector
        def _(v):
            for g in range(NT):
                o = (g % 2) * D
                xt = xs[:, o:o + D]
                st = stt[:, (g % 2) * 8:(g % 2) * 8 + 8]
                v.wait_ge(p["ldx%d" % (g % 2)], 16 * (g // 2 + 1))
                emit_ln_v(v, xt, st, sqt[:, :], None).then_inc(p["stat"], 1)
                v.wait_ge(p["lnr"], g + 1)
                v.tensor_scalar(lnf[:, :], xt, st[:, 2:3], st[:, 4:5],
                                ALU.subtract, ALU.mult)
                v.drain()
                if g >= 2:
                    v.wait_ge(p["st%d" % (g % 2)], 16 * (g // 2))
                lo = lnp[:, o:o + D]
                lnfv = lnf.rearrange("p (a two) -> p two a", two=2)
                v.tensor_copy(lo[:, 0:512], lnfv[:, 0, :])
                v.drain()
                v.tensor_copy(lo[:, 512:1024], lnfv[:, 1, :]).then_inc(p["sq"], 1)
                lof = lnpf[:, o:o + D]
                v.tensor_copy(lof[:, 0:512], lnfv[:, 0, :])
                v.drain()
                v.tensor_copy(lof[:, 512:1024], lnfv[:, 1, :]).then_inc(p["sqf"], 1)
                v.wait_ge(p["tr"], g + 1)
                if g >= 2:
                    v.wait_ge(p["su%d" % (g % 2)], 16 * (g // 2))
                v.tensor_copy(lnT[:, o:o + D], ptr[:, :]).then_inc(p["trc"], 1)
                v.wait_ge(p["trf"], g + 1)
                if g >= 2:
                    v.wait_ge(p["suf%d" % (g % 2)], 16 * (g // 2))
                v.tensor_copy(lnTf[:, o:o + D], ptrF[:, :]).then_inc(p["trfc"], 1)

        @blk.scalar
        def _(s):
            for g in range(NT):
                st = stt[:, (g % 2) * 8:(g % 2) * 8 + 8]
                s.wait_ge(p["stat"], g + 1)
                emit_rstd_s(s, st).then_inc(p["lnr"], 1)

        @blk.tensor
        def _(t):
            t.wait_ge(p["ldc"], 32)
            for g in range(NT):
                o = (g % 2) * D
                t.wait_ge(p["sq"], g + 1)
                if g >= 1:
                    t.wait_ge(p["trc"], g)
                for k in range(NDC):
                    ins = t.transpose(ptr[:, k * 128:(k + 1) * 128],
                                      lnp[:, o + k * 128:o + (k + 1) * 128], ident[:])
                ins.then_inc(p["tr"], 1)
                t.wait_ge(p["sqf"], g + 1)
                if g >= 1:
                    t.wait_ge(p["trfc"], g)
                for k in range(NDC):
                    ins = t.transpose(ptrF[:, k * 128:(k + 1) * 128],
                                      lnpf[:, o + k * 128:o + (k + 1) * 128], identF[:])
                ins.then_inc(p["trf"], 1)

    p.close()
    return nc


def build_ph_l1f_mid():
    nc = bass.Bass("TRN2", target_bir_lowering=False)
    lnp = nc.dram_tensor("lnp", [L1TT, D], BF16, kind="ExternalInput").ap()
    lnpT = nc.dram_tensor("lnpT", [L1TT, D], F32, kind="ExternalInput").ap()
    gwTp = nc.dram_tensor("gwTp", [D, L1E], F32, kind="ExternalInput").ap()
    gup = nc.dram_tensor("gup", [MYE, 2, 512, L1I], I32, kind="ExternalInput").ap()
    gusce = nc.dram_tensor("gusce", [MYE, 2, 512, L1I], BF16, kind="ExternalInput").ap()
    dnp = nc.dram_tensor("dnp", [MYE, 384, D], I32, kind="ExternalInput").ap()
    dnsce = nc.dram_tensor("dnsce", [MYE, 384, D], BF16, kind="ExternalInput").ap()
    io65 = nc.dram_tensor("io65", [1, 65], F32, kind="ExternalInput").ap()
    io2k = nc.dram_tensor("io2k", [128, 2048], F32, kind="ExternalInput").ap()
    iocol = nc.dram_tensor("iocol", [128, 1], F32, kind="ExternalInput").ap()
    trilm = nc.dram_tensor("trilm", [128, 128], BF16, kind="ExternalInput").ap()
    idm = nc.dram_tensor("idm", [128, 128], BF16, kind="ExternalInput").ap()
    eo_o = nc.dram_tensor("eo", [MYE * L1CAP, D], BF16, kind="ExternalOutput").ap()
    slot_o = nc.dram_tensor("slot", [L1TT, 4], F32, kind="ExternalOutput").ap()
    wk_o = nc.dram_tensor("wk4", [L1TT, 4], F32, kind="ExternalOutput").ap()
    buf_d = nc.dram_tensor("bufd", [MYE * L1CAP, D], BF16, kind="Internal").ap()

    p = P(nc)
    p.sem("ldc", "gwl", "bcst", "bcpy", "lnl0", "lnl1", "lpl0", "lpl1",
          "gmm", "gcp", "rt1", "scl", "sel", "pfx", "pxc", "ofd", "rt2",
          "zbuf", "scd", "sto", "pkl", "deq", "wgud",
          "nio", "ohr", "tkm", "trw", "tcz", "tmr", "tkq", "tka", "cst",
          "bld", "btr", "btrc", "hA", "sil", "hB", "hpp", "htr", "htrc",
          "dmm", "eev", "est")

    lnTs = nc.alloc_sbuf_tensor("lnTs", [128, 2 * D], F32)
    lnps = nc.alloc_sbuf_tensor("lnps", [128, 2 * D], BF16)
    gwb = nc.alloc_sbuf_tensor("gwb", [128, 8 * 64], F32)
    lgt = nc.alloc_sbuf_tensor("lgt", [128, 64], F32)
    selb = nc.alloc_sbuf_tensor("selb", [128, 64], BF16)
    eqh = nc.alloc_sbuf_tensor("eqh", [128, 4 * 64], F32)
    eqs = nc.alloc_sbuf_tensor("eqs", [128, 64], F32)
    ek = nc.alloc_sbuf_tensor("ek", [128, 32], F32)
    off = nc.alloc_sbuf_tensor("off", [1, 64], F32)
    io64 = nc.alloc_sbuf_tensor("io64", [128, 64], F32)
    ebc = nc.alloc_sbuf_tensor("ebc", [128, 1], F32)
    io65r = nc.alloc_sbuf_tensor("io65r", [1, 65], F32)
    tril = nc.alloc_sbuf_tensor("tril", [128, 128], BF16)
    ones1 = nc.alloc_sbuf_tensor("ones1", [1, 128], BF16)
    tgt = nc.alloc_sbuf_tensor("tgt", [128, 4], I32)
    sw8 = nc.alloc_sbuf_tensor("sw8", [128, 8], F32)
    ident = nc.alloc_sbuf_tensor("ident", [128, 128], BF16)
    wgu = nc.alloc_sbuf_tensor("wgu", [128, 16 * L1I], BF16)
    wdn = nc.alloc_sbuf_tensor("wdn", [128, 6 * D], BF16)
    pk1 = nc.alloc_sbuf_tensor("pk1", [128, 4 * L1I], I32)
    pk2 = nc.alloc_sbuf_tensor("pk2", [128, 3 * D], I32)
    sce1 = nc.alloc_sbuf_tensor("sce1", [128, 4 * L1I], BF16)
    sce2 = nc.alloc_sbuf_tensor("sce2", [128, 3 * D], BF16)
    dqa = nc.alloc_sbuf_tensor("dqa", [128, 4 * L1I], I32)
    dqb = nc.alloc_sbuf_tensor("dqb", [128, 4 * L1I], I32)
    dqu = nc.alloc_sbuf_tensor("dqu", [128, 4 * L1I], U16)
    dqv = nc.alloc_sbuf_tensor("dqv", [128, 4 * L1I], U16)
    dqz = nc.alloc_sbuf_tensor("dqz", [128, 4 * L1I], BF16)
    dqo = nc.alloc_sbuf_tensor("dqo", [128, 4 * L1I], BF16)
    bts = nc.alloc_sbuf_tensor("bts", [128, D], BF16)
    btT = nc.alloc_sbuf_tensor("btT", [128, D], BF16)
    hb = nc.alloc_sbuf_tensor("hb", [128, L1I], F32)
    hp = nc.alloc_sbuf_tensor("hp", [128, L1I], BF16)
    hpT = nc.alloc_sbuf_tensor("hpT", [128, L1I], BF16)
    eos = nc.alloc_sbuf_tensor("eos", [128, D], BF16)
    ohsum = nc.alloc_sbuf_tensor("ohsum", [128, 2048], F32)
    ohbuf = nc.alloc_sbuf_tensor("ohbuf", [128, 2048], F32)
    io2kb = nc.alloc_sbuf_tensor("io2kb", [128, 2048], F32)
    tokrow = nc.alloc_sbuf_tensor("tokrow", [1, 2048], F32)
    tokmap = nc.alloc_sbuf_tensor("tokmap", [128, 16], I32)
    tokc = nc.alloc_sbuf_tensor("tokc", [128, 1], F32)
    iocolb = nc.alloc_sbuf_tensor("iocolb", [128, 1], F32)
    ones11 = nc.alloc_sbuf_tensor("ones11", [1, 1], F32)
    ones1f = nc.alloc_sbuf_tensor("ones1f", [1, 128], F32)
    onecb = nc.alloc_sbuf_tensor("onecb", [128, 1], BF16)
    ptr = nc.alloc_psum_tensor("ptr", [128, 1024], BF16)
    pgA = nc.alloc_psum_tensor("pgA", [128, 512], F32)
    pgB = nc.alloc_psum_tensor("pgB", [128, 512], F32)
    pxA = nc.alloc_psum_tensor("pxA", [128, 512], F32)
    pxB = nc.alloc_psum_tensor("pxB", [128, 512], F32)
    pp1 = nc.alloc_psum_tensor("pp1", [128, 512], F32)
    pp2 = nc.alloc_psum_tensor("pp2", [128, 512], F32)
    ptk = nc.alloc_psum_tensor("ptk", [1, 512], F32)

    NBT = MYE * L1CAP // 128

    with nc.Block() as blk:
        @blk.sync
        def _(sync):
            sync.dma_start(out=ident[:], in_=idm[:]).then_inc(p["ldc"], 16)
            sync.dma_start(out=tril[:], in_=trilm[:]).then_inc(p["ldc"], 16)
            sync.dma_start(out=io65r[0:1, :], in_=io65[:, :]).then_inc(p["ldc"], 16)
            sync.dma_start(out=io2kb[:], in_=io2k[:]).then_inc(p["nio"], 16)
            sync.dma_start(out=iocolb[:], in_=iocol[:]).then_inc(p["nio"], 16)
            for g in range(L1NTT):
                if g >= 2:
                    sync.wait_ge(p["gcp"], g - 1)
                sync.dma_start(out=lnTs[:, (g % 2) * D:(g % 2 + 1) * D],
                               in_=lnpT[g * 128:(g + 1) * 128, :]).then_inc(
                    p["lnl%d" % (g % 2)], 16)
            for e in range(MYE):
                for mt in range(2):
                    if e + mt > 0:
                        sync.wait_ge(p["deq"], 3 * e + mt)
                    sync.dma_start(out=pk1.rearrange("p (a b) -> p a b", a=4),
                                   in_=gup[e, mt].rearrange("(a p) b -> p a b", p=128)
                                   ).then_inc(p["pkl"], 16)
                    sync.dma_start(out=sce1.rearrange("p (a b) -> p a b", a=4),
                                   in_=gusce[e, mt].rearrange("(a p) b -> p a b", p=128)
                                   ).then_inc(p["pkl"], 16)
                sync.wait_ge(p["deq"], 3 * e + 2)
                sync.dma_start(out=pk2.rearrange("p (a b) -> p a b", a=3),
                               in_=dnp[e].rearrange("(a p) b -> p a b", p=128)
                               ).then_inc(p["pkl"], 16)
                sync.dma_start(out=sce2.rearrange("p (a b) -> p a b", a=3),
                               in_=dnsce[e].rearrange("(a p) b -> p a b", p=128)
                               ).then_inc(p["pkl"], 16)

        @blk.gpsimd
        def _(gp):
            for k in range(NDC):
                gp.dma_start(out=gwb[:, k * 64:(k + 1) * 64],
                             in_=gwTp[k * 128:(k + 1) * 128, :]).then_inc(p["gwl"], 16)
            for g in range(L1NTT):
                gp.wait_ge(p["rt2"], g + 1)
                gp.dma_start(out=slot_o[g * 128:(g + 1) * 128, :],
                             in_=sw8[:, 0:4]).then_inc(p["sto"], 16)
                gp.dma_start(out=wk_o[g * 128:(g + 1) * 128, :],
                             in_=sw8[:, 4:8]).then_inc(p["sto"], 16)
            for e in range(MYE):
                for bt in range(2):
                    u = e * 2 + bt
                    if u == 0:
                        gp.wait_ge(p["tmr"], 16)
                    if u >= 1:
                        gp.wait_ge(p["btr"], u)
                    gp.indirect_dma_start(
                        out=bts[:, :], out_offset=None,
                        in_=lnp, in_offset=IndirectOffsetOnAxis(tokmap[:, u:u + 1], 0),
                        bounds_check=L1TT - 1, oob_is_err=False,
                    ).then_inc(p["bld"], 16)
                    gp.wait_ge(p["eev"], u + 1)
                    gp.dma_start(out=eo_o[u * 128:(u + 1) * 128, :],
                                 in_=eos[:, :]).then_inc(p["est"], 16)
            gp.wait_ge(p["est"], 16 * NBT)
            gp.wait_ge(p["sto"], 32 * L1NTT)

        @blk.vector
        def _(v):
            v.memset(ones1[:, :], 1.0)
            v.memset(ones1f[:, :], 1.0)
            v.memset(onecb[:, :], 1.0)
            v.drain()
            v.sem_inc(p["bcst"], 1)
            v.wait_ge(p["bcst"], 2)
            v.tensor_copy(io64[:, :], pp1[:, 0:64])
            v.drain()
            v.tensor_copy(ebc[:, :], pp1[:, 64:65])
            v.memset(off[:, :], 0.0)
            v.drain()
            v.sem_inc(p["bcpy"], 1)
            for g in range(L1NTT):
                v.wait_ge(p["gmm"], g + 1)
                pg = pgA if g % 2 == 0 else pgB
                v.tensor_copy(lgt[:, :], pg[:, 0:64]).then_inc(p["gcp"], 1)
                v.drain()
                for k in range(4):
                    v.tensor_reduce(ek[:, k:k + 1], lgt[:, :], axis=AX.X, op=ALU.max)
                    v.drain()
                    e4 = eqh[:, k * 64:(k + 1) * 64]
                    v.tensor_scalar(e4, lgt[:, :], ek[:, k:k + 1], None, ALU.is_equal)
                    v.drain()
                    if k < 3:
                        v.scalar_tensor_tensor(lgt[:, :], e4, -1e30, lgt[:, :],
                                               ALU.mult, ALU.add)
                    v.tensor_mul(eqs[:, :], e4, io64[:, :])
                    v.drain()
                    v.tensor_reduce(ek[:, 4 + k:5 + k], eqs[:, :], axis=AX.X, op=ALU.max)
                    v.drain()
                v.tensor_scalar(ek[:, 15:16], ek[:, 0:1], -1.0, None,
                                ALU.mult).then_inc(p["rt1"], 1)
                v.tensor_add(eqs[:, :], eqh[:, 0:64], eqh[:, 64:128])
                v.drain()
                v.tensor_add(eqs[:, :], eqs[:, :], eqh[:, 128:192])
                v.drain()
                v.tensor_add(eqs[:, :], eqs[:, :], eqh[:, 192:256])
                v.drain()
                v.tensor_copy(selb[:, :], eqs[:, :]).then_inc(p["sel"], 1)
                v.wait_ge(p["pfx"], g + 1)
                px = pxA if g % 2 == 0 else pxB
                for k in range(4):
                    v.tensor_mul(eqh[:, k * 64:(k + 1) * 64],
                                 eqh[:, k * 64:(k + 1) * 64], px[:, 0:64])
                v.drain()
                for k in range(4):
                    v.tensor_reduce(ek[:, 16 + k:17 + k],
                                    eqh[:, k * 64:(k + 1) * 64], axis=AX.X, op=ALU.add)
                v.drain()
                v.wait_ge(p["cst"], g + 1)
                v.tensor_add(off[0:1, :], off[0:1, :], pp2[0:1, 0:64]
                             ).then_inc(p["pxc"], 1)
                v.drain()
                v.sem_inc(p["ofd"], 1)
                v.wait_ge(p["scl"], g + 1)
                v.tensor_reduce(ek[:, 12:13], ek[:, 8:12], axis=AX.X, op=ALU.add)
                v.drain()
                v.reciprocal(ek[:, 13:14], ek[:, 12:13])
                v.drain()
                v.tensor_scalar(ek[:, 8:12], ek[:, 8:12], ek[:, 13:14], None, ALU.mult)
                pos4 = ek[:, 16:20]
                v.tensor_scalar(ek[:, 20:24], ek[:, 4:8], ebc[:, 0:1], None,
                                ALU.subtract)
                v.tensor_scalar(ek[:, 24:28], pos4, 256.0, None, ALU.is_lt)
                v.drain()
                if g >= 1:
                    v.wait_ge(p["sto"], 32 * g)
                v.tensor_scalar(eqs[:, 0:4], ek[:, 4:8], 256.0, None, ALU.mult)
                v.drain()
                v.tensor_add(eqs[:, 0:4], eqs[:, 0:4], pos4)
                v.drain()
                v.tensor_mul(sw8[:, 0:4], eqs[:, 0:4], ek[:, 24:28])
                v.tensor_mul(sw8[:, 4:8], ek[:, 8:12], ek[:, 24:28])
                v.tensor_scalar(eqs[:, 4:8], ek[:, 20:24], 256.0, None, ALU.mult)
                v.drain()
                v.tensor_add(eqs[:, 4:8], eqs[:, 4:8], pos4)
                v.tensor_scalar(eqs[:, 8:12], pos4, 255.5, 1e6, ALU.is_gt, ALU.mult)
                v.tensor_scalar(eqs[:, 12:16], ek[:, 20:24], -0.5, 1e6, ALU.is_lt, ALU.mult)
                v.tensor_scalar(eqs[:, 16:20], ek[:, 20:24], 7.5, 1e6, ALU.is_gt, ALU.mult)
                v.drain()
                v.tensor_add(eqs[:, 4:8], eqs[:, 4:8], eqs[:, 8:12])
                v.drain()
                v.tensor_add(eqs[:, 4:8], eqs[:, 4:8], eqs[:, 12:16])
                v.drain()
                v.tensor_add(eqs[:, 4:8], eqs[:, 4:8], eqs[:, 16:20])
                v.drain()
                v.tensor_copy(tgt[:, :], eqs[:, 4:8]).then_inc(p["rt2"], 1)
                v.drain()
                if g == 0:
                    v.wait_ge(p["nio"], 32)
                    v.memset(ones11[0:1, :], 1.0)
                    v.memset(tokrow[0:1, :], 0.0)
                    v.drain()
                if g >= 1:
                    v.wait_ge(p["tkq"], 4 * g)
                v.tensor_scalar(ohsum[:, :], io2kb[:, :], eqs[:, 4:5], None,
                                ALU.is_equal)
                v.drain()
                for kk in range(1, 4):
                    v.tensor_scalar(ohbuf[:, :], io2kb[:, :], eqs[:, 4 + kk:5 + kk],
                                    None, ALU.is_equal)
                    v.drain()
                    v.tensor_add(ohsum[:, :], ohsum[:, :], ohbuf[:, :])
                    v.drain()
                v.tensor_scalar(tokc[:, :], iocolb[:, :], float(g * 128), None,
                                ALU.add)
                v.drain()
                v.sem_inc(p["ohr"], 1)
                for q in range(4):
                    G = 4 * g + q
                    v.wait_ge(p["tkq"], G + 1)
                    v.tensor_add(tokrow[0:1, q * 512:(q + 1) * 512],
                                 tokrow[0:1, q * 512:(q + 1) * 512], ptk[0:1, 0:512])
                    v.drain()
                    v.sem_inc(p["tka"], 1)
            v.sem_inc(p["trw"], 1)
            for cc in range(16):
                v.wait_ge(p["tcz"], cc + 1)
                v.tensor_copy(tokmap[:, cc:cc + 1], pp1[:, cc:cc + 1])
                v.drain()
                v.sem_inc(p["tmr"], 1)
            for e in range(MYE):
                if e >= 1:
                    v.wait_ge(p["dmm"], 2 * e)
                for mt in range(2):
                    v.wait_ge(p["pkl"], 16 * (e * 6 + 2 * mt + 2))
                    for nib in range(2):
                        emit_dequant(v, pk1[:, :], nib * 4, dqa[:, :], dqb[:, :],
                                     dqu[:, :], dqv[:, :], dqz[:, :], dqo[:, :])
                        dst = wgu[:, (mt * 8 + nib * 4) * L1I:(mt * 8 + nib * 4 + 4) * L1I]
                        v.tensor_mul(dst, dqo[:, :], sce1[:, :])
                        v.drain()
                    v.sem_inc(p["deq"], 1)
                v.wait_ge(p["pkl"], 16 * (e * 6 + 6))
                for nib in range(2):
                    emit_dequant(v, pk2[:, :], nib * 4, dqa[:, 0:3 * D], dqb[:, 0:3 * D],
                                 dqu[:, 0:3 * D], dqv[:, 0:3 * D], dqz[:, 0:3 * D],
                                 dqo[:, 0:3 * D])
                    dst = wdn[:, nib * 3 * D:(nib + 1) * 3 * D]
                    v.tensor_mul(dst, dqo[:, 0:3 * D], sce2[:, :])
                    v.drain()
                v.sem_inc(p["deq"], 1)
                v.sem_inc(p["wgud"], 1)
                for bt in range(2):
                    u = e * 2 + bt
                    v.wait_ge(p["btr"], u + 1)
                    v.tensor_copy(btT[:, :], ptr[:, :]).then_inc(p["btrc"], 1)
                    v.wait_ge(p["hB"], u + 1)
                    hbe = hb.rearrange("p (a two) -> p two a", two=2)
                    p1e = pp1.rearrange("p (a two) -> p two a", two=2)
                    p2e = pp2.rearrange("p (a two) -> p two a", two=2)
                    v.tensor_tensor(hp[:, 0:256], hbe[:, 0, 0:256], p1e[:, 0, :], ALU.mult)
                    v.tensor_tensor(hp[:, 256:384], hbe[:, 0, 256:384], p2e[:, 0, 0:128], ALU.mult)
                    v.tensor_tensor(hp[:, 384:640], hbe[:, 1, 0:256], p1e[:, 1, :], ALU.mult)
                    ins = v.tensor_tensor(hp[:, 640:768], hbe[:, 1, 256:384],
                                          p2e[:, 1, 0:128], ALU.mult)
                    ins.then_inc(p["hpp"], 1)
                    v.drain()
                    v.wait_ge(p["htr"], u + 1)
                    v.tensor_copy(hpT[:, 0:768], ptr[:, 0:768]).then_inc(p["htrc"], 1)
                    v.wait_ge(p["dmm"], u + 1)
                    v.tensor_copy(eos[:, 0:512], pp1[:, :])
                    v.drain()
                    v.tensor_copy(eos[:, 512:1024], pp2[:, :]).then_inc(p["eev"], 1)
                    v.drain()

        @blk.scalar
        def _(s):
            for g in range(L1NTT):
                s.wait_ge(p["rt1"], g + 1)
                s.activation(ek[:, 8:12], ek[:, 0:4], AF.Exp, bias=ek[:, 15:16],
                             scale=1.0).then_inc(p["scl"], 1)
            for e in range(MYE):
                for bt in range(2):
                    u = e * 2 + bt
                    s.wait_ge(p["hA"], u + 1)
                    s.activation(hb[:, 0:512], pp1[:, :], AF.Silu, bias=0.0, scale=1.0)
                    s.drain()
                    s.activation(hb[:, 512:768], pp2[:, 0:256], AF.Silu, bias=0.0,
                                 scale=1.0).then_inc(p["sil"], 1)

        @blk.tensor
        def _(t):
            t.wait_ge(p["ldc"], 16 * 3)
            t.wait_ge(p["gwl"], 16 * 8)
            t.wait_ge(p["bcst"], 1)
            t.matmul(pp1[:, 0:65], ones1f[0:1, :], io65r[0:1, :],
                     start=True, stop=True).then_inc(p["bcst"], 1)
            t.wait_ge(p["bcpy"], 1)
            for g in range(L1NTT):
                pg = pgA if g % 2 == 0 else pgB
                if g >= 2:
                    t.wait_ge(p["gcp"], g - 1)
                t.wait_ge(p["lnl%d" % (g % 2)], 16 * (g // 2 + 1))
                for k in range(NDC):
                    ins = t.matmul(pg[:, 0:64],
                                   lnTs[:, (g % 2) * D + k * 128:(g % 2) * D + (k + 1) * 128],
                                   gwb[:, k * 64:(k + 1) * 64],
                                   start=(k == 0), stop=(k == NDC - 1))
                ins.then_inc(p["gmm"], 1)
                t.wait_ge(p["sel"], g + 1)
                t.wait_ge(p["ofd"], g)
                if g >= 2:
                    t.wait_ge(p["pxc"], g - 1)
                px = pxA if g % 2 == 0 else pxB
                t.matmul(px[:, 0:64], tril[:, :], selb[:, :], start=True, stop=False)
                t.matmul(px[:, 0:64], ones1f[0:1, :], off[0:1, :],
                         start=False, stop=True).then_inc(p["pfx"], 1)
                t.matmul(pp2[0:1, 0:64], onecb[:, 0:1], selb[:, :],
                         start=True, stop=True).then_inc(p["cst"], 1)
                t.wait_ge(p["ohr"], g + 1)
                for q in range(4):
                    G = 4 * g + q
                    if G >= 1:
                        t.wait_ge(p["tka"], G)
                    ins = t.matmul(ptk[0:1, 0:512], tokc[:, 0:1],
                                   ohsum[:, q * 512:(q + 1) * 512],
                                   start=True, stop=True)
                    ins.then_inc(p["tkq"], 1)
            t.wait_ge(p["trw"], 1)
            for cc in range(16):
                t.matmul(pp1[:, cc:cc + 1], tokrow[0:1, cc * 128:(cc + 1) * 128],
                         ones11[0:1, :], start=True, stop=True).then_inc(p["tcz"], 1)
            t.wait_ge(p["tmr"], 16)
            for e in range(MYE):
                t.wait_ge(p["wgud"], e + 1)
                for bt in range(2):
                    u = e * 2 + bt
                    t.wait_ge(p["bld"], 16 * (u + 1))
                    if u >= 1:
                        t.wait_ge(p["btrc"], u)
                    for k in range(NDC):
                        ins = t.transpose(ptr[:, k * 128:(k + 1) * 128],
                                          bts[:, k * 128:(k + 1) * 128], ident[:])
                    ins.then_inc(p["btr"], 1)
                    t.wait_ge(p["btrc"], u + 1)
                    if u >= 1:
                        t.wait_ge(p["eev"], u)
                    for k in range(NDC):
                        t.matmul(pp1[:, :], btT[:, k * 128:(k + 1) * 128],
                                 wgu[:, k * L1I:k * L1I + 512],
                                 start=(k == 0), stop=(k == NDC - 1))
                    for k in range(NDC):
                        ins = t.matmul(pp2[:, 0:256], btT[:, k * 128:(k + 1) * 128],
                                       wgu[:, k * L1I + 512:(k + 1) * L1I],
                                       start=(k == 0), stop=(k == NDC - 1))
                    ins.then_inc(p["hA"], 1)
                    t.wait_ge(p["sil"], u + 1)
                    for k in range(NDC):
                        t.matmul(pp1[:, :], btT[:, k * 128:(k + 1) * 128],
                                 wgu[:, (8 + k) * L1I:(8 + k) * L1I + 512],
                                 start=(k == 0), stop=(k == NDC - 1))
                    for k in range(NDC):
                        ins = t.matmul(pp2[:, 0:256], btT[:, k * 128:(k + 1) * 128],
                                       wgu[:, (8 + k) * L1I + 512:(8 + k + 1) * L1I],
                                       start=(k == 0), stop=(k == NDC - 1))
                    ins.then_inc(p["hB"], 1)
                    t.wait_ge(p["hpp"], u + 1)
                    if u >= 1:
                        t.wait_ge(p["htrc"], u)
                    for k in range(6):
                        ins = t.transpose(ptr[:, k * 128:(k + 1) * 128],
                                          hp[:, k * 128:(k + 1) * 128], ident[:])
                    ins.then_inc(p["htr"], 1)
                    t.wait_ge(p["htrc"], u + 1)
                    for k in range(6):
                        t.matmul(pp1[:, :], hpT[:, k * 128:(k + 1) * 128],
                                 wdn[:, k * D:k * D + 512],
                                 start=(k == 0), stop=(k == 5))
                    for k in range(6):
                        ins = t.matmul(pp2[:, :], hpT[:, k * 128:(k + 1) * 128],
                                       wdn[:, k * D + 512:(k + 1) * D],
                                       start=(k == 0), stop=(k == 5))
                    ins.then_inc(p["dmm"], 1)

    p.close()
    return nc


def build_ph_l1f_post():
    nc = bass.Bass("TRN2", target_bir_lowering=False)
    x_in = nc.dram_tensor("x", [TPC, D], F32, kind="ExternalInput").ap()
    lnp = nc.dram_tensor("lnp", [TPC, D], BF16, kind="ExternalInput").ap()
    lnpT = nc.dram_tensor("lnpT", [TPC, D], BF16, kind="ExternalInput").ap()
    eo_all = nc.dram_tensor("eoall", [L1E * L1CAP, D], BF16, kind="ExternalInput").ap()
    slot = nc.dram_tensor("slot", [TPC, 4], F32, kind="ExternalInput").ap()
    wk4 = nc.dram_tensor("wk4", [TPC, 4], F32, kind="ExternalInput").ap()
    shp = nc.dram_tensor("shp", [2, 512, L1I], I32, kind="ExternalInput").ap()
    shsc = nc.dram_tensor("shsc", [2, 512, L1I], BF16, kind="ExternalInput").ap()
    sdp = nc.dram_tensor("sdp", [384, D], I32, kind="ExternalInput").ap()
    sdsc = nc.dram_tensor("sdsc", [384, D], BF16, kind="ExternalInput").ap()
    idm = nc.dram_tensor("idm", [128, 128], BF16, kind="ExternalInput").ap()
    x_out = nc.dram_tensor("xo", [TPC, D], F32, kind="ExternalOutput").ap()

    p = P(nc)
    p.sem("ldc", "pkl", "deq", "ldx0", "ldx1", "lnl0", "lnl1", "swl",
          "tgc", "gth", "acc", "hA", "sil", "hB", "hpp", "htr", "htrc",
          "dmm", "fin", "st0", "st1")

    xs = nc.alloc_sbuf_tensor("xs", [128, 2 * D], F32)
    lnps = nc.alloc_sbuf_tensor("lnps", [128, 2 * D], BF16)
    lnTs = nc.alloc_sbuf_tensor("lnTs", [128, 2 * D], BF16)
    sw = nc.alloc_sbuf_tensor("sw", [128, 16], F32)
    tgt = nc.alloc_sbuf_tensor("tgti", [128, 4], I32)
    gth = nc.alloc_sbuf_tensor("gth", [128, D], BF16)
    acc = nc.alloc_sbuf_tensor("acc", [128, D], F32)
    wsg = nc.alloc_sbuf_tensor("wsg", [128, 16 * L1I], BF16)
    wsd = nc.alloc_sbuf_tensor("wsd", [128, 6 * D], BF16)
    pk1 = nc.alloc_sbuf_tensor("pk1", [128, 4 * L1I], I32)
    pk2 = nc.alloc_sbuf_tensor("pk2", [128, 3 * D], I32)
    sce1 = nc.alloc_sbuf_tensor("sce1", [128, 4 * L1I], BF16)
    sce2 = nc.alloc_sbuf_tensor("sce2", [128, 3 * D], BF16)
    dqa = nc.alloc_sbuf_tensor("dqa", [128, 4 * L1I], I32)
    dqb = nc.alloc_sbuf_tensor("dqb", [128, 4 * L1I], I32)
    dqu = nc.alloc_sbuf_tensor("dqu", [128, 4 * L1I], U16)
    dqv = nc.alloc_sbuf_tensor("dqv", [128, 4 * L1I], U16)
    dqz = nc.alloc_sbuf_tensor("dqz", [128, 4 * L1I], BF16)
    dqo = nc.alloc_sbuf_tensor("dqo", [128, 4 * L1I], BF16)
    hb = nc.alloc_sbuf_tensor("hb", [128, L1I], F32)
    hp = nc.alloc_sbuf_tensor("hp", [128, L1I], BF16)
    hpT = nc.alloc_sbuf_tensor("hpT", [128, L1I], BF16)
    ot = nc.alloc_sbuf_tensor("ot", [128, 2 * D], F32)
    ident = nc.alloc_sbuf_tensor("ident", [128, 128], BF16)
    ptr = nc.alloc_psum_tensor("ptr", [128, 1024], BF16)
    pp1 = nc.alloc_psum_tensor("pp1", [128, 512], F32)
    pp2 = nc.alloc_psum_tensor("pp2", [128, 512], F32)

    with nc.Block() as blk:
        @blk.sync
        def _(sync):
            sync.dma_start(out=ident[:], in_=idm[:]).then_inc(p["ldc"], 16)
            for mt in range(2):
                sync.dma_start(out=pk1.rearrange("p (a b) -> p a b", a=4),
                               in_=shp[mt].rearrange("(a p) b -> p a b", p=128)
                               ).then_inc(p["pkl"], 16)
                sync.dma_start(out=sce1.rearrange("p (a b) -> p a b", a=4),
                               in_=shsc[mt].rearrange("(a p) b -> p a b", p=128)
                               ).then_inc(p["pkl"], 16)
                sync.wait_ge(p["deq"], mt + 1)
            sync.dma_start(out=pk2.rearrange("p (a b) -> p a b", a=3),
                           in_=sdp.rearrange("(a p) b -> p a b", p=128)
                           ).then_inc(p["pkl"], 16)
            sync.dma_start(out=sce2.rearrange("p (a b) -> p a b", a=3),
                           in_=sdsc.rearrange("(a p) b -> p a b", p=128)
                           ).then_inc(p["pkl"], 16)
            for g in range(NT):
                if g >= 2:
                    sync.wait_ge(p["st%d" % (g % 2)], 16 * (g // 2))
                sync.dma_start(out=xs[:, (g % 2) * D:(g % 2 + 1) * D],
                               in_=x_in[g * 128:(g + 1) * 128, :]).then_inc(
                    p["ldx%d" % (g % 2)], 16)
                if g >= 2:
                    sync.wait_ge(p["hB"], g - 1)
                sync.dma_start(out=lnTs[:, (g % 2) * D:(g % 2 + 1) * D],
                               in_=lnpT[g * 128:(g + 1) * 128, :]).then_inc(
                    p["lnl%d" % (g % 2)], 16)

        @blk.gpsimd
        def _(gp):
            for g in range(NT):
                gp.wait_ge(p["acc"], 4 * g)
                gp.dma_start(out=sw[:, 0:4], in_=slot[g * 128:(g + 1) * 128, :]
                             ).then_inc(p["swl"], 16)
                gp.dma_start(out=sw[:, 4:8], in_=wk4[g * 128:(g + 1) * 128, :]
                             ).then_inc(p["swl"], 16)
                gp.wait_ge(p["tgc"], g + 1)
                for k in range(4):
                    gp.indirect_dma_start(
                        out=gth[:, :], out_offset=None,
                        in_=eo_all, in_offset=IndirectOffsetOnAxis(tgt[:, k:k + 1], 0),
                    ).then_inc(p["gth"], 16)
                    gp.wait_ge(p["acc"], g * 4 + k + 1)
                gp.wait_ge(p["fin"], g + 1)
                gp.dma_start(out=x_out[g * 128:(g + 1) * 128, :],
                             in_=ot[:, (g % 2) * D:(g % 2 + 1) * D]).then_inc(
                    p["st%d" % (g % 2)], 16)
            gp.wait_ge(p["st0"], 16 * NT // 2)
            gp.wait_ge(p["st1"], 16 * NT // 2)

        @blk.vector
        def _(v):
            for mt in range(2):
                v.wait_ge(p["pkl"], 16 * (2 * mt + 2))
                for nib in range(2):
                    emit_dequant(v, pk1[:, :], nib * 4, dqa[:, :], dqb[:, :],
                                 dqu[:, :], dqv[:, :], dqz[:, :], dqo[:, :])
                    dst = wsg[:, (mt * 8 + nib * 4) * L1I:(mt * 8 + nib * 4 + 4) * L1I]
                    v.tensor_mul(dst, dqo[:, :], sce1[:, :])
                    v.drain()
                v.sem_inc(p["deq"], 1)
            v.wait_ge(p["pkl"], 16 * 6)
            for nib in range(2):
                emit_dequant(v, pk2[:, :], nib * 4, dqa[:, 0:3 * D], dqb[:, 0:3 * D],
                             dqu[:, 0:3 * D], dqv[:, 0:3 * D], dqz[:, 0:3 * D],
                             dqo[:, 0:3 * D])
                dst = wsd[:, nib * 3 * D:(nib + 1) * 3 * D]
                v.tensor_mul(dst, dqo[:, 0:3 * D], sce2[:, :])
                v.drain()
            v.sem_inc(p["deq"], 1)
            for g in range(NT):
                o = (g % 2) * D
                v.wait_ge(p["swl"], 32 * (g + 1))
                v.tensor_copy(tgt[:, :], sw[:, 0:4]).then_inc(p["tgc"], 1)
                v.drain()
                v.memset(acc[:, :], 0.0)
                v.drain()
                for k in range(4):
                    v.wait_ge(p["gth"], 16 * (g * 4 + k + 1))
                    v.scalar_tensor_tensor(acc[:, :], gth[:, :], sw[:, 4 + k:5 + k],
                                           acc[:, :], ALU.mult, ALU.add).then_inc(p["acc"], 1)
                    v.drain()
                v.wait_ge(p["hB"], g + 1)
                hbe = hb.rearrange("p (a two) -> p two a", two=2)
                p1e = pp1.rearrange("p (a two) -> p two a", two=2)
                p2e = pp2.rearrange("p (a two) -> p two a", two=2)
                v.tensor_tensor(hp[:, 0:256], hbe[:, 0, 0:256], p1e[:, 0, :], ALU.mult)
                v.tensor_tensor(hp[:, 256:384], hbe[:, 0, 256:384], p2e[:, 0, 0:128], ALU.mult)
                v.tensor_tensor(hp[:, 384:640], hbe[:, 1, 0:256], p1e[:, 1, :], ALU.mult)
                ins = v.tensor_tensor(hp[:, 640:768], hbe[:, 1, 256:384],
                                      p2e[:, 1, 0:128], ALU.mult)
                ins.then_inc(p["hpp"], 1)
                v.drain()
                v.wait_ge(p["htr"], g + 1)
                v.tensor_copy(hpT[:, 0:768], ptr[:, 0:768]).then_inc(p["htrc"], 1)
                v.wait_ge(p["dmm"], g + 1)
                v.wait_ge(p["ldx%d" % (g % 2)], 16 * (g // 2 + 1))
                if g >= 2:
                    v.wait_ge(p["st%d" % (g % 2)], 16 * (g // 2))
                v.tensor_add(acc[:, 0:512], acc[:, 0:512], pp1[:, :])
                v.drain()
                v.tensor_add(acc[:, 512:1024], acc[:, 512:1024], pp2[:, :])
                v.drain()
                otv = ot[:, o:o + D].rearrange("p (a two) -> p two a", two=2)
                xv = xs[:, o:o + D].rearrange("p (a two) -> p two a", two=2)
                v.scalar_tensor_tensor(otv[:, 0, :], acc[:, 0:512], 0.5,
                                       xv[:, 0, :], ALU.mult, ALU.add)
                v.drain()
                ins = v.scalar_tensor_tensor(otv[:, 1, :], acc[:, 512:1024], 0.5,
                                             xv[:, 1, :], ALU.mult, ALU.add)
                ins.then_inc(p["fin"], 1)
                v.drain()

        @blk.scalar
        def _(s):
            for g in range(NT):
                s.wait_ge(p["hA"], g + 1)
                s.activation(hb[:, 0:512], pp1[:, :], AF.Silu, bias=0.0, scale=1.0)
                s.drain()
                s.activation(hb[:, 512:768], pp2[:, 0:256], AF.Silu, bias=0.0,
                             scale=1.0).then_inc(p["sil"], 1)

        @blk.tensor
        def _(t):
            t.wait_ge(p["ldc"], 16)
            t.wait_ge(p["deq"], 3)
            for g in range(NT):
                od = (g % 2) * D
                t.wait_ge(p["lnl%d" % (g % 2)], 16 * (g // 2 + 1))
                if g >= 1:
                    t.wait_ge(p["fin"], g)
                for k in range(NDC):
                    t.matmul(pp1[:, :], lnTs[:, od + k * 128:od + (k + 1) * 128],
                             wsg[:, k * L1I:k * L1I + 512],
                             start=(k == 0), stop=(k == NDC - 1))
                for k in range(NDC):
                    ins = t.matmul(pp2[:, 0:256], lnTs[:, od + k * 128:od + (k + 1) * 128],
                                   wsg[:, k * L1I + 512:(k + 1) * L1I],
                                   start=(k == 0), stop=(k == NDC - 1))
                ins.then_inc(p["hA"], 1)
                t.wait_ge(p["sil"], g + 1)
                for k in range(NDC):
                    t.matmul(pp1[:, :], lnTs[:, od + k * 128:od + (k + 1) * 128],
                             wsg[:, (8 + k) * L1I:(8 + k) * L1I + 512],
                             start=(k == 0), stop=(k == NDC - 1))
                for k in range(NDC):
                    ins = t.matmul(pp2[:, 0:256], lnTs[:, od + k * 128:od + (k + 1) * 128],
                                   wsg[:, (8 + k) * L1I + 512:(8 + k + 1) * L1I],
                                   start=(k == 0), stop=(k == NDC - 1))
                ins.then_inc(p["hB"], 1)
                t.wait_ge(p["hpp"], g + 1)
                if g >= 1:
                    t.wait_ge(p["htrc"], g)
                for k in range(6):
                    ins = t.transpose(ptr[:, k * 128:(k + 1) * 128],
                                      hp[:, k * 128:(k + 1) * 128], ident[:])
                ins.then_inc(p["htr"], 1)
                t.wait_ge(p["htrc"], g + 1)
                for k in range(6):
                    t.matmul(pp1[:, :], hpT[:, k * 128:(k + 1) * 128],
                             wsd[:, k * D:k * D + 512],
                             start=(k == 0), stop=(k == 5))
                for k in range(6):
                    ins = t.matmul(pp2[:, :], hpT[:, k * 128:(k + 1) * 128],
                                   wsd[:, k * D + 512:(k + 1) * D],
                                   start=(k == 0), stop=(k == 5))
                ins.then_inc(p["dmm"], 1)

    p.close()
    return nc


def _expand_scales(scales, G=128):
    return np.repeat(np.asarray(scales, np.float32), G, axis=-1)


def _perm_d(n):
    return np.concatenate([np.arange(0, n, 2), np.arange(1, n, 2)])


def maps_l1f_pre(inputs, xsh):
    idf = np.eye(128, dtype=np.float32)
    return [{"x": xsh[c], "idm": _IDM, "idmF": idf} for c in range(NCORE)]


def maps_l1f_mid(inputs, aux):
    lnp_full, lnpT_full = aux["lnp"], aux["lnpT"]
    gwTp = hT(f32(inputs["L1_ffn_gate_w"]))[_perm_d(D), :]
    gwTp = np.ascontiguousarray(gwTp)
    trilm = np.tril(np.ones((128, 128), np.float32), -1).T.astype(BF)
    maps = []
    gp_ = np.asarray(inputs["L1_ffn_gate_packed"])
    up_ = np.asarray(inputs["L1_ffn_up_packed"])
    dp_ = np.asarray(inputs["L1_ffn_down_packed"])
    gsc = _expand_scales(inputs["L1_ffn_gate_scales"])
    usc = _expand_scales(inputs["L1_ffn_up_scales"])
    dsc = _expand_scales(inputs["L1_ffn_down_scales"])
    for c in range(NCORE):
        sl = slice(c * MYE, (c + 1) * MYE)
        gupc = np.stack([np.transpose(gp_[sl], (0, 2, 1)),
                         np.transpose(up_[sl], (0, 2, 1))], axis=1)
        gusce = np.stack([np.transpose(gsc[sl], (0, 2, 1))[:, ::2, :],
                          np.transpose(usc[sl], (0, 2, 1))[:, ::2, :]], axis=1)
        dnpc = np.transpose(dp_[sl], (0, 2, 1))
        dnpc = dnpc[:, :, _perm_d(D)]
        dnsce = np.transpose(dsc[sl], (0, 2, 1))[:, ::2, :]
        dnsce = dnsce[:, :, _perm_d(D)]
        io65 = np.zeros((1, 65), np.float32)
        io65[0, 0:64] = np.arange(64)
        io65[0, 64] = c * MYE
        maps.append({
            "io2k": np.ascontiguousarray(
                np.broadcast_to(np.arange(2048, dtype=np.float32), (128, 2048))),
            "iocol": np.arange(128, dtype=np.float32).reshape(128, 1),
            "lnp": lnp_full, "lnpT": lnpT_full, "gwTp": gwTp,
            "gup": np.ascontiguousarray(gupc.astype(np.int32)),
            "gusce": np.ascontiguousarray(gusce.astype(BF)),
            "dnp": np.ascontiguousarray(dnpc.astype(np.int32)),
            "dnsce": np.ascontiguousarray(dnsce.astype(BF)),
            "io65": io65, "trilm": trilm, "idm": _IDM,
        })
    return maps


def maps_l1f_post(inputs, xsh, aux):
    eo_all = aux["eo_all"]
    shp = np.stack([np.asarray(inputs["L1_ffn_sh_gate_packed"]).T,
                    np.asarray(inputs["L1_ffn_sh_up_packed"]).T])
    shsc = np.stack([_expand_scales(inputs["L1_ffn_sh_gate_scales"]).T[::2, :],
                     _expand_scales(inputs["L1_ffn_sh_up_scales"]).T[::2, :]])
    sdp = np.asarray(inputs["L1_ffn_sh_down_packed"]).T[:, _perm_d(D)]
    sdsc = _expand_scales(inputs["L1_ffn_sh_down_scales"]).T[::2, :][:, _perm_d(D)]
    maps = []
    for c in range(NCORE):
        maps.append({
            "x": xsh[c], "lnp": aux["lnp_sh"][c], "lnpT": aux["lnpT_sh"][c],
            "eoall": eo_all, "slot": aux["slot"][c], "wk4": aux["wk4"][c],
            "shp": np.ascontiguousarray(shp.astype(np.int32)),
            "shsc": np.ascontiguousarray(shsc.astype(BF)),
            "sdp": np.ascontiguousarray(sdp.astype(np.int32)),
            "sdsc": np.ascontiguousarray(sdsc.astype(BF)),
            "idm": _IDM,
        })
    return maps



L3E = 64


def build_ph_l3f():
    nc = bass.Bass("TRN2", target_bir_lowering=False)
    x_in = nc.dram_tensor("x", [TPC, D], F32, kind="ExternalInput").ap()
    egT = nc.dram_tensor("egT", [L3E, D, 512], BF16, kind="ExternalInput").ap()
    euT = nc.dram_tensor("euT", [L3E, D, 512], BF16, kind="ExternalInput").ap()
    edT = nc.dram_tensor("edT", [L3E, 512, D], BF16, kind="ExternalInput").ap()
    sgT = nc.dram_tensor("sgT", [D, D], BF16, kind="ExternalInput").ap()
    suT = nc.dram_tensor("suT", [D, D], BF16, kind="ExternalInput").ap()
    sdT = nc.dram_tensor("sdT", [D, D], BF16, kind="ExternalInput").ap()
    gwT = nc.dram_tensor("gwT", [D, L3E], BF16, kind="ExternalInput").ap()
    gbias = nc.dram_tensor("gbias", [128, L3E], F32, kind="ExternalInput").ap()
    idm = nc.dram_tensor("idm", [128, 128], BF16, kind="ExternalInput").ap()
    idmF = nc.dram_tensor("idmF", [128, 128], F32, kind="ExternalInput").ap()
    cdram = nc.dram_tensor("cdram", [L3E, TPC], BF16, kind="Internal").ap()
    x_out = nc.dram_tensor("xo", [TPC, D], F32, kind="ExternalOutput").ap()

    H2 = 2048
    NTH = H2 // 128
    NGH = H2 // 512

    p = P(nc)
    p.sem("xld", "stat", "lnr", "lnf", "tr", "trc", "gsc", "rsg", "rcons",
          "ctr", "ctc", "cdw", "swl", "stge", "ssl", "shst", "sdw", "sacc",
          "wld", "bcm", "bcc", "tge", "ssl2", "hst", "dwn", "acv",
          "xld2", "wtr", "wev", "sto")

    xs = nc.alloc_sbuf_tensor("xs", [128, D], F32)
    scr = nc.alloc_sbuf_tensor("scr", [128, D], F32)
    lnb = nc.alloc_sbuf_tensor("lnb", [128, D], BF16)
    stt = nc.alloc_sbuf_tensor("stt", [128, 8], F32)
    lnT = nc.alloc_sbuf_tensor("lnT", [128, NDC * H2], BF16)
    acc = nc.alloc_sbuf_tensor("acc", [128, NDC * H2], F32)
    web = nc.alloc_sbuf_tensor("web", [128, 2 * 8 * 512], BF16)
    weu = nc.alloc_sbuf_tensor("weu", [128, 2 * 8 * 512], BF16)
    wed = nc.alloc_sbuf_tensor("wed", [128, 2 * 4 * D], BF16)
    hsT = nc.alloc_sbuf_tensor("hsT", [128, 8 * 512], BF16)
    sil = nc.alloc_sbuf_tensor("sil", [128, 2 * 512], F32)
    cbc = nc.alloc_sbuf_tensor("cbc", [128, 512], F32)
    csb = nc.alloc_sbuf_tensor("csb", [1, 2 * NGH * 512], BF16)
    ctk = nc.alloc_sbuf_tensor("ctk", [128, 2 * L3E], F32)
    ctkT = nc.alloc_sbuf_tensor("ctkT", [L3E, H2], BF16)
    gwb = nc.alloc_sbuf_tensor("gwb", [128, 8 * L3E], BF16)
    gbb = nc.alloc_sbuf_tensor("gbb", [128, L3E], F32)
    sc0 = nc.alloc_sbuf_tensor("sc0", [128, L3E], F32)
    mw = nc.alloc_sbuf_tensor("mw", [128, L3E], F32)
    mk = nc.alloc_sbuf_tensor("mk", [128, L3E], F32)
    sel = nc.alloc_sbuf_tensor("sel", [128, L3E], F32)
    ge64 = nc.alloc_sbuf_tensor("ge64", [128, L3E], F32)
    cs64 = nc.alloc_sbuf_tensor("cs64", [128, L3E], F32)
    gs8 = nc.alloc_sbuf_tensor("gs8", [128, 8], F32)
    gm8 = nc.alloc_sbuf_tensor("gm8", [128, 8], F32)
    ge8 = nc.alloc_sbuf_tensor("ge8", [128, 8], F32)
    cs8 = nc.alloc_sbuf_tensor("cs8", [128, 8], F32)
    on8 = nc.alloc_sbuf_tensor("on8", [128, 8], F32)
    mxc = nc.alloc_sbuf_tensor("mxc", [128, 4], F32)
    ones1 = nc.alloc_sbuf_tensor("ones1", [1, 128], BF16)
    ident = nc.alloc_sbuf_tensor("ident", [128, 128], BF16)
    identF = nc.alloc_sbuf_tensor("identF", [128, 128], F32)
    pgu = nc.alloc_psum_tensor("pgu", [128, 2048], F32)
    pdb = nc.alloc_psum_tensor("pdb", [128, 2048], F32)

    ptr = pdb[:, 0:1024].bitcast(BF16)

    def lntile(hp, k, t0, w):
        return lnT[:, k * H2 + t0:k * H2 + t0 + w]

    with nc.Block() as blk:
        @blk.sync
        def _(sync):
            sync.dma_start(out=ident[:], in_=idm[:]).then_inc(p["xld"], 16)
            sync.dma_start(out=identF[:], in_=idmF[:]).then_inc(p["xld"], 16)
            sync.dma_start(out=gbb[:], in_=gbias[:]).then_inc(p["xld"], 16)
            for k in range(NDC):
                sync.dma_start(out=gwb[:, k * L3E:(k + 1) * L3E],
                               in_=gwT[k * 128:(k + 1) * 128, :]).then_inc(p["xld"], 16)
            for hp in range(2):
                if hp == 1:
                    sync.wait_ge(p["wev"], 32)
                for i in range(NTH):
                    gi = hp * NTH + i
                    if gi >= 1:
                        sync.wait_ge(p["lnf"], gi)
                    sync.dma_start(out=xs[:], in_=x_in[hp * H2 + i * 128:
                                                       hp * H2 + (i + 1) * 128, :]
                                   ).then_inc(p["xld"], 16)
                sync.wait_ge(p["ctc"], NTH * (hp + 1))
                sync.dma_start(out=cdram[0:L3E, hp * H2:(hp + 1) * H2],
                               in_=ctkT[0:L3E, :]).then_inc(p["cdw"], 16)
                sync.wait_ge(p["acv"], 2 + 32 * L3E * (hp + 1))
                for j in range(NTH):
                    gj = hp * NTH + j
                    if gj >= 1:
                        sync.wait_ge(p["wev"], 2 * gj)
                    sync.dma_start(out=xs[:], in_=x_in[hp * H2 + j * 128:
                                                       hp * H2 + (j + 1) * 128, :]
                                   ).then_inc(p["xld2"], 16)

        @blk.gpsimd
        def _(gp):
            for hp in range(2):
                if hp == 1:
                    gp.wait_ge(p["acv"], 2 + 32 * L3E)
                for k in range(NDC):
                    gp.dma_start(out=web[:, k * 1024:(k + 1) * 1024],
                                 in_=sgT[k * 128:(k + 1) * 128, :]).then_inc(p["swl"], 16)
                    gp.dma_start(out=weu[:, k * 1024:(k + 1) * 1024],
                                 in_=suT[k * 128:(k + 1) * 128, :]).then_inc(p["swl"], 16)
                    gp.dma_start(out=wed[:, k * 1024:(k + 1) * 1024],
                                 in_=sdT[k * 128:(k + 1) * 128, :]).then_inc(p["swl"], 16)
                gp.wait_ge(p["cdw"], 16 * (hp + 1))
                gp.wait_ge(p["sdw"], 32 * (hp + 1))
                gp.wait_ge(p["stge"], 32 * (hp + 1))
                for e in range(L3E):
                    eg = hp * L3E + e
                    sl = e % 2
                    if e >= 2:
                        gp.wait_ge(p["acv"], 2 + 32 * (eg - 1))
                    for k in range(NDC):
                        gp.dma_start(out=web[:, (sl * 8 + k) * 512:(sl * 8 + k + 1) * 512],
                                     in_=egT[e, k * 128:(k + 1) * 128, :]
                                     ).then_inc(p["wld"], 16)
                        gp.dma_start(out=weu[:, (sl * 8 + k) * 512:(sl * 8 + k + 1) * 512],
                                     in_=euT[e, k * 128:(k + 1) * 128, :]
                                     ).then_inc(p["wld"], 16)
                    for ic in range(4):
                        gp.dma_start(out=wed[:, (sl * 4 + ic) * D:(sl * 4 + ic + 1) * D],
                                     in_=edT[e, ic * 128:(ic + 1) * 128, :]
                                     ).then_inc(p["wld"], 16)
                    gp.dma_start(
                        out=csb[0:1, sl * NGH * 512:(sl + 1) * NGH * 512],
                        in_=cdram[e:e + 1, hp * H2:(hp + 1) * H2]
                        ).then_inc(p["wld"], 16)
                for j in range(NTH):
                    gj = hp * NTH + j
                    gp.wait_ge(p["wev"], 2 * gj + 2)
                    gp.dma_start(out=x_out[hp * H2 + j * 128:hp * H2 + (j + 1) * 128, :],
                                 in_=scr[:, :]).then_inc(p["sto"], 16)
            gp.wait_ge(p["sto"], 16 * 2 * NTH)

        @blk.scalar
        def _(s):
            for hp in range(2):
                for i in range(NTH):
                    gi = hp * NTH + i
                    s.wait_ge(p["stat"], gi + 1)
                    emit_rstd_s(s, stt).then_inc(p["lnr"], 1)
                    s.wait_ge(p["gsc"], gi + 1)
                    if gi >= 1:
                        s.wait_ge(p["rcons"], gi)
                    s.activation(sc0[:, :], pgu[:, 0:L3E], AF.Sigmoid
                                 ).then_inc(p["rsg"], 1)
                for g in range(NGH):
                    for ic in range(8):
                        u = (hp * NGH + g) * 8 + ic
                        s.wait_ge(p["stge"], u + 1)
                        if u >= 2:
                            s.wait_ge(p["shst"], u - 1)
                        s.activation(sil[:, (ic % 2) * 512:(ic % 2 + 1) * 512],
                                     pgu[:, (ic % 2) * 512:(ic % 2 + 1) * 512],
                                     AF.Silu).then_inc(p["ssl"], 1)
                for e in range(L3E):
                    eg = hp * L3E + e
                    for g in range(NGH):
                        for w in range(2):
                            W = eg * 8 + g * 2 + w
                            s.wait_ge(p["tge"], W + 1)
                            for ic2 in range(2):
                                ic = w * 2 + ic2
                                if W >= 1:
                                    s.wait_ge(p["hst"], W)
                                s.activation(sil[:, ic2 * 512:(ic2 + 1) * 512],
                                             pgu[:, ic2 * 512:(ic2 + 1) * 512],
                                             AF.Silu).then_inc(p["ssl2"], 1)

        @blk.tensor
        def _(t):
            t.wait_ge(p["xld"], 16 * (3 + NDC))
            for hp in range(2):
                if hp == 1:
                    t.wait_ge(p["wev"], 32)
                for i in range(NTH):
                    gi = hp * NTH + i
                    t.wait_ge(p["lnf"], gi + 1)
                    if gi >= 1:
                        t.wait_ge(p["trc"], gi)
                    for k in range(NDC):
                        ins = t.transpose(ptr[:, k * 128:(k + 1) * 128],
                                          lnb[:, k * 128:(k + 1) * 128], ident[:])
                    ins.then_inc(p["tr"], 1)
                    t.wait_ge(p["trc"], gi + 1)
                    if gi >= 1:
                        t.wait_ge(p["rsg"], gi)
                    for k in range(NDC):
                        ins = t.matmul(pgu[:, 0:L3E], lntile(hp, k, i * 128, 128),
                                       gwb[:, k * L3E:(k + 1) * L3E],
                                       start=(k == 0), stop=(k == NDC - 1))
                    ins.then_inc(p["gsc"], 1)
                    t.wait_ge(p["rcons"], gi + 1)
                    if gi >= 1:
                        t.wait_ge(p["ctc"], gi)
                    t.transpose(pdb[0:L3E, 1024:1152],
                                ctk[:, (gi % 2) * L3E:(gi % 2 + 1) * L3E],
                                identF[:]).then_inc(p["ctr"], 1)
                t.wait_ge(p["swl"], 16 * 24 * (hp + 1))
                t.wait_ge(p["trc"], NTH * (hp + 1))
                t.wait_ge(p["rsg"], NTH * (hp + 1))
                for g in range(NGH):
                    tg = g * 512
                    for ic in range(8):
                        u = (hp * NGH + g) * 8 + ic
                        if u >= 2:
                            t.wait_ge(p["shst"], u - 1)
                        for k in range(NDC):
                            ins = t.matmul(pgu[:, (ic % 2) * 512:(ic % 2 + 1) * 512],
                                           web[:, k * 1024 + ic * 128:k * 1024 + ic * 128 + 128],
                                           lntile(hp, k, tg, 512),
                                           start=(k == 0), stop=(k == NDC - 1))
                        for k in range(NDC):
                            ins = t.matmul(pgu[:, 1024 + (ic % 2) * 512:1024 + (ic % 2 + 1) * 512],
                                           weu[:, k * 1024 + ic * 128:k * 1024 + ic * 128 + 128],
                                           lntile(hp, k, tg, 512),
                                           start=(k == 0), stop=(k == NDC - 1))
                        ins.then_inc(p["stge"], 1)
                    t.wait_ge(p["shst"], (hp * NGH + g) * 8 + 8)
                    for hc in range(8):
                        dg = (hp * NGH + g) * 8 + hc
                        if dg >= 4:
                            t.wait_ge(p["sacc"], dg - 3)
                        for ic in range(8):
                            ins = t.matmul(pdb[:, (hc % 4) * 512:(hc % 4 + 1) * 512],
                                           wed[:, ic * 1024 + hc * 128:ic * 1024 + hc * 128 + 128],
                                           hsT[:, ic * 512:(ic + 1) * 512],
                                           start=(ic == 0), stop=(ic == 7))
                        ins.then_inc(p["sdw"], 1)
                for e in range(L3E):
                    eg = hp * L3E + e
                    sl = e % 2
                    t.wait_ge(p["wld"], 16 * 21 * (eg + 1))
                    if e == 0:
                        t.wait_ge(p["sacc"], 32 * (hp + 1))
                        t.wait_ge(p["shst"], 32 * (hp + 1))
                    for g in range(NGH):
                        tg = g * 512
                        if eg * NGH + g >= 1:
                            t.wait_ge(p["bcc"], eg * NGH + g)
                        t.matmul(pdb[:, 1536:2048], ones1[0:1, :],
                                 csb[0:1, (sl * NGH + g) * 512:(sl * NGH + g + 1) * 512],
                                 start=True, stop=True).then_inc(p["bcm"], 1)
                        for w in range(2):
                            W = eg * 8 + g * 2 + w
                            if W >= 1:
                                t.wait_ge(p["hst"], W)
                            for ic2 in range(2):
                                ic = w * 2 + ic2
                                for k in range(NDC):
                                    ins = t.matmul(
                                        pgu[:, ic2 * 512:(ic2 + 1) * 512],
                                        web[:, (sl * 8 + k) * 512 + ic * 128:
                                            (sl * 8 + k) * 512 + ic * 128 + 128],
                                        lntile(hp, k, tg, 512),
                                        start=(k == 0), stop=(k == NDC - 1))
                                for k in range(NDC):
                                    ins = t.matmul(
                                        pgu[:, 1024 + ic2 * 512:1024 + (ic2 + 1) * 512],
                                        weu[:, (sl * 8 + k) * 512 + ic * 128:
                                            (sl * 8 + k) * 512 + ic * 128 + 128],
                                        lntile(hp, k, tg, 512),
                                        start=(k == 0), stop=(k == NDC - 1))
                            ins.then_inc(p["tge"], 1)
                        t.wait_ge(p["hst"], eg * 8 + g * 2 + 2)
                        for hc in range(8):
                            dg = eg * 32 + g * 8 + hc
                            t.wait_ge(p["acv"], dg + 1)
                            for ic in range(4):
                                ins = t.matmul(
                                    pdb[:, (hc % 2) * 512:(hc % 2 + 1) * 512],
                                    wed[:, (sl * 4 + ic) * D + hc * 128:
                                        (sl * 4 + ic) * D + hc * 128 + 128],
                                    hsT[:, ic * 512:(ic + 1) * 512],
                                    start=(ic == 0), stop=(ic == 3))
                            ins.then_inc(p["dwn"], 1)
                t.wait_ge(p["acv"], 2 + 32 * L3E * (hp + 1))
                for j in range(NTH):
                    gj = hp * NTH + j
                    for dgrp in range(2):
                        wq = 2 * gj + dgrp
                        if wq >= 2:
                            t.wait_ge(p["wev"], wq - 1)
                        for hh in range(4):
                            hc = dgrp * 4 + hh
                            ins = t.transpose(
                                pdb[:, dgrp * 512 + hh * 128:dgrp * 512 + hh * 128 + 128],
                                acc[:, hc * H2 + j * 128:hc * H2 + j * 128 + 128],
                                identF[:])
                        ins.then_inc(p["wtr"], 1)

        @blk.vector
        def _(v):
            v.memset(ones1[:, :], 1.0)
            v.memset(on8[:, :], 1.0)
            v.drain()
            v.sem_inc(p["acv"], 2)
            for hp in range(2):
                if hp == 1:
                    v.wait_ge(p["sto"], 256)
                    v.wait_ge(p["cdw"], 16)
                for i in range(NTH):
                    gi = hp * NTH + i
                    v.wait_ge(p["xld"], 16 * (3 + NDC + gi + 1))
                    emit_ln_v(v, xs[:], stt, scr[:], None).then_inc(p["stat"], 1)
                    v.wait_ge(p["lnr"], gi + 1)
                    v.tensor_scalar(lnb[:], xs[:], stt[:, 2:3], stt[:, 4:5],
                                    ALU.subtract, ALU.mult)
                    v.drain()
                    v.sem_inc(p["lnf"], 1)
                    v.wait_ge(p["tr"], gi + 1)
                    for k in range(NDC):
                        ins = v.tensor_copy(lntile(hp, k, i * 128, 128),
                                            ptr[:, k * 128:(k + 1) * 128])
                    v.drain()
                    ins.then_inc(p["trc"], 1)
                    v.wait_ge(p["rsg"], gi + 1)
                    v.tensor_add(sc0[:], sc0[:], gbb[:])
                    v.memset(gm8[:], 0.0)
                    v.memset(sel[:], 0.0)
                    v.memset(mxc[:, 1:2], 0.0)
                    v.memset(mxc[:, 3:4], 0.0)
                    v.drain()
                    for g in range(8):
                        v.reduce_max(gs8[:, g:g + 1], sc0[:, 8 * g:8 * g + 8], axis=AX.X)
                    v.drain()
                    for j in range(4):
                        v.reduce_max(mxc[:, 0:1], gs8[:], axis=AX.X)
                        v.drain()
                        v.tensor_scalar(ge8[:], gs8[:], mxc[:, 0:1], None, ALU.is_ge)
                        v.drain()
                        v.tensor_tensor_scan(cs8[:], ge8[:],
                                             mxc[:, 3:4].broadcast_to((128, 8)),
                                             0.0, ALU.add, ALU.add)
                        v.drain()
                        v.tensor_scalar(cs8[:], cs8[:], 1.0, None, ALU.is_le)
                        v.drain()
                        v.tensor_mul(ge8[:], ge8[:], cs8[:])
                        v.drain()
                        v.tensor_add(gm8[:], gm8[:], ge8[:])
                        v.drain()
                        v.scalar_tensor_tensor(gs8[:], ge8[:], -1e30, gs8[:],
                                               ALU.mult, ALU.add)
                        v.drain()
                    for g in range(8):
                        v.tensor_scalar(mw[:, 8 * g:8 * g + 8], on8[:],
                                        gm8[:, g:g + 1], None, ALU.mult)
                    v.drain()
                    v.tensor_mul(mw[:], sc0[:], mw[:])
                    v.drain()
                    v.tensor_copy(mk[:], mw[:])
                    v.drain()
                    for j in range(6):
                        v.reduce_max(mxc[:, 0:1], mw[:], axis=AX.X)
                        v.drain()
                        v.tensor_scalar(ge64[:], mw[:], mxc[:, 0:1], None, ALU.is_ge)
                        v.drain()
                        v.tensor_tensor_scan(cs64[:], ge64[:],
                                             mxc[:, 3:4].broadcast_to((128, 64)),
                                             0.0, ALU.add, ALU.add)
                        v.drain()
                        v.tensor_scalar(cs64[:], cs64[:], 1.0, None, ALU.is_le)
                        v.drain()
                        v.tensor_mul(ge64[:], ge64[:], cs64[:])
                        v.drain()
                        v.tensor_add(sel[:], sel[:], ge64[:])
                        v.tensor_add(mxc[:, 1:2], mxc[:, 1:2], mxc[:, 0:1])
                        v.drain()
                        v.scalar_tensor_tensor(mw[:], ge64[:], -1e30, mw[:],
                                               ALU.mult, ALU.add)
                        v.drain()
                    v.tensor_scalar(mxc[:, 1:2], mxc[:, 1:2], 1e-8, None, ALU.add)
                    v.drain()
                    v.reciprocal(mxc[:, 2:3], mxc[:, 1:2])
                    v.drain()
                    v.tensor_mul(mk[:], mk[:], sel[:])
                    v.drain()
                    if gi >= 2:
                        v.wait_ge(p["ctr"], gi - 1)
                    v.tensor_scalar(ctk[:, (gi % 2) * L3E:(gi % 2 + 1) * L3E], mk[:],
                                    mxc[:, 2:3], None, ALU.mult)
                    v.drain()
                    v.sem_inc(p["rcons"], 1)
                    v.wait_ge(p["ctr"], gi + 1)
                    v.tensor_copy(ctkT[0:L3E, i * 128:(i + 1) * 128],
                                  pdb[0:L3E, 1024:1152])
                    v.drain()
                    v.sem_inc(p["ctc"], 1)
                for g in range(NGH):
                    tg = g * 512
                    for ic in range(8):
                        u = (hp * NGH + g) * 8 + ic
                        v.wait_ge(p["ssl"], u + 1)
                        v.tensor_mul(sil[:, (ic % 2) * 512:(ic % 2 + 1) * 512],
                                     sil[:, (ic % 2) * 512:(ic % 2 + 1) * 512],
                                     pgu[:, 1024 + (ic % 2) * 512:1024 + (ic % 2 + 1) * 512])
                        v.drain()
                        v.tensor_copy(hsT[:, ic * 512:(ic + 1) * 512],
                                      sil[:, (ic % 2) * 512:(ic % 2 + 1) * 512])
                        v.drain()
                        v.sem_inc(p["shst"], 1)
                    for hc in range(8):
                        dg = (hp * NGH + g) * 8 + hc
                        v.wait_ge(p["sdw"], dg + 1)
                        v.tensor_copy(acc[:, hc * H2 + tg:hc * H2 + tg + 512],
                                      pdb[:, (hc % 4) * 512:(hc % 4 + 1) * 512])
                        v.drain()
                        v.sem_inc(p["sacc"], 1)
                for e in range(L3E):
                    eg = hp * L3E + e
                    for g in range(NGH):
                        tg = g * 512
                        v.wait_ge(p["bcm"], eg * NGH + g + 1)
                        v.tensor_copy(cbc[:], pdb[:, 1536:2048])
                        v.drain()
                        v.sem_inc(p["bcc"], 1)
                        for w in range(2):
                            W = eg * 8 + g * 2 + w
                            for ic2 in range(2):
                                ic = w * 2 + ic2
                                v.wait_ge(p["ssl2"], 2 * W + ic2 + 1)
                                v.tensor_mul(sil[:, ic2 * 512:(ic2 + 1) * 512],
                                             sil[:, ic2 * 512:(ic2 + 1) * 512],
                                             pgu[:, 1024 + ic2 * 512:1024 + (ic2 + 1) * 512])
                                v.drain()
                                v.tensor_mul(hsT[:, ic * 512:(ic + 1) * 512],
                                             sil[:, ic2 * 512:(ic2 + 1) * 512], cbc[:])
                                v.drain()
                            v.sem_inc(p["hst"], 1)
                        for hc in range(8):
                            dg = eg * 32 + g * 8 + hc
                            v.wait_ge(p["dwn"], dg + 1)
                            v.tensor_add(acc[:, hc * H2 + tg:hc * H2 + tg + 512],
                                         acc[:, hc * H2 + tg:hc * H2 + tg + 512],
                                         pdb[:, (hc % 2) * 512:(hc % 2 + 1) * 512])
                            v.drain()
                            v.sem_inc(p["acv"], 1)
                for j in range(NTH):
                    gj = hp * NTH + j
                    for dgrp in range(2):
                        wq = 2 * gj + dgrp
                        v.wait_ge(p["wtr"], wq + 1)
                        v.wait_ge(p["xld2"], 16 * (gj + 1))
                        if gj >= 1 and dgrp == 0:
                            v.wait_ge(p["sto"], 16 * gj)
                        v.scalar_tensor_tensor(scr[:, dgrp * 512:(dgrp + 1) * 512],
                                               pdb[:, dgrp * 512:(dgrp + 1) * 512],
                                               0.5, xs[:, dgrp * 512:(dgrp + 1) * 512],
                                               ALU.mult, ALU.add)
                        v.drain()
                        v.sem_inc(p["wev"], 1)

    p.close()
    return nc


def maps_l3f(inputs, xsh):
    egT = np.ascontiguousarray(np.transpose(f32(inputs["L3_ffn_eg_w"]), (0, 2, 1)).astype(BF))
    euT = np.ascontiguousarray(np.transpose(f32(inputs["L3_ffn_eu_w"]), (0, 2, 1)).astype(BF))
    edT = np.ascontiguousarray(np.transpose(f32(inputs["L3_ffn_ed_w"]), (0, 2, 1)).astype(BF))
    sgT = np.ascontiguousarray(hT(f32(inputs["L3_ffn_sg_w"])).astype(BF))
    suT = np.ascontiguousarray(hT(f32(inputs["L3_ffn_su_w"])).astype(BF))
    sdT = np.ascontiguousarray(hT(f32(inputs["L3_ffn_sd_w"])).astype(BF))
    gwT = np.ascontiguousarray(hT(f32(inputs["L3_ffn_gate_w"])).astype(BF))
    gbias = np.ascontiguousarray(np.broadcast_to(f32(inputs["L3_ffn_gate_bias"]), (128, 64)))
    idmF = np.eye(128, dtype=np.float32)
    return [{"x": xsh[c], "egT": egT, "euT": euT, "edT": edT, "sgT": sgT,
             "suT": suT, "sdT": sdT, "gwT": gwT, "gbias": gbias,
             "idm": _IDM, "idmF": idmF} for c in range(NCORE)]




class Ser:
    def __init__(self):
        self.items = []

    def ld(self, out, in_):
        self.items.append(("sync", lambda e, o=out, i=in_: e.dma_start(out=o, in_=i), 16))

    def ldg(self, pairs):
        def f(e, ps=pairs):
            for (o, i) in ps:
                e.dma_start(out=o, in_=i).then_inc(self.gsem, 16)
            return None
        self.items.append(("sync", f, 16 * len(pairs)))

    def st(self, out, in_):
        self.items.append(("gpsimd", lambda e, o=out, i=in_: e.dma_start(out=o, in_=i), 16))

    def stg(self, pairs):
        def f(e, ps=pairs):
            for (o, i) in ps:
                e.dma_start(out=o, in_=i).then_inc(self.gsem, 16)
            return None
        self.items.append(("gpsimd", f, 16 * len(pairs)))

    def gp(self, fn):
        self.items.append(("gpsimd", fn, 16))

    def mmg(self, triples):
        def f(e, ts=triples):
            n = len(ts)
            for j, (o, l, r) in enumerate(ts):
                ins = e.matmul(o, l, r, start=(j == 0), stop=(j == n - 1))
            return ins
        self.items.append(("tensor", f, 1))

    def mms(self, groups):
        def f(e, gs=groups):
            for ts in gs:
                n = len(ts)
                for j, (o, l, r) in enumerate(ts):
                    ins = e.matmul(o, l, r, start=(j == 0), stop=(j == n - 1))
            return ins
        self.items.append(("tensor", f, 1))

    def tr(self, out, in_, ident):
        self.items.append(("tensor",
                           lambda e, o=out, i=in_, d=ident: e.transpose(o, i, d), 1))

    def v(self, fn):
        self.items.append(("vector", fn, 1))

    def s(self, fn):
        self.items.append(("scalar", fn, 1))

    def build(self, nc):
        eng_items = {k: [] for k in ("sync", "gpsimd", "vector", "scalar", "tensor")}
        tot = 0
        for (eng, fn, inc) in self.items:
            eng_items[eng].append((tot, fn, inc))
            tot += inc
        total = tot
        with nc.semaphore("gsem") as gsem, nc.Block() as blk:
            self.gsem = gsem
            def mk(engname):
                def body(e):
                    for (before, fn, inc) in eng_items[engname]:
                        if before > 0:
                            e.wait_ge(gsem, before)
                        ins = fn(e)
                        if ins is not None:
                            ins.then_inc(gsem, inc)
                    if engname == "gpsimd":
                        e.wait_ge(gsem, total)
                return body
            blk.sync(mk("sync"))
            blk.gpsimd(mk("gpsimd"))
            blk.vector(mk("vector"))
            blk.scalar(mk("scalar"))
            blk.tensor(mk("tensor"))
        return nc



L2H, L2NCk = 4, SEQ // 128


def build_ph_l2a():
    nc = bass.Bass("TRN2", target_bir_lowering=False)
    x_in = nc.dram_tensor("x", [TPC, D], F32, kind="ExternalInput").ap()
    wqd = nc.dram_tensor("wq", [D, D], BF16, kind="ExternalInput").ap()
    wkd = nc.dram_tensor("wk", [D, D], BF16, kind="ExternalInput").ap()
    wvd = nc.dram_tensor("wv", [D, D], BF16, kind="ExternalInput").ap()
    wbd = nc.dram_tensor("wb", [128, 8 * L2H], BF16, kind="ExternalInput").ap()
    cvw = nc.dram_tensor("cvw", [128, 3 * 32], F32, kind="ExternalInput").ap()
    lmw = nc.dram_tensor("lmw", [128, 8 * 32], F32, kind="ExternalInput").ap()
    rw1 = nc.dram_tensor("rw1", [1152, 2176], BF16, kind="ExternalInput").ap()
    rb1 = nc.dram_tensor("rb1", [128, 17], F32, kind="ExternalInput").ap()
    rw2 = nc.dram_tensor("rw2", [2176, 16], BF16, kind="ExternalInput").ap()
    rb2 = nc.dram_tensor("rb2", [16, 1], F32, kind="ExternalInput").ap()
    mxw = nc.dram_tensor("mxw", [128, 8], F32, kind="ExternalInput").ap()
    onw = nc.dram_tensor("onw", [128, 8], F32, kind="ExternalInput").ap()
    wod = nc.dram_tensor("wo", [D, D], BF16, kind="ExternalInput").ap()
    mku = nc.dram_tensor("mku", [128, 128], F32, kind="ExternalInput").ap()
    mkl = nc.dram_tensor("mkl", [128, 128], F32, kind="ExternalInput").ap()
    mki = nc.dram_tensor("mki", [128, 128], BF16, kind="ExternalInput").ap()
    idm = nc.dram_tensor("idm", [128, 128], BF16, kind="ExternalInput").ap()
    idmF = nc.dram_tensor("idmF", [128, 128], F32, kind="ExternalInput").ap()
    d_ln = nc.dram_tensor("dln", [D, TPC], BF16, kind="Internal").ap()
    d_q = nc.dram_tensor("dq", [D, TPC], BF16, kind="Internal").ap()
    d_k = nc.dram_tensor("dk2", [D, TPC], BF16, kind="Internal").ap()
    d_v = nc.dram_tensor("dv", [D, TPC], BF16, kind="Internal").ap()
    d_lo = nc.dram_tensor("dlo", [D, TPC], BF16, kind="Internal").ap()
    d_mi = nc.dram_tensor("dmi", [D, TPC], BF16, kind="Internal").ap()
    d_do = nc.dram_tensor("ddo", [D, TPC], BF16, kind="Internal").ap()
    d_be = nc.dram_tensor("dbe", [L2H, TPC], F32, kind="Internal").ap()
    x_out = nc.dram_tensor("xo", [TPC, D], F32, kind="ExternalOutput").ap()

    xs = nc.alloc_sbuf_tensor("xs", [128, D], F32)
    lnb = nc.alloc_sbuf_tensor("lnb", [128, D], BF16)
    stt = nc.alloc_sbuf_tensor("stt", [128, 8], F32)
    lnsq = nc.alloc_sbuf_tensor("lnsq", [128, 8 * SEQ], BF16)
    wqb = nc.alloc_sbuf_tensor("wqb", [128, 8 * D], BF16)
    rawb = nc.alloc_sbuf_tensor("rawb", [128, 8 * (SEQ + 32)], BF16)
    outbf = nc.alloc_sbuf_tensor("outbf", [128, 8 * SEQ], BF16)
    cvac = nc.alloc_sbuf_tensor("cvac", [128, SEQ], F32)
    besb = nc.alloc_sbuf_tensor("besb", [1, SEQ], F32)
    wbb = nc.alloc_sbuf_tensor("wbb", [128, 8 * L2H], BF16)
    cvb = nc.alloc_sbuf_tensor("cvb", [128, 3 * 32], F32)
    lmb = nc.alloc_sbuf_tensor("lmb", [128, 8 * 32], F32)
    ident = nc.alloc_sbuf_tensor("ident", [128, 128], BF16)
    identF = nc.alloc_sbuf_tensor("identF", [128, 128], F32)
    qtc = nc.alloc_sbuf_tensor("qtc", [128, 256], BF16)
    ktc = nc.alloc_sbuf_tensor("ktc", [128, 256], BF16)
    vtc = nc.alloc_sbuf_tensor("vtc", [128, 256], BF16)
    ber = nc.alloc_sbuf_tensor("ber", [1, 128], F32)
    onec = nc.alloc_sbuf_tensor("onec", [128, 1], F32)
    one1 = nc.alloc_sbuf_tensor("one1", [1, 128], F32)
    onebc = nc.alloc_sbuf_tensor("onebc", [128, 1], BF16)
    rowsc = nc.alloc_sbuf_tensor("rowsc", [1, 256], F32)
    bcst = nc.alloc_sbuf_tensor("bcst", [128, 3 * 128], F32)
    ften = nc.alloc_sbuf_tensor("ften", [128, 4 * 128], F32)
    kbt = nc.alloc_sbuf_tensor("kbt", [128, 256], BF16)
    vbt = nc.alloc_sbuf_tensor("vbt", [128, 256], BF16)
    ktok = nc.alloc_sbuf_tensor("ktok", [128, 256], BF16)
    kbtok = nc.alloc_sbuf_tensor("kbtok", [128, 256], BF16)
    vbtok = nc.alloc_sbuf_tensor("vbtok", [128, 256], BF16)
    mstk = nc.alloc_sbuf_tensor("mstk", [128, 7 * 128], F32)
    nstk = nc.alloc_sbuf_tensor("nstk", [128, 7 * 128], F32)
    rr = nc.alloc_sbuf_tensor("rr", [128, 256], F32)
    rrb = nc.alloc_sbuf_tensor("rrb", [128, 128], BF16)
    usb = nc.alloc_sbuf_tensor("usb", [128, 256], BF16)
    uisb = nc.alloc_sbuf_tensor("uisb", [128, 256], BF16)
    wmt = nc.alloc_sbuf_tensor("wmt", [128, 256], BF16)
    attns = nc.alloc_sbuf_tensor("attns", [128, 128], BF16)
    ssb = nc.alloc_sbuf_tensor("ssb", [128, 512], F32)
    sbf = nc.alloc_sbuf_tensor("sbf", [128, 512], BF16)
    dosb = nc.alloc_sbuf_tensor("dosb", [128, 256], BF16)
    rw1s = nc.alloc_sbuf_tensor("rw1s", [128, 9 * 128], BF16)
    rw2b = nc.alloc_sbuf_tensor("rw2b", [128, 17 * 16], BF16)
    rb1b = nc.alloc_sbuf_tensor("rb1b", [128, 17], F32)
    rb2b = nc.alloc_sbuf_tensor("rb2b", [16, 1], F32)
    mxwb = nc.alloc_sbuf_tensor("mxwb", [128, 8], F32)
    onwb = nc.alloc_sbuf_tensor("onwb", [128, 8], F32)
    featT = nc.alloc_sbuf_tensor("featT", [128, 128], BF16)
    lgsb = nc.alloc_sbuf_tensor("lgsb", [16, 128], F32)
    ptok = nc.alloc_sbuf_tensor("ptok", [128, 16], F32)
    rmrow = nc.alloc_sbuf_tensor("rmrow", [1, 128], F32)
    rbc = nc.alloc_sbuf_tensor("rbc", [128, 128], F32)
    osb = nc.alloc_sbuf_tensor("osb", [128, D], F32)
    sqsc = nc.alloc_sbuf_tensor("sqsc", [128, 256], BF16)
    otile = outbf[:, 0:5120]
    h1sb = outbf[:, 5120:5120 + 17 * 128]
    onT = outbf[:, 7296:7296 + 1024]
    rowbk = rawb[0:1, 0:56 * 128]
    prow = rawb[0:1, 8192:8192 + 4096].bitcast(F32)
    pbc = lnsq[:, 0:4096].bitcast(F32)
    mixd = lnsq[:, 4096:6144].bitcast(F32)
    p0 = nc.alloc_psum_tensor("p0", [128, 512], F32)
    p1 = nc.alloc_psum_tensor("p1", [128, 512], F32)
    p2 = nc.alloc_psum_tensor("p2", [128, 512], F32)
    p3 = nc.alloc_psum_tensor("p3", [128, 512], F32)
    ptr = nc.alloc_psum_tensor("ptr", [128, 1024], BF16)
    prw = nc.alloc_psum_tensor("prw", [1, 512], F32)
    pcA = nc.alloc_psum_tensor("pcA", [128, 256], F32)
    pcB = nc.alloc_psum_tensor("pcB", [128, 256], F32)

    S = Ser()

    S.ldg([(ident[:], idm[:]), (identF[:], idmF[:])])
    S.ldg([(wbb[:], wbd[:]), (cvb[:], cvw[:]), (lmb[:], lmw[:]),
           (rb1b[:], rb1[:]), (rb2b[:], rb2[:]), (mxwb[:], mxw[:]),
           (onwb[:], onw[:]),
           (rw2b[:].rearrange("p (a c) -> p a c", a=17),
            rw2[:].rearrange("(a p) c -> p a c", p=128))])
    mu_sb = nc.alloc_sbuf_tensor("mu_sb", [128, 128], F32)
    ml_sb = nc.alloc_sbuf_tensor("ml_sb", [128, 128], F32)
    mi_sb = nc.alloc_sbuf_tensor("mi_sb", [128, 128], BF16)
    S.ldg([(mu_sb[:], mku[:]), (ml_sb[:], mkl[:]), (mi_sb[:], mki[:])])
    S.v(lambda v: v.memset(onec[:], 1.0))
    S.v(lambda v: v.memset(one1[:], 1.0))
    S.v(lambda v: v.memset(onebc[:], 1.0))
    S.v(lambda v: v.memset(featT[:], 0.0))

    for i in range(NT):
        S.ld(xs[:], x_in[i * 128:(i + 1) * 128, :])
        S.v(lambda v: emit_ln_v(v, xs[:], stt, osb[:], None))
        S.s(lambda s: emit_rstd_s(s, stt))
        S.v(lambda v: v.tensor_scalar(lnb[:], xs[:], stt[:, 2:3], stt[:, 4:5],
                                      ALU.subtract, ALU.mult))
        for k in range(NDC):
            S.tr(ptr[:, k * 128:(k + 1) * 128], lnb[:, k * 128:(k + 1) * 128], ident[:])
        S.v(lambda v: v.tensor_copy(lnb[:], ptr[:, 0:1024]))
        S.stg([(d_ln[k * 128:(k + 1) * 128, i * 128:(i + 1) * 128],
                lnb[:, k * 128:(k + 1) * 128]) for k in range(NDC)])

    for s in range(BPC):
        S.ldg([(lnsq[:, k * SEQ:(k + 1) * SEQ],
                d_ln[k * 128:(k + 1) * 128, s * SEQ:(s + 1) * SEQ]) for k in range(NDC)])
        for pr, (wsrc, ddst) in enumerate([(wqd, d_q), (wkd, d_k), (wvd, d_v)]):
            S.ldg([(wqb[:, k * D:(k + 1) * D], wsrc[k * 128:(k + 1) * 128, :])
                   for k in range(NDC)])
            for dc in range(NDC):
                for tg in range(4):
                    S.mmg([(p0[:, 0:512],
                            wqb[:, k * D + dc * 128:k * D + dc * 128 + 128],
                            lnsq[:, k * SEQ + tg * 512:k * SEQ + (tg + 1) * 512])
                           for k in range(NDC)])
                    S.v(lambda v, dc=dc, tg=tg: v.tensor_copy(
                        rawb[:, dc * (SEQ + 32) + 32 + tg * 512:
                             dc * (SEQ + 32) + 32 + (tg + 1) * 512], p0[:, 0:512]))
                S.v(lambda v, dc=dc: v.memset(rawb[:, dc * (SEQ + 32):dc * (SEQ + 32) + 32], 0.0))
                base = dc * (SEQ + 32) + 32
                S.v(lambda v, dc=dc, pr=pr, base=base: v.tensor_scalar(
                    cvac[:], rawb[:, base - 3:base - 3 + SEQ],
                    cvb[:, (pr * 8 + dc) * 4:(pr * 8 + dc) * 4 + 1], None, ALU.mult))
                for tau in range(1, 4):
                    S.v(lambda v, dc=dc, pr=pr, tau=tau, base=base: v.scalar_tensor_tensor(
                        cvac[:], rawb[:, base - 3 + tau:base - 3 + tau + SEQ],
                        cvb[:, (pr * 8 + dc) * 4 + tau:(pr * 8 + dc) * 4 + tau + 1],
                        cvac[:], ALU.mult, ALU.add))
                S.s(lambda s_, dc=dc: s_.activation(
                    outbf[:, dc * SEQ:(dc + 1) * SEQ], cvac[:], AF.Silu))
            S.stg([(ddst[k * 128:(k + 1) * 128, s * SEQ:(s + 1) * SEQ],
                    outbf[:, k * SEQ:(k + 1) * SEQ]) for k in range(NDC)])
        for h in range(L2H):
            for tg in range(4):
                S.mmg([(prw[0:1, 0:512],
                        wbb[:, k * L2H + h:k * L2H + h + 1],
                        lnsq[:, k * SEQ + tg * 512:k * SEQ + (tg + 1) * 512])
                       for k in range(NDC)])
                S.s(lambda s_, tg=tg: s_.activation(
                    besb[0:1, tg * 512:(tg + 1) * 512], prw[0:1, 0:512], AF.Sigmoid))
            S.st(d_be[h:h + 1, s * SEQ:(s + 1) * SEQ], besb[0:1, :])
        for dc in range(NDC):
            S.v(lambda v, dc=dc: v.memset(rawb[:, dc * (SEQ + 32):dc * (SEQ + 32) + 32], 0.0))
            S.v(lambda v, dc=dc: v.tensor_copy(
                rawb[:, dc * (SEQ + 32) + 32:dc * (SEQ + 32) + 32 + SEQ],
                outbf[:, dc * SEQ:(dc + 1) * SEQ]))
        for (KK, off, ddst) in [(7, 0, d_lo), (25, 7, d_mi)]:
            for dc in range(NDC):
                base = dc * (SEQ + 32) + 32
                S.v(lambda v, dc=dc, base=base, KK=KK, off=off: v.tensor_scalar(
                    cvac[:], rawb[:, base - KK + 1:base - KK + 1 + SEQ],
                    lmb[:, dc * 32 + off:dc * 32 + off + 1], None, ALU.mult))
                for tau in range(1, KK):
                    S.v(lambda v, dc=dc, base=base, KK=KK, off=off, tau=tau:
                        v.scalar_tensor_tensor(
                            cvac[:], rawb[:, base - KK + 1 + tau:base - KK + 1 + tau + SEQ],
                            lmb[:, dc * 32 + off + tau:dc * 32 + off + tau + 1],
                            cvac[:], ALU.mult, ALU.add))
                S.v(lambda v, dc=dc: v.tensor_copy(
                    outbf[:, dc * SEQ:(dc + 1) * SEQ], cvac[:]))
            S.stg([(ddst[k * 128:(k + 1) * 128, s * SEQ:(s + 1) * SEQ],
                    outbf[:, k * SEQ:(k + 1) * SEQ]) for k in range(NDC)])

    for s in range(BPC):
        for h in range(L2H):
            S.v(lambda v: v.memset(ssb[:], 0.0))
            for c in range(L2NCk):
                t0 = s * SEQ + c * 128
                r0 = h * 256
                S.ldg([(qtc[:, hf * 128:(hf + 1) * 128],
                        d_q[r0 + hf * 128:r0 + (hf + 1) * 128, t0:t0 + 128])
                       for hf in range(2)] +
                      [(ktc[:, hf * 128:(hf + 1) * 128],
                        d_k[r0 + hf * 128:r0 + (hf + 1) * 128, t0:t0 + 128])
                       for hf in range(2)] +
                      [(vtc[:, hf * 128:(hf + 1) * 128],
                        d_v[r0 + hf * 128:r0 + (hf + 1) * 128, t0:t0 + 128])
                       for hf in range(2)] +
                      [(ber[0:1, :], d_be[h:h + 1, t0:t0 + 128])])
                for (qq, tens) in [(0, qtc), (1, ktc)]:
                    S.v(lambda v, t=tens: v.tensor_mul(ften[:, 0:128], t[:, 0:128], t[:, 0:128]))
                    S.v(lambda v, t=tens: v.tensor_mul(ften[:, 128:256], t[:, 128:256], t[:, 128:256]))
                    S.mmg([(prw[0:1, 0:128], onec[:], ften[:, 0:128]),
                           (prw[0:1, 0:128], onec[:], ften[:, 128:256])])
                    S.v(lambda v, qq=qq: v.tensor_scalar(
                        rowsc[0:1, qq * 128:(qq + 1) * 128], prw[0:1, 0:128],
                        1e-12, None, ALU.add))
                    S.s(lambda s_, qq=qq: s_.activation(
                        rowsc[0:1, qq * 128:(qq + 1) * 128],
                        rowsc[0:1, qq * 128:(qq + 1) * 128],
                        AF.Ln, bias=0.0, scale=1.0))
                    S.s(lambda s_, qq=qq: s_.activation(
                        rowsc[0:1, qq * 128:(qq + 1) * 128],
                        rowsc[0:1, qq * 128:(qq + 1) * 128],
                        AF.Exp, bias=0.0, scale=-0.5))
                    S.mmg([(pcA[:, 0:128], one1[0:1, :], rowsc[0:1, qq * 128:(qq + 1) * 128])])
                    S.v(lambda v, qq=qq: v.tensor_copy(bcst[:, qq * 128:(qq + 1) * 128],
                                                       pcA[:, 0:128]))
                    S.v(lambda v, t=tens, qq=qq: v.tensor_mul(
                        t[:, 0:128], t[:, 0:128], bcst[:, qq * 128:(qq + 1) * 128]))
                    S.v(lambda v, t=tens, qq=qq: v.tensor_mul(
                        t[:, 128:256], t[:, 128:256], bcst[:, qq * 128:(qq + 1) * 128]))
                S.mmg([(pcA[:, 0:128], one1[0:1, :], ber[0:1, :])])
                S.v(lambda v: v.tensor_copy(bcst[:, 256:384], pcA[:, 0:128]))
                S.v(lambda v: v.tensor_mul(kbt[:, 0:128], ktc[:, 0:128], bcst[:, 256:384]))
                S.v(lambda v: v.tensor_mul(kbt[:, 128:256], ktc[:, 128:256], bcst[:, 256:384]))
                S.v(lambda v: v.tensor_mul(vbt[:, 0:128], vtc[:, 0:128], bcst[:, 256:384]))
                S.v(lambda v: v.tensor_mul(vbt[:, 128:256], vtc[:, 128:256], bcst[:, 256:384]))
                for (src, dst) in [(ktc, ktok), (kbt, kbtok), (vbt, vbtok)]:
                    S.tr(ptr[:, 0:128], src[:, 0:128], ident[:])
                    S.tr(ptr[:, 128:256], src[:, 128:256], ident[:])
                    S.v(lambda v, dst=dst: v.tensor_copy(dst[:, 0:256], ptr[:, 0:256]))
                S.mmg([(p0[:, 0:128], ktc[:, 0:128], kbt[:, 0:128]),
                       (p0[:, 0:128], ktc[:, 128:256], kbt[:, 128:256])])
                S.mmg([(p0[:, 128:256], kbt[:, 0:128], ktc[:, 0:128]),
                       (p0[:, 128:256], kbt[:, 128:256], ktc[:, 128:256])])
                S.v(lambda v: v.tensor_mul(nstk[:, 0:128], p0[:, 0:128], mu_sb[:]))
                S.v(lambda v: v.tensor_mul(mstk[:, 0:128], p0[:, 128:256], ml_sb[:]))
                for lev in range(6):
                    S.mms([[(p1[:, 0:128], nstk[:, lev * 128:(lev + 1) * 128],
                             mstk[:, lev * 128:(lev + 1) * 128])],
                           [(p1[:, 128:256], mstk[:, lev * 128:(lev + 1) * 128],
                             nstk[:, lev * 128:(lev + 1) * 128])]])
                    S.v(lambda v, lev=lev: v.tensor_copy(
                        mstk[:, (lev + 1) * 128:(lev + 2) * 128], p1[:, 0:128]))
                    S.v(lambda v, lev=lev: v.tensor_copy(
                        nstk[:, (lev + 1) * 128:(lev + 2) * 128], p1[:, 128:256]))
                S.v(lambda v: v.tensor_add(rr[:, 0:128], nstk[:, 6 * 128:7 * 128], identF[:]))
                S.v(lambda v: v.tensor_add(rr[:, 128:256], mstk[:, 6 * 128:7 * 128], identF[:]))
                for lev in range(5, -1, -1):
                    S.mms([[(p1[:, 0:128], rr[:, 128:256], nstk[:, lev * 128:(lev + 1) * 128])],
                           [(p1[:, 128:256], nstk[:, lev * 128:(lev + 1) * 128], rr[:, 128:256])]])
                    S.v(lambda v: v.tensor_add(rr[:, 0:128], p1[:, 0:128], rr[:, 0:128]))
                    S.v(lambda v: v.tensor_add(rr[:, 128:256], p1[:, 128:256], rr[:, 128:256]))
                S.v(lambda v: v.tensor_copy(rrb[:], rr[:, 0:128]))
                S.mmg([(pcA[:, 0:256], rrb[:], vbtok[:, 0:256])])
                S.v(lambda v: v.tensor_copy(usb[:], pcA[:, 0:256]))
                S.mms([[(p1[:, 0:128], kbtok[:, 0:128], rrb[:])],
                       [(p1[:, 128:256], kbtok[:, 128:256], rrb[:])]])
                S.v(lambda v: v.tensor_copy(wmt[:, 0:256], p1[:, 0:256]))
                S.mmg([(p2[:, 0:128], ktc[:, 0:128], qtc[:, 0:128]),
                       (p2[:, 0:128], ktc[:, 128:256], qtc[:, 128:256])])
                S.v(lambda v: v.tensor_mul(attns[:], p2[:, 0:128], mi_sb[:]))
                S.v(lambda v: v.tensor_copy(sbf[:, 0:256], ssb[:, 0:256]))
                S.v(lambda v: v.tensor_copy(sbf[:, 256:512], ssb[:, 256:512]))
                S.mmg([(pcB[:, 0:256], wmt[:, 0:128], sbf[:, 0:256]),
                       (pcB[:, 0:256], wmt[:, 128:256], sbf[:, 256:512])])
                S.v(lambda v: v.tensor_sub(uisb[:], usb[:], pcB[:, 0:256]))
                S.mms([[(p3[:, 0:128], sbf[:, 0:128], qtc[:, 0:128]),
                        (p3[:, 0:128], sbf[:, 256:384], qtc[:, 128:256]),
                        (p3[:, 0:128], uisb[:, 0:128], attns[:])],
                       [(p3[:, 128:256], sbf[:, 128:256], qtc[:, 0:128]),
                        (p3[:, 128:256], sbf[:, 384:512], qtc[:, 128:256]),
                        (p3[:, 128:256], uisb[:, 128:256], attns[:])]])
                S.v(lambda v: v.tensor_copy(dosb[:, 0:256], p3[:, 0:256]))
                S.mms([[(pcA[:, 0:256], ktok[:, 0:128], uisb[:, 0:256])],
                       [(pcB[:, 0:256], ktok[:, 128:256], uisb[:, 0:256])]])
                S.v(lambda v: v.tensor_add(ssb[:, 0:256], ssb[:, 0:256], pcA[:, 0:256]))
                S.v(lambda v: v.tensor_add(ssb[:, 256:512], ssb[:, 256:512], pcB[:, 0:256]))
                S.stg([(d_do[r0 + hf * 128:r0 + (hf + 1) * 128, t0:t0 + 128],
                        dosb[:, hf * 128:(hf + 1) * 128]) for hf in range(2)])

    S.ldg([(wqb[:, k * D:(k + 1) * D], wod[k * 128:(k + 1) * 128, :])
           for k in range(NDC)])
    PAIRS = [(0, 1), (0, 2), (0, 3), (1, 2), (1, 3), (2, 3)]
    for i in range(NT):
        t0 = i * 128
        S.ldg([(otile[:, (j * 8 + k) * 128:(j * 8 + k + 1) * 128],
                dd[k * 128:(k + 1) * 128, t0:t0 + 128])
               for (j, dd) in enumerate([d_lo, d_mi, d_do, d_v, d_ln])
               for k in range(NDC)])
        S.ld(xs[:], x_in[t0:t0 + 128, :])
        for j in range(4):
            for h in range(L2H):
                ch0 = (j * 8 + h * 2) * 128
                S.mmg([(prw[0:1, 0:128], onebc[:], otile[:, ch0:ch0 + 128]),
                       (prw[0:1, 0:128], onebc[:], otile[:, ch0 + 128:ch0 + 256])])
                S.v(lambda v: v.tensor_scalar(rowsc[0:1, 0:128], prw[0:1, 0:128],
                                              1.0 / 256.0, None, ALU.mult))
                S.v(lambda v, j=j, h=h: v.tensor_copy(
                    rowbk[0:1, (2 * j * 4 + h) * 128:(2 * j * 4 + h) * 128 + 128],
                    rowsc[0:1, 0:128]))
                S.v(lambda v, ch0=ch0: v.tensor_mul(
                    sqsc[:, 0:128], otile[:, ch0:ch0 + 128], otile[:, ch0:ch0 + 128]))
                S.v(lambda v, ch0=ch0: v.tensor_mul(
                    sqsc[:, 128:256], otile[:, ch0 + 128:ch0 + 256],
                    otile[:, ch0 + 128:ch0 + 256]))
                S.mmg([(prw[0:1, 128:256], onebc[:], sqsc[:, 0:128]),
                       (prw[0:1, 128:256], onebc[:], sqsc[:, 128:256])])
                S.v(lambda v: v.tensor_mul(rowsc[0:1, 128:256], rowsc[0:1, 0:128],
                                           rowsc[0:1, 0:128]))
                S.v(lambda v: v.tensor_scalar(rowsc[0:1, 0:128], prw[0:1, 128:256],
                                              1.0 / 255.0, None, ALU.mult))
                S.v(lambda v, j=j, h=h: v.scalar_tensor_tensor(
                    rowbk[0:1, ((2 * j + 1) * 4 + h) * 128:((2 * j + 1) * 4 + h) * 128 + 128],
                    rowsc[0:1, 128:256], -256.0 / 255.0, rowsc[0:1, 0:128],
                    ALU.mult, ALU.add))
        for pi, (ja, jb) in enumerate(PAIRS):
            for h in range(L2H):
                ca = (ja * 8 + h * 2) * 128
                cb = (jb * 8 + h * 2) * 128
                S.v(lambda v, ca=ca, cb=cb: v.tensor_mul(
                    sqsc[:, 0:128], otile[:, ca:ca + 128], otile[:, cb:cb + 128]))
                S.v(lambda v, ca=ca, cb=cb: v.tensor_mul(
                    sqsc[:, 128:256], otile[:, ca + 128:ca + 256], otile[:, cb + 128:cb + 256]))
                S.mmg([(prw[0:1, 0:128], onebc[:], sqsc[:, 0:128]),
                       (prw[0:1, 0:128], onebc[:], sqsc[:, 128:256])])
                S.v(lambda v, pi=pi, h=h: v.tensor_scalar(
                    rowbk[0:1, ((8 + pi) * 4 + h) * 128:((8 + pi) * 4 + h) * 128 + 128],
                    prw[0:1, 0:128], 1.0 / 256.0, None, ALU.mult))
        S.gp(lambda e: e.dma_start(out=featT[0:56, :], in_=rowbk[0:1, :]))
        for m in range(17):
            S.ldg([(rw1s[:].rearrange("p (a c) -> p a c", a=9),
                    rw1[:, m * 128:(m + 1) * 128].rearrange("(a p) c -> p a c", p=128))])
            S.mmg([(p0[:, 0:128],
                    rw1s[:, kr * 128:(kr + 1) * 128],
                    otile[:, (4 * 8 + kr) * 128:(4 * 8 + kr + 1) * 128])
                   for kr in range(8)] +
                  [(p0[:, 0:128], rw1s[:, 8 * 128:9 * 128], featT[:, :])])
            S.v(lambda v, m=m: v.tensor_scalar(cvac[:, 0:128], p0[:, 0:128],
                                               rb1b[:, m:m + 1], None, ALU.add))
            S.s(lambda s_, m=m: s_.activation(h1sb[:, m * 128:(m + 1) * 128],
                                              cvac[:, 0:128], AF.Silu))
        S.mmg([(p1[0:16, 0:128], rw2b[:, m * 16:(m + 1) * 16],
                h1sb[:, m * 128:(m + 1) * 128]) for m in range(17)])
        S.v(lambda v: v.tensor_scalar(lgsb[:], p1[0:16, 0:128], rb2b[:, 0:1],
                                      None, ALU.add))
        S.tr(p2[:, 0:16], lgsb[:], identF[0:16, 0:16])
        S.v(lambda v: v.tensor_copy(ptok[:], p2[:, 0:16]))
        for h in range(L2H):
            S.v(lambda v, h=h: v.reduce_max(stt[:, 0:1], ptok[:, 4 * h:4 * h + 4], axis=AX.X))
            S.v(lambda v, h=h: v.tensor_scalar(ptok[:, 4 * h:4 * h + 4],
                                               ptok[:, 4 * h:4 * h + 4],
                                               stt[:, 0:1], None, ALU.subtract))
            S.s(lambda s_, h=h: s_.activation(ptok[:, 4 * h:4 * h + 4],
                                              ptok[:, 4 * h:4 * h + 4], AF.Exp))
            S.v(lambda v, h=h: v.tensor_reduce(stt[:, 1:2], ptok[:, 4 * h:4 * h + 4],
                                               axis=AX.X, op=ALU.add))
            S.v(lambda v: v.reciprocal(stt[:, 2:3], stt[:, 1:2]))
            S.v(lambda v, h=h: v.tensor_scalar(ptok[:, 4 * h:4 * h + 4],
                                               ptok[:, 4 * h:4 * h + 4],
                                               stt[:, 2:3], None, ALU.mult))
        S.v(lambda v: v.tensor_scalar(ptok[:], ptok[:], 0.96, 0.01, ALU.mult, ALU.add))
        S.tr(p2[0:16, 0:128], ptok[:], identF[:])
        S.v(lambda v: v.tensor_copy(lgsb[:], p2[0:16, 0:128]))
        S.gp(lambda e: e.dma_start(out=prow[0:1, :], in_=lgsb[:]))
        for grp in range(4):
            S.mms([[(p2[:, q * 128:(q + 1) * 128], one1[0:1, :],
                     prow[0:1, (grp * 4 + q) * 128:(grp * 4 + q + 1) * 128])]
                   for q in range(4)])
            S.v(lambda v, grp=grp: v.tensor_copy(pbc[:, grp * 512:(grp + 1) * 512],
                                                 p2[:, 0:512]))
        for dc in range(NDC):
            h = dc // 2
            S.v(lambda v, dc=dc, h=h: v.tensor_mul(
                mixd[:, dc * 128:(dc + 1) * 128], pbc[:, (h * 4 + 0) * 128:(h * 4 + 1) * 128],
                otile[:, (0 * 8 + dc) * 128:(0 * 8 + dc + 1) * 128]))
            for jj in range(1, 4):
                S.v(lambda v, dc=dc, h=h, jj=jj: v.tensor_mul(
                    ften[:, 0:128], pbc[:, (h * 4 + jj) * 128:(h * 4 + jj + 1) * 128],
                    otile[:, (jj * 8 + dc) * 128:(jj * 8 + dc + 1) * 128]))
                S.v(lambda v, dc=dc: v.tensor_add(
                    mixd[:, dc * 128:(dc + 1) * 128], mixd[:, dc * 128:(dc + 1) * 128],
                    ften[:, 0:128]))
        for (rnd, eps, gain) in [(0, 1e-5, mxwb), (1, 1e-5, onwb)]:
            for h in range(L2H):
                S.v(lambda v, h=h: v.tensor_mul(ften[:, 0:128],
                                                mixd[:, h * 256:h * 256 + 128],
                                                mixd[:, h * 256:h * 256 + 128]))
                S.v(lambda v, h=h: v.tensor_mul(ften[:, 128:256],
                                                mixd[:, h * 256 + 128:h * 256 + 256],
                                                mixd[:, h * 256 + 128:h * 256 + 256]))
                S.mmg([(prw[0:1, 0:128], onec[:], ften[:, 0:128]),
                       (prw[0:1, 0:128], onec[:], ften[:, 128:256])])
                S.v(lambda v, eps=eps: v.tensor_scalar(rmrow[0:1, :], prw[0:1, 0:128],
                                                       1.0 / 256.0, eps,
                                                       ALU.mult, ALU.add))
                S.s(lambda s_: s_.activation(rmrow[0:1, :], rmrow[0:1, :],
                                             AF.Ln, bias=0.0, scale=1.0))
                S.s(lambda s_: s_.activation(rmrow[0:1, :], rmrow[0:1, :],
                                             AF.Exp, bias=0.0, scale=-0.5))
                S.mmg([(p2[:, 0:128], one1[0:1, :], rmrow[0:1, :])])
                S.v(lambda v: v.tensor_copy(rbc[:], p2[:, 0:128]))
                for hf in range(2):
                    dc = h * 2 + hf
                    if rnd == 0:
                        S.v(lambda v, dc=dc: v.tensor_mul(
                            mixd[:, dc * 128:(dc + 1) * 128],
                            mixd[:, dc * 128:(dc + 1) * 128], rbc[:]))
                        S.v(lambda v, dc=dc, gain=gain: v.tensor_scalar(
                            mixd[:, dc * 128:(dc + 1) * 128],
                            mixd[:, dc * 128:(dc + 1) * 128],
                            gain[:, dc:dc + 1], None, ALU.mult))
                    else:
                        S.v(lambda v, dc=dc: v.tensor_mul(
                            mixd[:, dc * 128:(dc + 1) * 128],
                            mixd[:, dc * 128:(dc + 1) * 128], rbc[:]))
                        S.v(lambda v, dc=dc, gain=gain: v.tensor_scalar(
                            onT[:, dc * 128:(dc + 1) * 128],
                            mixd[:, dc * 128:(dc + 1) * 128],
                            gain[:, dc:dc + 1], None, ALU.mult))
        for dgrp in range(2):
            S.mmg([(p0[:, 0:512], onT[:, k * 128:(k + 1) * 128],
                    wqb[:, k * D + dgrp * 512:k * D + dgrp * 512 + 512])
                   for k in range(NDC)])
            S.v(lambda v, dgrp=dgrp: v.tensor_add(
                osb[:, dgrp * 512:(dgrp + 1) * 512], p0[:, 0:512],
                xs[:, dgrp * 512:(dgrp + 1) * 512]))
        S.st(x_out[t0:t0 + 128, :], osb[:])

    return S.build(nc)


def maps_l2a(inputs, xsh):
    def chunkify(a):
        a = f32(a)
        w = a.shape[1]
        return np.ascontiguousarray(a.reshape(8, 128, w).transpose(1, 0, 2).reshape(128, 8 * w))
    wq = np.ascontiguousarray(f32(inputs["L2_attn_Wq"]).astype(BF))
    wk = np.ascontiguousarray(f32(inputs["L2_attn_Wk"]).astype(BF))
    wv = np.ascontiguousarray(f32(inputs["L2_attn_Wv"]).astype(BF))
    wb = np.ascontiguousarray(chunkify(inputs["L2_attn_Wb"]).astype(BF))
    cvw = np.ascontiguousarray(np.concatenate(
        [chunkify(inputs["L2_attn_conv_%s_w" % n]) for n in "qkv"], axis=1))
    lmw_ = np.concatenate([f32(inputs["L2_attn_local_w"]), f32(inputs["L2_attn_mid_w"])], 1)
    lmw = np.ascontiguousarray(chunkify(lmw_))
    rw1p = np.zeros((1152, 2176), np.float32)
    rw1p[:1080, :2160] = f32(inputs["L2_attn_r_w1"])
    rb1p = np.zeros((2176,), np.float32)
    rb1p[:2160] = f32(inputs["L2_attn_r_b1"])
    rw2p = np.zeros((2176, 16), np.float32)
    rw2p[:2160] = f32(inputs["L2_attn_r_w2"])
    ones = np.ones((128, 128), np.float32)
    mku = np.ascontiguousarray(-np.triu(ones, 1))
    mkl = np.ascontiguousarray(-np.tril(ones, -1))
    mki = np.ascontiguousarray(np.triu(ones, 0).astype(BF))
    mxflat = f32(inputs["L2_attn_mix_w"]).reshape(-1)
    onflat = np.tile(f32(inputs["L2_attn_onorm_w"]), 4)
    return [{"x": xsh[c], "wq": wq, "wk": wk, "wv": wv, "wb": wb,
             "cvw": cvw, "lmw": lmw,
             "rw1": np.ascontiguousarray(rw1p.astype(BF)),
             "rb1": np.ascontiguousarray(rb1p.reshape(17, 128).T.copy()),
             "rw2": np.ascontiguousarray(rw2p.astype(BF)),
             "rb2": f32(inputs["L2_attn_r_b2"]).reshape(16, 1),
             "mxw": np.ascontiguousarray(mxflat.reshape(8, 128).T.copy()),
             "onw": np.ascontiguousarray(onflat.reshape(8, 128).T.copy()),
             "wo": np.ascontiguousarray(f32(inputs["L2_attn_Wo"]).astype(BF)),
             "mku": mku, "mkl": mkl, "mki": mki,
             "idm": _IDM, "idmF": np.eye(128, dtype=np.float32)}
            for c in range(NCORE)]



L3NH, L3DK, L3DV, L3C = 4, 128, 256, 32


def build_ph_l3a():
    nc = bass.Bass("TRN2", target_bir_lowering=False)
    x_in = nc.dram_tensor("x", [TPC, D], F32, kind="ExternalInput").ap()
    xpm = nc.dram_tensor("xpm", [128, 8], F32, kind="ExternalInput").ap()
    xw1 = nc.dram_tensor("xw1", [128, 8 * 32], BF16, kind="ExternalInput").ap()
    xw2 = nc.dram_tensor("xw2", [32, 160], BF16, kind="ExternalInput").ap()
    xb2 = nc.dram_tensor("xb2", [128, 1], F32, kind="ExternalInput").ap()
    xb2c = nc.dram_tensor("xb2c", [32, 1], F32, kind="ExternalInput").ap()
    xow = nc.dram_tensor("xow", [32, 5 * 1024], BF16, kind="ExternalInput").ap()
    xbs = nc.dram_tensor("xbs", [128, 8 * 5], F32, kind="ExternalInput").ap()
    rwp = nc.dram_tensor("rwp", [128, 8 * 512], BF16, kind="ExternalInput").ap()
    kwp = nc.dram_tensor("kwp", [128, 8 * 512], BF16, kind="ExternalInput").ap()
    vwp = nc.dram_tensor("vwp", [128, 8 * 1024], BF16, kind="ExternalInput").ap()
    gwp = nc.dram_tensor("gwp", [128, 8 * 1024], BF16, kind="ExternalInput").ap()
    ww1 = nc.dram_tensor("ww1", [128, 8 * 64], BF16, kind="ExternalInput").ap()
    ww2 = nc.dram_tensor("ww2", [64, 512], BF16, kind="ExternalInput").ap()
    wbc = nc.dram_tensor("wbc", [128, 4], F32, kind="ExternalInput").ap()
    bon = nc.dram_tensor("bon", [128, 4], F32, kind="ExternalInput").ap()
    gnw = nc.dram_tensor("gnw", [128, 1024], F32, kind="ExternalInput").ap()
    gnb = nc.dram_tensor("gnb", [128, 1024], F32, kind="ExternalInput").ap()
    owp = nc.dram_tensor("owp", [128, 8 * 1024], BF16, kind="ExternalInput").ap()
    mq32 = nc.dram_tensor("mq32", [32, 32], BF16, kind="ExternalInput").ap()
    id32 = nc.dram_tensor("id32", [32, 32], BF16, kind="ExternalInput").ap()
    idm = nc.dram_tensor("idm", [128, 128], BF16, kind="ExternalInput").ap()
    d_h = nc.dram_tensor("dh", [D, TPC], BF16, kind="Internal").ap()
    d_dl = nc.dram_tensor("ddl", [D, TPC], BF16, kind="Internal").ap()
    d_r = nc.dram_tensor("dr", [512, TPC], BF16, kind="Internal").ap()
    d_w = nc.dram_tensor("dw", [512, TPC], BF16, kind="Internal").ap()
    d_k = nc.dram_tensor("dk3", [512, TPC], BF16, kind="Internal").ap()
    d_v = nc.dram_tensor("dv3", [1024, TPC], BF16, kind="Internal").ap()
    d_g = nc.dram_tensor("dg3", [1024, TPC], BF16, kind="Internal").ap()
    d_o = nc.dram_tensor("do3", [TPC, 1024], BF16, kind="Internal").ap()
    x_out = nc.dram_tensor("xo", [TPC, D], F32, kind="ExternalOutput").ap()

    xs = nc.alloc_sbuf_tensor("xs", [128, D], F32)
    lnb = nc.alloc_sbuf_tensor("lnb", [128, D], BF16)
    stt = nc.alloc_sbuf_tensor("stt", [128, 8], F32)
    scr = nc.alloc_sbuf_tensor("scr", [128, D], F32)
    innb = nc.alloc_sbuf_tensor("innb", [128, 8 * SEQ], BF16)
    outb = nc.alloc_sbuf_tensor("outb", [128, 4 * SEQ], BF16)
    hTs = nc.alloc_sbuf_tensor("hTs", [128, SEQ], BF16)
    xlb = nc.alloc_sbuf_tensor("xlb", [128, SEQ], BF16)
    hTc = nc.alloc_sbuf_tensor("hTc", [128, 512], BF16)
    dTc = nc.alloc_sbuf_tensor("dTc", [128, 512], BF16)
    ftmp = nc.alloc_sbuf_tensor("ftmp", [128, 512], F32)
    x0sb = nc.alloc_sbuf_tensor("x0sb", [32, SEQ], BF16)
    x1t = nc.alloc_sbuf_tensor("x1t", [128, SEQ], BF16)
    x1b = nc.alloc_sbuf_tensor("x1b", [32, SEQ], BF16)
    x1s = nc.alloc_sbuf_tensor("x1s", [32, SEQ], BF16)
    w1sb = nc.alloc_sbuf_tensor("w1sb", [64, SEQ], BF16)
    wq1 = nc.alloc_sbuf_tensor("wq1", [128, 8 * 512], BF16)
    wq2 = nc.alloc_sbuf_tensor("wq2", [128, 8 * 1024], BF16)
    ww1b = nc.alloc_sbuf_tensor("ww1b", [128, 8 * 64], BF16)
    ww2b = nc.alloc_sbuf_tensor("ww2b", [64, 512], BF16)
    wbcb = nc.alloc_sbuf_tensor("wbcb", [128, 4], F32)
    bonb = nc.alloc_sbuf_tensor("bonb", [128, 4], F32)
    xw1b = nc.alloc_sbuf_tensor("xw1b", [128, 8 * 32], BF16)
    xw2b = nc.alloc_sbuf_tensor("xw2b", [32, 160], BF16)
    xb2b = nc.alloc_sbuf_tensor("xb2b", [128, 1], F32)
    xb2cb = nc.alloc_sbuf_tensor("xb2cb", [32, 1], F32)
    xowb = nc.alloc_sbuf_tensor("xowb", [32, 5 * 1024], BF16)
    xbsb = nc.alloc_sbuf_tensor("xbsb", [128, 8 * 5], F32)
    xpmb = nc.alloc_sbuf_tensor("xpmb", [128, 8], F32)
    gnwb = nc.alloc_sbuf_tensor("gnwb", [128, 1024], F32)
    gnbb = nc.alloc_sbuf_tensor("gnbb", [128, 1024], F32)
    ident = nc.alloc_sbuf_tensor("ident", [128, 128], BF16)
    mq32b = nc.alloc_sbuf_tensor("mq32b", [32, 32], BF16)
    id32b = nc.alloc_sbuf_tensor("id32b", [32, 32], BF16)
    onec = nc.alloc_sbuf_tensor("onec", [128, 1], F32)
    one1 = nc.alloc_sbuf_tensor("one1", [1, 128], F32)
    rTc = nc.alloc_sbuf_tensor("rTc", [128, 32], BF16)
    kTc = nc.alloc_sbuf_tensor("kTc", [128, 32], BF16)
    lwc = nc.alloc_sbuf_tensor("lwc", [128, 32], F32)
    lwb = nc.alloc_sbuf_tensor("lwb", [128, 32], BF16)
    Lc = nc.alloc_sbuf_tensor("Lc", [128, 32], F32)
    Lex = nc.alloc_sbuf_tensor("Lex", [128, 32], F32)
    e1 = nc.alloc_sbuf_tensor("e1", [128, 32], F32)
    aT = nc.alloc_sbuf_tensor("aT", [128, 32], BF16)
    bT = nc.alloc_sbuf_tensor("bT", [128, 32], BF16)
    bpT = nc.alloc_sbuf_tensor("bpT", [128, 32], BF16)
    dcol = nc.alloc_sbuf_tensor("dcol", [128, 1], F32)
    vtok = nc.alloc_sbuf_tensor("vtok", [32, 256], BF16)
    bptok = nc.alloc_sbuf_tensor("bptok", [32, 128], BF16)
    drow = nc.alloc_sbuf_tensor("drow", [1, 32], F32)
    ptf = nc.alloc_sbuf_tensor("ptf", [32, 32], BF16)
    osc = nc.alloc_sbuf_tensor("osc", [32, 256], BF16)
    f32a = nc.alloc_sbuf_tensor("f32a", [128, 256], F32)
    f32b = nc.alloc_sbuf_tensor("f32b", [32, 32], F32)
    ssb = nc.alloc_sbuf_tensor("ssb", [128, 256], F32)
    sbf = nc.alloc_sbuf_tensor("sbf", [128, 256], BF16)
    gtok = nc.alloc_sbuf_tensor("gtok", [128, 1024], BF16)
    gsil = nc.alloc_sbuf_tensor("gsil", [128, 1024], F32)
    ogn = nc.alloc_sbuf_tensor("ogn", [128, 1024], F32)
    ooT = nc.alloc_sbuf_tensor("ooT", [128, 1024], BF16)
    osb = nc.alloc_sbuf_tensor("osb", [128, D], F32)
    otl = nc.alloc_sbuf_tensor("otl", [128, 1024], BF16)
    p0 = nc.alloc_psum_tensor("p0", [128, 512], F32)
    px0 = nc.alloc_psum_tensor("px0", [128, 2048], F32)
    ptr = nc.alloc_psum_tensor("ptr", [128, 1024], BF16)
    pc = nc.alloc_psum_tensor("pc", [128, 256], F32)
    pc2 = nc.alloc_psum_tensor("pc2", [128, 256], F32)

    S = Ser()
    S.ldg([(ident[:], idm[:]), (mq32b[:], mq32[:]), (id32b[:], id32[:]),
           (xpmb[:], xpm[:]), (xw1b[:], xw1[:]), (xw2b[:], xw2[:]),
           (xb2b[:], xb2[:]), (xb2cb[:], xb2c[:]),
           (xowb[:], xow[:]), (xbsb[:], xbs[:]),
           (ww1b[:], ww1[:]), (ww2b[:], ww2[:]), (wbcb[:], wbc[:]),
           (bonb[:], bon[:]), (gnwb[:], gnw[:]), (gnbb[:], gnb[:])])
    S.v(lambda v: v.memset(onec[:], 1.0))
    S.v(lambda v: v.memset(one1[:], 1.0))
    S.v(lambda v: v.memset(stt[:, 7:8], 0.0))

    for i in range(NT):
        S.ld(xs[:], x_in[i * 128:(i + 1) * 128, :])
        S.v(lambda v: emit_ln_v(v, xs[:], stt, scr[:], None))
        S.s(lambda s: emit_rstd_s(s, stt))
        S.v(lambda v: v.tensor_scalar(lnb[:], xs[:], stt[:, 2:3], stt[:, 4:5],
                                      ALU.subtract, ALU.mult))
        for k in range(NDC):
            S.tr(ptr[:, k * 128:(k + 1) * 128], lnb[:, k * 128:(k + 1) * 128], ident[:])
        S.v(lambda v: v.tensor_copy(lnb[:], ptr[:, 0:1024]))
        S.stg([(d_h[k * 128:(k + 1) * 128, i * 128:(i + 1) * 128],
                lnb[:, k * 128:(k + 1) * 128]) for k in range(NDC)])

    for s in range(BPC):
        t00 = s * SEQ
        for k in range(NDC):
            S.ld(hTs[:], d_h[k * 128:(k + 1) * 128, t00:t00 + SEQ])
            S.v(lambda v: v.tensor_sub(xlb[:, 1:SEQ], hTs[:, 0:SEQ - 1], hTs[:, 1:SEQ]))
            S.v(lambda v: v.tensor_scalar(xlb[:, 0:1], hTs[:, 0:1], -1.0, None, ALU.mult))
            S.st(d_dl[k * 128:(k + 1) * 128, t00:t00 + SEQ], xlb[:, 0:SEQ])
            S.v(lambda v, k=k: v.tensor_scalar(scr[:, 0:1024], xlb[:, 0:1024],
                                               xpmb[:, k:k + 1], None, ALU.mult))
            S.v(lambda v, k=k: v.tensor_scalar(scr[:, 0:1024].bitcast(BF16)[:, 0:1024],
                                               xlb[:, 1024:2048], xpmb[:, k:k + 1],
                                               None, ALU.mult)) if False else None
            S.v(lambda v: v.tensor_add(xlb[:, 0:SEQ].bitcast(F32)[:, 0:1024], scr[:, 0:1024],
                                       scr[:, 0:1024])) if False else None
            for hf in range(2):
                S.v(lambda v, k=k, hf=hf: v.tensor_scalar(
                    scr[:, 0:1024], xlb[:, hf * 1024:(hf + 1) * 1024],
                    xpmb[:, k:k + 1], None, ALU.mult))
                S.v(lambda v, hf=hf: v.tensor_add(
                    xlb[:, hf * 1024:(hf + 1) * 1024], scr[:, 0:1024],
                    hTs[:, hf * 1024:(hf + 1) * 1024]))
            for tg in range(4):
                S.items.append(("tensor", (lambda e, k=k, tg=tg: e.matmul(
                    px0[0:32, tg * 512:(tg + 1) * 512],
                    xw1b[:, k * 32:(k + 1) * 32],
                    xlb[:, tg * 512:(tg + 1) * 512],
                    start=(k == 0), stop=(k == NDC - 1))), 1))
        for tg in range(4):
            S.s(lambda s_, tg=tg: s_.activation(x0sb[0:32, tg * 512:(tg + 1) * 512],
                                                px0[0:32, tg * 512:(tg + 1) * 512],
                                                AF.Tanh))
        for tg in range(4):
            S.mmg([(p0[:, 0:512], xw2b[0:32, 0:128], x0sb[0:32, tg * 512:(tg + 1) * 512])])
            S.v(lambda v, tg=tg: v.tensor_scalar(ftmp[:], p0[:, 0:512],
                                                 xb2b[:, 0:1], None, ALU.add))
            S.s(lambda s_, tg=tg: s_.activation(x1t[:, tg * 512:(tg + 1) * 512],
                                                ftmp[:], AF.Tanh))
            S.mmg([(px0[0:32, 0:512], xw2b[0:32, 128:160], x0sb[0:32, tg * 512:(tg + 1) * 512])])
            S.v(lambda v, tg=tg: v.tensor_scalar(ftmp[0:32, 0:512], px0[0:32, 0:512],
                                                 xb2cb[0:32, 0:1], None, ALU.add))
            S.s(lambda s_, tg=tg: s_.activation(x1b[0:32, tg * 512:(tg + 1) * 512],
                                                ftmp[0:32, 0:512], AF.Tanh))
        for n, (wsl, dd, nout) in enumerate([
                ("r", d_r, 4), ("w", d_w, 4), ("k", d_k, 4),
                ("v", d_v, 8), ("g", d_g, 8)]):
            if n < 4:
                S.gp(lambda e, n=n: e.dma_start(out=x1s[0:32, :],
                                                in_=x1t[n * 32:(n + 1) * 32, :]))
            else:
                S.gp(lambda e: e.dma_start(out=x1s[0:32, :], in_=x1b[0:32, :]))
            for k in range(NDC):
                for tg in range(4):
                    S.mmg([(p0[:, 0:512], xowb[0:32, n * 1024 + k * 128:n * 1024 + (k + 1) * 128],
                            x1s[0:32, tg * 512:(tg + 1) * 512])])
                    S.ldg([(hTc[:], d_h[k * 128:(k + 1) * 128,
                                        t00 + tg * 512:t00 + (tg + 1) * 512]),
                           (dTc[:], d_dl[k * 128:(k + 1) * 128,
                                         t00 + tg * 512:t00 + (tg + 1) * 512])])
                    S.v(lambda v, n=n, k=k: v.tensor_scalar(
                        ftmp[:], p0[:, 0:512], xbsb[:, n * 8 + k:n * 8 + k + 1],
                        None, ALU.add))
                    S.v(lambda v: v.tensor_mul(ftmp[:], ftmp[:], dTc[:]))
                    S.v(lambda v, k=k, tg=tg: v.tensor_add(
                        innb[:, k * SEQ + tg * 512:k * SEQ + (tg + 1) * 512],
                        ftmp[:], hTc[:]))
            if n in (0, 2):
                S.ldg([(wq1[:], (rwp if n == 0 else kwp)[:])])
                wsrc, wwid = wq1, 512
            elif n in (3, 4):
                S.ldg([(wq2[:], (vwp if n == 3 else gwp)[:])])
                wsrc, wwid = wq2, 1024
            else:
                wsrc, wwid = None, None
            if n != 1:
                nch = wwid // 128
                for oc in range(nch):
                    for tg in range(4):
                        S.mmg([(p0[:, 0:512],
                                wsrc[:, k * wwid + oc * 128:k * wwid + oc * 128 + 128],
                                innb[:, k * SEQ + tg * 512:k * SEQ + (tg + 1) * 512])
                               for k in range(NDC)])
                        S.v(lambda v, oc=oc, tg=tg: v.tensor_copy(
                            outb[:, (oc % 4) * SEQ + tg * 512:(oc % 4) * SEQ + (tg + 1) * 512],
                            p0[:, 0:512]))
                    if oc % 4 == 3:
                        ob = oc - 3
                        S.stg([(dd[(ob + j) * 128:(ob + j + 1) * 128, t00:t00 + SEQ],
                                outb[:, j * SEQ:(j + 1) * SEQ]) for j in range(4)])
            else:
                for tg in range(4):
                    S.mmg([(p0[0:64, 0:512], ww1b[:, k * 64:(k + 1) * 64],
                            innb[:, k * SEQ + tg * 512:k * SEQ + (tg + 1) * 512])
                           for k in range(NDC)])
                    S.s(lambda s_, tg=tg: s_.activation(
                        w1sb[0:64, tg * 512:(tg + 1) * 512], p0[0:64, 0:512], AF.Tanh))
                for oc in range(4):
                    for tg in range(4):
                        S.mmg([(p0[:, 0:512], ww2b[0:64, oc * 128:(oc + 1) * 128],
                                w1sb[0:64, tg * 512:(tg + 1) * 512])])
                        S.v(lambda v, oc=oc: v.tensor_scalar(
                            ftmp[:], p0[:, 0:512], wbcb[:, oc:oc + 1], None, ALU.add))
                        S.s(lambda s_: s_.activation(ftmp[:], ftmp[:], AF.Exp))
                        S.v(lambda v, oc=oc, tg=tg: v.tensor_scalar(
                            outb[:, oc * SEQ + tg * 512:oc * SEQ + (tg + 1) * 512],
                            ftmp[:], -1.0, None, ALU.mult))
                S.stg([(d_w[j * 128:(j + 1) * 128, t00:t00 + SEQ],
                        outb[:, j * SEQ:(j + 1) * SEQ]) for j in range(4)])

    for s in range(BPC):
        for h in range(L3NH):
            S.v(lambda v: v.memset(ssb[:], 0.0))
            S.v(lambda v: v.memset(sbf[:], 0.0))
            for c in range(SEQ // L3C):
                t0 = s * SEQ + c * L3C
                S.ldg([(rTc[:], d_r[h * 128:(h + 1) * 128, t0:t0 + L3C]),
                       (kTc[:], d_k[h * 128:(h + 1) * 128, t0:t0 + L3C]),
                       (lwb[:], d_w[h * 128:(h + 1) * 128, t0:t0 + L3C])] +
                      [(vtok[0:32, hf * 128:(hf + 1) * 128].bitcast(BF16),
                        d_v[1, 1].bitcast(BF16)) for hf in range(0)])
                for hf in range(2):
                    S.tr(ptr[0:32, hf * 128:(hf + 1) * 128],
                         nc.dummy if False else None, ident[:]) if False else None
                S.ldg([(aT[:], d_v[h * 256:h * 256 + 128, t0:t0 + L3C]),
                       (bT[:], d_v[h * 256 + 128:h * 256 + 256, t0:t0 + L3C])])
                S.tr(ptr[0:32, 0:128], aT[:], ident[:])
                S.tr(ptr[0:32, 128:256], bT[:], ident[:])
                S.v(lambda v: v.tensor_copy(vtok[0:32, 0:256], ptr[0:32, 0:256]))
                S.v(lambda v: v.tensor_copy(lwc[:], lwb[:]))
                S.v(lambda v: v.tensor_tensor_scan(Lc[:], lwc[:],
                                                   stt[:, 7:8].broadcast_to((128, 32)),
                                                   0.0, ALU.add, ALU.add))
                S.v(lambda v: v.tensor_sub(Lex[:], Lc[:], lwc[:]))
                S.s(lambda s_: s_.activation(e1[:], Lex[:], AF.Exp))
                S.v(lambda v: v.tensor_mul(aT[:], rTc[:], e1[:]))
                S.s(lambda s_: s_.activation(e1[:], Lc[:], AF.Exp, bias=0.0, scale=-1.0))
                S.v(lambda v: v.tensor_mul(bT[:], kTc[:], e1[:]))
                S.s(lambda s_: s_.activation(dcol[:], Lc[:, 31:32], AF.Exp))
                S.v(lambda v: v.tensor_scalar(bpT[:], bT[:], dcol[:, 0:1], None, ALU.mult))
                S.tr(ptr[0:32, 256:384], bpT[:], ident[:])
                S.v(lambda v: v.tensor_copy(bptok[0:32, :], ptr[0:32, 256:384]))
                S.mmg([(p0[0:32, 0:32], bT[:], aT[:])])
                S.v(lambda v: v.tensor_mul(f32a[:, 0:32], rTc[:], kTc[:]))
                S.v(lambda v, h=h: v.tensor_scalar(f32a[:, 0:32], f32a[:, 0:32],
                                                   bonb[:, h:h + 1], None, ALU.mult))
                S.mmg([(px0[0:1, 0:32], onec[:], f32a[:, 0:32])])
                S.v(lambda v: v.tensor_copy(drow[0:1, :], px0[0:1, 0:32]))
                S.mmg([(px0[0:32, 512:544], drow[0:1, :], one1[0:1, 0:32])])
                S.v(lambda v: v.tensor_mul(ptf[0:32, :], p0[0:32, 0:32], mq32b[:]))
                S.v(lambda v: v.tensor_mul(f32b[0:32, :], px0[0:32, 512:544], id32b[:]))
                S.v(lambda v: v.tensor_add(ptf[0:32, :], ptf[0:32, :], f32b[0:32, :]))
                S.mms([[(pc[0:32, 0:256], aT[:], sbf[:, 0:256]),
                        (pc[0:32, 0:256], ptf[0:32, :], vtok[0:32, :])]])
                S.v(lambda v: v.tensor_copy(vtok[0:32, 0:256], pc[0:32, 0:256])
                    ) if False else None
                S.v(lambda v: v.tensor_copy(aT[:, 0:32].bitcast(BF16), aT[:, 0:32])
                    ) if False else None
                S.v(lambda v: v.tensor_copy(bpT[:], bpT[:])) if False else None
                S.v(lambda v: v.tensor_copy(osc[0:32, 0:256], pc[0:32, 0:256]))
                S.items.append(("gpsimd", (lambda e, t0=t0, h=h: e.dma_start(
                    out=d_o[t0:t0 + L3C, h * 256:(h + 1) * 256],
                    in_=osc[0:32, 0:256])), 16))
                S.mms([[(pc2[:, 0:256], bptok[0:32, :], vtok[0:32, :])]])
                S.v(lambda v: v.scalar_tensor_tensor(ssb[:], ssb[:], dcol[:, 0:1],
                                                     pc2[:, 0:256], ALU.mult, ALU.add))
                S.v(lambda v: v.tensor_copy(sbf[:], ssb[:]))

    S.ldg([(wq2[:], owp[:])])
    for i in range(NT):
        t0 = i * 128
        S.ld(otl[:], d_o[t0:t0 + 128, :])
        S.ldg([(gtok[:, k * 128:(k + 1) * 128],
                d_g[k * 128:(k + 1) * 128, t0:t0 + 128]) for k in range(NDC)])
        for k in range(NDC):
            S.tr(ptr[:, k * 128:(k + 1) * 128], gtok[:, k * 128:(k + 1) * 128], ident[:])
        S.v(lambda v: v.tensor_copy(gtok[:], ptr[:, 0:1024]))
        S.ld(xs[:], x_in[t0:t0 + 128, :])
        S.s(lambda s_: s_.activation(gsil[:], gtok[:], AF.Silu))
        for h in range(L3NH):
            sl = slice(h * 256, (h + 1) * 256)
            S.v(lambda v, sl=sl: v.tensor_reduce(stt[:, 0:1], otl[:, sl], axis=AX.X,
                                                 op=ALU.add))
            S.v(lambda v: v.tensor_scalar(stt[:, 0:1], stt[:, 0:1], 1.0 / 256.0,
                                          None, ALU.mult))
            S.v(lambda v, sl=sl: v.tensor_mul(f32a[:, 0:256], otl[:, sl], otl[:, sl]))
            S.v(lambda v: v.tensor_reduce(stt[:, 1:2], f32a[:, 0:256], axis=AX.X,
                                          op=ALU.add))
            S.v(lambda v: v.tensor_scalar(stt[:, 1:2], stt[:, 1:2], 1.0 / 256.0,
                                          None, ALU.mult))
            S.v(lambda v: v.tensor_mul(stt[:, 2:3], stt[:, 0:1], stt[:, 0:1]))
            S.v(lambda v: v.tensor_sub(stt[:, 1:2], stt[:, 1:2], stt[:, 2:3]))
            S.v(lambda v: v.tensor_scalar(stt[:, 1:2], stt[:, 1:2], 1e-5, None, ALU.add))
            S.s(lambda s_: s_.activation(stt[:, 3:4], stt[:, 1:2], AF.Ln))
            S.s(lambda s_: s_.activation(stt[:, 3:4], stt[:, 3:4], AF.Exp,
                                         bias=0.0, scale=-0.5))
            S.v(lambda v, sl=sl: v.tensor_scalar(ogn[:, sl], otl[:, sl], stt[:, 0:1],
                                                 stt[:, 3:4], ALU.subtract, ALU.mult))
        S.v(lambda v: v.tensor_mul(ogn[:], ogn[:], gnwb[:]))
        S.v(lambda v: v.tensor_add(ogn[:], ogn[:], gnbb[:]))
        S.v(lambda v: v.tensor_mul(ogn[:], ogn[:], gsil[:]))
        S.v(lambda v: v.tensor_copy(otl[:], ogn[:]))
        for k in range(NDC):
            S.tr(ptr[:, k * 128:(k + 1) * 128], otl[:, k * 128:(k + 1) * 128], ident[:])
        S.v(lambda v: v.tensor_copy(ooT[:], ptr[:, 0:1024]))
        for dgrp in range(2):
            S.mmg([(p0[:, 0:512], ooT[:, k * 128:(k + 1) * 128],
                    wq2[:, k * 1024 + dgrp * 512:k * 1024 + dgrp * 512 + 512])
                   for k in range(NDC)])
            S.v(lambda v, dgrp=dgrp: v.tensor_add(osb[:, dgrp * 512:(dgrp + 1) * 512],
                                                  p0[:, 0:512],
                                                  xs[:, dgrp * 512:(dgrp + 1) * 512]))
        S.st(x_out[t0:t0 + 128, :], osb[:])

    return S.build(nc)


def maps_l3a(inputs, xsh):
    def chunkify(a):
        a = f32(a)
        w = a.shape[1]
        return np.ascontiguousarray(
            a.reshape(8, 128, w).transpose(1, 0, 2).reshape(128, 8 * w).astype(BF))
    g = lambda n: f32(inputs["L3_attn_" + n])
    xow_ = g("x_out_w")
    xow = np.concatenate([hT(xow_[:, n * 32:(n + 1) * 32]) for n in range(5)], 1)
    xbs_ = g("x_bias")
    xbs = np.concatenate([xbs_[n].reshape(8, 128).T for n in range(5)], 1)
    ones32 = np.ones((32, 32), np.float32)
    return [{"x": xsh[c],
             "xpm": np.ascontiguousarray(g("x_proj_mu").reshape(8, 128).T.copy()),
             "xw1": chunkify(hT(g("xl_w1"))),
             "xw2": np.ascontiguousarray(hT(g("xl_w2")).astype(BF)),
             "xb2": np.ascontiguousarray(g("xl_b2")[:128].reshape(128, 1)),
             "xb2c": np.ascontiguousarray(g("xl_b2")[128:].reshape(32, 1)),
             "xow": np.ascontiguousarray(xow.astype(BF)),
             "xbs": np.ascontiguousarray(xbs),
             "rwp": chunkify(hT(g("r_w"))), "kwp": chunkify(hT(g("k_w"))),
             "vwp": chunkify(hT(g("v_w"))), "gwp": chunkify(hT(g("g_w"))),
             "ww1": chunkify(hT(g("w_w1"))),
             "ww2": np.ascontiguousarray(hT(g("w_w2")).astype(BF)),
             "wbc": np.ascontiguousarray(g("w_b").reshape(4, 128).T.copy()),
             "bon": np.ascontiguousarray(g("bonus").T.copy()),
             "gnw": np.ascontiguousarray(np.broadcast_to(g("gn_weight"), (128, 1024))),
             "gnb": np.ascontiguousarray(np.broadcast_to(g("gn_bias"), (128, 1024))),
             "owp": chunkify(hT(g("o_w"))),
             "mq32": np.ascontiguousarray(np.triu(ones32, 1).astype(BF)),
             "id32": np.ascontiguousarray(np.eye(32, dtype=np.float32).astype(BF)),
             "idm": _IDM} for c in range(NCORE)]



_NCS = {}


def _get_nc(name, builder):
    if name not in _NCS:
        _NCS[name] = builder()
    return _NCS[name]


_IDM = np.eye(128, dtype=BF)


def maps_l0f(inputs, xsh):
    gwT = hT(f32(inputs["L0_ffn_gate_w"]))
    weT = np.ascontiguousarray(np.transpose(f32(inputs["L0_ffn_expert_w"]), (0, 2, 1)))
    return [{"x": xsh[c], "gwT": gwT, "weT": weT, "idm": _IDM} for c in range(NCORE)]


def run_phase(name, inputs, xsh):
    builder, mapper = PHASES[name]
    nc = _get_nc(name, builder)
    res = run_bass_kernel_spmd(nc, mapper(inputs, xsh), core_ids=list(range(NCORE)))
    out = np.stack([r["xo"] for r in res.results])
    return out, res


PHASES = {
    "l0f": (build_ph_l0f, maps_l0f),
    "l2f": (build_ph_l2f, maps_l2f),
    "l0a": (build_ph_l0a, maps_l0a),
    "l1a": (build_ph_l1a, maps_l1a),
    "l3f": (build_ph_l3f, maps_l3f),
    "l2a": (build_ph_l2a, maps_l2a),
    "l3a": (build_ph_l3a, maps_l3a),
}


def run_l1f(inputs, xsh):
    nc1 = _get_nc("l1f_pre", build_ph_l1f_pre)
    r1 = run_bass_kernel_spmd(nc1, maps_l1f_pre(inputs, xsh), core_ids=list(range(NCORE)))
    lnp_sh = [r["lnp"] for r in r1.results]
    lnpT_sh = [r["lnpT"] for r in r1.results]
    lnp_full = np.ascontiguousarray(np.concatenate(lnp_sh, 0))
    lnpT_full = np.ascontiguousarray(np.concatenate([r["lnfT"] for r in r1.results], 0))
    nc2 = _get_nc("l1f_mid", build_ph_l1f_mid)
    r2 = run_bass_kernel_spmd(nc2, maps_l1f_mid(
        inputs, {"lnp": lnp_full, "lnpT": lnpT_full}), core_ids=list(range(NCORE)))
    eo_all = np.ascontiguousarray(np.concatenate([r["eo"] for r in r2.results], 0))
    aux = {"eo_all": eo_all,
           "slot": [r2.results[c]["slot"][c * TPC:(c + 1) * TPC] for c in range(NCORE)],
           "wk4": [r2.results[c]["wk4"][c * TPC:(c + 1) * TPC] for c in range(NCORE)],
           "lnp_sh": lnp_sh, "lnpT_sh": lnpT_sh}
    nc3 = _get_nc("l1f_post", build_ph_l1f_post)
    r3 = run_bass_kernel_spmd(nc3, maps_l1f_post(inputs, xsh, aux),
                              core_ids=list(range(NCORE)))
    return np.stack([r["xo"] for r in r3.results])

CHAIN = ["l0f", "l0a", "l0f", "L1F", "l1a", "L1F",
         "l2f", "l2a", "l2f", "l3f", "l3a", "l3f"]


def kernel(**inputs):
    x = f32(inputs["L0_attn_x"]).reshape(NCORE, TPC, D)
    for name in CHAIN:
        if name == "L1F":
            x = run_l1f(inputs, x)
        else:
            x, _ = run_phase(name, inputs, x)
    return np.ascontiguousarray(np.asarray(x, np.float32)).reshape(BATCH, SEQ, D)


# version 03 (final)
# speedup vs baseline: 1.0023x; 1.0023x over previous
import numpy as np
import ml_dtypes
import concourse.bass as bass
import concourse.mybir as mybir
from concourse.bass_utils import run_bass_kernel_spmd
from concourse.bass import IndirectOffsetOnAxis

F32 = mybir.dt.float32
BF16 = mybir.dt.bfloat16
I32 = mybir.dt.int32
U16 = mybir.dt.uint16
AF = mybir.ActivationFunctionType
ALU = mybir.AluOpType
AX = mybir.AxisListType
BF = ml_dtypes.bfloat16

NCORE = 8
BATCH, SEQ, D = 16, 2048, 1024
BPC = BATCH // NCORE
TPC = BPC * SEQ
NT = TPC // 128
NDC = D // 128
LNEPS = 1e-5


def f32(x):
    return np.ascontiguousarray(x, dtype=np.float32)


def hT(w):
    return np.ascontiguousarray(np.asarray(w).T)


class P:

    def __init__(self, nc):
        self.nc = nc
        self.sems = {}
        self._ctx = []

    def sem(self, *names):
        for name in names:
            h = self.nc.semaphore(name)
            s = h.__enter__()
            self._ctx.append(h)
            self.sems[name] = s

    def __getitem__(self, k):
        return self.sems[k]

    def close(self):
        for h in reversed(self._ctx):
            h.__exit__(None, None, None)


def emit_ln_v(v, xt, st, sqv, lnf_out):
    su, ssq, mean, var = st[:, 0:1], st[:, 1:2], st[:, 2:3], st[:, 3:4]
    v.tensor_reduce(su, xt, axis=AX.X, op=ALU.add)
    v.tensor_tensor(sqv, xt, xt, ALU.mult)
    v.drain()
    v.tensor_reduce(ssq, sqv, axis=AX.X, op=ALU.add)
    v.tensor_scalar(mean, su, 1.0 / D, None, ALU.mult)
    v.drain()
    v.tensor_scalar(var, ssq, 1.0 / D, LNEPS, ALU.mult, ALU.add)
    v.tensor_mul(su, mean, mean)
    v.drain()
    return v.tensor_sub(var, var, su)


def emit_rstd_s(s, st):
    s.activation(st[:, 4:5], st[:, 3:4], AF.Ln, bias=0.0, scale=1.0)
    s.drain()
    return s.activation(st[:, 4:5], st[:, 4:5], AF.Exp, bias=0.0, scale=-0.5)




def build_ph_l0f():
    nc = bass.Bass("TRN2", target_bir_lowering=False)
    x_in = nc.dram_tensor("x", [TPC, D], F32, kind="ExternalInput").ap()
    gwT = nc.dram_tensor("gwT", [D, 8], F32, kind="ExternalInput").ap()
    weT = nc.dram_tensor("weT", [8, D, D], F32, kind="ExternalInput").ap()
    idm = nc.dram_tensor("idm", [128, 128], BF16, kind="ExternalInput").ap()
    x_out = nc.dram_tensor("xo", [TPC, D], F32, kind="ExternalOutput").ap()

    p = P(nc)
    p.sem("ldx0", "ldx1", "ldw", "ldc", "sq", "stat", "lnr", "tr", "trc", "gsc", "msk",
          "yev", "yco", "fin", "st0", "st1")

    xs = nc.alloc_sbuf_tensor("xs", [128, 2 * D], F32)
    lnf = nc.alloc_sbuf_tensor("lnf", [128, 2 * D], F32)
    lnb = nc.alloc_sbuf_tensor("lnb", [128, 2 * D], BF16)
    lnT = nc.alloc_sbuf_tensor("lnT", [128, 2 * D], BF16)
    sqt = nc.alloc_sbuf_tensor("sqs", [128, 2 * D], F32)
    ot = nc.alloc_sbuf_tensor("ot", [128, 2 * D], F32)
    stt = nc.alloc_sbuf_tensor("stt", [128, 16], F32)
    wb = nc.alloc_sbuf_tensor("wb", [128, 65536], BF16)
    gwb = nc.alloc_sbuf_tensor("gwb", [128, 64], BF16)
    ident = nc.alloc_sbuf_tensor("ident", [128, 128], BF16)
    sct = nc.alloc_sbuf_tensor("sc", [128, 32], F32)
    acct = nc.alloc_sbuf_tensor("acc", [128, 2 * D], F32)
    ptr = nc.alloc_psum_tensor("ptr", [128, D], BF16)
    psc = nc.alloc_psum_tensor("psc", [128, 512], F32)
    pyA = nc.alloc_psum_tensor("pyA", [128, 1024], F32)
    pyB = nc.alloc_psum_tensor("pyB", [128, 1024], F32)
    with nc.Block() as blk:
        @blk.sync
        def _(sync):
            sync.dma_start(out=ident[:], in_=idm[:]).then_inc(p["ldc"], 16)
            for i in range(NT):
                if i >= 2:
                    sync.wait_ge(p["st%d" % (i % 2)], 16 * (i // 2))
                sync.dma_start(out=xs[:, (i % 2) * D:(i % 2 + 1) * D],
                               in_=x_in[i * 128:(i + 1) * 128, :]).then_inc(p["ldx%d" % (i % 2)], 16)

        @blk.gpsimd
        def _(gp):
            for e in range(8):
                for k in range(8):
                    gp.dma_start(out=wb[:, (e * 16 + k * 2) * 512:(e * 16 + k * 2 + 2) * 512],
                                 in_=weT[e, k * 128:(k + 1) * 128, :]).then_inc(p["ldw"], 16)
            for k in range(8):
                gp.dma_start(out=gwb[:, k * 8:(k + 1) * 8],
                             in_=gwT[k * 128:(k + 1) * 128, :]).then_inc(p["ldw"], 16)
            for i in range(NT):
                gp.wait_ge(p["fin"], i + 1)
                gp.dma_start(out=x_out[i * 128:(i + 1) * 128, :],
                             in_=ot[:, (i % 2) * D:(i % 2 + 1) * D]).then_inc(p["st%d" % (i % 2)], 16)
            gp.wait_ge(p["st0"], 16 * (NT // 2))
            gp.wait_ge(p["st1"], 16 * (NT // 2))

        @blk.vector
        def _(v):
            for i in range(NT):
                o = (i % 2) * D
                xt = xs[:, o:o + D]
                st = stt[:, (i % 2) * 8:(i % 2) * 8 + 8]
                v.wait_ge(p["ldx%d" % (i % 2)], 16 * (i // 2 + 1))
                if i >= 2:
                    v.wait_ge(p["st%d" % (i % 2)], 16 * (i // 2))
                emit_ln_v(v, xt, st, sqt[:, o:o + D], None).then_inc(p["stat"], 1)
                v.wait_ge(p["lnr"], i + 1)
                v.tensor_scalar(lnf[:, o:o + D], xt, st[:, 2:3], st[:, 4:5],
                                ALU.subtract, ALU.mult)
                v.drain()
                v.tensor_copy(lnb[:, o:o + D], lnf[:, o:o + D]).then_inc(p["sq"], 1)
                v.wait_ge(p["tr"], i + 1)
                v.tensor_copy(lnT[:, o:o + D], ptr[:, :]).then_inc(p["trc"], 1)
                v.wait_ge(p["gsc"], i + 1)
                s0 = sct[:, (i % 2) * 16:(i % 2) * 16 + 8]
                s1 = sct[:, (i % 2) * 16 + 8:(i % 2) * 16 + 16]
                v.tensor_copy(s0, psc[:, 0:8]).then_inc(p["msk"], 1)
                v.drain()
                v.reduce_max(st[:, 5:6], s0, axis=AX.X)
                v.drain()
                v.tensor_scalar(s0, s0, st[:, 5:6], None, ALU.is_ge)
                v.memset(st[:, 6:7], 0.0)
                v.drain()
                v.tensor_tensor_scan(s1, s0, st[:, 6:7].broadcast_to((128, 8)),
                                     0.0, ALU.add, ALU.add)
                v.drain()
                v.tensor_scalar(s1, s1, 1.0, None, ALU.is_le)
                v.drain()
                v.tensor_mul(s0, s0, s1)
                av = acct[:, o:o + D]
                v.memset(av, 0.0)
                v.drain()
                for e in range(8):
                    v.wait_ge(p["yev"], i * 8 + e + 1)
                    py = pyA if e % 2 == 0 else pyB
                    v.scalar_tensor_tensor(av[:, 0:512], py[:, 0:512], s0[:, e:e + 1],
                                           av[:, 0:512], ALU.mult, ALU.add)
                    v.scalar_tensor_tensor(av[:, 512:1024], py[:, 512:1024], s0[:, e:e + 1],
                                           av[:, 512:1024], ALU.mult, ALU.add).then_inc(p["yco"], 1)
                    v.drain()
                v.tensor_add(av, av, lnf[:, o:o + D])
                v.drain()
                v.scalar_tensor_tensor(ot[:, o:o + D], av, 0.5, xt,
                                       ALU.mult, ALU.add).then_inc(p["fin"], 1)

        @blk.scalar
        def _(s):
            for i in range(NT):
                st = stt[:, (i % 2) * 8:(i % 2) * 8 + 8]
                s.wait_ge(p["stat"], i + 1)
                emit_rstd_s(s, st).then_inc(p["lnr"], 1)

        @blk.tensor
        def _(t):
            t.wait_ge(p["ldw"], 16 * 72)
            t.wait_ge(p["ldc"], 16)
            for i in range(NT):
                o = (i % 2) * D
                t.wait_ge(p["sq"], i + 1)
                if i >= 1:
                    t.wait_ge(p["trc"], i)
                for k in range(NDC):
                    ins = t.transpose(ptr[:, k * 128:(k + 1) * 128],
                                      lnb[:, o + k * 128:o + (k + 1) * 128], ident[:])
                ins.then_inc(p["tr"], 1)
                t.wait_ge(p["trc"], i + 1)
                if i >= 1:
                    t.wait_ge(p["msk"], i)
                for k in range(8):
                    ins = t.matmul(psc[:, 0:8], lnT[:, o + k * 128:o + (k + 1) * 128],
                                   gwb[:, k * 8:(k + 1) * 8], start=(k == 0), stop=(k == 7))
                ins.then_inc(p["gsc"], 1)
                for e in range(8):
                    if i * 8 + e >= 2:
                        t.wait_ge(p["yco"], i * 8 + e - 1)
                    py = pyA if e % 2 == 0 else pyB
                    for half in range(2):
                        for k in range(8):
                            ins = t.matmul(py[:, half * 512:(half + 1) * 512],
                                           lnT[:, o + k * 128:o + (k + 1) * 128],
                                           wb[:, (e * 16 + k * 2 + half) * 512:
                                              (e * 16 + k * 2 + half + 1) * 512],
                                           start=(k == 0), stop=(k == 7))
                    ins.then_inc(p["yev"], 1)

    p.close()
    return nc






def build_ph_l2f():
    nc = bass.Bass("TRN2", target_bir_lowering=False)
    x_in = nc.dram_tensor("x", [TPC, D], F32, kind="ExternalInput").ap()
    protoT = nc.dram_tensor("protoT", [D, 32], F32, kind="ExternalInput").ap()
    BaT = nc.dram_tensor("BaT", [D, D], F32, kind="ExternalInput").ap()
    Aa = nc.dram_tensor("Aa", [D, D], F32, kind="ExternalInput").ap()
    biasA = nc.dram_tensor("biasA", [32, D], F32, kind="ExternalInput").ap()
    tempv = nc.dram_tensor("tempv", [1, 1], F32, kind="ExternalInput").ap()
    idm = nc.dram_tensor("idm", [128, 128], BF16, kind="ExternalInput").ap()
    x_out = nc.dram_tensor("xo", [TPC, D], F32, kind="ExternalOutput").ap()

    p = P(nc)
    p.sem("ldx0", "ldx1", "ldw", "ldc", "stat", "lnr", "sq", "tr", "trc", "pd", "dst",
          "sqd", "rt", "rtb", "wm", "wme", "wmt", "wmtc", "wmtd", "hmm", "hme",
          "hmc", "htr", "htrc", "yac", "fin", "st0", "st1", "ldx20", "ldx21",
          "p2a", "p2b", "p2c", "p2d", "rtcs")

    xs = nc.alloc_sbuf_tensor("xs", [128, 2 * D], F32)
    lnf = nc.alloc_sbuf_tensor("lnf", [128, 2 * D], F32)
    lnb = nc.alloc_sbuf_tensor("lnb", [128, D], BF16)
    lnTr = nc.alloc_sbuf_tensor("lnTr", [128, NT * D], BF16)
    dstr = nc.alloc_sbuf_tensor("dstr", [128, NT * 32], F32)
    sqt = nc.alloc_sbuf_tensor("sqs", [128, 2 * D], F32)
    ot = nc.alloc_sbuf_tensor("ot", [128, 2 * D], F32)
    stt = nc.alloc_sbuf_tensor("stt", [128, 32], F32)
    BaTb = nc.alloc_sbuf_tensor("BaTb", [128, NDC * D], BF16)
    Aab = nc.alloc_sbuf_tensor("Aab", [128, NDC * D], BF16)
    bsb = nc.alloc_sbuf_tensor("bsb", [32, D], BF16)
    ptf = nc.alloc_sbuf_tensor("ptf", [128, 256], F32)
    ptb = nc.alloc_sbuf_tensor("ptb", [128, 256], BF16)
    onesf = nc.alloc_sbuf_tensor("onesf", [128, 128], F32)
    p2r = nc.alloc_sbuf_tensor("p2r", [128, 32], F32)
    rtc = nc.alloc_sbuf_tensor("rtc", [128, 4], F32)
    wmk = nc.alloc_sbuf_tensor("wmask", [128, 64], F32)
    wmb = nc.alloc_sbuf_tensor("wmb", [128, 32], BF16)
    wmtb = nc.alloc_sbuf_tensor("wmtb", [32, 128], BF16)
    hm = nc.alloc_sbuf_tensor("hm", [128, D], BF16)
    hmT = nc.alloc_sbuf_tensor("hmT", [128, D], BF16)
    ident = nc.alloc_sbuf_tensor("ident", [128, 128], BF16)
    tmps = nc.alloc_sbuf_tensor("tmps", [1, 4], F32)
    ptr = nc.alloc_psum_tensor("ptr", [128, D], BF16)
    pdd = nc.alloc_psum_tensor("pdd", [128, 512], F32)
    ph = nc.alloc_psum_tensor("ph", [128, 1024], F32)
    py = nc.alloc_psum_tensor("py", [128, 1024], F32)
    pmisc = nc.alloc_psum_tensor("pmisc", [128, 512], F32)
    pwt = nc.alloc_psum_tensor("pwt", [32, 1024], BF16)
    with nc.Block() as blk:
        @blk.sync
        def _(sync):
            sync.dma_start(out=ident[:], in_=idm[:]).then_inc(p["ldc"], 16)
            sync.dma_start(out=tmps[0:1, 0:1], in_=tempv[:, :]).then_inc(p["ldc"], 16)
            for k in range(NDC):
                sync.dma_start(out=ptf[:, k * 32:(k + 1) * 32],
                               in_=protoT[k * 128:(k + 1) * 128, :]).then_inc(p["ldc"], 16)
            for i in range(NT):
                if i >= 2:
                    sync.wait_ge(p["sq"], i - 1)
                sync.dma_start(out=xs[:, (i % 2) * D:(i % 2 + 1) * D],
                               in_=x_in[i * 128:(i + 1) * 128, :]).then_inc(p["ldx%d" % (i % 2)], 16)
            for i in range(NT):
                if i < 2:
                    sync.wait_ge(p["sq"], NT)
                else:
                    sync.wait_ge(p["st%d" % (i % 2)], 16 * (i // 2))
                sync.dma_start(out=xs[:, (i % 2) * D:(i % 2 + 1) * D],
                               in_=x_in[i * 128:(i + 1) * 128, :]).then_inc(p["ldx2%d" % (i % 2)], 16)

        @blk.gpsimd
        def _(gp):
            for k in range(NDC):
                gp.dma_start(out=BaTb[:, k * D:(k + 1) * D],
                             in_=BaT[k * 128:(k + 1) * 128, :]).then_inc(p["ldw"], 16)
                gp.dma_start(out=Aab[:, k * D:(k + 1) * D],
                             in_=Aa[k * 128:(k + 1) * 128, :]).then_inc(p["ldw"], 16)
                gp.dma_start(out=ptb[:, k * 32:(k + 1) * 32],
                             in_=protoT[k * 128:(k + 1) * 128, :]).then_inc(p["ldw"], 16)
            gp.dma_start(out=bsb[:, :], in_=biasA[:, :]).then_inc(p["ldw"], 16)
            for i in range(NT):
                gp.wait_ge(p["fin"], i + 1)
                gp.dma_start(out=x_out[i * 128:(i + 1) * 128, :],
                             in_=ot[:, (i % 2) * D:(i % 2 + 1) * D]).then_inc(p["st%d" % (i % 2)], 16)
            gp.wait_ge(p["st0"], 16 * (NT // 2))
            gp.wait_ge(p["st1"], 16 * (NT // 2))

        @blk.vector
        def _(v):
            v.wait_ge(p["ldc"], 16 * (2 + NDC))
            v.memset(onesf[:, :], 1.0)
            for k in range(NDC):
                ins = v.tensor_mul(ptf[:, k * 32:(k + 1) * 32], ptf[:, k * 32:(k + 1) * 32],
                                   ptf[:, k * 32:(k + 1) * 32])
            ins.then_inc(p["p2a"], 1)
            v.drain()
            v.tensor_scalar(tmps[0:1, 1:2], tmps[0:1, 0:1], -1.0, None, ALU.mult)
            v.drain()
            v.tensor_max(tmps[0:1, 2:3], tmps[0:1, 0:1], tmps[0:1, 1:2])
            v.drain()
            v.tensor_scalar(tmps[0:1, 2:3], tmps[0:1, 2:3], 0.1, None, ALU.max)
            v.drain()
            v.reciprocal(tmps[0:1, 3:4], tmps[0:1, 2:3]).then_inc(p["rt"], 1)
            v.wait_ge(p["rtb"], 1)
            v.tensor_copy(rtc[:, 0:1], pmisc[:, 0:1]).then_inc(p["rtcs"], 1)
            v.wait_ge(p["p2b"], 1)
            v.tensor_copy(p2r[0:1, :], pmisc[0:1, 32:64]).then_inc(p["p2c"], 1)
            v.wait_ge(p["p2d"], 1)
            v.tensor_copy(p2r[:, :], pmisc[:, 64:96])
            v.drain()
            v.tensor_scalar(rtc[:, 1:2], rtc[:, 0:1], -1.0, None, ALU.mult)
            v.drain()
            for i in range(NT):
                o = (i % 2) * D
                xt = xs[:, o:o + D]
                st = stt[:, (i % 2) * 8:(i % 2) * 8 + 8]
                v.wait_ge(p["ldx%d" % (i % 2)], 16 * (i // 2 + 1))
                emit_ln_v(v, xt, st, sqt[:, o:o + D], None).then_inc(p["stat"], 1)
                v.wait_ge(p["lnr"], i + 1)
                v.reciprocal(st[:, 4:5], st[:, 4:5])
                v.drain()
                v.tensor_scalar(lnf[:, o:o + D], xt, st[:, 2:3], st[:, 4:5],
                                ALU.subtract, ALU.mult)
                v.drain()
                v.tensor_copy(lnb[:, :], lnf[:, o:o + D]).then_inc(p["sq"], 1)
                v.tensor_mul(sqt[:, o:o + D], lnf[:, o:o + D], lnf[:, o:o + D])
                v.drain()
                v.tensor_reduce(st[:, 5:6], sqt[:, o:o + D], axis=AX.X, op=ALU.add)
                v.wait_ge(p["tr"], i + 1)
                v.tensor_copy(lnTr[:, i * D:(i + 1) * D], ptr[:, :]).then_inc(p["trc"], 1)
                v.wait_ge(p["pd"], i + 1)
                dsl = dstr[:, i * 32:(i + 1) * 32]
                v.drain()
                v.tensor_scalar(dsl, pdd[:, 0:32], -2.0, st[:, 5:6], ALU.mult, ALU.add)
                v.drain()
                v.tensor_add(dsl, dsl, p2r[:, :]).then_inc(p["hmm"], 1)
                v.drain()
                v.tensor_scalar(dsl, dsl, 0.0, None, ALU.max).then_inc(p["dst"], 1)
            for i in range(NT):
                o = (i % 2) * D
                st = stt[:, (i % 2) * 8:(i % 2) * 8 + 8]
                dsl = dstr[:, i * 32:(i + 1) * 32]
                wcur = wmk[:, 0:32]
                wsec = wmk[:, 32:64]
                v.wait_ge(p["sqd"], i + 1)
                v.tensor_reduce(st[:, 0:1], dsl, axis=AX.X, op=ALU.min)
                v.drain()
                v.tensor_mul(st[:, 1:2], st[:, 0:1], rtc[:, 0:1]).then_inc(p["wm"], 1)
                v.wait_ge(p["wme"], i + 1)
                v.reduce_max(st[:, 2:3], wcur, axis=AX.X)
                v.drain()
                v.tensor_scalar(wsec, wcur, st[:, 2:3], None, ALU.is_equal)
                v.drain()
                v.scalar_tensor_tensor(wsec, wsec, -1e30, wcur, ALU.mult, ALU.add)
                v.drain()
                v.reduce_max(st[:, 3:4], wsec, axis=AX.X)
                v.drain()
                v.tensor_add(st[:, 4:5], st[:, 2:3], st[:, 3:4])
                v.tensor_scalar(wsec, wcur, st[:, 3:4], None, ALU.is_ge)
                v.drain()
                v.reciprocal(st[:, 5:6], st[:, 4:5])
                v.tensor_mul(wcur, wcur, wsec)
                v.drain()
                v.tensor_scalar(wcur, wcur, st[:, 5:6], None, ALU.mult)
                v.drain()
                v.tensor_copy(wmb[:, :], wcur).then_inc(p["wmt"], 1)
                v.wait_ge(p["hme"], i + 1)
                for pr in range(32):
                    ins = v.tensor_scalar(hm[:, pr * 32:(pr + 1) * 32], ph[:, pr * 32:(pr + 1) * 32],
                                          wcur[:, pr:pr + 1], None, ALU.mult)
                ins.then_inc(p["hmc"], 1)
                v.drain()
                v.wait_ge(p["htr"], i + 1)
                v.tensor_copy(hmT[:, :], ptr[:, :]).then_inc(p["htrc"], 1)
                v.wait_ge(p["wmtc"], i + 1)
                v.tensor_copy(wmtb[:, :], pwt[:, 0:128]).then_inc(p["wmtd"], 1)
                v.wait_ge(p["yac"], i + 1)
                v.wait_ge(p["ldx2%d" % (i % 2)], 16 * (i // 2 + 1))
                if i >= 2:
                    v.wait_ge(p["st%d" % (i % 2)], 16 * (i // 2))
                v.scalar_tensor_tensor(ot[:, o:o + 512], py[:, 0:512], 0.5,
                                       xs[:, o:o + 512], ALU.mult, ALU.add)
                v.scalar_tensor_tensor(ot[:, o + 512:o + D], py[:, 512:1024], 0.5,
                                       xs[:, o + 512:o + D], ALU.mult, ALU.add).then_inc(p["fin"], 1)

        @blk.scalar
        def _(s):
            for i in range(NT):
                st = stt[:, (i % 2) * 8:(i % 2) * 8 + 8]
                s.wait_ge(p["stat"], i + 1)
                s.activation(st[:, 4:5], st[:, 3:4], AF.Sqrt, bias=0.0, scale=1.0).then_inc(p["lnr"], 1)
                s.wait_ge(p["dst"], i + 1)
                dsl = dstr[:, i * 32:(i + 1) * 32]
                s.activation(dsl, dsl, AF.Sqrt, bias=0.0, scale=1.0).then_inc(p["sqd"], 1)
            for i in range(NT):
                st = stt[:, (i % 2) * 8:(i % 2) * 8 + 8]
                dsl = dstr[:, i * 32:(i + 1) * 32]
                s.wait_ge(p["wm"], i + 1)
                if i >= 1:
                    s.wait_ge(p["hmc"], i)
                s.activation(wmk[:, 0:32], dsl, AF.Exp, bias=st[:, 1:2],
                             scale=rtc[:, 1:2]).then_inc(p["wme"], 1)

        @blk.tensor
        def _(t):
            t.wait_ge(p["ldw"], 16 * 25)
            t.wait_ge(p["ldc"], 16 * (2 + NDC))
            t.wait_ge(p["rt"], 1)
            t.matmul(pmisc[:, 0:1], onesf[0:1, :], tmps[0:1, 3:4],
                     start=True, stop=True).then_inc(p["rtb"], 1)
            t.wait_ge(p["p2a"], 1)
            t.wait_ge(p["rtcs"], 1)
            for k in range(NDC):
                ins = t.matmul(pmisc[0:1, 32:64], onesf[:, 0:1], ptf[:, k * 32:(k + 1) * 32],
                               start=(k == 0), stop=(k == NDC - 1))
            ins.then_inc(p["p2b"], 1)
            t.wait_ge(p["p2c"], 1)
            t.matmul(pmisc[:, 64:96], onesf[0:1, :], p2r[0:1, :],
                     start=True, stop=True).then_inc(p["p2d"], 1)
            for i in range(NT):
                o = (i % 2) * D
                t.wait_ge(p["sq"], i + 1)
                if i >= 1:
                    t.wait_ge(p["trc"], i)
                for k in range(NDC):
                    ins = t.transpose(ptr[:, k * 128:(k + 1) * 128],
                                      lnb[:, k * 128:(k + 1) * 128], ident[:])
                ins.then_inc(p["tr"], 1)
                t.wait_ge(p["trc"], i + 1)
                if i >= 1:
                    t.wait_ge(p["hmm"], i)
                for k in range(NDC):
                    ins = t.matmul(pdd[:, 0:32], lnTr[:, i * D + k * 128:i * D + (k + 1) * 128],
                                   ptb[:, k * 32:(k + 1) * 32], start=(k == 0), stop=(k == NDC - 1))
                ins.then_inc(p["pd"], 1)
            for i in range(NT):
                if i >= 1:
                    t.wait_ge(p["hmc"], i)
                for half in range(2):
                    for k in range(NDC):
                        ins = t.matmul(ph[:, half * 512:(half + 1) * 512],
                                       lnTr[:, i * D + k * 128:i * D + (k + 1) * 128],
                                       BaTb[:, k * D + half * 512:k * D + (half + 1) * 512],
                                       start=(k == 0), stop=(k == NDC - 1))
                ins.then_inc(p["hme"], 1)
                t.wait_ge(p["wmt"], i + 1)
                if i >= 1:
                    t.wait_ge(p["wmtd"], i)
                t.transpose(pwt[:, 0:128], wmb[:, :], ident[:]).then_inc(p["wmtc"], 1)
                t.wait_ge(p["hmc"], i + 1)
                if i >= 1:
                    t.wait_ge(p["htrc"], i)
                for k in range(NDC):
                    ins = t.transpose(ptr[:, k * 128:(k + 1) * 128],
                                      hm[:, k * 128:(k + 1) * 128], ident[:])
                ins.then_inc(p["htr"], 1)
                t.wait_ge(p["htrc"], i + 1)
                t.wait_ge(p["wmtd"], i + 1)
                if i >= 1:
                    t.wait_ge(p["fin"], i)
                for half in range(2):
                    for k in range(NDC):
                        t.matmul(py[:, half * 512:(half + 1) * 512],
                                 hmT[:, k * 128:(k + 1) * 128],
                                 Aab[:, k * D + half * 512:k * D + (half + 1) * 512],
                                 start=(k == 0), stop=False)
                    ins = t.matmul(py[:, half * 512:(half + 1) * 512], wmtb[:, :],
                                   bsb[:, half * 512:(half + 1) * 512],
                                   start=False, stop=True)
                ins.then_inc(p["yac"], 1)

    p.close()
    return nc


def maps_l2f(inputs, xsh):
    proto = f32(inputs["L2_ffn_prototypes"])
    Bm = f32(inputs["L2_ffn_B"])
    Am = f32(inputs["L2_ffn_A"])
    bias = f32(inputs["L2_ffn_bias"])
    Baug = np.zeros((32, 32, D), np.float32)
    Baug[:, :16, :] = Bm
    BaT = hT(Baug.reshape(1024, D))
    Aaug = np.zeros((32, 32, D), np.float32)
    Aaug[:, :16, :] = np.transpose(Am, (0, 2, 1))
    Aa = f32(Aaug.reshape(1024, D))
    tempv = f32(inputs["L2_ffn_temp"]).reshape(1, 1)
    pT = hT(proto)
    return [{"x": xsh[c], "protoT": pT, "BaT": BaT, "Aa": Aa, "biasA": bias,
             "tempv": tempv, "idm": _IDM} for c in range(NCORE)]





L0A_NTB = SEQ // 128


def build_ph_l0a():
    nc = bass.Bass("TRN2", target_bir_lowering=False)
    x_in = nc.dram_tensor("x", [TPC, D], F32, kind="ExternalInput").ap()
    wqT = nc.dram_tensor("wqT", [D + 128, D], F32, kind="ExternalInput").ap()
    wkT = nc.dram_tensor("wkT", [D + 128, D], F32, kind="ExternalInput").ap()
    wvT = nc.dram_tensor("wvT", [D + 128, D], F32, kind="ExternalInput").ap()
    woT = nc.dram_tensor("woT", [D + 128, D], F32, kind="ExternalInput").ap()
    bmask = nc.dram_tensor("bmask", [4, 128, 512], BF16, kind="ExternalInput").ap()
    idm = nc.dram_tensor("idm", [128, 128], BF16, kind="ExternalInput").ap()
    x_out = nc.dram_tensor("xo", [TPC, D], F32, kind="ExternalOutput").ap()
    o_dram = nc.dram_tensor("odr", [TPC, D], BF16, kind="Internal").ap()

    NTB = L0A_NTB
    HGN = 8
    SCALE = 1.0 / np.sqrt(128.0)
    p = P(nc)
    p.sem("ldc", "ldw", "ldx0", "ldx1", "xl0", "xl1", "stat", "lnr", "qnr",
          "sq", "tr", "trc", "qkv", "qke", "nsq", "qnm", "knm", "qtr", "qtrc",
          "ktr", "ktrc", "sTr", "aTr", "avu", "avd", "oev", "otr", "otrc",
          "opj", "ope", "fin", "st0", "st1", "on1", "ogd", "ogs", "obl", "obc")

    xs = nc.alloc_sbuf_tensor("xs", [128, 2 * D], F32)
    xr = nc.alloc_sbuf_tensor("xr", [128, 2 * D], F32)
    lnb = nc.alloc_sbuf_tensor("lnb", [128, 2 * D], BF16)
    sqt = nc.alloc_sbuf_tensor("sqs", [128, D], F32)
    qf = nc.alloc_sbuf_tensor("qf", [128, 512], F32)
    ot = nc.alloc_sbuf_tensor("ot", [128, 2 * D], F32)
    stt = nc.alloc_sbuf_tensor("stt", [128, 16], F32)
    nst = nc.alloc_sbuf_tensor("nst", [128, 64], F32)
    lnT = nc.alloc_sbuf_tensor("lnT", [128, NTB * D], BF16)
    wq = nc.alloc_sbuf_tensor("wq", [128, 8 * D], BF16)
    wk = nc.alloc_sbuf_tensor("wk", [128, 8 * D], BF16)
    wv = nc.alloc_sbuf_tensor("wv", [128, 8 * D], BF16)
    wo = nc.alloc_sbuf_tensor("wo", [128, 8 * D], BF16)
    biasb = nc.alloc_sbuf_tensor("biasb", [1, 4 * D], BF16)
    qn = nc.alloc_sbuf_tensor("qn", [128, 512], BF16)
    kn = nc.alloc_sbuf_tensor("kn", [128, 512], BF16)
    qT = nc.alloc_sbuf_tensor("qT", [128, 4 * SEQ], BF16)
    kT = nc.alloc_sbuf_tensor("kT", [128, 4 * SEQ], BF16)
    vab = nc.alloc_sbuf_tensor("vab", [128, NTB * 520], BF16)
    aT = nc.alloc_sbuf_tensor("aT", [128, 2 * 512], BF16)
    ident = nc.alloc_sbuf_tensor("ident", [128, 128], BF16)
    bmk = nc.alloc_sbuf_tensor("bmk", [128, 4 * 512], BF16)
    ones1 = nc.alloc_sbuf_tensor("ones1", [1, 128], BF16)
    og = nc.alloc_sbuf_tensor("og", [128, 4 * 512], BF16)
    obt = nc.alloc_sbuf_tensor("obt", [128, D], BF16)
    ptr = nc.alloc_psum_tensor("ptr", [128, 1024], BF16)
    ppA = nc.alloc_psum_tensor("ppA", [128, 512], F32)
    ppB = nc.alloc_psum_tensor("ppB", [128, 512], F32)
    psT0 = nc.alloc_psum_tensor("psT0", [128, 512], F32)
    psT1 = nc.alloc_psum_tensor("psT1", [128, 512], F32)
    pso0 = nc.alloc_psum_tensor("pso0", [128, 512], F32)
    pso1 = nc.alloc_psum_tensor("pso1", [128, 512], F32)

    sched = [(h, qc) for qc in range(4) for h in range(HGN)]

    def kmax(qc):
        return 4 * qc + 4

    n_kt = sum(kmax(qc) for _, qc in sched)
    band_cum = []
    nb = 0
    for (_h, qc) in sched:
        for kt in range(kmax(qc)):
            nb += 1 if kt >= 4 * qc else 0
            band_cum.append(nb)
    n_band = nb
    NU = len(sched)
    unit_end = []
    _c = 0
    for (_h, qc) in sched:
        _c += kmax(qc)
        unit_end.append(_c)

    with nc.Block() as blk:
        @blk.sync
        def _(sync):
            sync.dma_start(out=ident[:], in_=idm[:]).then_inc(p["ldc"], 16)
            for o in range(4):
                sync.dma_start(out=bmk[:, o * 512:(o + 1) * 512],
                               in_=bmask[o]).then_inc(p["ldc"], 16)
            for b in range(BPC):
                for i in range(NTB):
                    g = b * NTB + i
                    if g >= 2:
                        sync.wait_ge(p["sq"], g - 1)
                    sync.dma_start(out=xs[:, (g % 2) * D:(g % 2 + 1) * D],
                                   in_=x_in[g * 128:(g + 1) * 128, :]).then_inc(
                        p["ldx%d" % (g % 2)], 16)
                for i in range(NTB):
                    g = b * NTB + i
                    if g >= 2:
                        sync.wait_ge(p["fin"], g - 1)
                    sync.dma_start(out=xr[:, (g % 2) * D:(g % 2 + 1) * D],
                                   in_=x_in[g * 128:(g + 1) * 128, :]).then_inc(
                        p["xl%d" % (g % 2)], 16)

        @blk.gpsimd
        def _(gp):
            for wi, (wsb, wdr) in enumerate(((wq, wqT), (wk, wkT), (wv, wvT), (wo, woT))):
                for k in range(8):
                    gp.dma_start(out=wsb[:, k * D:(k + 1) * D],
                                 in_=wdr[k * 128:(k + 1) * 128, :]).then_inc(p["ldw"], 16)
                gp.dma_start(out=biasb[0:1, wi * D:(wi + 1) * D],
                             in_=wdr[1024:1025, :]).then_inc(p["ldw"], 16)
            for b in range(BPC):
                for hg in range(2):
                    hb = b * 2 + hg
                    for qc in range(4):
                        gq = hb * 4 + qc
                        gp.wait_ge(p["ogd"], gq + 1)
                        for qt in range(4):
                            gt = b * NTB + qc * 4 + qt
                            gp.dma_start(
                                out=o_dram[gt * 128:(gt + 1) * 128, hg * 512:(hg + 1) * 512],
                                in_=og[:, qt * 512:(qt + 1) * 512]).then_inc(p["ogs"], 16)
                for i in range(NTB):
                    g = b * NTB + i
                    if i == 0:
                        gp.wait_ge(p["ogs"], 512 * (b + 1))
                    if g >= 1:
                        gp.wait_ge(p["otr"], g)
                    gp.dma_start(out=obt[:, :],
                                 in_=o_dram[g * 128:(g + 1) * 128, :]).then_inc(p["obl"], 16)
                for i in range(NTB):
                    g = b * NTB + i
                    gp.wait_ge(p["fin"], g + 1)
                    gp.dma_start(out=x_out[g * 128:(g + 1) * 128, :],
                                 in_=ot[:, (g % 2) * D:(g % 2 + 1) * D]).then_inc(
                        p["st%d" % (g % 2)], 16)
            gp.wait_ge(p["st0"], 16 * NT // 2)
            gp.wait_ge(p["st1"], 16 * NT // 2)

        @blk.vector
        def _(v):
            v.memset(ones1[:, :], 1.0).then_inc(p["on1"], 1)
            for b in range(BPC):
                for i in range(NTB):
                    g = b * NTB + i
                    o = (g % 2) * D
                    xt = xs[:, o:o + D]
                    st = stt[:, (g % 2) * 8:(g % 2) * 8 + 8]
                    v.wait_ge(p["ldx%d" % (g % 2)], 16 * (g // 2 + 1))
                    emit_ln_v(v, xt, st, sqt[:, :], None).then_inc(p["stat"], 1)
                    v.wait_ge(p["lnr"], g + 1)
                    v.reciprocal(st[:, 4:5], st[:, 4:5])
                    v.drain()
                    v.tensor_scalar(lnb[:, o:o + D], xt, st[:, 2:3], st[:, 4:5],
                                    ALU.subtract, ALU.mult).then_inc(p["sq"], 1)
                    v.wait_ge(p["tr"], g + 1)
                    v.tensor_copy(lnT[:, i * D:(i + 1) * D], ptr[:, :]).then_inc(p["trc"], 1)
                for hg in range(2):
                    hb = (b * 2 + hg)
                    v.memset(vab[:, :], 1.0)
                    v.drain()
                    for i in range(NTB):
                        nidx = hb * 48 + i * 3
                        st8 = nst[:, (i % 2) * 32:(i % 2) * 32 + 32]
                        for which in range(3):
                            n = nidx + which
                            v.wait_ge(p["qkv"], n + 1)
                            pp = ppA if n % 2 == 0 else ppB
                            if which == 2:
                                for h in range(8):
                                    ins = v.tensor_copy(
                                        vab[:, i * 520 + h * 65:i * 520 + h * 65 + 64],
                                        pp[:, h * 64:(h + 1) * 64])
                                ins.then_inc(p["qke"], 1)
                                v.drain()
                                continue
                            v.tensor_copy(qf[:, :], pp[:, :]).then_inc(p["qke"], 1)
                            v.drain()
                            v.tensor_mul(sqt[:, 0:512], qf[:, :], qf[:, :])
                            v.drain()
                            for h in range(8):
                                ins = v.tensor_reduce(st8[:, h:h + 1],
                                                      sqt[:, h * 64:(h + 1) * 64],
                                                      axis=AX.X, op=ALU.add)
                            ins.then_inc(p["nsq"], 1)
                            v.wait_ge(p["qnr"], hb * 32 + i * 2 + which + 1)
                            v.tensor_scalar(st8[:, 16:24], st8[:, 16:24], 1e-8, None, ALU.max)
                            v.drain()
                            v.reciprocal(st8[:, 0:8], st8[:, 16:24])
                            v.drain()
                            dst = qn if which == 0 else kn
                            for h in range(8):
                                ins = v.tensor_scalar(dst[:, h * 64:(h + 1) * 64],
                                                      qf[:, h * 64:(h + 1) * 64],
                                                      st8[:, h:h + 1], None, ALU.mult)
                            ins.then_inc(p["qnm" if which == 0 else "knm"], 1)
                            v.drain()
                        g2 = hb * NTB + i
                        v.wait_ge(p["qtr"], g2 + 1)
                        for hp in range(4):
                            ins = v.tensor_copy(
                                qT[:, hp * SEQ + i * 128:hp * SEQ + (i + 1) * 128],
                                ptr[:, hp * 128:(hp + 1) * 128])
                        ins.then_inc(p["qtrc"], 1)
                        v.wait_ge(p["ktr"], g2 + 1)
                        for hp in range(4):
                            ins = v.tensor_copy(
                                kT[:, hp * SEQ + i * 128:hp * SEQ + (i + 1) * 128],
                                ptr[:, hp * 128:(hp + 1) * 128])
                        ins.then_inc(p["ktrc"], 1)
                    nunit = 0
                    nbk = 0
                    for (h, qc) in sched:
                        for kt in range(kmax(qc)):
                            nbk += 1
                            if kt >= 4 * qc:
                                o_ = kt - 4 * qc
                                sl = aT[:, ((nbk - 1) % 2) * 512:((nbk - 1) % 2 + 1) * 512]
                                v.wait_ge(p["aTr"], hb * n_kt + nbk)
                                v.tensor_mul(sl, sl, bmk[:, o_ * 512:(o_ + 1) * 512]).then_inc(
                                    p["avd"], 1)
                                v.drain()
                        nunit += 1
                        if h == 0:
                            gq = hb * 4 + qc
                            if gq >= 1:
                                v.wait_ge(p["ogs"], 64 * gq)
                        v.wait_ge(p["avu"], hb * n_kt + unit_end[nunit - 1])
                        pso = pso0 if nunit % 2 else pso1
                        st = stt[:, 0:8]
                        for qt in range(4):
                            v.tensor_copy(st[:, qt:qt + 1], pso[:, qt * 65 + 64:qt * 65 + 65])
                        v.drain()
                        v.reciprocal(st[:, 4:8], st[:, 0:4])
                        v.drain()
                        for qt in range(4):
                            ins = v.tensor_scalar(
                                og[:, qt * 512 + h * 64:qt * 512 + (h + 1) * 64],
                                pso[:, qt * 65:qt * 65 + 64],
                                st[:, 4 + qt:5 + qt], None, ALU.mult)
                        ins.then_inc(p["ope"], 1)
                        v.drain()
                        if h == HGN - 1:
                            v.sem_inc(p["ogd"], 1)
                for i in range(NTB):
                    g = b * NTB + i
                    v.wait_ge(p["otr"], g + 1)
                    v.tensor_copy(lnT[:, i * D:(i + 1) * D], ptr[:, :]).then_inc(p["otrc"], 1)
                for i in range(NTB):
                    g = b * NTB + i
                    o = (g % 2) * D
                    if g >= 2:
                        v.wait_ge(p["st%d" % (g % 2)], 16 * (g // 2))
                    v.wait_ge(p["xl%d" % (g % 2)], 16 * (g // 2 + 1))
                    for half in range(2):
                        n = g * 2 + half
                        v.wait_ge(p["opj"], n + 1)
                        pp = ppA if n % 2 == 0 else ppB
                        ins = v.scalar_tensor_tensor(
                            ot[:, o + half * 512:o + (half + 1) * 512], pp[:, :], 0.5,
                            xr[:, o + half * 512:o + (half + 1) * 512], ALU.mult, ALU.add)
                        if half == 1:
                            ins.then_inc(p["fin"], 1)
                    v.drain()

        @blk.scalar
        def _(s):
            for b in range(BPC):
                for i in range(NTB):
                    g = b * NTB + i
                    st = stt[:, (g % 2) * 8:(g % 2) * 8 + 8]
                    s.wait_ge(p["stat"], g + 1)
                    s.activation(st[:, 4:5], st[:, 3:4], AF.Sqrt, bias=0.0,
                                 scale=1.0).then_inc(p["lnr"], 1)
                for hg in range(2):
                    hb = b * 2 + hg
                    for i in range(NTB):
                        for which in range(2):
                            st8 = nst[:, (i % 2) * 32:(i % 2) * 32 + 32]
                            s.wait_ge(p["nsq"], hb * 32 + i * 2 + which + 1)
                            s.activation(st8[:, 16:24], st8[:, 0:8], AF.Sqrt, bias=0.0,
                                         scale=1.0).then_inc(p["qnr"], 1)
                    nbk = 0
                    for (h, qc) in sched:
                        for kt in range(kmax(qc)):
                            nbk += 1
                            gkt = hb * n_kt + nbk
                            s.wait_ge(p["sTr"], gkt)
                            if gkt > 2:
                                s.wait_ge(p["avu"], gkt - 2)
                            psT = psT0 if nbk % 2 else psT1
                            s.activation(aT[:, ((nbk - 1) % 2) * 512:((nbk - 1) % 2 + 1) * 512],
                                         psT[:, :], AF.Exp, bias=0.0,
                                         scale=SCALE / 1.5).then_inc(p["aTr"], 1)

        @blk.tensor
        def _(t):
            t.wait_ge(p["ldw"], 16 * 36)
            t.wait_ge(p["ldc"], 16 * 5)
            t.wait_ge(p["on1"], 1)
            for b in range(BPC):
                for i in range(NTB):
                    g = b * NTB + i
                    o = (g % 2) * D
                    t.wait_ge(p["sq"], g + 1)
                    if g >= 1:
                        t.wait_ge(p["trc"], g)
                    if b >= 1 and i == 0:
                        t.wait_ge(p["otrc"], b * NTB)
                        t.wait_ge(p["fin"], b * NTB)
                    for k in range(NDC):
                        ins = t.transpose(ptr[:, k * 128:(k + 1) * 128],
                                          lnb[:, o + k * 128:o + (k + 1) * 128], ident[:])
                    ins.then_inc(p["tr"], 1)
                for hg in range(2):
                    hb = b * 2 + hg
                    for i in range(NTB):
                        nidx = hb * 48 + i * 3
                        g2 = hb * NTB + i
                        for which, wsb in enumerate((wq, wk, wv)):
                            n = nidx + which
                            pp = ppA if n % 2 == 0 else ppB
                            if n >= 2:
                                t.wait_ge(p["qke"], n - 1)
                            for k in range(NDC):
                                t.matmul(pp[:, :], lnT[:, i * D + k * 128:i * D + (k + 1) * 128],
                                         wsb[:, k * D + hg * 512:k * D + hg * 512 + 512],
                                         start=(k == 0), stop=False)
                            t.matmul(pp[:, :], ones1[0:1, :],
                                     biasb[0:1, which * D + hg * 512:which * D + hg * 512 + 512],
                                     start=False, stop=True).then_inc(p["qkv"], 1)
                        t.wait_ge(p["qnm"], g2 + 1)
                        t.wait_ge(p["trc"], (b + 1) * NTB)
                        if g2 >= 1:
                            t.wait_ge(p["qtrc"], g2)
                        if hg == 0 and i == 0:
                            t.wait_ge(p["otrc"], b * NTB)
                        for h in range(8):
                            ins = t.transpose(ptr[64 * (h % 2):64 * (h % 2) + 64,
                                                  (h // 2) * 128:(h // 2) * 128 + 128],
                                              qn[:, h * 64:(h + 1) * 64], ident[:])
                        ins.then_inc(p["qtr"], 1)
                        t.wait_ge(p["knm"], g2 + 1)
                        t.wait_ge(p["qtrc"], g2 + 1)
                        for h in range(8):
                            ins = t.transpose(ptr[64 * (h % 2):64 * (h % 2) + 64,
                                                  (h // 2) * 128:(h // 2) * 128 + 128],
                                              kn[:, h * 64:(h + 1) * 64], ident[:])
                        ins.then_inc(p["ktr"], 1)
                        t.wait_ge(p["ktrc"], g2 + 1)
                    nunit = 0
                    nbk = 0
                    for (h, qc) in sched:
                        hp, hl = h // 2, h % 2
                        nunit += 1
                        gu = hb * NU + nunit
                        pso = pso0 if nunit % 2 else pso1
                        if gu > 2:
                            t.wait_ge(p["ope"], gu - 2)
                        for kt in range(kmax(qc)):
                            nbk += 1
                            gkt = hb * n_kt + nbk
                            psT = psT0 if nbk % 2 else psT1
                            if gkt > 2:
                                t.wait_ge(p["aTr"], gkt - 2)
                            t.matmul(psT[:, :],
                                     kT[64 * hl:64 * hl + 64,
                                        hp * SEQ + kt * 128:hp * SEQ + (kt + 1) * 128],
                                     qT[64 * hl:64 * hl + 64,
                                        hp * SEQ + qc * 512:hp * SEQ + (qc + 1) * 512],
                                     start=True, stop=True).then_inc(p["sTr"], 1)
                            if kt >= 4 * qc:
                                t.wait_ge(p["avd"], hb * n_band + band_cum[nbk - 1])
                            else:
                                t.wait_ge(p["aTr"], gkt)
                            sl = aT[:, ((nbk - 1) % 2) * 512:((nbk - 1) % 2 + 1) * 512]
                            for qt in range(4):
                                ins = t.matmul(pso[:, qt * 65:qt * 65 + 65],
                                               sl[:, qt * 128:(qt + 1) * 128],
                                               vab[:, kt * 520 + h * 65:kt * 520 + h * 65 + 65],
                                               start=(kt == 0 and qt == 0),
                                               stop=(kt == kmax(qc) - 1 and qt == 3))
                            ins.then_inc(p["avu"], 1)
                for i in range(NTB):
                    g = b * NTB + i
                    if i == 0:
                        t.wait_ge(p["ktrc"], (b * 2 + 2) * NTB)
                    if g >= 1:
                        t.wait_ge(p["otrc"], g)
                    t.wait_ge(p["obl"], 16 * (g + 1))
                    for k in range(NDC):
                        ins = t.transpose(ptr[:, k * 128:(k + 1) * 128],
                                          obt[:, k * 128:(k + 1) * 128], ident[:])
                    ins.then_inc(p["otr"], 1)
                for i in range(NTB):
                    g = b * NTB + i
                    if i == 0:
                        t.wait_ge(p["otrc"], (b + 1) * NTB)
                        t.wait_ge(p["qke"], (b + 1) * 96)
                    for half in range(2):
                        n = g * 2 + half
                        pp = ppA if n % 2 == 0 else ppB
                        if n >= 2:
                            t.wait_ge(p["fin"], n // 2)
                        for k in range(NDC):
                            t.matmul(pp[:, :], lnT[:, i * D + k * 128:i * D + (k + 1) * 128],
                                     wo[:, k * D + half * 512:k * D + (half + 1) * 512],
                                     start=(k == 0), stop=False)
                        t.matmul(pp[:, :], ones1[0:1, :],
                                 biasb[0:1, 3 * D + half * 512:3 * D + (half + 1) * 512],
                                 start=False, stop=True).then_inc(p["opj"], 1)

    p.close()
    return nc


def maps_l0a(inputs, xsh):
    def wstack(w, bias):
        wt = hT(f32(w))
        pad = np.zeros((128, D), np.float32)
        pad[0] = f32(bias)
        return np.ascontiguousarray(np.concatenate([wt, pad], 0))
    bm = np.zeros((4, 128, 512), np.float32)
    for o in range(4):
        j = np.arange(128)[:, None]
        ii = np.arange(512)[None, :]
        bm[o] = (ii >= j + 128 * o).astype(np.float32)
    bm = bm.astype(BF)
    return [{"x": xsh[c], "wqT": wstack(inputs["L0_attn_Wq"], inputs["L0_attn_bq"]),
             "wkT": wstack(inputs["L0_attn_Wk"], inputs["L0_attn_bk"]),
             "wvT": wstack(inputs["L0_attn_Wv"], inputs["L0_attn_bv"]),
             "woT": wstack(inputs["L0_attn_Wo"], inputs["L0_attn_bo"]),
             "bmask": bm, "idm": _IDM} for c in range(NCORE)]




def build_ph_l1a():
    nc = bass.Bass("TRN2", target_bir_lowering=False)
    x_in = nc.dram_tensor("x", [TPC, D], F32, kind="ExternalInput").ap()
    wstk = nc.dram_tensor("wstk", [6 * D, D], F32, kind="ExternalInput").ap()
    bstk = nc.dram_tensor("bstk", [1, 6 * D], F32, kind="ExternalInput").ap()
    blkm = nc.dram_tensor("blkm", [128, 128], BF16, kind="ExternalInput").ap()
    idm = nc.dram_tensor("idm", [128, 128], BF16, kind="ExternalInput").ap()
    x_out = nc.dram_tensor("xo", [TPC, D], F32, kind="ExternalOutput").ap()

    SC = float(1.0 / np.sqrt(128.0))
    p = P(nc)
    p.sem("ldc", "ldw", "ldx0", "ldx1", "stat", "lnr", "sq", "tr", "trc",
          "pj", "pje", "tr2", "trc2", "tr3", "trc3", "qkv", "qkve",
          "tr4", "trc4", "sT", "aTe", "msk", "oev", "avm", "tr5", "trc5",
          "opj", "fin", "st0", "st1", "on1")

    xs = nc.alloc_sbuf_tensor("xs", [128, 2 * D], F32)
    lnb = nc.alloc_sbuf_tensor("lnb", [128, 2 * D], BF16)
    sqt = nc.alloc_sbuf_tensor("sqs", [128, D], F32)
    ot = nc.alloc_sbuf_tensor("ot", [128, 2 * D], F32)
    stt = nc.alloc_sbuf_tensor("stt", [128, 16], F32)
    lnT = nc.alloc_sbuf_tensor("lnT", [128, D], BF16)
    qkf = nc.alloc_sbuf_tensor("qkf", [128, D], BF16)
    vf = nc.alloc_sbuf_tensor("vf", [128, D], BF16)
    qkT = nc.alloc_sbuf_tensor("qkT", [128, D], BF16)
    vT = nc.alloc_sbuf_tensor("vT", [128, D], BF16)
    qb = nc.alloc_sbuf_tensor("qb", [128, D], BF16)
    kb = nc.alloc_sbuf_tensor("kb", [128, D], BF16)
    qbT = nc.alloc_sbuf_tensor("qbT", [128, D], BF16)
    kbT = nc.alloc_sbuf_tensor("kbT", [128, D], BF16)
    vaug = nc.alloc_sbuf_tensor("vaug", [128, 8 * 129], BF16)
    aTb = nc.alloc_sbuf_tensor("aTb", [128, 2 * 128], BF16)
    ob = nc.alloc_sbuf_tensor("ob", [128, D], BF16)
    w6 = [nc.alloc_sbuf_tensor("w6_%d" % i, [128, 8 * D], BF16) for i in range(6)]
    b6 = nc.alloc_sbuf_tensor("b6", [1, 6 * D], BF16)
    ident = nc.alloc_sbuf_tensor("ident", [128, 128], BF16)
    bmsk = nc.alloc_sbuf_tensor("bmsk", [128, 128], BF16)
    ones1 = nc.alloc_sbuf_tensor("ones1", [1, 128], BF16)
    ptr = nc.alloc_psum_tensor("ptr", [128, 1024], BF16)
    ppA = nc.alloc_psum_tensor("ppA", [128, 512], F32)
    ppB = nc.alloc_psum_tensor("ppB", [128, 512], F32)
    psT0 = nc.alloc_psum_tensor("psT0", [128, 512], F32)
    psT1 = nc.alloc_psum_tensor("psT1", [128, 512], F32)
    pso0 = nc.alloc_psum_tensor("pso0", [128, 512], F32)
    pso1 = nc.alloc_psum_tensor("pso1", [128, 512], F32)

    def trp(t, sem, dst_rows, src, n=8):
        for k in range(n):
            ins = t.transpose(ptr[:, k * 128:(k + 1) * 128],
                              src[:, k * 128:(k + 1) * 128], ident[:])
        ins.then_inc(p[sem], 1)

    with nc.Block() as blk:
        @blk.sync
        def _(sync):
            sync.dma_start(out=ident[:], in_=idm[:]).then_inc(p["ldc"], 16)
            sync.dma_start(out=bmsk[:], in_=blkm[:]).then_inc(p["ldc"], 16)
            for g in range(NT):
                if g >= 2:
                    sync.wait_ge(p["st%d" % (g % 2)], 16 * (g // 2))
                sync.dma_start(out=xs[:, (g % 2) * D:(g % 2 + 1) * D],
                               in_=x_in[g * 128:(g + 1) * 128, :]).then_inc(
                    p["ldx%d" % (g % 2)], 16)

        @blk.gpsimd
        def _(gp):
            for wi in range(6):
                for k in range(8):
                    gp.dma_start(out=w6[wi][:, k * D:(k + 1) * D],
                                 in_=wstk[wi * D + k * 128:wi * D + (k + 1) * 128, :]
                                 ).then_inc(p["ldw"], 16)
            gp.dma_start(out=b6[0:1, :], in_=bstk[:, :]).then_inc(p["ldw"], 16)
            for g in range(NT):
                gp.wait_ge(p["fin"], g + 1)
                gp.dma_start(out=x_out[g * 128:(g + 1) * 128, :],
                             in_=ot[:, (g % 2) * D:(g % 2 + 1) * D]).then_inc(
                    p["st%d" % (g % 2)], 16)
            gp.wait_ge(p["st0"], 16 * NT // 2)
            gp.wait_ge(p["st1"], 16 * NT // 2)

        @blk.vector
        def _(v):
            v.memset(ones1[:, :], 1.0)
            v.memset(vaug[:, :], 1.0).then_inc(p["on1"], 1)
            for g in range(NT):
                o = (g % 2) * D
                xt = xs[:, o:o + D]
                st = stt[:, (g % 2) * 8:(g % 2) * 8 + 8]
                v.wait_ge(p["ldx%d" % (g % 2)], 16 * (g // 2 + 1))
                emit_ln_v(v, xt, st, sqt[:, :], None).then_inc(p["stat"], 1)
                v.wait_ge(p["lnr"], g + 1)
                v.tensor_scalar(lnb[:, o:o + D], xt, st[:, 2:3], st[:, 4:5],
                                ALU.subtract, ALU.mult).then_inc(p["sq"], 1)
                v.wait_ge(p["tr"], g + 1)
                v.tensor_copy(lnT[:, :], ptr[:, :]).then_inc(p["trc"], 1)
                for n in range(4):
                    v.wait_ge(p["pj"], g * 4 + n + 1)
                    pp = ppA if n % 2 == 0 else ppB
                    dst = qkf if n < 2 else vf
                    v.tensor_copy(dst[:, (n % 2) * 512:(n % 2) * 512 + 512],
                                  pp[:, :]).then_inc(p["pje"], 1)
                v.wait_ge(p["tr2"], g + 1)
                v.tensor_copy(qkT[:, :], ptr[:, :]).then_inc(p["trc2"], 1)
                v.wait_ge(p["tr3"], g + 1)
                v.tensor_copy(vT[:, :], ptr[:, :]).then_inc(p["trc3"], 1)
                for n in range(6):
                    v.wait_ge(p["qkv"], g * 6 + n + 1)
                    pp = ppA if n % 2 == 0 else ppB
                    if n >= 4 and n == 4:
                        v.wait_ge(p["oev"], g * 8)
                    if n < 2:
                        v.tensor_scalar(qb[:, (n % 2) * 512:(n % 2) * 512 + 512],
                                        pp[:, :], SC, None, ALU.mult).then_inc(p["qkve"], 1)
                    elif n < 4:
                        v.tensor_copy(kb[:, (n % 2) * 512:(n % 2) * 512 + 512],
                                      pp[:, :]).then_inc(p["qkve"], 1)
                    else:
                        for h in range(4):
                            hh = (n - 4) * 4 + h
                            ins = v.tensor_copy(vaug[:, hh * 129:hh * 129 + 128],
                                                pp[:, h * 128:(h + 1) * 128])
                        ins.then_inc(p["qkve"], 1)
                v.wait_ge(p["tr4"], g * 2 + 1)
                v.tensor_copy(qbT[:, :], ptr[:, :]).then_inc(p["trc4"], 1)
                v.wait_ge(p["tr4"], g * 2 + 2)
                v.tensor_copy(kbT[:, :], ptr[:, :]).then_inc(p["trc4"], 1)
                for h in range(8):
                    u = g * 8 + h
                    v.wait_ge(p["aTe"], u + 1)
                    sl = aTb[:, (u % 2) * 128:(u % 2) * 128 + 128]
                    v.tensor_mul(sl, sl, bmsk[:, :]).then_inc(p["msk"], 1)
                    v.drain()
                    v.wait_ge(p["oev"], u + 1)
                    pso = pso0 if u % 2 == 0 else pso1
                    st2 = stt[:, 0:2]
                    v.tensor_copy(st2[:, 0:1], pso[:, 128:129])
                    v.drain()
                    v.reciprocal(st2[:, 1:2], st2[:, 0:1])
                    v.drain()
                    v.tensor_scalar(ob[:, h * 128:(h + 1) * 128], pso[:, 0:128],
                                    st2[:, 1:2], None, ALU.mult).then_inc(p["avm"], 1)
                    v.drain()
                v.wait_ge(p["tr5"], g + 1)
                v.tensor_copy(lnT[:, :], ptr[:, :]).then_inc(p["trc5"], 1)
                for half in range(2):
                    n = g * 2 + half
                    v.wait_ge(p["opj"], n + 1)
                    pp = ppA if n % 2 == 0 else ppB
                    if g >= 2 and half == 0:
                        v.wait_ge(p["st%d" % (g % 2)], 16 * (g // 2))
                    ins = v.tensor_add(ot[:, o + half * 512:o + (half + 1) * 512],
                                       pp[:, :], xt[:, half * 512:(half + 1) * 512])
                    if half == 1:
                        ins.then_inc(p["fin"], 1)
                v.drain()

        @blk.scalar
        def _(s):
            for g in range(NT):
                st = stt[:, (g % 2) * 8:(g % 2) * 8 + 8]
                s.wait_ge(p["stat"], g + 1)
                emit_rstd_s(s, st).then_inc(p["lnr"], 1)
                for h in range(8):
                    u = g * 8 + h
                    s.wait_ge(p["sT"], u + 1)
                    if u >= 2:
                        s.wait_ge(p["oev"], u - 1)
                    psT = psT0 if u % 2 == 0 else psT1
                    s.activation(aTb[:, (u % 2) * 128:(u % 2) * 128 + 128],
                                 psT[:, 0:128], AF.Exp, bias=0.0,
                                 scale=1.0).then_inc(p["aTe"], 1)

        @blk.tensor
        def _(t):
            t.wait_ge(p["ldw"], 16 * 49)
            t.wait_ge(p["ldc"], 16 * 2)
            t.wait_ge(p["on1"], 1)
            for g in range(NT):
                o = (g % 2) * D
                t.wait_ge(p["sq"], g + 1)
                if g >= 1:
                    t.wait_ge(p["trc5"], g)
                trp(t, "tr", None, lnb[:, o:o + D])
                t.wait_ge(p["trc"], g + 1)
                for n in range(4):
                    gn = g * 4 + n
                    pp = ppA if n % 2 == 0 else ppB
                    if gn >= 2:
                        t.wait_ge(p["pje"], gn - 1)
                    wi = 0 if n < 2 else 1
                    half = n % 2
                    for k in range(NDC):
                        t.matmul(pp[:, :], lnT[:, k * 128:(k + 1) * 128],
                                 w6[wi][:, k * D + half * 512:k * D + (half + 1) * 512],
                                 start=(k == 0), stop=False)
                    t.matmul(pp[:, :], ones1[0:1, :],
                             b6[0:1, wi * D + half * 512:wi * D + (half + 1) * 512],
                             start=False, stop=True).then_inc(p["pj"], 1)
                t.wait_ge(p["pje"], g * 4 + 2)
                trp(t, "tr2", None, qkf)
                t.wait_ge(p["trc2"], g + 1)
                t.wait_ge(p["pje"], g * 4 + 4)
                trp(t, "tr3", None, vf)
                t.wait_ge(p["trc3"], g + 1)
                for n in range(6):
                    gn = g * 6 + n
                    pp = ppA if n % 2 == 0 else ppB
                    if gn >= 2:
                        t.wait_ge(p["qkve"], gn - 1)
                    wi = 2 + n // 2
                    half = n % 2
                    src = qkT if n < 4 else vT
                    for k in range(NDC):
                        t.matmul(pp[:, :], src[:, k * 128:(k + 1) * 128],
                                 w6[wi][:, k * D + half * 512:k * D + (half + 1) * 512],
                                 start=(k == 0), stop=False)
                    t.matmul(pp[:, :], ones1[0:1, :],
                             b6[0:1, wi * D + half * 512:wi * D + (half + 1) * 512],
                             start=False, stop=True).then_inc(p["qkv"], 1)
                t.wait_ge(p["qkve"], g * 6 + 2)
                trp(t, "tr4", None, qb)
                t.wait_ge(p["trc4"], g * 2 + 1)
                t.wait_ge(p["qkve"], g * 6 + 4)
                trp(t, "tr4", None, kb)
                t.wait_ge(p["trc4"], g * 2 + 2)
                t.wait_ge(p["qkve"], g * 6 + 6)
                for h in range(8):
                    u = g * 8 + h
                    psT = psT0 if u % 2 == 0 else psT1
                    if u >= 2:
                        t.wait_ge(p["aTe"], u - 1)
                    t.matmul(psT[:, 0:128], kbT[:, h * 128:(h + 1) * 128],
                             qbT[:, h * 128:(h + 1) * 128],
                             start=True, stop=True).then_inc(p["sT"], 1)
                    t.wait_ge(p["msk"], u + 1)
                    pso = pso0 if u % 2 == 0 else pso1
                    if u >= 2:
                        t.wait_ge(p["avm"], u - 1)
                    t.matmul(pso[:, 0:129], aTb[:, (u % 2) * 128:(u % 2) * 128 + 128],
                             vaug[:, h * 129:(h + 1) * 129],
                             start=True, stop=True).then_inc(p["oev"], 1)
                t.wait_ge(p["avm"], g * 8 + 8)
                trp(t, "tr5", None, ob)
                t.wait_ge(p["trc5"], g + 1)
                for half in range(2):
                    n = g * 2 + half
                    pp = ppA if n % 2 == 0 else ppB
                    if n >= 2:
                        t.wait_ge(p["fin"], n // 2)
                    for k in range(NDC):
                        t.matmul(pp[:, :], lnT[:, k * 128:(k + 1) * 128],
                                 w6[5][:, k * D + half * 512:k * D + (half + 1) * 512],
                                 start=(k == 0), stop=False)
                    t.matmul(pp[:, :], ones1[0:1, :],
                             b6[0:1, 5 * D + half * 512:5 * D + (half + 1) * 512],
                             start=False, stop=True).then_inc(p["opj"], 1)

    p.close()
    return nc


def maps_l1a(inputs, xsh):
    wq, wk, wv = np.split(f32(inputs["L1_attn_in_proj_w"]), 3, axis=0)
    bq, bk, bv = np.split(f32(inputs["L1_attn_in_proj_b"]), 3)
    ws = [hT(f32(inputs["L1_attn_proj_qk_w"])), hT(f32(inputs["L1_attn_proj_v_w"])),
          hT(wq), hT(wk), hT(wv), hT(f32(inputs["L1_attn_out_proj_w"]))]
    bs = [f32(inputs["L1_attn_proj_qk_b"]), f32(inputs["L1_attn_proj_v_b"]),
          bq, bk, bv, f32(inputs["L1_attn_out_proj_b"])]
    wstk = np.ascontiguousarray(np.concatenate(ws, 0))
    bstk = np.ascontiguousarray(np.concatenate(bs)).reshape(1, 6 * D)
    j = np.arange(128)[:, None]
    ii = np.arange(128)[None, :]
    bm = (((ii // 4) == (j // 4)) & (ii != j)).astype(np.float32).astype(BF)
    return [{"x": xsh[c], "wstk": wstk, "bstk": bstk, "blkm": bm, "idm": _IDM}
            for c in range(NCORE)]



L1E, L1CAP, L1I, MYE = 64, 256, 768, 8
L1TT = NCORE * TPC
L1NTT = L1TT // 128


def emit_dequant(v, src_pk, nshift, i32a, i32b, u16a, u16b, dqz, dqo):
    n = src_pk.shape[-1]
    t1 = i32a.bitcast(U16)[:, 0:n]
    t2 = i32b.bitcast(U16)[:, 0:n]
    v.tensor_copy(u16a, src_pk)
    v.drain()
    if nshift:
        v.tensor_scalar(u16a, u16a, 4, None, ALU.logical_shift_right)
        v.drain()
    v.tensor_scalar(u16b, u16a, 7, None, ALU.bitwise_and)
    v.drain()
    v.tensor_scalar(u16a, u16a, 8, 12, ALU.bitwise_and,
                    ALU.logical_shift_left)
    v.tensor_scalar(dqz, u16b, 1, None, ALU.min)
    v.tensor_scalar(t1, u16b, 1, None, ALU.is_le)
    v.drain()
    v.tensor_scalar(t1, t1, 6, None, ALU.logical_shift_left)
    v.tensor_scalar(t2, u16b, 6, None, ALU.logical_shift_left)
    v.drain()
    v.tensor_sub(t2, t2, t1)
    v.drain()
    v.tensor_scalar(t2, t2, 16128, None, ALU.add)
    v.drain()
    v.tensor_add(u16a, u16a, t2)
    v.drain()
    v.tensor_mul(dqo, u16a.bitcast(BF16), dqz)
    v.drain()


def build_ph_l1f_pre():
    nc = bass.Bass("TRN2", target_bir_lowering=False)
    x_in = nc.dram_tensor("x", [TPC, D], F32, kind="ExternalInput").ap()
    idm = nc.dram_tensor("idm", [128, 128], BF16, kind="ExternalInput").ap()
    idmF = nc.dram_tensor("idmF", [128, 128], F32, kind="ExternalInput").ap()
    lnp_o = nc.dram_tensor("lnp", [TPC, D], BF16, kind="ExternalOutput").ap()
    lnpT_o = nc.dram_tensor("lnpT", [TPC, D], BF16, kind="ExternalOutput").ap()
    lnfT_o = nc.dram_tensor("lnfT", [TPC, D], F32, kind="ExternalOutput").ap()

    p = P(nc)
    p.sem("ldc", "ldx0", "ldx1", "stat", "lnr", "sq", "tr", "trc",
          "st0", "st1", "su0", "su1", "sqf", "trf", "trfc", "suf0", "suf1")
    xs = nc.alloc_sbuf_tensor("xs", [128, 2 * D], F32)
    lnf = nc.alloc_sbuf_tensor("lnf", [128, D], F32)
    lnp = nc.alloc_sbuf_tensor("lnpb", [128, 2 * D], BF16)
    lnT = nc.alloc_sbuf_tensor("lnT", [128, 2 * D], BF16)
    lnpf = nc.alloc_sbuf_tensor("lnpf", [128, 2 * D], F32)
    lnTf = nc.alloc_sbuf_tensor("lnTf", [128, 2 * D], F32)
    sqt = nc.alloc_sbuf_tensor("sqs", [128, D], F32)
    stt = nc.alloc_sbuf_tensor("stt", [128, 16], F32)
    ident = nc.alloc_sbuf_tensor("ident", [128, 128], BF16)
    identF = nc.alloc_sbuf_tensor("identF", [128, 128], F32)
    ptr = nc.alloc_psum_tensor("ptr", [128, 1024], BF16)
    ptrF = nc.alloc_psum_tensor("ptrF", [128, 1024], F32)

    with nc.Block() as blk:
        @blk.sync
        def _(sync):
            sync.dma_start(out=ident[:], in_=idm[:]).then_inc(p["ldc"], 16)
            sync.dma_start(out=identF[:], in_=idmF[:]).then_inc(p["ldc"], 16)
            for g in range(NT):
                if g >= 2:
                    sync.wait_ge(p["st%d" % (g % 2)], 16 * (g // 2))
                sync.dma_start(out=xs[:, (g % 2) * D:(g % 2 + 1) * D],
                               in_=x_in[g * 128:(g + 1) * 128, :]).then_inc(
                    p["ldx%d" % (g % 2)], 16)

        @blk.gpsimd
        def _(gp):
            for g in range(NT):
                gp.wait_ge(p["sq"], g + 1)
                gp.dma_start(out=lnp_o[g * 128:(g + 1) * 128, :],
                             in_=lnp[:, (g % 2) * D:(g % 2 + 1) * D]).then_inc(
                    p["st%d" % (g % 2)], 16)
                gp.wait_ge(p["trc"], g + 1)
                gp.dma_start(out=lnpT_o[g * 128:(g + 1) * 128, :],
                             in_=lnT[:, (g % 2) * D:(g % 2 + 1) * D]).then_inc(
                    p["su%d" % (g % 2)], 16)
                gp.wait_ge(p["trfc"], g + 1)
                gp.dma_start(out=lnfT_o[g * 128:(g + 1) * 128, :],
                             in_=lnTf[:, (g % 2) * D:(g % 2 + 1) * D]).then_inc(
                    p["suf%d" % (g % 2)], 16)
            gp.wait_ge(p["st0"], 16 * NT // 2)
            gp.wait_ge(p["st1"], 16 * NT // 2)
            gp.wait_ge(p["su0"], 16 * NT // 2)
            gp.wait_ge(p["su1"], 16 * NT // 2)
            gp.wait_ge(p["suf0"], 16 * NT // 2)
            gp.wait_ge(p["suf1"], 16 * NT // 2)

        @blk.vector
        def _(v):
            for g in range(NT):
                o = (g % 2) * D
                xt = xs[:, o:o + D]
                st = stt[:, (g % 2) * 8:(g % 2) * 8 + 8]
                v.wait_ge(p["ldx%d" % (g % 2)], 16 * (g // 2 + 1))
                emit_ln_v(v, xt, st, sqt[:, :], None).then_inc(p["stat"], 1)
                v.wait_ge(p["lnr"], g + 1)
                v.tensor_scalar(lnf[:, :], xt, st[:, 2:3], st[:, 4:5],
                                ALU.subtract, ALU.mult)
                v.drain()
                if g >= 2:
                    v.wait_ge(p["st%d" % (g % 2)], 16 * (g // 2))
                lo = lnp[:, o:o + D]
                lnfv = lnf.rearrange("p (a two) -> p two a", two=2)
                v.tensor_copy(lo[:, 0:512], lnfv[:, 0, :])
                v.drain()
                v.tensor_copy(lo[:, 512:1024], lnfv[:, 1, :]).then_inc(p["sq"], 1)
                lof = lnpf[:, o:o + D]
                v.tensor_copy(lof[:, 0:512], lnfv[:, 0, :])
                v.drain()
                v.tensor_copy(lof[:, 512:1024], lnfv[:, 1, :]).then_inc(p["sqf"], 1)
                v.wait_ge(p["tr"], g + 1)
                if g >= 2:
                    v.wait_ge(p["su%d" % (g % 2)], 16 * (g // 2))
                v.tensor_copy(lnT[:, o:o + D], ptr[:, :]).then_inc(p["trc"], 1)
                v.wait_ge(p["trf"], g + 1)
                if g >= 2:
                    v.wait_ge(p["suf%d" % (g % 2)], 16 * (g // 2))
                v.tensor_copy(lnTf[:, o:o + D], ptrF[:, :]).then_inc(p["trfc"], 1)

        @blk.scalar
        def _(s):
            for g in range(NT):
                st = stt[:, (g % 2) * 8:(g % 2) * 8 + 8]
                s.wait_ge(p["stat"], g + 1)
                emit_rstd_s(s, st).then_inc(p["lnr"], 1)

        @blk.tensor
        def _(t):
            t.wait_ge(p["ldc"], 32)
            for g in range(NT):
                o = (g % 2) * D
                t.wait_ge(p["sq"], g + 1)
                if g >= 1:
                    t.wait_ge(p["trc"], g)
                for k in range(NDC):
                    ins = t.transpose(ptr[:, k * 128:(k + 1) * 128],
                                      lnp[:, o + k * 128:o + (k + 1) * 128], ident[:])
                ins.then_inc(p["tr"], 1)
                t.wait_ge(p["sqf"], g + 1)
                if g >= 1:
                    t.wait_ge(p["trfc"], g)
                for k in range(NDC):
                    ins = t.transpose(ptrF[:, k * 128:(k + 1) * 128],
                                      lnpf[:, o + k * 128:o + (k + 1) * 128], identF[:])
                ins.then_inc(p["trf"], 1)

    p.close()
    return nc


def build_ph_l1f_mid():
    nc = bass.Bass("TRN2", target_bir_lowering=False)
    lnp = nc.dram_tensor("lnp", [L1TT, D], BF16, kind="ExternalInput").ap()
    lnpT = nc.dram_tensor("lnpT", [L1TT, D], F32, kind="ExternalInput").ap()
    gwTp = nc.dram_tensor("gwTp", [D, L1E], F32, kind="ExternalInput").ap()
    gup = nc.dram_tensor("gup", [MYE, 2, 512, L1I], I32, kind="ExternalInput").ap()
    gusce = nc.dram_tensor("gusce", [MYE, 2, 512, L1I], BF16, kind="ExternalInput").ap()
    dnp = nc.dram_tensor("dnp", [MYE, 384, D], I32, kind="ExternalInput").ap()
    dnsce = nc.dram_tensor("dnsce", [MYE, 384, D], BF16, kind="ExternalInput").ap()
    io65 = nc.dram_tensor("io65", [1, 65], F32, kind="ExternalInput").ap()
    io2k = nc.dram_tensor("io2k", [128, 2048], F32, kind="ExternalInput").ap()
    iocol = nc.dram_tensor("iocol", [128, 1], F32, kind="ExternalInput").ap()
    trilm = nc.dram_tensor("trilm", [128, 128], BF16, kind="ExternalInput").ap()
    idm = nc.dram_tensor("idm", [128, 128], BF16, kind="ExternalInput").ap()
    eo_o = nc.dram_tensor("eo", [MYE * L1CAP, D], BF16, kind="ExternalOutput").ap()
    slot_o = nc.dram_tensor("slot", [L1TT, 4], F32, kind="ExternalOutput").ap()
    wk_o = nc.dram_tensor("wk4", [L1TT, 4], F32, kind="ExternalOutput").ap()
    buf_d = nc.dram_tensor("bufd", [MYE * L1CAP, D], BF16, kind="Internal").ap()

    p = P(nc)
    p.sem("ldc", "gwl", "bcst", "bcpy", "lnl0", "lnl1", "lpl0", "lpl1",
          "gmm", "gcp", "rt1", "scl", "sel", "pfx", "pxc", "ofd", "rt2",
          "zbuf", "scd", "sto", "pkl", "deq", "wgud",
          "nio", "ohr", "tkm", "trw", "tcz", "tmr", "tkq", "tka", "cst",
          "bld", "btr", "btrc", "hA", "sil", "hB", "hpp", "htr", "htrc",
          "dmm", "eev", "est")

    lnTs = nc.alloc_sbuf_tensor("lnTs", [128, 2 * D], F32)
    lnps = nc.alloc_sbuf_tensor("lnps", [128, 2 * D], BF16)
    gwb = nc.alloc_sbuf_tensor("gwb", [128, 8 * 64], F32)
    lgt = nc.alloc_sbuf_tensor("lgt", [128, 64], F32)
    selb = nc.alloc_sbuf_tensor("selb", [128, 64], BF16)
    eqh = nc.alloc_sbuf_tensor("eqh", [128, 4 * 64], F32)
    eqs = nc.alloc_sbuf_tensor("eqs", [128, 64], F32)
    ek = nc.alloc_sbuf_tensor("ek", [128, 32], F32)
    off = nc.alloc_sbuf_tensor("off", [1, 64], F32)
    io64 = nc.alloc_sbuf_tensor("io64", [128, 64], F32)
    ebc = nc.alloc_sbuf_tensor("ebc", [128, 1], F32)
    io65r = nc.alloc_sbuf_tensor("io65r", [1, 65], F32)
    tril = nc.alloc_sbuf_tensor("tril", [128, 128], BF16)
    ones1 = nc.alloc_sbuf_tensor("ones1", [1, 128], BF16)
    tgt = nc.alloc_sbuf_tensor("tgt", [128, 4], I32)
    sw8 = nc.alloc_sbuf_tensor("sw8", [128, 8], F32)
    ident = nc.alloc_sbuf_tensor("ident", [128, 128], BF16)
    wgu = nc.alloc_sbuf_tensor("wgu", [128, 16 * L1I], BF16)
    wdn = nc.alloc_sbuf_tensor("wdn", [128, 6 * D], BF16)
    pk1 = nc.alloc_sbuf_tensor("pk1", [128, 4 * L1I], I32)
    pk2 = nc.alloc_sbuf_tensor("pk2", [128, 3 * D], I32)
    sce1 = nc.alloc_sbuf_tensor("sce1", [128, 4 * L1I], BF16)
    sce2 = nc.alloc_sbuf_tensor("sce2", [128, 3 * D], BF16)
    dqa = nc.alloc_sbuf_tensor("dqa", [128, 4 * L1I], I32)
    dqb = nc.alloc_sbuf_tensor("dqb", [128, 4 * L1I], I32)
    dqu = nc.alloc_sbuf_tensor("dqu", [128, 4 * L1I], U16)
    dqv = nc.alloc_sbuf_tensor("dqv", [128, 4 * L1I], U16)
    dqz = nc.alloc_sbuf_tensor("dqz", [128, 4 * L1I], BF16)
    dqo = nc.alloc_sbuf_tensor("dqo", [128, 4 * L1I], BF16)
    bts = nc.alloc_sbuf_tensor("bts", [128, D], BF16)
    btT = nc.alloc_sbuf_tensor("btT", [128, D], BF16)
    hb = nc.alloc_sbuf_tensor("hb", [128, L1I], F32)
    hp = nc.alloc_sbuf_tensor("hp", [128, L1I], BF16)
    hpT = nc.alloc_sbuf_tensor("hpT", [128, L1I], BF16)
    eos = nc.alloc_sbuf_tensor("eos", [128, D], BF16)
    ohsum = nc.alloc_sbuf_tensor("ohsum", [128, 2048], F32)
    ohbuf = nc.alloc_sbuf_tensor("ohbuf", [128, 2048], F32)
    io2kb = nc.alloc_sbuf_tensor("io2kb", [128, 2048], F32)
    tokrow = nc.alloc_sbuf_tensor("tokrow", [1, 2048], F32)
    tokmap = nc.alloc_sbuf_tensor("tokmap", [128, 16], I32)
    tokc = nc.alloc_sbuf_tensor("tokc", [128, 1], F32)
    iocolb = nc.alloc_sbuf_tensor("iocolb", [128, 1], F32)
    ones11 = nc.alloc_sbuf_tensor("ones11", [1, 1], F32)
    ones1f = nc.alloc_sbuf_tensor("ones1f", [1, 128], F32)
    onecb = nc.alloc_sbuf_tensor("onecb", [128, 1], BF16)
    ptr = nc.alloc_psum_tensor("ptr", [128, 1024], BF16)
    pgA = nc.alloc_psum_tensor("pgA", [128, 512], F32)
    pgB = nc.alloc_psum_tensor("pgB", [128, 512], F32)
    pxA = nc.alloc_psum_tensor("pxA", [128, 512], F32)
    pxB = nc.alloc_psum_tensor("pxB", [128, 512], F32)
    pp1 = nc.alloc_psum_tensor("pp1", [128, 512], F32)
    pp2 = nc.alloc_psum_tensor("pp2", [128, 512], F32)
    ptk = nc.alloc_psum_tensor("ptk", [1, 512], F32)

    NBT = MYE * L1CAP // 128

    with nc.Block() as blk:
        @blk.sync
        def _(sync):
            sync.dma_start(out=ident[:], in_=idm[:]).then_inc(p["ldc"], 16)
            sync.dma_start(out=tril[:], in_=trilm[:]).then_inc(p["ldc"], 16)
            sync.dma_start(out=io65r[0:1, :], in_=io65[:, :]).then_inc(p["ldc"], 16)
            sync.dma_start(out=io2kb[:], in_=io2k[:]).then_inc(p["nio"], 16)
            sync.dma_start(out=iocolb[:], in_=iocol[:]).then_inc(p["nio"], 16)
            for g in range(L1NTT):
                if g >= 2:
                    sync.wait_ge(p["gcp"], g - 1)
                sync.dma_start(out=lnTs[:, (g % 2) * D:(g % 2 + 1) * D],
                               in_=lnpT[g * 128:(g + 1) * 128, :]).then_inc(
                    p["lnl%d" % (g % 2)], 16)
            for e in range(MYE):
                for mt in range(2):
                    if e + mt > 0:
                        sync.wait_ge(p["deq"], 3 * e + mt)
                    sync.dma_start(out=pk1.rearrange("p (a b) -> p a b", a=4),
                                   in_=gup[e, mt].rearrange("(a p) b -> p a b", p=128)
                                   ).then_inc(p["pkl"], 16)
                    sync.dma_start(out=sce1.rearrange("p (a b) -> p a b", a=4),
                                   in_=gusce[e, mt].rearrange("(a p) b -> p a b", p=128)
                                   ).then_inc(p["pkl"], 16)
                sync.wait_ge(p["deq"], 3 * e + 2)
                sync.dma_start(out=pk2.rearrange("p (a b) -> p a b", a=3),
                               in_=dnp[e].rearrange("(a p) b -> p a b", p=128)
                               ).then_inc(p["pkl"], 16)
                sync.dma_start(out=sce2.rearrange("p (a b) -> p a b", a=3),
                               in_=dnsce[e].rearrange("(a p) b -> p a b", p=128)
                               ).then_inc(p["pkl"], 16)

        @blk.gpsimd
        def _(gp):
            for k in range(NDC):
                gp.dma_start(out=gwb[:, k * 64:(k + 1) * 64],
                             in_=gwTp[k * 128:(k + 1) * 128, :]).then_inc(p["gwl"], 16)
            for g in range(L1NTT):
                gp.wait_ge(p["rt2"], g + 1)
                gp.dma_start(out=slot_o[g * 128:(g + 1) * 128, :],
                             in_=sw8[:, 0:4]).then_inc(p["sto"], 16)
                gp.dma_start(out=wk_o[g * 128:(g + 1) * 128, :],
                             in_=sw8[:, 4:8]).then_inc(p["sto"], 16)
            for e in range(MYE):
                for bt in range(2):
                    u = e * 2 + bt
                    if u == 0:
                        gp.wait_ge(p["tmr"], 16)
                    if u >= 1:
                        gp.wait_ge(p["btr"], u)
                    gp.indirect_dma_start(
                        out=bts[:, :], out_offset=None,
                        in_=lnp, in_offset=IndirectOffsetOnAxis(tokmap[:, u:u + 1], 0),
                        bounds_check=L1TT - 1, oob_is_err=False,
                    ).then_inc(p["bld"], 16)
                    gp.wait_ge(p["eev"], u + 1)
                    gp.dma_start(out=eo_o[u * 128:(u + 1) * 128, :],
                                 in_=eos[:, :]).then_inc(p["est"], 16)
            gp.wait_ge(p["est"], 16 * NBT)
            gp.wait_ge(p["sto"], 32 * L1NTT)

        @blk.vector
        def _(v):
            v.memset(ones1[:, :], 1.0)
            v.memset(ones1f[:, :], 1.0)
            v.memset(onecb[:, :], 1.0)
            v.drain()
            v.sem_inc(p["bcst"], 1)
            v.wait_ge(p["bcst"], 2)
            v.tensor_copy(io64[:, :], pp1[:, 0:64])
            v.drain()
            v.tensor_copy(ebc[:, :], pp1[:, 64:65])
            v.memset(off[:, :], 0.0)
            v.drain()
            v.sem_inc(p["bcpy"], 1)
            for g in range(L1NTT):
                v.wait_ge(p["gmm"], g + 1)
                pg = pgA if g % 2 == 0 else pgB
                v.tensor_copy(lgt[:, :], pg[:, 0:64]).then_inc(p["gcp"], 1)
                v.drain()
                for k in range(4):
                    v.tensor_reduce(ek[:, k:k + 1], lgt[:, :], axis=AX.X, op=ALU.max)
                    v.drain()
                    e4 = eqh[:, k * 64:(k + 1) * 64]
                    v.tensor_scalar(e4, lgt[:, :], ek[:, k:k + 1], None, ALU.is_equal)
                    v.drain()
                    if k < 3:
                        v.scalar_tensor_tensor(lgt[:, :], e4, -1e30, lgt[:, :],
                                               ALU.mult, ALU.add)
                    v.tensor_mul(eqs[:, :], e4, io64[:, :])
                    v.drain()
                    v.tensor_reduce(ek[:, 4 + k:5 + k], eqs[:, :], axis=AX.X, op=ALU.max)
                    v.drain()
                v.tensor_scalar(ek[:, 15:16], ek[:, 0:1], -1.0, None,
                                ALU.mult).then_inc(p["rt1"], 1)
                v.tensor_add(eqs[:, :], eqh[:, 0:64], eqh[:, 64:128])
                v.drain()
                v.tensor_add(eqs[:, :], eqs[:, :], eqh[:, 128:192])
                v.drain()
                v.tensor_add(eqs[:, :], eqs[:, :], eqh[:, 192:256])
                v.drain()
                v.tensor_copy(selb[:, :], eqs[:, :]).then_inc(p["sel"], 1)
                v.wait_ge(p["pfx"], g + 1)
                px = pxA if g % 2 == 0 else pxB
                for k in range(4):
                    v.tensor_mul(eqh[:, k * 64:(k + 1) * 64],
                                 eqh[:, k * 64:(k + 1) * 64], px[:, 0:64])
                v.drain()
                for k in range(4):
                    v.tensor_reduce(ek[:, 16 + k:17 + k],
                                    eqh[:, k * 64:(k + 1) * 64], axis=AX.X, op=ALU.add)
                v.drain()
                v.wait_ge(p["cst"], g + 1)
                v.tensor_add(off[0:1, :], off[0:1, :], pp2[0:1, 0:64]
                             ).then_inc(p["pxc"], 1)
                v.drain()
                v.sem_inc(p["ofd"], 1)
                v.wait_ge(p["scl"], g + 1)
                v.tensor_reduce(ek[:, 12:13], ek[:, 8:12], axis=AX.X, op=ALU.add)
                v.drain()
                v.reciprocal(ek[:, 13:14], ek[:, 12:13])
                v.drain()
                v.tensor_scalar(ek[:, 8:12], ek[:, 8:12], ek[:, 13:14], None, ALU.mult)
                pos4 = ek[:, 16:20]
                v.tensor_scalar(ek[:, 20:24], ek[:, 4:8], ebc[:, 0:1], None,
                                ALU.subtract)
                v.tensor_scalar(ek[:, 24:28], pos4, 256.0, None, ALU.is_lt)
                v.drain()
                if g >= 1:
                    v.wait_ge(p["sto"], 32 * g)
                v.tensor_scalar(eqs[:, 0:4], ek[:, 4:8], 256.0, None, ALU.mult)
                v.drain()
                v.tensor_add(eqs[:, 0:4], eqs[:, 0:4], pos4)
                v.drain()
                v.tensor_mul(sw8[:, 0:4], eqs[:, 0:4], ek[:, 24:28])
                v.tensor_mul(sw8[:, 4:8], ek[:, 8:12], ek[:, 24:28])
                v.tensor_scalar(eqs[:, 4:8], ek[:, 20:24], 256.0, None, ALU.mult)
                v.drain()
                v.tensor_add(eqs[:, 4:8], eqs[:, 4:8], pos4)
                v.tensor_scalar(eqs[:, 8:12], pos4, 255.5, 1e6, ALU.is_gt, ALU.mult)
                v.tensor_scalar(eqs[:, 12:16], ek[:, 20:24], -0.5, 1e6, ALU.is_lt, ALU.mult)
                v.tensor_scalar(eqs[:, 16:20], ek[:, 20:24], 7.5, 1e6, ALU.is_gt, ALU.mult)
                v.drain()
                v.tensor_add(eqs[:, 4:8], eqs[:, 4:8], eqs[:, 8:12])
                v.drain()
                v.tensor_add(eqs[:, 4:8], eqs[:, 4:8], eqs[:, 12:16])
                v.drain()
                v.tensor_add(eqs[:, 4:8], eqs[:, 4:8], eqs[:, 16:20])
                v.drain()
                v.tensor_copy(tgt[:, :], eqs[:, 4:8]).then_inc(p["rt2"], 1)
                v.drain()
                if g == 0:
                    v.wait_ge(p["nio"], 32)
                    v.memset(ones11[0:1, :], 1.0)
                    v.memset(tokrow[0:1, :], 0.0)
                    v.drain()
                if g >= 1:
                    v.wait_ge(p["tkq"], 4 * g)
                v.tensor_scalar(ohsum[:, :], io2kb[:, :], eqs[:, 4:5], None,
                                ALU.is_equal)
                v.drain()
                for kk in range(1, 4):
                    v.tensor_scalar(ohbuf[:, :], io2kb[:, :], eqs[:, 4 + kk:5 + kk],
                                    None, ALU.is_equal)
                    v.drain()
                    v.tensor_add(ohsum[:, :], ohsum[:, :], ohbuf[:, :])
                    v.drain()
                v.tensor_scalar(tokc[:, :], iocolb[:, :], float(g * 128), None,
                                ALU.add)
                v.drain()
                v.sem_inc(p["ohr"], 1)
                for q in range(4):
                    G = 4 * g + q
                    v.wait_ge(p["tkq"], G + 1)
                    v.tensor_add(tokrow[0:1, q * 512:(q + 1) * 512],
                                 tokrow[0:1, q * 512:(q + 1) * 512], ptk[0:1, 0:512])
                    v.drain()
                    v.sem_inc(p["tka"], 1)
            v.sem_inc(p["trw"], 1)
            for cc in range(16):
                v.wait_ge(p["tcz"], cc + 1)
                v.tensor_copy(tokmap[:, cc:cc + 1], pp1[:, cc:cc + 1])
                v.drain()
                v.sem_inc(p["tmr"], 1)
            for e in range(MYE):
                if e >= 1:
                    v.wait_ge(p["dmm"], 2 * e)
                for mt in range(2):
                    v.wait_ge(p["pkl"], 16 * (e * 6 + 2 * mt + 2))
                    for nib in range(2):
                        emit_dequant(v, pk1[:, :], nib * 4, dqa[:, :], dqb[:, :],
                                     dqu[:, :], dqv[:, :], dqz[:, :], dqo[:, :])
                        dst = wgu[:, (mt * 8 + nib * 4) * L1I:(mt * 8 + nib * 4 + 4) * L1I]
                        v.tensor_mul(dst, dqo[:, :], sce1[:, :])
                        v.drain()
                    v.sem_inc(p["deq"], 1)
                v.wait_ge(p["pkl"], 16 * (e * 6 + 6))
                for nib in range(2):
                    emit_dequant(v, pk2[:, :], nib * 4, dqa[:, 0:3 * D], dqb[:, 0:3 * D],
                                 dqu[:, 0:3 * D], dqv[:, 0:3 * D], dqz[:, 0:3 * D],
                                 dqo[:, 0:3 * D])
                    dst = wdn[:, nib * 3 * D:(nib + 1) * 3 * D]
                    v.tensor_mul(dst, dqo[:, 0:3 * D], sce2[:, :])
                    v.drain()
                v.sem_inc(p["deq"], 1)
                v.sem_inc(p["wgud"], 1)
                for bt in range(2):
                    u = e * 2 + bt
                    v.wait_ge(p["btr"], u + 1)
                    v.tensor_copy(btT[:, :], ptr[:, :]).then_inc(p["btrc"], 1)
                    v.wait_ge(p["hB"], u + 1)
                    hbe = hb.rearrange("p (a two) -> p two a", two=2)
                    p1e = pp1.rearrange("p (a two) -> p two a", two=2)
                    p2e = pp2.rearrange("p (a two) -> p two a", two=2)
                    v.tensor_tensor(hp[:, 0:256], hbe[:, 0, 0:256], p1e[:, 0, :], ALU.mult)
                    v.tensor_tensor(hp[:, 256:384], hbe[:, 0, 256:384], p2e[:, 0, 0:128], ALU.mult)
                    v.tensor_tensor(hp[:, 384:640], hbe[:, 1, 0:256], p1e[:, 1, :], ALU.mult)
                    ins = v.tensor_tensor(hp[:, 640:768], hbe[:, 1, 256:384],
                                          p2e[:, 1, 0:128], ALU.mult)
                    ins.then_inc(p["hpp"], 1)
                    v.drain()
                    v.wait_ge(p["htr"], u + 1)
                    v.tensor_copy(hpT[:, 0:768], ptr[:, 0:768]).then_inc(p["htrc"], 1)
                    v.wait_ge(p["dmm"], u + 1)
                    v.tensor_copy(eos[:, 0:512], pp1[:, :])
                    v.drain()
                    v.tensor_copy(eos[:, 512:1024], pp2[:, :]).then_inc(p["eev"], 1)
                    v.drain()

        @blk.scalar
        def _(s):
            for g in range(L1NTT):
                s.wait_ge(p["rt1"], g + 1)
                s.activation(ek[:, 8:12], ek[:, 0:4], AF.Exp, bias=ek[:, 15:16],
                             scale=1.0).then_inc(p["scl"], 1)
            for e in range(MYE):
                for bt in range(2):
                    u = e * 2 + bt
                    s.wait_ge(p["hA"], u + 1)
                    s.activation(hb[:, 0:512], pp1[:, :], AF.Silu, bias=0.0, scale=1.0)
                    s.drain()
                    s.activation(hb[:, 512:768], pp2[:, 0:256], AF.Silu, bias=0.0,
                                 scale=1.0).then_inc(p["sil"], 1)

        @blk.tensor
        def _(t):
            t.wait_ge(p["ldc"], 16 * 3)
            t.wait_ge(p["gwl"], 16 * 8)
            t.wait_ge(p["bcst"], 1)
            t.matmul(pp1[:, 0:65], ones1f[0:1, :], io65r[0:1, :],
                     start=True, stop=True).then_inc(p["bcst"], 1)
            t.wait_ge(p["bcpy"], 1)
            for g in range(L1NTT):
                pg = pgA if g % 2 == 0 else pgB
                if g >= 2:
                    t.wait_ge(p["gcp"], g - 1)
                t.wait_ge(p["lnl%d" % (g % 2)], 16 * (g // 2 + 1))
                for k in range(NDC):
                    ins = t.matmul(pg[:, 0:64],
                                   lnTs[:, (g % 2) * D + k * 128:(g % 2) * D + (k + 1) * 128],
                                   gwb[:, k * 64:(k + 1) * 64],
                                   start=(k == 0), stop=(k == NDC - 1))
                ins.then_inc(p["gmm"], 1)
                t.wait_ge(p["sel"], g + 1)
                t.wait_ge(p["ofd"], g)
                if g >= 2:
                    t.wait_ge(p["pxc"], g - 1)
                px = pxA if g % 2 == 0 else pxB
                t.matmul(px[:, 0:64], tril[:, :], selb[:, :], start=True, stop=False)
                t.matmul(px[:, 0:64], ones1f[0:1, :], off[0:1, :],
                         start=False, stop=True).then_inc(p["pfx"], 1)
                t.matmul(pp2[0:1, 0:64], onecb[:, 0:1], selb[:, :],
                         start=True, stop=True).then_inc(p["cst"], 1)
                t.wait_ge(p["ohr"], g + 1)
                for q in range(4):
                    G = 4 * g + q
                    if G >= 1:
                        t.wait_ge(p["tka"], G)
                    ins = t.matmul(ptk[0:1, 0:512], tokc[:, 0:1],
                                   ohsum[:, q * 512:(q + 1) * 512],
                                   start=True, stop=True)
                    ins.then_inc(p["tkq"], 1)
            t.wait_ge(p["trw"], 1)
            for cc in range(16):
                t.matmul(pp1[:, cc:cc + 1], tokrow[0:1, cc * 128:(cc + 1) * 128],
                         ones11[0:1, :], start=True, stop=True).then_inc(p["tcz"], 1)
            t.wait_ge(p["tmr"], 16)
            for e in range(MYE):
                t.wait_ge(p["wgud"], e + 1)
                for bt in range(2):
                    u = e * 2 + bt
                    t.wait_ge(p["bld"], 16 * (u + 1))
                    if u >= 1:
                        t.wait_ge(p["btrc"], u)
                    for k in range(NDC):
                        ins = t.transpose(ptr[:, k * 128:(k + 1) * 128],
                                          bts[:, k * 128:(k + 1) * 128], ident[:])
                    ins.then_inc(p["btr"], 1)
                    t.wait_ge(p["btrc"], u + 1)
                    if u >= 1:
                        t.wait_ge(p["eev"], u)
                    for k in range(NDC):
                        t.matmul(pp1[:, :], btT[:, k * 128:(k + 1) * 128],
                                 wgu[:, k * L1I:k * L1I + 512],
                                 start=(k == 0), stop=(k == NDC - 1))
                    for k in range(NDC):
                        ins = t.matmul(pp2[:, 0:256], btT[:, k * 128:(k + 1) * 128],
                                       wgu[:, k * L1I + 512:(k + 1) * L1I],
                                       start=(k == 0), stop=(k == NDC - 1))
                    ins.then_inc(p["hA"], 1)
                    t.wait_ge(p["sil"], u + 1)
                    for k in range(NDC):
                        t.matmul(pp1[:, :], btT[:, k * 128:(k + 1) * 128],
                                 wgu[:, (8 + k) * L1I:(8 + k) * L1I + 512],
                                 start=(k == 0), stop=(k == NDC - 1))
                    for k in range(NDC):
                        ins = t.matmul(pp2[:, 0:256], btT[:, k * 128:(k + 1) * 128],
                                       wgu[:, (8 + k) * L1I + 512:(8 + k + 1) * L1I],
                                       start=(k == 0), stop=(k == NDC - 1))
                    ins.then_inc(p["hB"], 1)
                    t.wait_ge(p["hpp"], u + 1)
                    if u >= 1:
                        t.wait_ge(p["htrc"], u)
                    for k in range(6):
                        ins = t.transpose(ptr[:, k * 128:(k + 1) * 128],
                                          hp[:, k * 128:(k + 1) * 128], ident[:])
                    ins.then_inc(p["htr"], 1)
                    t.wait_ge(p["htrc"], u + 1)
                    for k in range(6):
                        t.matmul(pp1[:, :], hpT[:, k * 128:(k + 1) * 128],
                                 wdn[:, k * D:k * D + 512],
                                 start=(k == 0), stop=(k == 5))
                    for k in range(6):
                        ins = t.matmul(pp2[:, :], hpT[:, k * 128:(k + 1) * 128],
                                       wdn[:, k * D + 512:(k + 1) * D],
                                       start=(k == 0), stop=(k == 5))
                    ins.then_inc(p["dmm"], 1)

    p.close()
    return nc


def build_ph_l1f_post():
    nc = bass.Bass("TRN2", target_bir_lowering=False)
    x_in = nc.dram_tensor("x", [TPC, D], F32, kind="ExternalInput").ap()
    lnp = nc.dram_tensor("lnp", [TPC, D], BF16, kind="ExternalInput").ap()
    lnpT = nc.dram_tensor("lnpT", [TPC, D], BF16, kind="ExternalInput").ap()
    eo_all = nc.dram_tensor("eoall", [L1E * L1CAP, D], BF16, kind="ExternalInput").ap()
    slot = nc.dram_tensor("slot", [TPC, 4], F32, kind="ExternalInput").ap()
    wk4 = nc.dram_tensor("wk4", [TPC, 4], F32, kind="ExternalInput").ap()
    shp = nc.dram_tensor("shp", [2, 512, L1I], I32, kind="ExternalInput").ap()
    shsc = nc.dram_tensor("shsc", [2, 512, L1I], BF16, kind="ExternalInput").ap()
    sdp = nc.dram_tensor("sdp", [384, D], I32, kind="ExternalInput").ap()
    sdsc = nc.dram_tensor("sdsc", [384, D], BF16, kind="ExternalInput").ap()
    idm = nc.dram_tensor("idm", [128, 128], BF16, kind="ExternalInput").ap()
    x_out = nc.dram_tensor("xo", [TPC, D], F32, kind="ExternalOutput").ap()

    p = P(nc)
    p.sem("ldc", "pkl", "deq", "ldx0", "ldx1", "lnl0", "lnl1", "swl",
          "tgc", "gth", "acc", "hA", "sil", "hB", "hpp", "htr", "htrc",
          "dmm", "fin", "st0", "st1")

    xs = nc.alloc_sbuf_tensor("xs", [128, 2 * D], F32)
    lnps = nc.alloc_sbuf_tensor("lnps", [128, 2 * D], BF16)
    lnTs = nc.alloc_sbuf_tensor("lnTs", [128, 2 * D], BF16)
    sw = nc.alloc_sbuf_tensor("sw", [128, 16], F32)
    tgt = nc.alloc_sbuf_tensor("tgti", [128, 4], I32)
    gth = nc.alloc_sbuf_tensor("gth", [128, D], BF16)
    acc = nc.alloc_sbuf_tensor("acc", [128, D], F32)
    wsg = nc.alloc_sbuf_tensor("wsg", [128, 16 * L1I], BF16)
    wsd = nc.alloc_sbuf_tensor("wsd", [128, 6 * D], BF16)
    pk1 = nc.alloc_sbuf_tensor("pk1", [128, 4 * L1I], I32)
    pk2 = nc.alloc_sbuf_tensor("pk2", [128, 3 * D], I32)
    sce1 = nc.alloc_sbuf_tensor("sce1", [128, 4 * L1I], BF16)
    sce2 = nc.alloc_sbuf_tensor("sce2", [128, 3 * D], BF16)
    dqa = nc.alloc_sbuf_tensor("dqa", [128, 4 * L1I], I32)
    dqb = nc.alloc_sbuf_tensor("dqb", [128, 4 * L1I], I32)
    dqu = nc.alloc_sbuf_tensor("dqu", [128, 4 * L1I], U16)
    dqv = nc.alloc_sbuf_tensor("dqv", [128, 4 * L1I], U16)
    dqz = nc.alloc_sbuf_tensor("dqz", [128, 4 * L1I], BF16)
    dqo = nc.alloc_sbuf_tensor("dqo", [128, 4 * L1I], BF16)
    hb = nc.alloc_sbuf_tensor("hb", [128, L1I], F32)
    hp = nc.alloc_sbuf_tensor("hp", [128, L1I], BF16)
    hpT = nc.alloc_sbuf_tensor("hpT", [128, L1I], BF16)
    ot = nc.alloc_sbuf_tensor("ot", [128, 2 * D], F32)
    ident = nc.alloc_sbuf_tensor("ident", [128, 128], BF16)
    ptr = nc.alloc_psum_tensor("ptr", [128, 1024], BF16)
    pp1 = nc.alloc_psum_tensor("pp1", [128, 512], F32)
    pp2 = nc.alloc_psum_tensor("pp2", [128, 512], F32)

    with nc.Block() as blk:
        @blk.sync
        def _(sync):
            sync.dma_start(out=ident[:], in_=idm[:]).then_inc(p["ldc"], 16)
            for mt in range(2):
                sync.dma_start(out=pk1.rearrange("p (a b) -> p a b", a=4),
                               in_=shp[mt].rearrange("(a p) b -> p a b", p=128)
                               ).then_inc(p["pkl"], 16)
                sync.dma_start(out=sce1.rearrange("p (a b) -> p a b", a=4),
                               in_=shsc[mt].rearrange("(a p) b -> p a b", p=128)
                               ).then_inc(p["pkl"], 16)
                sync.wait_ge(p["deq"], mt + 1)
            sync.dma_start(out=pk2.rearrange("p (a b) -> p a b", a=3),
                           in_=sdp.rearrange("(a p) b -> p a b", p=128)
                           ).then_inc(p["pkl"], 16)
            sync.dma_start(out=sce2.rearrange("p (a b) -> p a b", a=3),
                           in_=sdsc.rearrange("(a p) b -> p a b", p=128)
                           ).then_inc(p["pkl"], 16)
            for g in range(NT):
                if g >= 2:
                    sync.wait_ge(p["st%d" % (g % 2)], 16 * (g // 2))
                sync.dma_start(out=xs[:, (g % 2) * D:(g % 2 + 1) * D],
                               in_=x_in[g * 128:(g + 1) * 128, :]).then_inc(
                    p["ldx%d" % (g % 2)], 16)
                if g >= 2:
                    sync.wait_ge(p["hB"], g - 1)
                sync.dma_start(out=lnTs[:, (g % 2) * D:(g % 2 + 1) * D],
                               in_=lnpT[g * 128:(g + 1) * 128, :]).then_inc(
                    p["lnl%d" % (g % 2)], 16)

        @blk.gpsimd
        def _(gp):
            for g in range(NT):
                gp.wait_ge(p["acc"], 4 * g)
                gp.dma_start(out=sw[:, 0:4], in_=slot[g * 128:(g + 1) * 128, :]
                             ).then_inc(p["swl"], 16)
                gp.dma_start(out=sw[:, 4:8], in_=wk4[g * 128:(g + 1) * 128, :]
                             ).then_inc(p["swl"], 16)
                gp.wait_ge(p["tgc"], g + 1)
                for k in range(4):
                    gp.indirect_dma_start(
                        out=gth[:, :], out_offset=None,
                        in_=eo_all, in_offset=IndirectOffsetOnAxis(tgt[:, k:k + 1], 0),
                    ).then_inc(p["gth"], 16)
                    gp.wait_ge(p["acc"], g * 4 + k + 1)
                gp.wait_ge(p["fin"], g + 1)
                gp.dma_start(out=x_out[g * 128:(g + 1) * 128, :],
                             in_=ot[:, (g % 2) * D:(g % 2 + 1) * D]).then_inc(
                    p["st%d" % (g % 2)], 16)
            gp.wait_ge(p["st0"], 16 * NT // 2)
            gp.wait_ge(p["st1"], 16 * NT // 2)

        @blk.vector
        def _(v):
            for mt in range(2):
                v.wait_ge(p["pkl"], 16 * (2 * mt + 2))
                for nib in range(2):
                    emit_dequant(v, pk1[:, :], nib * 4, dqa[:, :], dqb[:, :],
                                 dqu[:, :], dqv[:, :], dqz[:, :], dqo[:, :])
                    dst = wsg[:, (mt * 8 + nib * 4) * L1I:(mt * 8 + nib * 4 + 4) * L1I]
                    v.tensor_mul(dst, dqo[:, :], sce1[:, :])
                    v.drain()
                v.sem_inc(p["deq"], 1)
            v.wait_ge(p["pkl"], 16 * 6)
            for nib in range(2):
                emit_dequant(v, pk2[:, :], nib * 4, dqa[:, 0:3 * D], dqb[:, 0:3 * D],
                             dqu[:, 0:3 * D], dqv[:, 0:3 * D], dqz[:, 0:3 * D],
                             dqo[:, 0:3 * D])
                dst = wsd[:, nib * 3 * D:(nib + 1) * 3 * D]
                v.tensor_mul(dst, dqo[:, 0:3 * D], sce2[:, :])
                v.drain()
            v.sem_inc(p["deq"], 1)
            for g in range(NT):
                o = (g % 2) * D
                v.wait_ge(p["swl"], 32 * (g + 1))
                v.tensor_copy(tgt[:, :], sw[:, 0:4]).then_inc(p["tgc"], 1)
                v.drain()
                v.memset(acc[:, :], 0.0)
                v.drain()
                for k in range(4):
                    v.wait_ge(p["gth"], 16 * (g * 4 + k + 1))
                    v.scalar_tensor_tensor(acc[:, :], gth[:, :], sw[:, 4 + k:5 + k],
                                           acc[:, :], ALU.mult, ALU.add).then_inc(p["acc"], 1)
                    v.drain()
                v.wait_ge(p["hB"], g + 1)
                hbe = hb.rearrange("p (a two) -> p two a", two=2)
                p1e = pp1.rearrange("p (a two) -> p two a", two=2)
                p2e = pp2.rearrange("p (a two) -> p two a", two=2)
                v.tensor_tensor(hp[:, 0:256], hbe[:, 0, 0:256], p1e[:, 0, :], ALU.mult)
                v.tensor_tensor(hp[:, 256:384], hbe[:, 0, 256:384], p2e[:, 0, 0:128], ALU.mult)
                v.tensor_tensor(hp[:, 384:640], hbe[:, 1, 0:256], p1e[:, 1, :], ALU.mult)
                ins = v.tensor_tensor(hp[:, 640:768], hbe[:, 1, 256:384],
                                      p2e[:, 1, 0:128], ALU.mult)
                ins.then_inc(p["hpp"], 1)
                v.drain()
                v.wait_ge(p["htr"], g + 1)
                v.tensor_copy(hpT[:, 0:768], ptr[:, 0:768]).then_inc(p["htrc"], 1)
                v.wait_ge(p["dmm"], g + 1)
                v.wait_ge(p["ldx%d" % (g % 2)], 16 * (g // 2 + 1))
                if g >= 2:
                    v.wait_ge(p["st%d" % (g % 2)], 16 * (g // 2))
                v.tensor_add(acc[:, 0:512], acc[:, 0:512], pp1[:, :])
                v.drain()
                v.tensor_add(acc[:, 512:1024], acc[:, 512:1024], pp2[:, :])
                v.drain()
                otv = ot[:, o:o + D].rearrange("p (a two) -> p two a", two=2)
                xv = xs[:, o:o + D].rearrange("p (a two) -> p two a", two=2)
                v.scalar_tensor_tensor(otv[:, 0, :], acc[:, 0:512], 0.5,
                                       xv[:, 0, :], ALU.mult, ALU.add)
                v.drain()
                ins = v.scalar_tensor_tensor(otv[:, 1, :], acc[:, 512:1024], 0.5,
                                             xv[:, 1, :], ALU.mult, ALU.add)
                ins.then_inc(p["fin"], 1)
                v.drain()

        @blk.scalar
        def _(s):
            for g in range(NT):
                s.wait_ge(p["hA"], g + 1)
                s.activation(hb[:, 0:512], pp1[:, :], AF.Silu, bias=0.0, scale=1.0)
                s.drain()
                s.activation(hb[:, 512:768], pp2[:, 0:256], AF.Silu, bias=0.0,
                             scale=1.0).then_inc(p["sil"], 1)

        @blk.tensor
        def _(t):
            t.wait_ge(p["ldc"], 16)
            t.wait_ge(p["deq"], 3)
            for g in range(NT):
                od = (g % 2) * D
                t.wait_ge(p["lnl%d" % (g % 2)], 16 * (g // 2 + 1))
                if g >= 1:
                    t.wait_ge(p["fin"], g)
                for k in range(NDC):
                    t.matmul(pp1[:, :], lnTs[:, od + k * 128:od + (k + 1) * 128],
                             wsg[:, k * L1I:k * L1I + 512],
                             start=(k == 0), stop=(k == NDC - 1))
                for k in range(NDC):
                    ins = t.matmul(pp2[:, 0:256], lnTs[:, od + k * 128:od + (k + 1) * 128],
                                   wsg[:, k * L1I + 512:(k + 1) * L1I],
                                   start=(k == 0), stop=(k == NDC - 1))
                ins.then_inc(p["hA"], 1)
                t.wait_ge(p["sil"], g + 1)
                for k in range(NDC):
                    t.matmul(pp1[:, :], lnTs[:, od + k * 128:od + (k + 1) * 128],
                             wsg[:, (8 + k) * L1I:(8 + k) * L1I + 512],
                             start=(k == 0), stop=(k == NDC - 1))
                for k in range(NDC):
                    ins = t.matmul(pp2[:, 0:256], lnTs[:, od + k * 128:od + (k + 1) * 128],
                                   wsg[:, (8 + k) * L1I + 512:(8 + k + 1) * L1I],
                                   start=(k == 0), stop=(k == NDC - 1))
                ins.then_inc(p["hB"], 1)
                t.wait_ge(p["hpp"], g + 1)
                if g >= 1:
                    t.wait_ge(p["htrc"], g)
                for k in range(6):
                    ins = t.transpose(ptr[:, k * 128:(k + 1) * 128],
                                      hp[:, k * 128:(k + 1) * 128], ident[:])
                ins.then_inc(p["htr"], 1)
                t.wait_ge(p["htrc"], g + 1)
                for k in range(6):
                    t.matmul(pp1[:, :], hpT[:, k * 128:(k + 1) * 128],
                             wsd[:, k * D:k * D + 512],
                             start=(k == 0), stop=(k == 5))
                for k in range(6):
                    ins = t.matmul(pp2[:, :], hpT[:, k * 128:(k + 1) * 128],
                                   wsd[:, k * D + 512:(k + 1) * D],
                                   start=(k == 0), stop=(k == 5))
                ins.then_inc(p["dmm"], 1)

    p.close()
    return nc


def _expand_scales(scales, G=128):
    return np.repeat(np.asarray(scales, np.float32), G, axis=-1)


def _perm_d(n):
    return np.concatenate([np.arange(0, n, 2), np.arange(1, n, 2)])


def maps_l1f_pre(inputs, xsh):
    idf = np.eye(128, dtype=np.float32)
    return [{"x": xsh[c], "idm": _IDM, "idmF": idf} for c in range(NCORE)]


def maps_l1f_mid(inputs, aux):
    lnp_full, lnpT_full = aux["lnp"], aux["lnpT"]
    gwTp = hT(f32(inputs["L1_ffn_gate_w"]))[_perm_d(D), :]
    gwTp = np.ascontiguousarray(gwTp)
    trilm = np.tril(np.ones((128, 128), np.float32), -1).T.astype(BF)
    maps = []
    gp_ = np.asarray(inputs["L1_ffn_gate_packed"])
    up_ = np.asarray(inputs["L1_ffn_up_packed"])
    dp_ = np.asarray(inputs["L1_ffn_down_packed"])
    gsc = _expand_scales(inputs["L1_ffn_gate_scales"])
    usc = _expand_scales(inputs["L1_ffn_up_scales"])
    dsc = _expand_scales(inputs["L1_ffn_down_scales"])
    for c in range(NCORE):
        sl = slice(c * MYE, (c + 1) * MYE)
        gupc = np.stack([np.transpose(gp_[sl], (0, 2, 1)),
                         np.transpose(up_[sl], (0, 2, 1))], axis=1)
        gusce = np.stack([np.transpose(gsc[sl], (0, 2, 1))[:, ::2, :],
                          np.transpose(usc[sl], (0, 2, 1))[:, ::2, :]], axis=1)
        dnpc = np.transpose(dp_[sl], (0, 2, 1))
        dnpc = dnpc[:, :, _perm_d(D)]
        dnsce = np.transpose(dsc[sl], (0, 2, 1))[:, ::2, :]
        dnsce = dnsce[:, :, _perm_d(D)]
        io65 = np.zeros((1, 65), np.float32)
        io65[0, 0:64] = np.arange(64)
        io65[0, 64] = c * MYE
        maps.append({
            "io2k": np.ascontiguousarray(
                np.broadcast_to(np.arange(2048, dtype=np.float32), (128, 2048))),
            "iocol": np.arange(128, dtype=np.float32).reshape(128, 1),
            "lnp": lnp_full, "lnpT": lnpT_full, "gwTp": gwTp,
            "gup": np.ascontiguousarray(gupc.astype(np.int32)),
            "gusce": np.ascontiguousarray(gusce.astype(BF)),
            "dnp": np.ascontiguousarray(dnpc.astype(np.int32)),
            "dnsce": np.ascontiguousarray(dnsce.astype(BF)),
            "io65": io65, "trilm": trilm, "idm": _IDM,
        })
    return maps


def maps_l1f_post(inputs, xsh, aux):
    eo_all = aux["eo_all"]
    shp = np.stack([np.asarray(inputs["L1_ffn_sh_gate_packed"]).T,
                    np.asarray(inputs["L1_ffn_sh_up_packed"]).T])
    shsc = np.stack([_expand_scales(inputs["L1_ffn_sh_gate_scales"]).T[::2, :],
                     _expand_scales(inputs["L1_ffn_sh_up_scales"]).T[::2, :]])
    sdp = np.asarray(inputs["L1_ffn_sh_down_packed"]).T[:, _perm_d(D)]
    sdsc = _expand_scales(inputs["L1_ffn_sh_down_scales"]).T[::2, :][:, _perm_d(D)]
    maps = []
    for c in range(NCORE):
        maps.append({
            "x": xsh[c], "lnp": aux["lnp_sh"][c], "lnpT": aux["lnpT_sh"][c],
            "eoall": eo_all, "slot": aux["slot"][c], "wk4": aux["wk4"][c],
            "shp": np.ascontiguousarray(shp.astype(np.int32)),
            "shsc": np.ascontiguousarray(shsc.astype(BF)),
            "sdp": np.ascontiguousarray(sdp.astype(np.int32)),
            "sdsc": np.ascontiguousarray(sdsc.astype(BF)),
            "idm": _IDM,
        })
    return maps



L3E = 64


def build_ph_l3f():
    nc = bass.Bass("TRN2", target_bir_lowering=False)
    x_in = nc.dram_tensor("x", [TPC, D], F32, kind="ExternalInput").ap()
    egT = nc.dram_tensor("egT", [L3E, D, 512], BF16, kind="ExternalInput").ap()
    euT = nc.dram_tensor("euT", [L3E, D, 512], BF16, kind="ExternalInput").ap()
    edT = nc.dram_tensor("edT", [L3E, 512, D], BF16, kind="ExternalInput").ap()
    sgT = nc.dram_tensor("sgT", [D, D], BF16, kind="ExternalInput").ap()
    suT = nc.dram_tensor("suT", [D, D], BF16, kind="ExternalInput").ap()
    sdT = nc.dram_tensor("sdT", [D, D], BF16, kind="ExternalInput").ap()
    gwT = nc.dram_tensor("gwT", [D, L3E], BF16, kind="ExternalInput").ap()
    gbias = nc.dram_tensor("gbias", [128, L3E], F32, kind="ExternalInput").ap()
    idm = nc.dram_tensor("idm", [128, 128], BF16, kind="ExternalInput").ap()
    idmF = nc.dram_tensor("idmF", [128, 128], F32, kind="ExternalInput").ap()
    cdram = nc.dram_tensor("cdram", [L3E, TPC], BF16, kind="Internal").ap()
    x_out = nc.dram_tensor("xo", [TPC, D], F32, kind="ExternalOutput").ap()

    H2 = 2048
    NTH = H2 // 128
    NGH = H2 // 512

    p = P(nc)
    p.sem("xld", "stat", "lnr", "lnf", "tr", "trc", "gsc", "rsg", "rcons",
          "ctr", "ctc", "cdw", "swl", "stge", "ssl", "shst", "sdw", "sacc",
          "wld", "bcm", "bcc", "tge", "ssl2", "hst", "dwn", "acv",
          "xld2", "wtr", "wev", "sto")

    xs = nc.alloc_sbuf_tensor("xs", [128, D], F32)
    scr = nc.alloc_sbuf_tensor("scr", [128, D], F32)
    lnb = nc.alloc_sbuf_tensor("lnb", [128, D], BF16)
    stt = nc.alloc_sbuf_tensor("stt", [128, 8], F32)
    lnT = nc.alloc_sbuf_tensor("lnT", [128, NDC * H2], BF16)
    acc = nc.alloc_sbuf_tensor("acc", [128, NDC * H2], F32)
    web = nc.alloc_sbuf_tensor("web", [128, 2 * 8 * 512], BF16)
    weu = nc.alloc_sbuf_tensor("weu", [128, 2 * 8 * 512], BF16)
    wed = nc.alloc_sbuf_tensor("wed", [128, 2 * 4 * D], BF16)
    hsT = nc.alloc_sbuf_tensor("hsT", [128, 8 * 512], BF16)
    sil = nc.alloc_sbuf_tensor("sil", [128, 2 * 512], F32)
    cbc = nc.alloc_sbuf_tensor("cbc", [128, 512], F32)
    csb = nc.alloc_sbuf_tensor("csb", [1, 2 * NGH * 512], BF16)
    ctk = nc.alloc_sbuf_tensor("ctk", [128, 2 * L3E], F32)
    ctkT = nc.alloc_sbuf_tensor("ctkT", [L3E, H2], BF16)
    gwb = nc.alloc_sbuf_tensor("gwb", [128, 8 * L3E], BF16)
    gbb = nc.alloc_sbuf_tensor("gbb", [128, L3E], F32)
    sc0 = nc.alloc_sbuf_tensor("sc0", [128, L3E], F32)
    mw = nc.alloc_sbuf_tensor("mw", [128, L3E], F32)
    mk = nc.alloc_sbuf_tensor("mk", [128, L3E], F32)
    sel = nc.alloc_sbuf_tensor("sel", [128, L3E], F32)
    ge64 = nc.alloc_sbuf_tensor("ge64", [128, L3E], F32)
    cs64 = nc.alloc_sbuf_tensor("cs64", [128, L3E], F32)
    gs8 = nc.alloc_sbuf_tensor("gs8", [128, 8], F32)
    gm8 = nc.alloc_sbuf_tensor("gm8", [128, 8], F32)
    ge8 = nc.alloc_sbuf_tensor("ge8", [128, 8], F32)
    cs8 = nc.alloc_sbuf_tensor("cs8", [128, 8], F32)
    on8 = nc.alloc_sbuf_tensor("on8", [128, 8], F32)
    mxc = nc.alloc_sbuf_tensor("mxc", [128, 4], F32)
    ones1 = nc.alloc_sbuf_tensor("ones1", [1, 128], BF16)
    ident = nc.alloc_sbuf_tensor("ident", [128, 128], BF16)
    identF = nc.alloc_sbuf_tensor("identF", [128, 128], F32)
    pgu = nc.alloc_psum_tensor("pgu", [128, 2048], F32)
    pdb = nc.alloc_psum_tensor("pdb", [128, 2048], F32)

    ptr = pdb[:, 0:1024].bitcast(BF16)

    def lntile(hp, k, t0, w):
        return lnT[:, k * H2 + t0:k * H2 + t0 + w]

    with nc.Block() as blk:
        @blk.sync
        def _(sync):
            sync.dma_start(out=ident[:], in_=idm[:]).then_inc(p["xld"], 16)
            sync.dma_start(out=identF[:], in_=idmF[:]).then_inc(p["xld"], 16)
            sync.dma_start(out=gbb[:], in_=gbias[:]).then_inc(p["xld"], 16)
            for k in range(NDC):
                sync.dma_start(out=gwb[:, k * L3E:(k + 1) * L3E],
                               in_=gwT[k * 128:(k + 1) * 128, :]).then_inc(p["xld"], 16)
            for hp in range(2):
                if hp == 1:
                    sync.wait_ge(p["wev"], 32)
                for i in range(NTH):
                    gi = hp * NTH + i
                    if gi >= 1:
                        sync.wait_ge(p["lnf"], gi)
                    sync.dma_start(out=xs[:], in_=x_in[hp * H2 + i * 128:
                                                       hp * H2 + (i + 1) * 128, :]
                                   ).then_inc(p["xld"], 16)
                sync.wait_ge(p["ctc"], NTH * (hp + 1))
                sync.dma_start(out=cdram[0:L3E, hp * H2:(hp + 1) * H2],
                               in_=ctkT[0:L3E, :]).then_inc(p["cdw"], 16)
                sync.wait_ge(p["acv"], 2 + 32 * L3E * (hp + 1))
                for j in range(NTH):
                    gj = hp * NTH + j
                    if gj >= 1:
                        sync.wait_ge(p["wev"], 2 * gj)
                    sync.dma_start(out=xs[:], in_=x_in[hp * H2 + j * 128:
                                                       hp * H2 + (j + 1) * 128, :]
                                   ).then_inc(p["xld2"], 16)

        @blk.gpsimd
        def _(gp):
            for hp in range(2):
                if hp == 1:
                    gp.wait_ge(p["acv"], 2 + 32 * L3E)
                for k in range(NDC):
                    gp.dma_start(out=web[:, k * 1024:(k + 1) * 1024],
                                 in_=sgT[k * 128:(k + 1) * 128, :]).then_inc(p["swl"], 16)
                    gp.dma_start(out=weu[:, k * 1024:(k + 1) * 1024],
                                 in_=suT[k * 128:(k + 1) * 128, :]).then_inc(p["swl"], 16)
                    gp.dma_start(out=wed[:, k * 1024:(k + 1) * 1024],
                                 in_=sdT[k * 128:(k + 1) * 128, :]).then_inc(p["swl"], 16)
                gp.wait_ge(p["cdw"], 16 * (hp + 1))
                gp.wait_ge(p["sdw"], 32 * (hp + 1))
                gp.wait_ge(p["stge"], 32 * (hp + 1))
                for e in range(L3E):
                    eg = hp * L3E + e
                    sl = e % 2
                    if e >= 2:
                        gp.wait_ge(p["acv"], 2 + 32 * (eg - 1))
                    for k in range(NDC):
                        gp.dma_start(out=web[:, (sl * 8 + k) * 512:(sl * 8 + k + 1) * 512],
                                     in_=egT[e, k * 128:(k + 1) * 128, :]
                                     ).then_inc(p["wld"], 16)
                        gp.dma_start(out=weu[:, (sl * 8 + k) * 512:(sl * 8 + k + 1) * 512],
                                     in_=euT[e, k * 128:(k + 1) * 128, :]
                                     ).then_inc(p["wld"], 16)
                    for ic in range(4):
                        gp.dma_start(out=wed[:, (sl * 4 + ic) * D:(sl * 4 + ic + 1) * D],
                                     in_=edT[e, ic * 128:(ic + 1) * 128, :]
                                     ).then_inc(p["wld"], 16)
                    gp.dma_start(
                        out=csb[0:1, sl * NGH * 512:(sl + 1) * NGH * 512],
                        in_=cdram[e:e + 1, hp * H2:(hp + 1) * H2]
                        ).then_inc(p["wld"], 16)
                for j in range(NTH):
                    gj = hp * NTH + j
                    gp.wait_ge(p["wev"], 2 * gj + 2)
                    gp.dma_start(out=x_out[hp * H2 + j * 128:hp * H2 + (j + 1) * 128, :],
                                 in_=scr[:, :]).then_inc(p["sto"], 16)
            gp.wait_ge(p["sto"], 16 * 2 * NTH)

        @blk.scalar
        def _(s):
            for hp in range(2):
                for i in range(NTH):
                    gi = hp * NTH + i
                    s.wait_ge(p["stat"], gi + 1)
                    emit_rstd_s(s, stt).then_inc(p["lnr"], 1)
                    s.wait_ge(p["gsc"], gi + 1)
                    if gi >= 1:
                        s.wait_ge(p["rcons"], gi)
                    s.activation(sc0[:, :], pgu[:, 0:L3E], AF.Sigmoid
                                 ).then_inc(p["rsg"], 1)
                for g in range(NGH):
                    for ic in range(8):
                        u = (hp * NGH + g) * 8 + ic
                        s.wait_ge(p["stge"], u + 1)
                        if u >= 2:
                            s.wait_ge(p["shst"], u - 1)
                        s.activation(sil[:, (ic % 2) * 512:(ic % 2 + 1) * 512],
                                     pgu[:, (ic % 2) * 512:(ic % 2 + 1) * 512],
                                     AF.Silu).then_inc(p["ssl"], 1)
                for e in range(L3E):
                    eg = hp * L3E + e
                    for g in range(NGH):
                        for w in range(2):
                            W = eg * 8 + g * 2 + w
                            s.wait_ge(p["tge"], W + 1)
                            for ic2 in range(2):
                                ic = w * 2 + ic2
                                if W >= 1:
                                    s.wait_ge(p["hst"], W)
                                s.activation(sil[:, ic2 * 512:(ic2 + 1) * 512],
                                             pgu[:, ic2 * 512:(ic2 + 1) * 512],
                                             AF.Silu).then_inc(p["ssl2"], 1)

        @blk.tensor
        def _(t):
            t.wait_ge(p["xld"], 16 * (3 + NDC))
            for hp in range(2):
                if hp == 1:
                    t.wait_ge(p["wev"], 32)
                for i in range(NTH):
                    gi = hp * NTH + i
                    t.wait_ge(p["lnf"], gi + 1)
                    if gi >= 1:
                        t.wait_ge(p["trc"], gi)
                    for k in range(NDC):
                        ins = t.transpose(ptr[:, k * 128:(k + 1) * 128],
                                          lnb[:, k * 128:(k + 1) * 128], ident[:])
                    ins.then_inc(p["tr"], 1)
                    t.wait_ge(p["trc"], gi + 1)
                    if gi >= 1:
                        t.wait_ge(p["rsg"], gi)
                    for k in range(NDC):
                        ins = t.matmul(pgu[:, 0:L3E], lntile(hp, k, i * 128, 128),
                                       gwb[:, k * L3E:(k + 1) * L3E],
                                       start=(k == 0), stop=(k == NDC - 1))
                    ins.then_inc(p["gsc"], 1)
                    t.wait_ge(p["rcons"], gi + 1)
                    if gi >= 1:
                        t.wait_ge(p["ctc"], gi)
                    t.transpose(pdb[0:L3E, 1024:1152],
                                ctk[:, (gi % 2) * L3E:(gi % 2 + 1) * L3E],
                                identF[:]).then_inc(p["ctr"], 1)
                t.wait_ge(p["swl"], 16 * 24 * (hp + 1))
                t.wait_ge(p["trc"], NTH * (hp + 1))
                t.wait_ge(p["rsg"], NTH * (hp + 1))
                for g in range(NGH):
                    tg = g * 512
                    for ic in range(8):
                        u = (hp * NGH + g) * 8 + ic
                        if u >= 2:
                            t.wait_ge(p["shst"], u - 1)
                        for k in range(NDC):
                            ins = t.matmul(pgu[:, (ic % 2) * 512:(ic % 2 + 1) * 512],
                                           web[:, k * 1024 + ic * 128:k * 1024 + ic * 128 + 128],
                                           lntile(hp, k, tg, 512),
                                           start=(k == 0), stop=(k == NDC - 1))
                        for k in range(NDC):
                            ins = t.matmul(pgu[:, 1024 + (ic % 2) * 512:1024 + (ic % 2 + 1) * 512],
                                           weu[:, k * 1024 + ic * 128:k * 1024 + ic * 128 + 128],
                                           lntile(hp, k, tg, 512),
                                           start=(k == 0), stop=(k == NDC - 1))
                        ins.then_inc(p["stge"], 1)
                    t.wait_ge(p["shst"], (hp * NGH + g) * 8 + 8)
                    for hc in range(8):
                        dg = (hp * NGH + g) * 8 + hc
                        if dg >= 4:
                            t.wait_ge(p["sacc"], dg - 3)
                        for ic in range(8):
                            ins = t.matmul(pdb[:, (hc % 4) * 512:(hc % 4 + 1) * 512],
                                           wed[:, ic * 1024 + hc * 128:ic * 1024 + hc * 128 + 128],
                                           hsT[:, ic * 512:(ic + 1) * 512],
                                           start=(ic == 0), stop=(ic == 7))
                        ins.then_inc(p["sdw"], 1)
                for e in range(L3E):
                    eg = hp * L3E + e
                    sl = e % 2
                    t.wait_ge(p["wld"], 16 * 21 * (eg + 1))
                    if e == 0:
                        t.wait_ge(p["sacc"], 32 * (hp + 1))
                        t.wait_ge(p["shst"], 32 * (hp + 1))
                    for g in range(NGH):
                        tg = g * 512
                        if eg * NGH + g >= 1:
                            t.wait_ge(p["bcc"], eg * NGH + g)
                        t.matmul(pdb[:, 1536:2048], ones1[0:1, :],
                                 csb[0:1, (sl * NGH + g) * 512:(sl * NGH + g + 1) * 512],
                                 start=True, stop=True).then_inc(p["bcm"], 1)
                        for w in range(2):
                            W = eg * 8 + g * 2 + w
                            if W >= 1:
                                t.wait_ge(p["hst"], W)
                            for ic2 in range(2):
                                ic = w * 2 + ic2
                                for k in range(NDC):
                                    ins = t.matmul(
                                        pgu[:, ic2 * 512:(ic2 + 1) * 512],
                                        web[:, (sl * 8 + k) * 512 + ic * 128:
                                            (sl * 8 + k) * 512 + ic * 128 + 128],
                                        lntile(hp, k, tg, 512),
                                        start=(k == 0), stop=(k == NDC - 1))
                                for k in range(NDC):
                                    ins = t.matmul(
                                        pgu[:, 1024 + ic2 * 512:1024 + (ic2 + 1) * 512],
                                        weu[:, (sl * 8 + k) * 512 + ic * 128:
                                            (sl * 8 + k) * 512 + ic * 128 + 128],
                                        lntile(hp, k, tg, 512),
                                        start=(k == 0), stop=(k == NDC - 1))
                            ins.then_inc(p["tge"], 1)
                        t.wait_ge(p["hst"], eg * 8 + g * 2 + 2)
                        for hc in range(8):
                            dg = eg * 32 + g * 8 + hc
                            t.wait_ge(p["acv"], dg + 1)
                            for ic in range(4):
                                ins = t.matmul(
                                    pdb[:, (hc % 2) * 512:(hc % 2 + 1) * 512],
                                    wed[:, (sl * 4 + ic) * D + hc * 128:
                                        (sl * 4 + ic) * D + hc * 128 + 128],
                                    hsT[:, ic * 512:(ic + 1) * 512],
                                    start=(ic == 0), stop=(ic == 3))
                            ins.then_inc(p["dwn"], 1)
                t.wait_ge(p["acv"], 2 + 32 * L3E * (hp + 1))
                for j in range(NTH):
                    gj = hp * NTH + j
                    for dgrp in range(2):
                        wq = 2 * gj + dgrp
                        if wq >= 2:
                            t.wait_ge(p["wev"], wq - 1)
                        for hh in range(4):
                            hc = dgrp * 4 + hh
                            ins = t.transpose(
                                pdb[:, dgrp * 512 + hh * 128:dgrp * 512 + hh * 128 + 128],
                                acc[:, hc * H2 + j * 128:hc * H2 + j * 128 + 128],
                                identF[:])
                        ins.then_inc(p["wtr"], 1)

        @blk.vector
        def _(v):
            v.memset(ones1[:, :], 1.0)
            v.memset(on8[:, :], 1.0)
            v.drain()
            v.sem_inc(p["acv"], 2)
            for hp in range(2):
                if hp == 1:
                    v.wait_ge(p["sto"], 256)
                    v.wait_ge(p["cdw"], 16)
                for i in range(NTH):
                    gi = hp * NTH + i
                    v.wait_ge(p["xld"], 16 * (3 + NDC + gi + 1))
                    emit_ln_v(v, xs[:], stt, scr[:], None).then_inc(p["stat"], 1)
                    v.wait_ge(p["lnr"], gi + 1)
                    v.tensor_scalar(lnb[:], xs[:], stt[:, 2:3], stt[:, 4:5],
                                    ALU.subtract, ALU.mult)
                    v.drain()
                    v.sem_inc(p["lnf"], 1)
                    v.wait_ge(p["tr"], gi + 1)
                    for k in range(NDC):
                        ins = v.tensor_copy(lntile(hp, k, i * 128, 128),
                                            ptr[:, k * 128:(k + 1) * 128])
                    v.drain()
                    ins.then_inc(p["trc"], 1)
                    v.wait_ge(p["rsg"], gi + 1)
                    v.tensor_add(sc0[:], sc0[:], gbb[:])
                    v.memset(gm8[:], 0.0)
                    v.memset(sel[:], 0.0)
                    v.memset(mxc[:, 1:2], 0.0)
                    v.memset(mxc[:, 3:4], 0.0)
                    v.drain()
                    for g in range(8):
                        v.reduce_max(gs8[:, g:g + 1], sc0[:, 8 * g:8 * g + 8], axis=AX.X)
                    v.drain()
                    for j in range(4):
                        v.reduce_max(mxc[:, 0:1], gs8[:], axis=AX.X)
                        v.drain()
                        v.tensor_scalar(ge8[:], gs8[:], mxc[:, 0:1], None, ALU.is_ge)
                        v.drain()
                        v.tensor_tensor_scan(cs8[:], ge8[:],
                                             mxc[:, 3:4].broadcast_to((128, 8)),
                                             0.0, ALU.add, ALU.add)
                        v.drain()
                        v.tensor_scalar(cs8[:], cs8[:], 1.0, None, ALU.is_le)
                        v.drain()
                        v.tensor_mul(ge8[:], ge8[:], cs8[:])
                        v.drain()
                        v.tensor_add(gm8[:], gm8[:], ge8[:])
                        v.drain()
                        v.scalar_tensor_tensor(gs8[:], ge8[:], -1e30, gs8[:],
                                               ALU.mult, ALU.add)
                        v.drain()
                    for g in range(8):
                        v.tensor_scalar(mw[:, 8 * g:8 * g + 8], on8[:],
                                        gm8[:, g:g + 1], None, ALU.mult)
                    v.drain()
                    v.tensor_mul(mw[:], sc0[:], mw[:])
                    v.drain()
                    v.tensor_copy(mk[:], mw[:])
                    v.drain()
                    for j in range(6):
                        v.reduce_max(mxc[:, 0:1], mw[:], axis=AX.X)
                        v.drain()
                        v.tensor_scalar(ge64[:], mw[:], mxc[:, 0:1], None, ALU.is_ge)
                        v.drain()
                        v.tensor_tensor_scan(cs64[:], ge64[:],
                                             mxc[:, 3:4].broadcast_to((128, 64)),
                                             0.0, ALU.add, ALU.add)
                        v.drain()
                        v.tensor_scalar(cs64[:], cs64[:], 1.0, None, ALU.is_le)
                        v.drain()
                        v.tensor_mul(ge64[:], ge64[:], cs64[:])
                        v.drain()
                        v.tensor_add(sel[:], sel[:], ge64[:])
                        v.tensor_add(mxc[:, 1:2], mxc[:, 1:2], mxc[:, 0:1])
                        v.drain()
                        v.scalar_tensor_tensor(mw[:], ge64[:], -1e30, mw[:],
                                               ALU.mult, ALU.add)
                        v.drain()
                    v.tensor_scalar(mxc[:, 1:2], mxc[:, 1:2], 1e-8, None, ALU.add)
                    v.drain()
                    v.reciprocal(mxc[:, 2:3], mxc[:, 1:2])
                    v.drain()
                    v.tensor_mul(mk[:], mk[:], sel[:])
                    v.drain()
                    if gi >= 2:
                        v.wait_ge(p["ctr"], gi - 1)
                    v.tensor_scalar(ctk[:, (gi % 2) * L3E:(gi % 2 + 1) * L3E], mk[:],
                                    mxc[:, 2:3], None, ALU.mult)
                    v.drain()
                    v.sem_inc(p["rcons"], 1)
                    v.wait_ge(p["ctr"], gi + 1)
                    v.tensor_copy(ctkT[0:L3E, i * 128:(i + 1) * 128],
                                  pdb[0:L3E, 1024:1152])
                    v.drain()
                    v.sem_inc(p["ctc"], 1)
                for g in range(NGH):
                    tg = g * 512
                    for ic in range(8):
                        u = (hp * NGH + g) * 8 + ic
                        v.wait_ge(p["ssl"], u + 1)
                        v.tensor_mul(sil[:, (ic % 2) * 512:(ic % 2 + 1) * 512],
                                     sil[:, (ic % 2) * 512:(ic % 2 + 1) * 512],
                                     pgu[:, 1024 + (ic % 2) * 512:1024 + (ic % 2 + 1) * 512])
                        v.drain()
                        v.tensor_copy(hsT[:, ic * 512:(ic + 1) * 512],
                                      sil[:, (ic % 2) * 512:(ic % 2 + 1) * 512])
                        v.drain()
                        v.sem_inc(p["shst"], 1)
                    for hc in range(8):
                        dg = (hp * NGH + g) * 8 + hc
                        v.wait_ge(p["sdw"], dg + 1)
                        v.tensor_copy(acc[:, hc * H2 + tg:hc * H2 + tg + 512],
                                      pdb[:, (hc % 4) * 512:(hc % 4 + 1) * 512])
                        v.drain()
                        v.sem_inc(p["sacc"], 1)
                for e in range(L3E):
                    eg = hp * L3E + e
                    for g in range(NGH):
                        tg = g * 512
                        v.wait_ge(p["bcm"], eg * NGH + g + 1)
                        v.tensor_copy(cbc[:], pdb[:, 1536:2048])
                        v.drain()
                        v.sem_inc(p["bcc"], 1)
                        for w in range(2):
                            W = eg * 8 + g * 2 + w
                            for ic2 in range(2):
                                ic = w * 2 + ic2
                                v.wait_ge(p["ssl2"], 2 * W + ic2 + 1)
                                v.tensor_mul(sil[:, ic2 * 512:(ic2 + 1) * 512],
                                             sil[:, ic2 * 512:(ic2 + 1) * 512],
                                             pgu[:, 1024 + ic2 * 512:1024 + (ic2 + 1) * 512])
                                v.drain()
                                v.tensor_mul(hsT[:, ic * 512:(ic + 1) * 512],
                                             sil[:, ic2 * 512:(ic2 + 1) * 512], cbc[:])
                                v.drain()
                            v.sem_inc(p["hst"], 1)
                        for hc in range(8):
                            dg = eg * 32 + g * 8 + hc
                            v.wait_ge(p["dwn"], dg + 1)
                            v.tensor_add(acc[:, hc * H2 + tg:hc * H2 + tg + 512],
                                         acc[:, hc * H2 + tg:hc * H2 + tg + 512],
                                         pdb[:, (hc % 2) * 512:(hc % 2 + 1) * 512])
                            v.drain()
                            v.sem_inc(p["acv"], 1)
                for j in range(NTH):
                    gj = hp * NTH + j
                    for dgrp in range(2):
                        wq = 2 * gj + dgrp
                        v.wait_ge(p["wtr"], wq + 1)
                        v.wait_ge(p["xld2"], 16 * (gj + 1))
                        if gj >= 1 and dgrp == 0:
                            v.wait_ge(p["sto"], 16 * gj)
                        v.scalar_tensor_tensor(scr[:, dgrp * 512:(dgrp + 1) * 512],
                                               pdb[:, dgrp * 512:(dgrp + 1) * 512],
                                               0.5, xs[:, dgrp * 512:(dgrp + 1) * 512],
                                               ALU.mult, ALU.add)
                        v.drain()
                        v.sem_inc(p["wev"], 1)

    p.close()
    return nc


def maps_l3f(inputs, xsh):
    egT = np.ascontiguousarray(np.transpose(f32(inputs["L3_ffn_eg_w"]), (0, 2, 1)).astype(BF))
    euT = np.ascontiguousarray(np.transpose(f32(inputs["L3_ffn_eu_w"]), (0, 2, 1)).astype(BF))
    edT = np.ascontiguousarray(np.transpose(f32(inputs["L3_ffn_ed_w"]), (0, 2, 1)).astype(BF))
    sgT = np.ascontiguousarray(hT(f32(inputs["L3_ffn_sg_w"])).astype(BF))
    suT = np.ascontiguousarray(hT(f32(inputs["L3_ffn_su_w"])).astype(BF))
    sdT = np.ascontiguousarray(hT(f32(inputs["L3_ffn_sd_w"])).astype(BF))
    gwT = np.ascontiguousarray(hT(f32(inputs["L3_ffn_gate_w"])).astype(BF))
    gbias = np.ascontiguousarray(np.broadcast_to(f32(inputs["L3_ffn_gate_bias"]), (128, 64)))
    idmF = np.eye(128, dtype=np.float32)
    return [{"x": xsh[c], "egT": egT, "euT": euT, "edT": edT, "sgT": sgT,
             "suT": suT, "sdT": sdT, "gwT": gwT, "gbias": gbias,
             "idm": _IDM, "idmF": idmF} for c in range(NCORE)]




class Ser:
    def __init__(self):
        self.items = []

    def ld(self, out, in_):
        self.items.append(("sync", lambda e, o=out, i=in_: e.dma_start(out=o, in_=i), 16))

    def ldg(self, pairs):
        def f(e, ps=pairs):
            for (o, i) in ps:
                e.dma_start(out=o, in_=i).then_inc(self.gsem, 16)
            return None
        self.items.append(("sync", f, 16 * len(pairs)))

    def st(self, out, in_):
        self.items.append(("gpsimd", lambda e, o=out, i=in_: e.dma_start(out=o, in_=i), 16))

    def stg(self, pairs):
        def f(e, ps=pairs):
            for (o, i) in ps:
                e.dma_start(out=o, in_=i).then_inc(self.gsem, 16)
            return None
        self.items.append(("gpsimd", f, 16 * len(pairs)))

    def gp(self, fn):
        self.items.append(("gpsimd", fn, 16))

    def mmg(self, triples):
        def f(e, ts=triples):
            n = len(ts)
            for j, (o, l, r) in enumerate(ts):
                ins = e.matmul(o, l, r, start=(j == 0), stop=(j == n - 1))
            return ins
        self.items.append(("tensor", f, 1))

    def mms(self, groups):
        def f(e, gs=groups):
            for ts in gs:
                n = len(ts)
                for j, (o, l, r) in enumerate(ts):
                    ins = e.matmul(o, l, r, start=(j == 0), stop=(j == n - 1))
            return ins
        self.items.append(("tensor", f, 1))

    def tr(self, out, in_, ident):
        self.items.append(("tensor",
                           lambda e, o=out, i=in_, d=ident: e.transpose(o, i, d), 1))

    def v(self, fn):
        self.items.append(("vector", fn, 1))

    def s(self, fn):
        self.items.append(("scalar", fn, 1))

    def build(self, nc):
        eng_items = {k: [] for k in ("sync", "gpsimd", "vector", "scalar", "tensor")}
        tot = 0
        for (eng, fn, inc) in self.items:
            eng_items[eng].append((tot, fn, inc))
            tot += inc
        total = tot
        with nc.semaphore("gsem") as gsem, nc.Block() as blk:
            self.gsem = gsem
            def mk(engname):
                def body(e):
                    for (before, fn, inc) in eng_items[engname]:
                        if before > 0:
                            e.wait_ge(gsem, before)
                        ins = fn(e)
                        if ins is not None:
                            ins.then_inc(gsem, inc)
                    if engname == "gpsimd":
                        e.wait_ge(gsem, total)
                return body
            blk.sync(mk("sync"))
            blk.gpsimd(mk("gpsimd"))
            blk.vector(mk("vector"))
            blk.scalar(mk("scalar"))
            blk.tensor(mk("tensor"))
        return nc



L2H, L2NCk = 4, SEQ // 128


def build_ph_l2a():
    nc = bass.Bass("TRN2", target_bir_lowering=False)
    x_in = nc.dram_tensor("x", [TPC, D], F32, kind="ExternalInput").ap()
    wqd = nc.dram_tensor("wq", [D, D], BF16, kind="ExternalInput").ap()
    wkd = nc.dram_tensor("wk", [D, D], BF16, kind="ExternalInput").ap()
    wvd = nc.dram_tensor("wv", [D, D], BF16, kind="ExternalInput").ap()
    wbd = nc.dram_tensor("wb", [128, 8 * L2H], BF16, kind="ExternalInput").ap()
    cvw = nc.dram_tensor("cvw", [128, 3 * 32], F32, kind="ExternalInput").ap()
    lmw = nc.dram_tensor("lmw", [128, 8 * 32], F32, kind="ExternalInput").ap()
    rw1 = nc.dram_tensor("rw1", [1152, 2176], BF16, kind="ExternalInput").ap()
    rb1 = nc.dram_tensor("rb1", [128, 17], F32, kind="ExternalInput").ap()
    rw2 = nc.dram_tensor("rw2", [2176, 16], BF16, kind="ExternalInput").ap()
    rb2 = nc.dram_tensor("rb2", [16, 1], F32, kind="ExternalInput").ap()
    mxw = nc.dram_tensor("mxw", [128, 8], F32, kind="ExternalInput").ap()
    onw = nc.dram_tensor("onw", [128, 8], F32, kind="ExternalInput").ap()
    wod = nc.dram_tensor("wo", [D, D], BF16, kind="ExternalInput").ap()
    mku = nc.dram_tensor("mku", [128, 128], F32, kind="ExternalInput").ap()
    mkl = nc.dram_tensor("mkl", [128, 128], F32, kind="ExternalInput").ap()
    mki = nc.dram_tensor("mki", [128, 128], BF16, kind="ExternalInput").ap()
    idm = nc.dram_tensor("idm", [128, 128], BF16, kind="ExternalInput").ap()
    idmF = nc.dram_tensor("idmF", [128, 128], F32, kind="ExternalInput").ap()
    d_ln = nc.dram_tensor("dln", [D, TPC], BF16, kind="Internal").ap()
    d_q = nc.dram_tensor("dq", [D, TPC], BF16, kind="Internal").ap()
    d_k = nc.dram_tensor("dk2", [D, TPC], BF16, kind="Internal").ap()
    d_v = nc.dram_tensor("dv", [D, TPC], BF16, kind="Internal").ap()
    d_lo = nc.dram_tensor("dlo", [D, TPC], BF16, kind="Internal").ap()
    d_mi = nc.dram_tensor("dmi", [D, TPC], BF16, kind="Internal").ap()
    d_do = nc.dram_tensor("ddo", [D, TPC], BF16, kind="Internal").ap()
    d_be = nc.dram_tensor("dbe", [L2H, TPC], F32, kind="Internal").ap()
    x_out = nc.dram_tensor("xo", [TPC, D], F32, kind="ExternalOutput").ap()

    xs = nc.alloc_sbuf_tensor("xs", [128, D], F32)
    lnb = nc.alloc_sbuf_tensor("lnb", [128, D], BF16)
    stt = nc.alloc_sbuf_tensor("stt", [128, 8], F32)
    lnsq = nc.alloc_sbuf_tensor("lnsq", [128, 8 * SEQ], BF16)
    wqb = nc.alloc_sbuf_tensor("wqb", [128, 8 * D], BF16)
    rawb = nc.alloc_sbuf_tensor("rawb", [128, 8 * (SEQ + 32)], BF16)
    outbf = nc.alloc_sbuf_tensor("outbf", [128, 8 * SEQ], BF16)
    cvac = nc.alloc_sbuf_tensor("cvac", [128, SEQ], F32)
    besb = nc.alloc_sbuf_tensor("besb", [1, SEQ], F32)
    wbb = nc.alloc_sbuf_tensor("wbb", [128, 8 * L2H], BF16)
    cvb = nc.alloc_sbuf_tensor("cvb", [128, 3 * 32], F32)
    lmb = nc.alloc_sbuf_tensor("lmb", [128, 8 * 32], F32)
    ident = nc.alloc_sbuf_tensor("ident", [128, 128], BF16)
    identF = nc.alloc_sbuf_tensor("identF", [128, 128], F32)
    qtc = nc.alloc_sbuf_tensor("qtc", [128, 256], BF16)
    ktc = nc.alloc_sbuf_tensor("ktc", [128, 256], BF16)
    vtc = nc.alloc_sbuf_tensor("vtc", [128, 256], BF16)
    ber = nc.alloc_sbuf_tensor("ber", [1, 128], F32)
    onec = nc.alloc_sbuf_tensor("onec", [128, 1], F32)
    one1 = nc.alloc_sbuf_tensor("one1", [1, 128], F32)
    onebc = nc.alloc_sbuf_tensor("onebc", [128, 1], BF16)
    rowsc = nc.alloc_sbuf_tensor("rowsc", [1, 256], F32)
    bcst = nc.alloc_sbuf_tensor("bcst", [128, 3 * 128], F32)
    ften = nc.alloc_sbuf_tensor("ften", [128, 4 * 128], F32)
    kbt = nc.alloc_sbuf_tensor("kbt", [128, 256], BF16)
    vbt = nc.alloc_sbuf_tensor("vbt", [128, 256], BF16)
    ktok = nc.alloc_sbuf_tensor("ktok", [128, 256], BF16)
    kbtok = nc.alloc_sbuf_tensor("kbtok", [128, 256], BF16)
    vbtok = nc.alloc_sbuf_tensor("vbtok", [128, 256], BF16)
    mstk = nc.alloc_sbuf_tensor("mstk", [128, 7 * 128], F32)
    nstk = nc.alloc_sbuf_tensor("nstk", [128, 7 * 128], F32)
    rr = nc.alloc_sbuf_tensor("rr", [128, 256], F32)
    rrb = nc.alloc_sbuf_tensor("rrb", [128, 128], BF16)
    usb = nc.alloc_sbuf_tensor("usb", [128, 256], BF16)
    uisb = nc.alloc_sbuf_tensor("uisb", [128, 256], BF16)
    wmt = nc.alloc_sbuf_tensor("wmt", [128, 256], BF16)
    attns = nc.alloc_sbuf_tensor("attns", [128, 128], BF16)
    ssb = nc.alloc_sbuf_tensor("ssb", [128, 512], F32)
    sbf = nc.alloc_sbuf_tensor("sbf", [128, 512], BF16)
    dosb = nc.alloc_sbuf_tensor("dosb", [128, 256], BF16)
    rw1s = nc.alloc_sbuf_tensor("rw1s", [128, 9 * 128], BF16)
    rw2b = nc.alloc_sbuf_tensor("rw2b", [128, 17 * 16], BF16)
    rb1b = nc.alloc_sbuf_tensor("rb1b", [128, 17], F32)
    rb2b = nc.alloc_sbuf_tensor("rb2b", [16, 1], F32)
    mxwb = nc.alloc_sbuf_tensor("mxwb", [128, 8], F32)
    onwb = nc.alloc_sbuf_tensor("onwb", [128, 8], F32)
    featT = nc.alloc_sbuf_tensor("featT", [128, 128], BF16)
    lgsb = nc.alloc_sbuf_tensor("lgsb", [16, 128], F32)
    ptok = nc.alloc_sbuf_tensor("ptok", [128, 16], F32)
    rmrow = nc.alloc_sbuf_tensor("rmrow", [1, 128], F32)
    rbc = nc.alloc_sbuf_tensor("rbc", [128, 128], F32)
    osb = nc.alloc_sbuf_tensor("osb", [128, D], F32)
    sqsc = nc.alloc_sbuf_tensor("sqsc", [128, 256], BF16)
    otile = outbf[:, 0:5120]
    h1sb = outbf[:, 5120:5120 + 17 * 128]
    onT = outbf[:, 7296:7296 + 1024]
    rowbk = rawb[0:1, 0:56 * 128]
    prow = rawb[0:1, 8192:8192 + 4096].bitcast(F32)
    pbc = lnsq[:, 0:4096].bitcast(F32)
    mixd = lnsq[:, 4096:6144].bitcast(F32)
    p0 = nc.alloc_psum_tensor("p0", [128, 512], F32)
    p1 = nc.alloc_psum_tensor("p1", [128, 512], F32)
    p2 = nc.alloc_psum_tensor("p2", [128, 512], F32)
    p3 = nc.alloc_psum_tensor("p3", [128, 512], F32)
    ptr = nc.alloc_psum_tensor("ptr", [128, 1024], BF16)
    prw = nc.alloc_psum_tensor("prw", [1, 512], F32)
    pcA = nc.alloc_psum_tensor("pcA", [128, 256], F32)
    pcB = nc.alloc_psum_tensor("pcB", [128, 256], F32)

    S = Ser()

    S.ldg([(ident[:], idm[:]), (identF[:], idmF[:])])
    S.ldg([(wbb[:], wbd[:]), (cvb[:], cvw[:]), (lmb[:], lmw[:]),
           (rb1b[:], rb1[:]), (rb2b[:], rb2[:]), (mxwb[:], mxw[:]),
           (onwb[:], onw[:]),
           (rw2b[:].rearrange("p (a c) -> p a c", a=17),
            rw2[:].rearrange("(a p) c -> p a c", p=128))])
    mu_sb = nc.alloc_sbuf_tensor("mu_sb", [128, 128], F32)
    ml_sb = nc.alloc_sbuf_tensor("ml_sb", [128, 128], F32)
    mi_sb = nc.alloc_sbuf_tensor("mi_sb", [128, 128], BF16)
    S.ldg([(mu_sb[:], mku[:]), (ml_sb[:], mkl[:]), (mi_sb[:], mki[:])])
    S.v(lambda v: v.memset(onec[:], 1.0))
    S.v(lambda v: v.memset(one1[:], 1.0))
    S.v(lambda v: v.memset(onebc[:], 1.0))
    S.v(lambda v: v.memset(featT[:], 0.0))

    for i in range(NT):
        S.ld(xs[:], x_in[i * 128:(i + 1) * 128, :])
        S.v(lambda v: emit_ln_v(v, xs[:], stt, osb[:], None))
        S.s(lambda s: emit_rstd_s(s, stt))
        S.v(lambda v: v.tensor_scalar(lnb[:], xs[:], stt[:, 2:3], stt[:, 4:5],
                                      ALU.subtract, ALU.mult))
        for k in range(NDC):
            S.tr(ptr[:, k * 128:(k + 1) * 128], lnb[:, k * 128:(k + 1) * 128], ident[:])
        S.v(lambda v: v.tensor_copy(lnb[:], ptr[:, 0:1024]))
        S.stg([(d_ln[k * 128:(k + 1) * 128, i * 128:(i + 1) * 128],
                lnb[:, k * 128:(k + 1) * 128]) for k in range(NDC)])

    for s in range(BPC):
        S.ldg([(lnsq[:, k * SEQ:(k + 1) * SEQ],
                d_ln[k * 128:(k + 1) * 128, s * SEQ:(s + 1) * SEQ]) for k in range(NDC)])
        for pr, (wsrc, ddst) in enumerate([(wqd, d_q), (wkd, d_k), (wvd, d_v)]):
            S.ldg([(wqb[:, k * D:(k + 1) * D], wsrc[k * 128:(k + 1) * 128, :])
                   for k in range(NDC)])
            for dc in range(NDC):
                for tg in range(4):
                    S.mmg([(p0[:, 0:512],
                            wqb[:, k * D + dc * 128:k * D + dc * 128 + 128],
                            lnsq[:, k * SEQ + tg * 512:k * SEQ + (tg + 1) * 512])
                           for k in range(NDC)])
                    S.v(lambda v, dc=dc, tg=tg: v.tensor_copy(
                        rawb[:, dc * (SEQ + 32) + 32 + tg * 512:
                             dc * (SEQ + 32) + 32 + (tg + 1) * 512], p0[:, 0:512]))
                S.v(lambda v, dc=dc: v.memset(rawb[:, dc * (SEQ + 32):dc * (SEQ + 32) + 32], 0.0))
                base = dc * (SEQ + 32) + 32
                S.v(lambda v, dc=dc, pr=pr, base=base: v.tensor_scalar(
                    cvac[:], rawb[:, base - 3:base - 3 + SEQ],
                    cvb[:, (pr * 8 + dc) * 4:(pr * 8 + dc) * 4 + 1], None, ALU.mult))
                for tau in range(1, 4):
                    S.v(lambda v, dc=dc, pr=pr, tau=tau, base=base: v.scalar_tensor_tensor(
                        cvac[:], rawb[:, base - 3 + tau:base - 3 + tau + SEQ],
                        cvb[:, (pr * 8 + dc) * 4 + tau:(pr * 8 + dc) * 4 + tau + 1],
                        cvac[:], ALU.mult, ALU.add))
                S.s(lambda s_, dc=dc: s_.activation(
                    outbf[:, dc * SEQ:(dc + 1) * SEQ], cvac[:], AF.Silu))
            S.stg([(ddst[k * 128:(k + 1) * 128, s * SEQ:(s + 1) * SEQ],
                    outbf[:, k * SEQ:(k + 1) * SEQ]) for k in range(NDC)])
        for h in range(L2H):
            for tg in range(4):
                S.mmg([(prw[0:1, 0:512],
                        wbb[:, k * L2H + h:k * L2H + h + 1],
                        lnsq[:, k * SEQ + tg * 512:k * SEQ + (tg + 1) * 512])
                       for k in range(NDC)])
                S.s(lambda s_, tg=tg: s_.activation(
                    besb[0:1, tg * 512:(tg + 1) * 512], prw[0:1, 0:512], AF.Sigmoid))
            S.st(d_be[h:h + 1, s * SEQ:(s + 1) * SEQ], besb[0:1, :])
        for dc in range(NDC):
            S.v(lambda v, dc=dc: v.memset(rawb[:, dc * (SEQ + 32):dc * (SEQ + 32) + 32], 0.0))
            S.v(lambda v, dc=dc: v.tensor_copy(
                rawb[:, dc * (SEQ + 32) + 32:dc * (SEQ + 32) + 32 + SEQ],
                outbf[:, dc * SEQ:(dc + 1) * SEQ]))
        for (KK, off, ddst) in [(7, 0, d_lo), (25, 7, d_mi)]:
            for dc in range(NDC):
                base = dc * (SEQ + 32) + 32
                S.v(lambda v, dc=dc, base=base, KK=KK, off=off: v.tensor_scalar(
                    cvac[:], rawb[:, base - KK + 1:base - KK + 1 + SEQ],
                    lmb[:, dc * 32 + off:dc * 32 + off + 1], None, ALU.mult))
                for tau in range(1, KK):
                    S.v(lambda v, dc=dc, base=base, KK=KK, off=off, tau=tau:
                        v.scalar_tensor_tensor(
                            cvac[:], rawb[:, base - KK + 1 + tau:base - KK + 1 + tau + SEQ],
                            lmb[:, dc * 32 + off + tau:dc * 32 + off + tau + 1],
                            cvac[:], ALU.mult, ALU.add))
                S.v(lambda v, dc=dc: v.tensor_copy(
                    outbf[:, dc * SEQ:(dc + 1) * SEQ], cvac[:]))
            S.stg([(ddst[k * 128:(k + 1) * 128, s * SEQ:(s + 1) * SEQ],
                    outbf[:, k * SEQ:(k + 1) * SEQ]) for k in range(NDC)])

    for s in range(BPC):
        for h in range(L2H):
            S.v(lambda v: v.memset(ssb[:], 0.0))
            for c in range(L2NCk):
                t0 = s * SEQ + c * 128
                r0 = h * 256
                S.ldg([(qtc[:, hf * 128:(hf + 1) * 128],
                        d_q[r0 + hf * 128:r0 + (hf + 1) * 128, t0:t0 + 128])
                       for hf in range(2)] +
                      [(ktc[:, hf * 128:(hf + 1) * 128],
                        d_k[r0 + hf * 128:r0 + (hf + 1) * 128, t0:t0 + 128])
                       for hf in range(2)] +
                      [(vtc[:, hf * 128:(hf + 1) * 128],
                        d_v[r0 + hf * 128:r0 + (hf + 1) * 128, t0:t0 + 128])
                       for hf in range(2)] +
                      [(ber[0:1, :], d_be[h:h + 1, t0:t0 + 128])])
                for (qq, tens) in [(0, qtc), (1, ktc)]:
                    S.v(lambda v, t=tens: v.tensor_mul(ften[:, 0:128], t[:, 0:128], t[:, 0:128]))
                    S.v(lambda v, t=tens: v.tensor_mul(ften[:, 128:256], t[:, 128:256], t[:, 128:256]))
                    S.mmg([(prw[0:1, 0:128], onec[:], ften[:, 0:128]),
                           (prw[0:1, 0:128], onec[:], ften[:, 128:256])])
                    S.v(lambda v, qq=qq: v.tensor_scalar(
                        rowsc[0:1, qq * 128:(qq + 1) * 128], prw[0:1, 0:128],
                        1e-12, None, ALU.add))
                    S.s(lambda s_, qq=qq: s_.activation(
                        rowsc[0:1, qq * 128:(qq + 1) * 128],
                        rowsc[0:1, qq * 128:(qq + 1) * 128],
                        AF.Ln, bias=0.0, scale=1.0))
                    S.s(lambda s_, qq=qq: s_.activation(
                        rowsc[0:1, qq * 128:(qq + 1) * 128],
                        rowsc[0:1, qq * 128:(qq + 1) * 128],
                        AF.Exp, bias=0.0, scale=-0.5))
                    S.mmg([(pcA[:, 0:128], one1[0:1, :], rowsc[0:1, qq * 128:(qq + 1) * 128])])
                    S.v(lambda v, qq=qq: v.tensor_copy(bcst[:, qq * 128:(qq + 1) * 128],
                                                       pcA[:, 0:128]))
                    S.v(lambda v, t=tens, qq=qq: v.tensor_mul(
                        t[:, 0:128], t[:, 0:128], bcst[:, qq * 128:(qq + 1) * 128]))
                    S.v(lambda v, t=tens, qq=qq: v.tensor_mul(
                        t[:, 128:256], t[:, 128:256], bcst[:, qq * 128:(qq + 1) * 128]))
                S.mmg([(pcA[:, 0:128], one1[0:1, :], ber[0:1, :])])
                S.v(lambda v: v.tensor_copy(bcst[:, 256:384], pcA[:, 0:128]))
                S.v(lambda v: v.tensor_mul(kbt[:, 0:128], ktc[:, 0:128], bcst[:, 256:384]))
                S.v(lambda v: v.tensor_mul(kbt[:, 128:256], ktc[:, 128:256], bcst[:, 256:384]))
                S.v(lambda v: v.tensor_mul(vbt[:, 0:128], vtc[:, 0:128], bcst[:, 256:384]))
                S.v(lambda v: v.tensor_mul(vbt[:, 128:256], vtc[:, 128:256], bcst[:, 256:384]))
                for (src, dst) in [(ktc, ktok), (kbt, kbtok), (vbt, vbtok)]:
                    S.tr(ptr[:, 0:128], src[:, 0:128], ident[:])
                    S.tr(ptr[:, 128:256], src[:, 128:256], ident[:])
                    S.v(lambda v, dst=dst: v.tensor_copy(dst[:, 0:256], ptr[:, 0:256]))
                S.mmg([(p0[:, 0:128], ktc[:, 0:128], kbt[:, 0:128]),
                       (p0[:, 0:128], ktc[:, 128:256], kbt[:, 128:256])])
                S.mmg([(p0[:, 128:256], kbt[:, 0:128], ktc[:, 0:128]),
                       (p0[:, 128:256], kbt[:, 128:256], ktc[:, 128:256])])
                S.v(lambda v: v.tensor_mul(nstk[:, 0:128], p0[:, 0:128], mu_sb[:]))
                S.v(lambda v: v.tensor_mul(mstk[:, 0:128], p0[:, 128:256], ml_sb[:]))
                for lev in range(6):
                    S.mms([[(p1[:, 0:128], nstk[:, lev * 128:(lev + 1) * 128],
                             mstk[:, lev * 128:(lev + 1) * 128])],
                           [(p1[:, 128:256], mstk[:, lev * 128:(lev + 1) * 128],
                             nstk[:, lev * 128:(lev + 1) * 128])]])
                    S.v(lambda v, lev=lev: v.tensor_copy(
                        mstk[:, (lev + 1) * 128:(lev + 2) * 128], p1[:, 0:128]))
                    S.v(lambda v, lev=lev: v.tensor_copy(
                        nstk[:, (lev + 1) * 128:(lev + 2) * 128], p1[:, 128:256]))
                S.v(lambda v: v.tensor_add(rr[:, 0:128], nstk[:, 6 * 128:7 * 128], identF[:]))
                S.v(lambda v: v.tensor_add(rr[:, 128:256], mstk[:, 6 * 128:7 * 128], identF[:]))
                for lev in range(5, -1, -1):
                    S.mms([[(p1[:, 0:128], rr[:, 128:256], nstk[:, lev * 128:(lev + 1) * 128])],
                           [(p1[:, 128:256], nstk[:, lev * 128:(lev + 1) * 128], rr[:, 128:256])]])
                    S.v(lambda v: v.tensor_add(rr[:, 0:128], p1[:, 0:128], rr[:, 0:128]))
                    S.v(lambda v: v.tensor_add(rr[:, 128:256], p1[:, 128:256], rr[:, 128:256]))
                S.v(lambda v: v.tensor_copy(rrb[:], rr[:, 0:128]))
                S.mmg([(pcA[:, 0:256], rrb[:], vbtok[:, 0:256])])
                S.v(lambda v: v.tensor_copy(usb[:], pcA[:, 0:256]))
                S.mms([[(p1[:, 0:128], kbtok[:, 0:128], rrb[:])],
                       [(p1[:, 128:256], kbtok[:, 128:256], rrb[:])]])
                S.v(lambda v: v.tensor_copy(wmt[:, 0:256], p1[:, 0:256]))
                S.mmg([(p2[:, 0:128], ktc[:, 0:128], qtc[:, 0:128]),
                       (p2[:, 0:128], ktc[:, 128:256], qtc[:, 128:256])])
                S.v(lambda v: v.tensor_mul(attns[:], p2[:, 0:128], mi_sb[:]))
                S.v(lambda v: v.tensor_copy(sbf[:, 0:256], ssb[:, 0:256]))
                S.v(lambda v: v.tensor_copy(sbf[:, 256:512], ssb[:, 256:512]))
                S.mmg([(pcB[:, 0:256], wmt[:, 0:128], sbf[:, 0:256]),
                       (pcB[:, 0:256], wmt[:, 128:256], sbf[:, 256:512])])
                S.v(lambda v: v.tensor_sub(uisb[:], usb[:], pcB[:, 0:256]))
                S.mms([[(p3[:, 0:128], sbf[:, 0:128], qtc[:, 0:128]),
                        (p3[:, 0:128], sbf[:, 256:384], qtc[:, 128:256]),
                        (p3[:, 0:128], uisb[:, 0:128], attns[:])],
                       [(p3[:, 128:256], sbf[:, 128:256], qtc[:, 0:128]),
                        (p3[:, 128:256], sbf[:, 384:512], qtc[:, 128:256]),
                        (p3[:, 128:256], uisb[:, 128:256], attns[:])]])
                S.v(lambda v: v.tensor_copy(dosb[:, 0:256], p3[:, 0:256]))
                S.mms([[(pcA[:, 0:256], ktok[:, 0:128], uisb[:, 0:256])],
                       [(pcB[:, 0:256], ktok[:, 128:256], uisb[:, 0:256])]])
                S.v(lambda v: v.tensor_add(ssb[:, 0:256], ssb[:, 0:256], pcA[:, 0:256]))
                S.v(lambda v: v.tensor_add(ssb[:, 256:512], ssb[:, 256:512], pcB[:, 0:256]))
                S.stg([(d_do[r0 + hf * 128:r0 + (hf + 1) * 128, t0:t0 + 128],
                        dosb[:, hf * 128:(hf + 1) * 128]) for hf in range(2)])

    S.ldg([(wqb[:, k * D:(k + 1) * D], wod[k * 128:(k + 1) * 128, :])
           for k in range(NDC)])
    PAIRS = [(0, 1), (0, 2), (0, 3), (1, 2), (1, 3), (2, 3)]
    for i in range(NT):
        t0 = i * 128
        S.ldg([(otile[:, (j * 8 + k) * 128:(j * 8 + k + 1) * 128],
                dd[k * 128:(k + 1) * 128, t0:t0 + 128])
               for (j, dd) in enumerate([d_lo, d_mi, d_do, d_v, d_ln])
               for k in range(NDC)])
        S.ld(xs[:], x_in[t0:t0 + 128, :])
        for j in range(4):
            for h in range(L2H):
                ch0 = (j * 8 + h * 2) * 128
                S.mmg([(prw[0:1, 0:128], onebc[:], otile[:, ch0:ch0 + 128]),
                       (prw[0:1, 0:128], onebc[:], otile[:, ch0 + 128:ch0 + 256])])
                S.v(lambda v: v.tensor_scalar(rowsc[0:1, 0:128], prw[0:1, 0:128],
                                              1.0 / 256.0, None, ALU.mult))
                S.v(lambda v, j=j, h=h: v.tensor_copy(
                    rowbk[0:1, (2 * j * 4 + h) * 128:(2 * j * 4 + h) * 128 + 128],
                    rowsc[0:1, 0:128]))
                S.v(lambda v, ch0=ch0: v.tensor_mul(
                    sqsc[:, 0:128], otile[:, ch0:ch0 + 128], otile[:, ch0:ch0 + 128]))
                S.v(lambda v, ch0=ch0: v.tensor_mul(
                    sqsc[:, 128:256], otile[:, ch0 + 128:ch0 + 256],
                    otile[:, ch0 + 128:ch0 + 256]))
                S.mmg([(prw[0:1, 128:256], onebc[:], sqsc[:, 0:128]),
                       (prw[0:1, 128:256], onebc[:], sqsc[:, 128:256])])
                S.v(lambda v: v.tensor_mul(rowsc[0:1, 128:256], rowsc[0:1, 0:128],
                                           rowsc[0:1, 0:128]))
                S.v(lambda v: v.tensor_scalar(rowsc[0:1, 0:128], prw[0:1, 128:256],
                                              1.0 / 255.0, None, ALU.mult))
                S.v(lambda v, j=j, h=h: v.scalar_tensor_tensor(
                    rowbk[0:1, ((2 * j + 1) * 4 + h) * 128:((2 * j + 1) * 4 + h) * 128 + 128],
                    rowsc[0:1, 128:256], -256.0 / 255.0, rowsc[0:1, 0:128],
                    ALU.mult, ALU.add))
        for pi, (ja, jb) in enumerate(PAIRS):
            for h in range(L2H):
                ca = (ja * 8 + h * 2) * 128
                cb = (jb * 8 + h * 2) * 128
                S.v(lambda v, ca=ca, cb=cb: v.tensor_mul(
                    sqsc[:, 0:128], otile[:, ca:ca + 128], otile[:, cb:cb + 128]))
                S.v(lambda v, ca=ca, cb=cb: v.tensor_mul(
                    sqsc[:, 128:256], otile[:, ca + 128:ca + 256], otile[:, cb + 128:cb + 256]))
                S.mmg([(prw[0:1, 0:128], onebc[:], sqsc[:, 0:128]),
                       (prw[0:1, 0:128], onebc[:], sqsc[:, 128:256])])
                S.v(lambda v, pi=pi, h=h: v.tensor_scalar(
                    rowbk[0:1, ((8 + pi) * 4 + h) * 128:((8 + pi) * 4 + h) * 128 + 128],
                    prw[0:1, 0:128], 1.0 / 256.0, None, ALU.mult))
        S.gp(lambda e: e.dma_start(out=featT[0:56, :], in_=rowbk[0:1, :]))
        for m in range(17):
            S.ldg([(rw1s[:].rearrange("p (a c) -> p a c", a=9),
                    rw1[:, m * 128:(m + 1) * 128].rearrange("(a p) c -> p a c", p=128))])
            S.mmg([(p0[:, 0:128],
                    rw1s[:, kr * 128:(kr + 1) * 128],
                    otile[:, (4 * 8 + kr) * 128:(4 * 8 + kr + 1) * 128])
                   for kr in range(8)] +
                  [(p0[:, 0:128], rw1s[:, 8 * 128:9 * 128], featT[:, :])])
            S.v(lambda v, m=m: v.tensor_scalar(cvac[:, 0:128], p0[:, 0:128],
                                               rb1b[:, m:m + 1], None, ALU.add))
            S.s(lambda s_, m=m: s_.activation(h1sb[:, m * 128:(m + 1) * 128],
                                              cvac[:, 0:128], AF.Silu))
        S.mmg([(p1[0:16, 0:128], rw2b[:, m * 16:(m + 1) * 16],
                h1sb[:, m * 128:(m + 1) * 128]) for m in range(17)])
        S.v(lambda v: v.tensor_scalar(lgsb[:], p1[0:16, 0:128], rb2b[:, 0:1],
                                      None, ALU.add))
        S.tr(p2[:, 0:16], lgsb[:], identF[0:16, 0:16])
        S.v(lambda v: v.tensor_copy(ptok[:], p2[:, 0:16]))
        for h in range(L2H):
            S.v(lambda v, h=h: v.reduce_max(stt[:, 0:1], ptok[:, 4 * h:4 * h + 4], axis=AX.X))
            S.v(lambda v, h=h: v.tensor_scalar(ptok[:, 4 * h:4 * h + 4],
                                               ptok[:, 4 * h:4 * h + 4],
                                               stt[:, 0:1], None, ALU.subtract))
            S.s(lambda s_, h=h: s_.activation(ptok[:, 4 * h:4 * h + 4],
                                              ptok[:, 4 * h:4 * h + 4], AF.Exp))
            S.v(lambda v, h=h: v.tensor_reduce(stt[:, 1:2], ptok[:, 4 * h:4 * h + 4],
                                               axis=AX.X, op=ALU.add))
            S.v(lambda v: v.reciprocal(stt[:, 2:3], stt[:, 1:2]))
            S.v(lambda v, h=h: v.tensor_scalar(ptok[:, 4 * h:4 * h + 4],
                                               ptok[:, 4 * h:4 * h + 4],
                                               stt[:, 2:3], None, ALU.mult))
        S.v(lambda v: v.tensor_scalar(ptok[:], ptok[:], 0.96, 0.01, ALU.mult, ALU.add))
        S.tr(p2[0:16, 0:128], ptok[:], identF[:])
        S.v(lambda v: v.tensor_copy(lgsb[:], p2[0:16, 0:128]))
        S.gp(lambda e: e.dma_start(out=prow[0:1, :], in_=lgsb[:]))
        for grp in range(4):
            S.mms([[(p2[:, q * 128:(q + 1) * 128], one1[0:1, :],
                     prow[0:1, (grp * 4 + q) * 128:(grp * 4 + q + 1) * 128])]
                   for q in range(4)])
            S.v(lambda v, grp=grp: v.tensor_copy(pbc[:, grp * 512:(grp + 1) * 512],
                                                 p2[:, 0:512]))
        for dc in range(NDC):
            h = dc // 2
            S.v(lambda v, dc=dc, h=h: v.tensor_mul(
                mixd[:, dc * 128:(dc + 1) * 128], pbc[:, (h * 4 + 0) * 128:(h * 4 + 1) * 128],
                otile[:, (0 * 8 + dc) * 128:(0 * 8 + dc + 1) * 128]))
            for jj in range(1, 4):
                S.v(lambda v, dc=dc, h=h, jj=jj: v.tensor_mul(
                    ften[:, 0:128], pbc[:, (h * 4 + jj) * 128:(h * 4 + jj + 1) * 128],
                    otile[:, (jj * 8 + dc) * 128:(jj * 8 + dc + 1) * 128]))
                S.v(lambda v, dc=dc: v.tensor_add(
                    mixd[:, dc * 128:(dc + 1) * 128], mixd[:, dc * 128:(dc + 1) * 128],
                    ften[:, 0:128]))
        for (rnd, eps, gain) in [(0, 1e-5, mxwb), (1, 1e-5, onwb)]:
            for h in range(L2H):
                S.v(lambda v, h=h: v.tensor_mul(ften[:, 0:128],
                                                mixd[:, h * 256:h * 256 + 128],
                                                mixd[:, h * 256:h * 256 + 128]))
                S.v(lambda v, h=h: v.tensor_mul(ften[:, 128:256],
                                                mixd[:, h * 256 + 128:h * 256 + 256],
                                                mixd[:, h * 256 + 128:h * 256 + 256]))
                S.mmg([(prw[0:1, 0:128], onec[:], ften[:, 0:128]),
                       (prw[0:1, 0:128], onec[:], ften[:, 128:256])])
                S.v(lambda v, eps=eps: v.tensor_scalar(rmrow[0:1, :], prw[0:1, 0:128],
                                                       1.0 / 256.0, eps,
                                                       ALU.mult, ALU.add))
                S.s(lambda s_: s_.activation(rmrow[0:1, :], rmrow[0:1, :],
                                             AF.Ln, bias=0.0, scale=1.0))
                S.s(lambda s_: s_.activation(rmrow[0:1, :], rmrow[0:1, :],
                                             AF.Exp, bias=0.0, scale=-0.5))
                S.mmg([(p2[:, 0:128], one1[0:1, :], rmrow[0:1, :])])
                S.v(lambda v: v.tensor_copy(rbc[:], p2[:, 0:128]))
                for hf in range(2):
                    dc = h * 2 + hf
                    if rnd == 0:
                        S.v(lambda v, dc=dc: v.tensor_mul(
                            mixd[:, dc * 128:(dc + 1) * 128],
                            mixd[:, dc * 128:(dc + 1) * 128], rbc[:]))
                        S.v(lambda v, dc=dc, gain=gain: v.tensor_scalar(
                            mixd[:, dc * 128:(dc + 1) * 128],
                            mixd[:, dc * 128:(dc + 1) * 128],
                            gain[:, dc:dc + 1], None, ALU.mult))
                    else:
                        S.v(lambda v, dc=dc: v.tensor_mul(
                            mixd[:, dc * 128:(dc + 1) * 128],
                            mixd[:, dc * 128:(dc + 1) * 128], rbc[:]))
                        S.v(lambda v, dc=dc, gain=gain: v.tensor_scalar(
                            onT[:, dc * 128:(dc + 1) * 128],
                            mixd[:, dc * 128:(dc + 1) * 128],
                            gain[:, dc:dc + 1], None, ALU.mult))
        for dgrp in range(2):
            S.mmg([(p0[:, 0:512], onT[:, k * 128:(k + 1) * 128],
                    wqb[:, k * D + dgrp * 512:k * D + dgrp * 512 + 512])
                   for k in range(NDC)])
            S.v(lambda v, dgrp=dgrp: v.tensor_add(
                osb[:, dgrp * 512:(dgrp + 1) * 512], p0[:, 0:512],
                xs[:, dgrp * 512:(dgrp + 1) * 512]))
        S.st(x_out[t0:t0 + 128, :], osb[:])

    return S.build(nc)


def maps_l2a(inputs, xsh):
    def chunkify(a):
        a = f32(a)
        w = a.shape[1]
        return np.ascontiguousarray(a.reshape(8, 128, w).transpose(1, 0, 2).reshape(128, 8 * w))
    wq = np.ascontiguousarray(f32(inputs["L2_attn_Wq"]).astype(BF))
    wk = np.ascontiguousarray(f32(inputs["L2_attn_Wk"]).astype(BF))
    wv = np.ascontiguousarray(f32(inputs["L2_attn_Wv"]).astype(BF))
    wb = np.ascontiguousarray(chunkify(inputs["L2_attn_Wb"]).astype(BF))
    cvw = np.ascontiguousarray(np.concatenate(
        [chunkify(inputs["L2_attn_conv_q_w"]),
         chunkify(inputs["L2_attn_conv_k_w"]),
         chunkify(inputs["L2_attn_conv_v_w"])], axis=1))
    lmw_ = np.concatenate([f32(inputs["L2_attn_local_w"]), f32(inputs["L2_attn_mid_w"])], 1)
    lmw = np.ascontiguousarray(chunkify(lmw_))
    rw1p = np.zeros((1152, 2176), np.float32)
    rw1p[:1080, :2160] = f32(inputs["L2_attn_r_w1"])
    rb1p = np.zeros((2176,), np.float32)
    rb1p[:2160] = f32(inputs["L2_attn_r_b1"])
    rw2p = np.zeros((2176, 16), np.float32)
    rw2p[:2160] = f32(inputs["L2_attn_r_w2"])
    ones = np.ones((128, 128), np.float32)
    mku = np.ascontiguousarray(-np.triu(ones, 1))
    mkl = np.ascontiguousarray(-np.tril(ones, -1))
    mki = np.ascontiguousarray(np.triu(ones, 0).astype(BF))
    mxflat = f32(inputs["L2_attn_mix_w"]).reshape(-1)
    onflat = np.tile(f32(inputs["L2_attn_onorm_w"]), 4)
    return [{"x": xsh[c], "wq": wq, "wk": wk, "wv": wv, "wb": wb,
             "cvw": cvw, "lmw": lmw,
             "rw1": np.ascontiguousarray(rw1p.astype(BF)),
             "rb1": np.ascontiguousarray(rb1p.reshape(17, 128).T.copy()),
             "rw2": np.ascontiguousarray(rw2p.astype(BF)),
             "rb2": f32(inputs["L2_attn_r_b2"]).reshape(16, 1),
             "mxw": np.ascontiguousarray(mxflat.reshape(8, 128).T.copy()),
             "onw": np.ascontiguousarray(onflat.reshape(8, 128).T.copy()),
             "wo": np.ascontiguousarray(f32(inputs["L2_attn_Wo"]).astype(BF)),
             "mku": mku, "mkl": mkl, "mki": mki,
             "idm": _IDM, "idmF": np.eye(128, dtype=np.float32)}
            for c in range(NCORE)]



L3NH, L3DK, L3DV, L3C = 4, 128, 256, 32


def build_ph_l3a():
    nc = bass.Bass("TRN2", target_bir_lowering=False)
    x_in = nc.dram_tensor("x", [TPC, D], F32, kind="ExternalInput").ap()
    xpm = nc.dram_tensor("xpm", [128, 8], F32, kind="ExternalInput").ap()
    xw1 = nc.dram_tensor("xw1", [128, 8 * 32], BF16, kind="ExternalInput").ap()
    xw2 = nc.dram_tensor("xw2", [32, 160], BF16, kind="ExternalInput").ap()
    xb2 = nc.dram_tensor("xb2", [128, 1], F32, kind="ExternalInput").ap()
    xb2c = nc.dram_tensor("xb2c", [32, 1], F32, kind="ExternalInput").ap()
    xow = nc.dram_tensor("xow", [32, 5 * 1024], BF16, kind="ExternalInput").ap()
    xbs = nc.dram_tensor("xbs", [128, 8 * 5], F32, kind="ExternalInput").ap()
    rwp = nc.dram_tensor("rwp", [128, 8 * 512], BF16, kind="ExternalInput").ap()
    kwp = nc.dram_tensor("kwp", [128, 8 * 512], BF16, kind="ExternalInput").ap()
    vwp = nc.dram_tensor("vwp", [128, 8 * 1024], BF16, kind="ExternalInput").ap()
    gwp = nc.dram_tensor("gwp", [128, 8 * 1024], BF16, kind="ExternalInput").ap()
    ww1 = nc.dram_tensor("ww1", [128, 8 * 64], BF16, kind="ExternalInput").ap()
    ww2 = nc.dram_tensor("ww2", [64, 512], BF16, kind="ExternalInput").ap()
    wbc = nc.dram_tensor("wbc", [128, 4], F32, kind="ExternalInput").ap()
    bon = nc.dram_tensor("bon", [128, 4], F32, kind="ExternalInput").ap()
    gnw = nc.dram_tensor("gnw", [128, 1024], F32, kind="ExternalInput").ap()
    gnb = nc.dram_tensor("gnb", [128, 1024], F32, kind="ExternalInput").ap()
    owp = nc.dram_tensor("owp", [128, 8 * 1024], BF16, kind="ExternalInput").ap()
    mq32 = nc.dram_tensor("mq32", [32, 32], BF16, kind="ExternalInput").ap()
    id32 = nc.dram_tensor("id32", [32, 32], BF16, kind="ExternalInput").ap()
    idm = nc.dram_tensor("idm", [128, 128], BF16, kind="ExternalInput").ap()
    d_h = nc.dram_tensor("dh", [D, TPC], BF16, kind="Internal").ap()
    d_dl = nc.dram_tensor("ddl", [D, TPC], BF16, kind="Internal").ap()
    d_r = nc.dram_tensor("dr", [512, TPC], BF16, kind="Internal").ap()
    d_w = nc.dram_tensor("dw", [512, TPC], BF16, kind="Internal").ap()
    d_k = nc.dram_tensor("dk3", [512, TPC], BF16, kind="Internal").ap()
    d_v = nc.dram_tensor("dv3", [1024, TPC], BF16, kind="Internal").ap()
    d_g = nc.dram_tensor("dg3", [1024, TPC], BF16, kind="Internal").ap()
    d_o = nc.dram_tensor("do3", [TPC, 1024], BF16, kind="Internal").ap()
    x_out = nc.dram_tensor("xo", [TPC, D], F32, kind="ExternalOutput").ap()

    xs = nc.alloc_sbuf_tensor("xs", [128, D], F32)
    lnb = nc.alloc_sbuf_tensor("lnb", [128, D], BF16)
    stt = nc.alloc_sbuf_tensor("stt", [128, 8], F32)
    scr = nc.alloc_sbuf_tensor("scr", [128, D], F32)
    innb = nc.alloc_sbuf_tensor("innb", [128, 8 * SEQ], BF16)
    outb = nc.alloc_sbuf_tensor("outb", [128, 4 * SEQ], BF16)
    hTs = nc.alloc_sbuf_tensor("hTs", [128, SEQ], BF16)
    xlb = nc.alloc_sbuf_tensor("xlb", [128, SEQ], BF16)
    hTc = nc.alloc_sbuf_tensor("hTc", [128, 512], BF16)
    dTc = nc.alloc_sbuf_tensor("dTc", [128, 512], BF16)
    ftmp = nc.alloc_sbuf_tensor("ftmp", [128, 512], F32)
    x0sb = nc.alloc_sbuf_tensor("x0sb", [32, SEQ], BF16)
    x1t = nc.alloc_sbuf_tensor("x1t", [128, SEQ], BF16)
    x1b = nc.alloc_sbuf_tensor("x1b", [32, SEQ], BF16)
    x1s = nc.alloc_sbuf_tensor("x1s", [32, SEQ], BF16)
    w1sb = nc.alloc_sbuf_tensor("w1sb", [64, SEQ], BF16)
    wq1 = nc.alloc_sbuf_tensor("wq1", [128, 8 * 512], BF16)
    wq2 = nc.alloc_sbuf_tensor("wq2", [128, 8 * 1024], BF16)
    ww1b = nc.alloc_sbuf_tensor("ww1b", [128, 8 * 64], BF16)
    ww2b = nc.alloc_sbuf_tensor("ww2b", [64, 512], BF16)
    wbcb = nc.alloc_sbuf_tensor("wbcb", [128, 4], F32)
    bonb = nc.alloc_sbuf_tensor("bonb", [128, 4], F32)
    xw1b = nc.alloc_sbuf_tensor("xw1b", [128, 8 * 32], BF16)
    xw2b = nc.alloc_sbuf_tensor("xw2b", [32, 160], BF16)
    xb2b = nc.alloc_sbuf_tensor("xb2b", [128, 1], F32)
    xb2cb = nc.alloc_sbuf_tensor("xb2cb", [32, 1], F32)
    xowb = nc.alloc_sbuf_tensor("xowb", [32, 5 * 1024], BF16)
    xbsb = nc.alloc_sbuf_tensor("xbsb", [128, 8 * 5], F32)
    xpmb = nc.alloc_sbuf_tensor("xpmb", [128, 8], F32)
    gnwb = nc.alloc_sbuf_tensor("gnwb", [128, 1024], F32)
    gnbb = nc.alloc_sbuf_tensor("gnbb", [128, 1024], F32)
    ident = nc.alloc_sbuf_tensor("ident", [128, 128], BF16)
    mq32b = nc.alloc_sbuf_tensor("mq32b", [32, 32], BF16)
    id32b = nc.alloc_sbuf_tensor("id32b", [32, 32], BF16)
    onec = nc.alloc_sbuf_tensor("onec", [128, 1], F32)
    one1 = nc.alloc_sbuf_tensor("one1", [1, 128], F32)
    rTc = nc.alloc_sbuf_tensor("rTc", [128, 32], BF16)
    kTc = nc.alloc_sbuf_tensor("kTc", [128, 32], BF16)
    lwc = nc.alloc_sbuf_tensor("lwc", [128, 32], F32)
    lwb = nc.alloc_sbuf_tensor("lwb", [128, 32], BF16)
    Lc = nc.alloc_sbuf_tensor("Lc", [128, 32], F32)
    Lex = nc.alloc_sbuf_tensor("Lex", [128, 32], F32)
    e1 = nc.alloc_sbuf_tensor("e1", [128, 32], F32)
    aT = nc.alloc_sbuf_tensor("aT", [128, 32], BF16)
    bT = nc.alloc_sbuf_tensor("bT", [128, 32], BF16)
    bpT = nc.alloc_sbuf_tensor("bpT", [128, 32], BF16)
    dcol = nc.alloc_sbuf_tensor("dcol", [128, 1], F32)
    vtok = nc.alloc_sbuf_tensor("vtok", [32, 256], BF16)
    bptok = nc.alloc_sbuf_tensor("bptok", [32, 128], BF16)
    drow = nc.alloc_sbuf_tensor("drow", [1, 32], F32)
    ptf = nc.alloc_sbuf_tensor("ptf", [32, 32], BF16)
    osc = nc.alloc_sbuf_tensor("osc", [32, 256], BF16)
    f32a = nc.alloc_sbuf_tensor("f32a", [128, 256], F32)
    f32b = nc.alloc_sbuf_tensor("f32b", [32, 32], F32)
    ssb = nc.alloc_sbuf_tensor("ssb", [128, 256], F32)
    sbf = nc.alloc_sbuf_tensor("sbf", [128, 256], BF16)
    gtok = nc.alloc_sbuf_tensor("gtok", [128, 1024], BF16)
    gsil = nc.alloc_sbuf_tensor("gsil", [128, 1024], F32)
    ogn = nc.alloc_sbuf_tensor("ogn", [128, 1024], F32)
    ooT = nc.alloc_sbuf_tensor("ooT", [128, 1024], BF16)
    osb = nc.alloc_sbuf_tensor("osb", [128, D], F32)
    otl = nc.alloc_sbuf_tensor("otl", [128, 1024], BF16)
    p0 = nc.alloc_psum_tensor("p0", [128, 512], F32)
    px0 = nc.alloc_psum_tensor("px0", [128, 2048], F32)
    ptr = nc.alloc_psum_tensor("ptr", [128, 1024], BF16)
    pc = nc.alloc_psum_tensor("pc", [128, 256], F32)
    pc2 = nc.alloc_psum_tensor("pc2", [128, 256], F32)

    S = Ser()
    S.ldg([(ident[:], idm[:]), (mq32b[:], mq32[:]), (id32b[:], id32[:]),
           (xpmb[:], xpm[:]), (xw1b[:], xw1[:]), (xw2b[:], xw2[:]),
           (xb2b[:], xb2[:]), (xb2cb[:], xb2c[:]),
           (xowb[:], xow[:]), (xbsb[:], xbs[:]),
           (ww1b[:], ww1[:]), (ww2b[:], ww2[:]), (wbcb[:], wbc[:]),
           (bonb[:], bon[:]), (gnwb[:], gnw[:]), (gnbb[:], gnb[:])])
    S.v(lambda v: v.memset(onec[:], 1.0))
    S.v(lambda v: v.memset(one1[:], 1.0))
    S.v(lambda v: v.memset(stt[:, 7:8], 0.0))

    for i in range(NT):
        S.ld(xs[:], x_in[i * 128:(i + 1) * 128, :])
        S.v(lambda v: emit_ln_v(v, xs[:], stt, scr[:], None))
        S.s(lambda s: emit_rstd_s(s, stt))
        S.v(lambda v: v.tensor_scalar(lnb[:], xs[:], stt[:, 2:3], stt[:, 4:5],
                                      ALU.subtract, ALU.mult))
        for k in range(NDC):
            S.tr(ptr[:, k * 128:(k + 1) * 128], lnb[:, k * 128:(k + 1) * 128], ident[:])
        S.v(lambda v: v.tensor_copy(lnb[:], ptr[:, 0:1024]))
        S.stg([(d_h[k * 128:(k + 1) * 128, i * 128:(i + 1) * 128],
                lnb[:, k * 128:(k + 1) * 128]) for k in range(NDC)])

    for s in range(BPC):
        t00 = s * SEQ
        for k in range(NDC):
            S.ld(hTs[:], d_h[k * 128:(k + 1) * 128, t00:t00 + SEQ])
            S.v(lambda v: v.tensor_sub(xlb[:, 1:SEQ], hTs[:, 0:SEQ - 1], hTs[:, 1:SEQ]))
            S.v(lambda v: v.tensor_scalar(xlb[:, 0:1], hTs[:, 0:1], -1.0, None, ALU.mult))
            S.st(d_dl[k * 128:(k + 1) * 128, t00:t00 + SEQ], xlb[:, 0:SEQ])
            S.v(lambda v, k=k: v.tensor_scalar(scr[:, 0:1024], xlb[:, 0:1024],
                                               xpmb[:, k:k + 1], None, ALU.mult))
            S.v(lambda v, k=k: v.tensor_scalar(scr[:, 0:1024].bitcast(BF16)[:, 0:1024],
                                               xlb[:, 1024:2048], xpmb[:, k:k + 1],
                                               None, ALU.mult)) if False else None
            S.v(lambda v: v.tensor_add(xlb[:, 0:SEQ].bitcast(F32)[:, 0:1024], scr[:, 0:1024],
                                       scr[:, 0:1024])) if False else None
            for hf in range(2):
                S.v(lambda v, k=k, hf=hf: v.tensor_scalar(
                    scr[:, 0:1024], xlb[:, hf * 1024:(hf + 1) * 1024],
                    xpmb[:, k:k + 1], None, ALU.mult))
                S.v(lambda v, hf=hf: v.tensor_add(
                    xlb[:, hf * 1024:(hf + 1) * 1024], scr[:, 0:1024],
                    hTs[:, hf * 1024:(hf + 1) * 1024]))
            for tg in range(4):
                S.items.append(("tensor", (lambda e, k=k, tg=tg: e.matmul(
                    px0[0:32, tg * 512:(tg + 1) * 512],
                    xw1b[:, k * 32:(k + 1) * 32],
                    xlb[:, tg * 512:(tg + 1) * 512],
                    start=(k == 0), stop=(k == NDC - 1))), 1))
        for tg in range(4):
            S.s(lambda s_, tg=tg: s_.activation(x0sb[0:32, tg * 512:(tg + 1) * 512],
                                                px0[0:32, tg * 512:(tg + 1) * 512],
                                                AF.Tanh))
        for tg in range(4):
            S.mmg([(p0[:, 0:512], xw2b[0:32, 0:128], x0sb[0:32, tg * 512:(tg + 1) * 512])])
            S.v(lambda v, tg=tg: v.tensor_scalar(ftmp[:], p0[:, 0:512],
                                                 xb2b[:, 0:1], None, ALU.add))
            S.s(lambda s_, tg=tg: s_.activation(x1t[:, tg * 512:(tg + 1) * 512],
                                                ftmp[:], AF.Tanh))
            S.mmg([(px0[0:32, 0:512], xw2b[0:32, 128:160], x0sb[0:32, tg * 512:(tg + 1) * 512])])
            S.v(lambda v, tg=tg: v.tensor_scalar(ftmp[0:32, 0:512], px0[0:32, 0:512],
                                                 xb2cb[0:32, 0:1], None, ALU.add))
            S.s(lambda s_, tg=tg: s_.activation(x1b[0:32, tg * 512:(tg + 1) * 512],
                                                ftmp[0:32, 0:512], AF.Tanh))
        for n, (wsl, dd, nout) in enumerate([
                ("r", d_r, 4), ("w", d_w, 4), ("k", d_k, 4),
                ("v", d_v, 8), ("g", d_g, 8)]):
            if n < 4:
                S.gp(lambda e, n=n: e.dma_start(out=x1s[0:32, :],
                                                in_=x1t[n * 32:(n + 1) * 32, :]))
            else:
                S.gp(lambda e: e.dma_start(out=x1s[0:32, :], in_=x1b[0:32, :]))
            for k in range(NDC):
                for tg in range(4):
                    S.mmg([(p0[:, 0:512], xowb[0:32, n * 1024 + k * 128:n * 1024 + (k + 1) * 128],
                            x1s[0:32, tg * 512:(tg + 1) * 512])])
                    S.ldg([(hTc[:], d_h[k * 128:(k + 1) * 128,
                                        t00 + tg * 512:t00 + (tg + 1) * 512]),
                           (dTc[:], d_dl[k * 128:(k + 1) * 128,
                                         t00 + tg * 512:t00 + (tg + 1) * 512])])
                    S.v(lambda v, n=n, k=k: v.tensor_scalar(
                        ftmp[:], p0[:, 0:512], xbsb[:, n * 8 + k:n * 8 + k + 1],
                        None, ALU.add))
                    S.v(lambda v: v.tensor_mul(ftmp[:], ftmp[:], dTc[:]))
                    S.v(lambda v, k=k, tg=tg: v.tensor_add(
                        innb[:, k * SEQ + tg * 512:k * SEQ + (tg + 1) * 512],
                        ftmp[:], hTc[:]))
            if n in (0, 2):
                S.ldg([(wq1[:], (rwp if n == 0 else kwp)[:])])
                wsrc, wwid = wq1, 512
            elif n in (3, 4):
                S.ldg([(wq2[:], (vwp if n == 3 else gwp)[:])])
                wsrc, wwid = wq2, 1024
            else:
                wsrc, wwid = None, None
            if n != 1:
                nch = wwid // 128
                for oc in range(nch):
                    for tg in range(4):
                        S.mmg([(p0[:, 0:512],
                                wsrc[:, k * wwid + oc * 128:k * wwid + oc * 128 + 128],
                                innb[:, k * SEQ + tg * 512:k * SEQ + (tg + 1) * 512])
                               for k in range(NDC)])
                        S.v(lambda v, oc=oc, tg=tg: v.tensor_copy(
                            outb[:, (oc % 4) * SEQ + tg * 512:(oc % 4) * SEQ + (tg + 1) * 512],
                            p0[:, 0:512]))
                    if oc % 4 == 3:
                        ob = oc - 3
                        S.stg([(dd[(ob + j) * 128:(ob + j + 1) * 128, t00:t00 + SEQ],
                                outb[:, j * SEQ:(j + 1) * SEQ]) for j in range(4)])
            else:
                for tg in range(4):
                    S.mmg([(p0[0:64, 0:512], ww1b[:, k * 64:(k + 1) * 64],
                            innb[:, k * SEQ + tg * 512:k * SEQ + (tg + 1) * 512])
                           for k in range(NDC)])
                    S.s(lambda s_, tg=tg: s_.activation(
                        w1sb[0:64, tg * 512:(tg + 1) * 512], p0[0:64, 0:512], AF.Tanh))
                for oc in range(4):
                    for tg in range(4):
                        S.mmg([(p0[:, 0:512], ww2b[0:64, oc * 128:(oc + 1) * 128],
                                w1sb[0:64, tg * 512:(tg + 1) * 512])])
                        S.v(lambda v, oc=oc: v.tensor_scalar(
                            ftmp[:], p0[:, 0:512], wbcb[:, oc:oc + 1], None, ALU.add))
                        S.s(lambda s_: s_.activation(ftmp[:], ftmp[:], AF.Exp))
                        S.v(lambda v, oc=oc, tg=tg: v.tensor_scalar(
                            outb[:, oc * SEQ + tg * 512:oc * SEQ + (tg + 1) * 512],
                            ftmp[:], -1.0, None, ALU.mult))
                S.stg([(d_w[j * 128:(j + 1) * 128, t00:t00 + SEQ],
                        outb[:, j * SEQ:(j + 1) * SEQ]) for j in range(4)])

    for s in range(BPC):
        for h in range(L3NH):
            S.v(lambda v: v.memset(ssb[:], 0.0))
            S.v(lambda v: v.memset(sbf[:], 0.0))
            for c in range(SEQ // L3C):
                t0 = s * SEQ + c * L3C
                S.ldg([(rTc[:], d_r[h * 128:(h + 1) * 128, t0:t0 + L3C]),
                       (kTc[:], d_k[h * 128:(h + 1) * 128, t0:t0 + L3C]),
                       (lwb[:], d_w[h * 128:(h + 1) * 128, t0:t0 + L3C])] +
                      [(vtok[0:32, hf * 128:(hf + 1) * 128].bitcast(BF16),
                        d_v[1, 1].bitcast(BF16)) for hf in range(0)])
                for hf in range(2):
                    S.tr(ptr[0:32, hf * 128:(hf + 1) * 128],
                         nc.dummy if False else None, ident[:]) if False else None
                S.ldg([(aT[:], d_v[h * 256:h * 256 + 128, t0:t0 + L3C]),
                       (bT[:], d_v[h * 256 + 128:h * 256 + 256, t0:t0 + L3C])])
                S.tr(ptr[0:32, 0:128], aT[:], ident[:])
                S.tr(ptr[0:32, 128:256], bT[:], ident[:])
                S.v(lambda v: v.tensor_copy(vtok[0:32, 0:256], ptr[0:32, 0:256]))
                S.v(lambda v: v.tensor_copy(lwc[:], lwb[:]))
                S.v(lambda v: v.tensor_tensor_scan(Lc[:], lwc[:],
                                                   stt[:, 7:8].broadcast_to((128, 32)),
                                                   0.0, ALU.add, ALU.add))
                S.v(lambda v: v.tensor_sub(Lex[:], Lc[:], lwc[:]))
                S.s(lambda s_: s_.activation(e1[:], Lex[:], AF.Exp))
                S.v(lambda v: v.tensor_mul(aT[:], rTc[:], e1[:]))
                S.s(lambda s_: s_.activation(e1[:], Lc[:], AF.Exp, bias=0.0, scale=-1.0))
                S.v(lambda v: v.tensor_mul(bT[:], kTc[:], e1[:]))
                S.s(lambda s_: s_.activation(dcol[:], Lc[:, 31:32], AF.Exp))
                S.v(lambda v: v.tensor_scalar(bpT[:], bT[:], dcol[:, 0:1], None, ALU.mult))
                S.tr(ptr[0:32, 256:384], bpT[:], ident[:])
                S.v(lambda v: v.tensor_copy(bptok[0:32, :], ptr[0:32, 256:384]))
                S.mmg([(p0[0:32, 0:32], bT[:], aT[:])])
                S.v(lambda v: v.tensor_mul(f32a[:, 0:32], rTc[:], kTc[:]))
                S.v(lambda v, h=h: v.tensor_scalar(f32a[:, 0:32], f32a[:, 0:32],
                                                   bonb[:, h:h + 1], None, ALU.mult))
                S.mmg([(px0[0:1, 0:32], onec[:], f32a[:, 0:32])])
                S.v(lambda v: v.tensor_copy(drow[0:1, :], px0[0:1, 0:32]))
                S.mmg([(px0[0:32, 512:544], drow[0:1, :], one1[0:1, 0:32])])
                S.v(lambda v: v.tensor_mul(ptf[0:32, :], p0[0:32, 0:32], mq32b[:]))
                S.v(lambda v: v.tensor_mul(f32b[0:32, :], px0[0:32, 512:544], id32b[:]))
                S.v(lambda v: v.tensor_add(ptf[0:32, :], ptf[0:32, :], f32b[0:32, :]))
                S.mms([[(pc[0:32, 0:256], aT[:], sbf[:, 0:256]),
                        (pc[0:32, 0:256], ptf[0:32, :], vtok[0:32, :])]])
                S.v(lambda v: v.tensor_copy(vtok[0:32, 0:256], pc[0:32, 0:256])
                    ) if False else None
                S.v(lambda v: v.tensor_copy(aT[:, 0:32].bitcast(BF16), aT[:, 0:32])
                    ) if False else None
                S.v(lambda v: v.tensor_copy(bpT[:], bpT[:])) if False else None
                S.v(lambda v: v.tensor_copy(osc[0:32, 0:256], pc[0:32, 0:256]))
                S.items.append(("gpsimd", (lambda e, t0=t0, h=h: e.dma_start(
                    out=d_o[t0:t0 + L3C, h * 256:(h + 1) * 256],
                    in_=osc[0:32, 0:256])), 16))
                S.mms([[(pc2[:, 0:256], bptok[0:32, :], vtok[0:32, :])]])
                S.v(lambda v: v.scalar_tensor_tensor(ssb[:], ssb[:], dcol[:, 0:1],
                                                     pc2[:, 0:256], ALU.mult, ALU.add))
                S.v(lambda v: v.tensor_copy(sbf[:], ssb[:]))

    S.ldg([(wq2[:], owp[:])])
    for i in range(NT):
        t0 = i * 128
        S.ld(otl[:], d_o[t0:t0 + 128, :])
        S.ldg([(gtok[:, k * 128:(k + 1) * 128],
                d_g[k * 128:(k + 1) * 128, t0:t0 + 128]) for k in range(NDC)])
        for k in range(NDC):
            S.tr(ptr[:, k * 128:(k + 1) * 128], gtok[:, k * 128:(k + 1) * 128], ident[:])
        S.v(lambda v: v.tensor_copy(gtok[:], ptr[:, 0:1024]))
        S.ld(xs[:], x_in[t0:t0 + 128, :])
        S.s(lambda s_: s_.activation(gsil[:], gtok[:], AF.Silu))
        for h in range(L3NH):
            sl = slice(h * 256, (h + 1) * 256)
            S.v(lambda v, sl=sl: v.tensor_reduce(stt[:, 0:1], otl[:, sl], axis=AX.X,
                                                 op=ALU.add))
            S.v(lambda v: v.tensor_scalar(stt[:, 0:1], stt[:, 0:1], 1.0 / 256.0,
                                          None, ALU.mult))
            S.v(lambda v, sl=sl: v.tensor_mul(f32a[:, 0:256], otl[:, sl], otl[:, sl]))
            S.v(lambda v: v.tensor_reduce(stt[:, 1:2], f32a[:, 0:256], axis=AX.X,
                                          op=ALU.add))
            S.v(lambda v: v.tensor_scalar(stt[:, 1:2], stt[:, 1:2], 1.0 / 256.0,
                                          None, ALU.mult))
            S.v(lambda v: v.tensor_mul(stt[:, 2:3], stt[:, 0:1], stt[:, 0:1]))
            S.v(lambda v: v.tensor_sub(stt[:, 1:2], stt[:, 1:2], stt[:, 2:3]))
            S.v(lambda v: v.tensor_scalar(stt[:, 1:2], stt[:, 1:2], 1e-5, None, ALU.add))
            S.s(lambda s_: s_.activation(stt[:, 3:4], stt[:, 1:2], AF.Ln))
            S.s(lambda s_: s_.activation(stt[:, 3:4], stt[:, 3:4], AF.Exp,
                                         bias=0.0, scale=-0.5))
            S.v(lambda v, sl=sl: v.tensor_scalar(ogn[:, sl], otl[:, sl], stt[:, 0:1],
                                                 stt[:, 3:4], ALU.subtract, ALU.mult))
        S.v(lambda v: v.tensor_mul(ogn[:], ogn[:], gnwb[:]))
        S.v(lambda v: v.tensor_add(ogn[:], ogn[:], gnbb[:]))
        S.v(lambda v: v.tensor_mul(ogn[:], ogn[:], gsil[:]))
        S.v(lambda v: v.tensor_copy(otl[:], ogn[:]))
        for k in range(NDC):
            S.tr(ptr[:, k * 128:(k + 1) * 128], otl[:, k * 128:(k + 1) * 128], ident[:])
        S.v(lambda v: v.tensor_copy(ooT[:], ptr[:, 0:1024]))
        for dgrp in range(2):
            S.mmg([(p0[:, 0:512], ooT[:, k * 128:(k + 1) * 128],
                    wq2[:, k * 1024 + dgrp * 512:k * 1024 + dgrp * 512 + 512])
                   for k in range(NDC)])
            S.v(lambda v, dgrp=dgrp: v.tensor_add(osb[:, dgrp * 512:(dgrp + 1) * 512],
                                                  p0[:, 0:512],
                                                  xs[:, dgrp * 512:(dgrp + 1) * 512]))
        S.st(x_out[t0:t0 + 128, :], osb[:])

    return S.build(nc)


def maps_l3a(inputs, xsh):
    def chunkify(a):
        a = f32(a)
        w = a.shape[1]
        return np.ascontiguousarray(
            a.reshape(8, 128, w).transpose(1, 0, 2).reshape(128, 8 * w).astype(BF))
    g = lambda n: f32(inputs["L3_attn_" + n])
    xow_ = g("x_out_w")
    xow = np.concatenate([hT(xow_[:, n * 32:(n + 1) * 32]) for n in range(5)], 1)
    xbs_ = g("x_bias")
    xbs = np.concatenate([xbs_[n].reshape(8, 128).T for n in range(5)], 1)
    ones32 = np.ones((32, 32), np.float32)
    return [{"x": xsh[c],
             "xpm": np.ascontiguousarray(g("x_proj_mu").reshape(8, 128).T.copy()),
             "xw1": chunkify(hT(g("xl_w1"))),
             "xw2": np.ascontiguousarray(hT(g("xl_w2")).astype(BF)),
             "xb2": np.ascontiguousarray(g("xl_b2")[:128].reshape(128, 1)),
             "xb2c": np.ascontiguousarray(g("xl_b2")[128:].reshape(32, 1)),
             "xow": np.ascontiguousarray(xow.astype(BF)),
             "xbs": np.ascontiguousarray(xbs),
             "rwp": chunkify(hT(g("r_w"))), "kwp": chunkify(hT(g("k_w"))),
             "vwp": chunkify(hT(g("v_w"))), "gwp": chunkify(hT(g("g_w"))),
             "ww1": chunkify(hT(g("w_w1"))),
             "ww2": np.ascontiguousarray(hT(g("w_w2")).astype(BF)),
             "wbc": np.ascontiguousarray(g("w_b").reshape(4, 128).T.copy()),
             "bon": np.ascontiguousarray(g("bonus").T.copy()),
             "gnw": np.ascontiguousarray(np.broadcast_to(g("gn_weight"), (128, 1024))),
             "gnb": np.ascontiguousarray(np.broadcast_to(g("gn_bias"), (128, 1024))),
             "owp": chunkify(hT(g("o_w"))),
             "mq32": np.ascontiguousarray(np.triu(ones32, 1).astype(BF)),
             "id32": np.ascontiguousarray(np.eye(32, dtype=np.float32).astype(BF)),
             "idm": _IDM} for c in range(NCORE)]



_NCS = {}


def _get_nc(name, builder):
    if name not in _NCS:
        _NCS[name] = builder()
    return _NCS[name]


_IDM = np.eye(128, dtype=BF)


def maps_l0f(inputs, xsh):
    gwT = hT(f32(inputs["L0_ffn_gate_w"]))
    weT = np.ascontiguousarray(np.transpose(f32(inputs["L0_ffn_expert_w"]), (0, 2, 1)))
    return [{"x": xsh[c], "gwT": gwT, "weT": weT, "idm": _IDM} for c in range(NCORE)]


def run_phase(name, inputs, xsh):
    builder, mapper = PHASES[name]
    nc = _get_nc(name, builder)
    res = run_bass_kernel_spmd(nc, mapper(inputs, xsh), core_ids=list(range(NCORE)))
    out = np.stack([r["xo"] for r in res.results])
    return out, res


PHASES = {
    "l0f": (build_ph_l0f, maps_l0f),
    "l2f": (build_ph_l2f, maps_l2f),
    "l0a": (build_ph_l0a, maps_l0a),
    "l1a": (build_ph_l1a, maps_l1a),
    "l3f": (build_ph_l3f, maps_l3f),
    "l2a": (build_ph_l2a, maps_l2a),
    "l3a": (build_ph_l3a, maps_l3a),
}


def run_l1f(inputs, xsh):
    nc1 = _get_nc("l1f_pre", build_ph_l1f_pre)
    r1 = run_bass_kernel_spmd(nc1, maps_l1f_pre(inputs, xsh), core_ids=list(range(NCORE)))
    lnp_sh = [r["lnp"] for r in r1.results]
    lnpT_sh = [r["lnpT"] for r in r1.results]
    lnp_full = np.ascontiguousarray(np.concatenate(lnp_sh, 0))
    lnpT_full = np.ascontiguousarray(np.concatenate([r["lnfT"] for r in r1.results], 0))
    nc2 = _get_nc("l1f_mid", build_ph_l1f_mid)
    r2 = run_bass_kernel_spmd(nc2, maps_l1f_mid(
        inputs, {"lnp": lnp_full, "lnpT": lnpT_full}), core_ids=list(range(NCORE)))
    eo_all = np.ascontiguousarray(np.concatenate([r["eo"] for r in r2.results], 0))
    aux = {"eo_all": eo_all,
           "slot": [r2.results[c]["slot"][c * TPC:(c + 1) * TPC] for c in range(NCORE)],
           "wk4": [r2.results[c]["wk4"][c * TPC:(c + 1) * TPC] for c in range(NCORE)],
           "lnp_sh": lnp_sh, "lnpT_sh": lnpT_sh}
    nc3 = _get_nc("l1f_post", build_ph_l1f_post)
    r3 = run_bass_kernel_spmd(nc3, maps_l1f_post(inputs, xsh, aux),
                              core_ids=list(range(NCORE)))
    return np.stack([r["xo"] for r in r3.results])

CHAIN = ["l0f", "l0a", "l0f", "L1F", "l1a", "L1F",
         "l2f", "l2a", "l2f", "l3f", "l3a", "l3f"]


def kernel(**inputs):
    x = f32(inputs["L0_attn_x"]).reshape(NCORE, TPC, D)
    for name in CHAIN:
        if name == "L1F":
            x = run_l1f(inputs, x)
        else:
            x, _ = run_phase(name, inputs, x)
    return np.ascontiguousarray(np.asarray(x, np.float32)).reshape(BATCH, SEQ, D)
